# version 02
# baseline (speedup 1.0000x reference)
import sys
import numpy as np
import concourse.bass as bass
import concourse.mybir as mybir
from concourse.bass_utils import run_bass_kernel_spmd

F32 = mybir.dt.float32
BF16 = mybir.dt.bfloat16
ALU = mybir.AluOpType
ACT = mybir.ActivationFunctionType
AXL = mybir.AxisListType

T = 2048
TH = T // 2
TC = 512
D = 1024
DI = 2048
DS = 16
DTR = 64
NH = 16
DH = 64
NE = 8
EC = 512
FF = 2816
FFC = FF // 128
EPS = 1e-5
RMAGIC = float(np.float32(1.5 * 2.0**23))
QBE = float(np.float32(128.0) - np.float32(1e-5))
ENGINES = ("tensor", "vector", "scalar", "gpsimd", "sync")


class Tok:
    __slots__ = ("eng", "idx")

    def __init__(self, eng, idx):
        self.eng, self.idx = eng, idx


class Sched:

    def __init__(self, nc):
        self.nc = nc
        self.ops = {e: [] for e in ENGINES}
        self.sem = {e: nc.alloc_semaphore(f"sem_{e}") for e in ENGINES}
        self.dsem = {e: nc.alloc_semaphore(f"dsem_{e}") for e in ENGINES}
        self.fence_buf = nc.alloc_sbuf_tensor("sched_fence", [128, 4],
                                              mybir.dt.float32)
        self.junk_sem = nc.alloc_semaphore("sem_junk")

    def op(self, eng, fn, deps=()):
        self.ops[eng].append([fn, list(deps), False, False,
                              sys._getframe(1).f_lineno])
        return Tok(eng, len(self.ops[eng]) - 1)

    def dma(self, eng, out, in_, deps=(), **kw):
        assert out is not None and in_ is not None
        fn = lambda e, out=out, in_=in_, kw=kw: e.dma_start(out=out, in_=in_, **kw)
        self.ops[eng].append([fn, list(deps), False, True,
                              sys._getframe(1).f_lineno])
        return Tok(eng, len(self.ops[eng]) - 1)

    def finalize(self, block):
        for e in ENGINES:
            for rec in self.ops[e]:
                if rec[3]:
                    rec[2] = True
                for t in rec[1]:
                    self.ops[t.eng][t.idx][2] = True
        self.incval = {e: [] for e in ENGINES}
        self.dincval = {e: [] for e in ENGINES}
        for e in ENGINES:
            c = dc = 0
            for rec in self.ops[e]:
                if rec[2]:
                    if rec[3]:
                        dc += 16
                    else:
                        c += 1
                self.incval[e].append(c)
                self.dincval[e].append(dc)

        GAPCOL = {"vector": 0, "scalar": 1, "gpsimd": 2}

        def emitter(eng_name):
            def run(eng):
                waited = {}
                empos = {}
                pos = 0

                def gap_op():
                    col = GAPCOL[eng_name]
                    fb = self.fence_buf[:, col:col + 1]
                    if eng_name == "scalar":
                        g = eng.activation(
                            out=fb, in_=fb,
                            func=mybir.ActivationFunctionType.Copy,
                            bias=0.0, scale=1.0)
                    else:
                        g = eng.memset(fb, 0.0)
                    g.then_inc(self.junk_sem, 1)
                    return g

                for i, (fn, deps, inc, is_dma, _ln) in enumerate(
                        self.ops[eng_name]):
                    for t in deps:
                        t_isdma = self.ops[t.eng][t.idx][3]
                        if (t.eng == eng_name and not t_isdma
                                and eng_name not in GAPCOL):
                            continue
                        if t_isdma:
                            sem, val, key = (self.dsem[t.eng],
                                             self.dincval[t.eng][t.idx],
                                             ("d", t.eng))
                        else:
                            sem, val, key = (self.sem[t.eng],
                                             self.incval[t.eng][t.idx],
                                             ("c", t.eng))
                        if waited.get(key, -1) < val:
                            eng.wait_ge(sem, val)
                            waited[key] = val
                    inst = fn(eng)
                    empos[i] = pos
                    pos += 1
                    if inc:
                        if eng_name == "vector" and not is_dma:
                            try:
                                small = any(int(o.ap[0][1]) < 128
                                            for o in inst.ins.outs)
                            except Exception:
                                small = True
                            if small:
                                inst = eng.memset(
                                    self.fence_buf[:, 0:1], 0.0)
                                pos += 1
                        inst.then_inc(self.dsem[eng_name] if is_dma
                                      else self.sem[eng_name],
                                      16 if is_dma else 1)
            return run

        for name in ENGINES:
            if self.ops[name]:
                getattr(block, name)(emitter(name))



def _t(a):
    return np.ascontiguousarray(np.asarray(a, dtype=np.float32).T)


def _c(a, shape=None):
    a = np.asarray(a, dtype=np.float32)
    if shape is not None:
        a = a.reshape(shape)
    return np.ascontiguousarray(a)


def _v(a):
    return np.ascontiguousarray(np.asarray(a, np.float32).reshape(-1, 128))


def prep_inputs(inp):
    w = {}
    w["w_ln1g"] = _v(inp["L0_attn_ln1_g"])
    w["w_ln1b"] = _v(inp["L0_attn_ln1_b"])
    w["w_ln2g"] = _v(inp["L0_attn_ln2_g"])
    w["w_ln2b"] = _v(inp["L0_attn_ln2_b"])
    w["w_W1T"] = _t(inp["L0_attn_W1"])
    w["w_b1"] = _v(inp["L0_attn_b1"])
    w["w_W2T"] = _t(inp["L0_attn_W2"])
    w["w_b2"] = _v(inp["L0_attn_b2"])
    w["w_inpT"] = _t(inp["L0_attn_in_proj_w"])
    w["w_convW"] = _c(inp["L0_attn_conv_w"], (DI, 4))
    w["w_convB"] = _v(inp["L0_attn_conv_b"])
    xp = np.asarray(inp["L0_attn_x_proj_w"], np.float32)
    xpp = np.zeros((128, DI), np.float32)
    xpp[0:64] = xp[0:64]
    xpp[64:80] = xp[64:80]
    xpp[96:112] = xp[80:96]
    w["w_xprojT"] = _t(xpp)
    w["w_dtprojT"] = _t(inp["L0_attn_dt_proj_w"])
    w["w_dtb"] = _v(inp["L0_attn_dt_proj_b"])
    w["w_Alog"] = _c(inp["L0_attn_A_log"])
    w["w_Dp"] = _v(inp["L0_attn_D_param"])
    w["w_outpT"] = _t(inp["L0_attn_out_proj_w"])
    w["w_clipx"] = _c(inp["L0_ffn_clip_x"], (1, 1))
    w["w_clipd"] = _c(inp["L0_ffn_clip_down"], (1, 1))
    w["w_wg"] = _c(inp["L0_ffn_w_gate"])
    w["w_wu"] = _c(inp["L0_ffn_w_up"])
    w["w_wd"] = _c(inp["L0_ffn_w_down"])
    Wa = np.asarray(inp["L1_attn_W_attn"], dtype=np.float32)
    w["w_WqT"] = _t(Wa[0:D])
    w["w_WkT"] = _t(Wa[D:2 * D])
    w["w_WvT"] = _t(Wa[2 * D:3 * D])
    w["w_WpT"] = _t(inp["L1_attn_W_proj"])
    w["w_gw"] = _c(inp["L1_ffn_gate_w"])
    w["w_moe1"] = _c(inp["L1_ffn_w1"])
    w["w_moeb1"] = _c(inp["L1_ffn_b1"], (NE * EC // 128, 128))
    w["w_moe2"] = _c(inp["L1_ffn_w2"])
    w["w_moeb2"] = _c(inp["L1_ffn_b2"], (NE * D // 128, 128))
    xs = [np.ascontiguousarray(np.asarray(inp["L0_attn_x"][c], np.float32).T)
          for c in range(8)]
    return w, xs


def alibi_slopes(n=16):
    import math
    start = 2.0 ** (-(2.0 ** -(math.log2(n) - 3)))
    return [start * start ** i for i in range(n)]


DRAM_SPECS = [
    ("w_ln1g", [8, 128]), ("w_ln1b", [8, 128]), ("w_ln2g", [8, 128]),
    ("w_ln2b", [8, 128]), ("w_W1T", [D, DI]), ("w_b1", [16, 128]),
    ("w_W2T", [DI, D]), ("w_b2", [8, 128]), ("w_inpT", [D, 2 * DI]),
    ("w_convW", [DI, 4]), ("w_convB", [16, 128]), ("w_xprojT", [DI, 128]),
    ("w_dtprojT", [DTR, DI]), ("w_dtb", [16, 128]), ("w_Alog", [DI, DS]),
    ("w_Dp", [16, 128]), ("w_outpT", [DI, D]), ("w_clipx", [1, 1]),
    ("w_clipd", [1, 1]), ("w_wg", [FF, D]), ("w_wu", [FF, D]),
    ("w_wd", [D, FF]), ("w_WqT", [D, D]), ("w_WkT", [D, D]),
    ("w_WvT", [D, D]), ("w_WpT", [D, D]), ("w_gw", [D, NE]),
    ("w_moe1", [NE, D, EC]), ("w_moeb1", [NE * EC // 128, 128]),
    ("w_moe2", [NE, EC, D]), ("w_moeb2", [NE * D // 128, 128]),
]


class Builder:
    def __init__(self, dbg=(), stop=None, drc=False):
        self.dbg = set(dbg) if dbg else set()
        self.stop = stop
        nc = bass.Bass("TRN2", target_bir_lowering=False,
                       detect_race_conditions=drc)
        self.nc = nc
        self.P = {}
        self.final_deps = []
        self.dbg_last = []
        self.tok_dbg = []
        self.din("x_fm", [D, T])
        for n, s in DRAM_SPECS:
            self.din(n, s)
        self.dout("out_fm", [D, T])
        self.dint("h_xacc", [D, T])
        self.dint("h_xq", [D, T], BF16)
        self.dint("h_qxf", [D, T])
        self.dint("h_winp", [D, 2 * DI], BF16)
        self.dint("h_moe1", [NE, D, EC], BF16)
        self.dint("h_moe2", [NE, EC, D], BF16)
        self.dint("cc_in", [4, 16])
        self.dint("cc_out", [4, 16])
        self.S = Sched(nc)
        self.psb_free = {}
        self.tok_cc = []
        self.PSA = nc.alloc_psum_tensor("psall", [128, 4096], F32)
        self.PSB = [self.PSA[:, 512 * i:512 * (i + 1)] for i in range(8)]
        self.scr = nc.alloc_sbuf_tensor("scr", [128, 10240], F32)
        self.scrbf = self.scr.bitcast(BF16)
        self.xnsl = nc.alloc_sbuf_tensor("xnsl", [128, 16384], BF16)
        self.abufA = nc.alloc_sbuf_tensor("abufA", [128, 16384], BF16)
        self.abufC = nc.alloc_sbuf_tensor("abufC", [128, 16384], BF16)
        self.wbufA = nc.alloc_sbuf_tensor("wbufA", [128, 16384], BF16)
        self.wbufB = nc.alloc_sbuf_tensor("wbufB", [128, 16640], BF16)

    def din(self, name, shape, dtype=F32):
        self.P[name] = self.nc.dram_tensor(name, shape, dtype, kind="ExternalInput")

    def dout(self, name, shape, dtype=F32):
        self.P[name] = self.nc.dram_tensor(name, shape, dtype, kind="ExternalOutput")

    def dint(self, name, shape, dtype=F32):
        self.P[name] = self.nc.dram_tensor(name, shape, dtype)

    def sbuf(self, name, shape, dtype=F32):
        self.P[name] = self.nc.alloc_sbuf_tensor(name, shape, dtype)
        return self.P[name]

    def dbg_out(self, tag, ap, deps):
        if tag not in self.dbg:
            return []
        t = self.nc.dram_tensor("dbg_" + tag, list(ap.shape), ap.dtype,
                                kind="ExternalOutput")
        tok = self.S.dma("sync", t.ap(), ap, deps=deps)
        self.final_deps.append(tok)
        self.tok_dbg.append(tok)
        return [tok]

    def dbg_dram(self, tag, dram, shape, deps):
        if tag not in self.dbg:
            return
        S = self.S
        t = self.nc.dram_tensor("dbg_" + tag, list(shape), F32,
                                kind="ExternalOutput")
        stg = self.scr[:, 0:shape[1]]
        last = list(deps) + self.dbg_last
        for r0 in range(0, shape[0], 128):
            l1 = S.dma("sync", stg, dram.ap()[r0:r0 + 128, :], deps=last)
            l2 = S.dma("sync", t.ap()[r0:r0 + 128, :], stg, deps=[l1])
            last = [l2]
        self.dbg_last = last
        self.final_deps.append(last[0])

    def load_cols(self, dram_name, n, psb=3):
        S, P = self.S, self.P
        if "lc_row" not in P:
            self.sbuf("lc_row", [32, 128], F32)
            self.lc_last = []
        row = P["lc_row"]
        col = self.sbuf(dram_name + "_c", [128, n], F32)
        t0 = S.dma("sync", row[0:n, :], P[dram_name].ap(), deps=self.lc_last)
        t1 = S.op("tensor", lambda t, n=n: t.matmul(
            self.PSB[psb][:, 0:n], row[0:n, :], P["id_f"][0:n, 0:n],
            start=True, stop=True),
            deps=[t0] + self.tok_const + self.psb_free.get(psb, []))
        t2 = S.op("scalar", lambda s, n=n: s.activation(
            out=col[:], in_=self.PSB[psb][:, 0:n], func=ACT.Copy,
            bias=0.0, scale=1.0), deps=[t1])
        self.psb_free[psb] = [t2]
        self.lc_last = [t2]
        return t2, col

    def bcast_row_psum(self, psum_ap, row_ap, deps, bf16=False):
        ones = self.P["ones1_bf" if bf16 else "ones1_f"]
        return self.S.op("tensor", lambda t, o=psum_ap, r=row_ap, on=ones:
                         t.matmul(o, on[:], r, start=True, stop=True),
                         deps=deps)

    def collective_max(self, n, deps, name):
        S, P = self.S, self.P
        res = self.sbuf(f"ccres_{name}", [4, 16], F32)
        t1 = S.dma("sync", P["cc_in"].ap(), P["ccst"][:], deps=deps)
        tok_cc = S.op("gpsimd", lambda g: g.collective_compute(
            "AllReduce", ALU.max, replica_groups=[list(range(8))],
            ins=[P["cc_in"].ap().opt()], outs=[P["cc_out"].ap().opt()]),
            deps=[t1])
        t2 = S.dma("sync", res[:], P["cc_out"].ap(), deps=[tok_cc])
        self.tok_cc = [t2]
        return t2, res

    def build(self):
        self.p0_constants()
        self.p1_outer_ln()
        if self.stop == "p1":
            return self.done()
        self.p2_signnet()
        if self.stop == "p2":
            return self.done()
        self.p4_mamba()
        if self.stop == "p4":
            return self.done()
        self.p5_ffn()
        if self.stop == "p5":
            return self.done()
        self.p6_attn()
        if self.stop == "p6":
            return self.done()
        self.p7_moe()
        return self.done()

    def done(self):
        S = self.S
        if self.final_deps:
            S.op("sync", lambda e: e.nop(), deps=self.final_deps)
        with self.nc.Block() as block:
            S.finalize(block)
        return self.nc

    def p0_constants(self):
        S, P = self.S, self.P
        self.sbuf("ones1_bf", [1, 128], BF16)
        self.sbuf("ones1_f", [1, 128], F32)
        self.sbuf("onescol_f", [128, 1], F32)
        self.sbuf("onescol_b", [128, 1], BF16)
        self.sbuf("eps1", [1, 1], F32)
        self.sbuf("id_bf", [128, 128], BF16)
        self.sbuf("id_f", [128, 128], F32)
        self.sbuf("ccst", [4, 16], F32)
        c0 = S.op("vector", lambda v: v.memset(P["ones1_bf"][:], 1.0))
        c1 = S.op("vector", lambda v: v.memset(P["ones1_f"][:], 1.0))
        c2 = S.op("vector", lambda v: v.memset(P["onescol_f"][:], 1.0))
        c2b = S.op("vector", lambda v: v.memset(P["onescol_b"][:], 1.0))
        c3 = S.op("vector", lambda v: v.memset(P["eps1"][:], EPS))
        i0 = S.op("vector", lambda v: v.memset(self.scr[:, 0:128], 1.0))
        i1 = S.op("gpsimd", lambda g: g.affine_select(
            P["id_bf"][:], self.scr[:, 0:128], pattern=[[-1, 128]],
            compare_op=ALU.is_equal, fill=0.0, base=0, channel_multiplier=1),
            deps=[i0])
        i2 = S.op("gpsimd", lambda g: g.affine_select(
            P["id_f"][:], self.scr[:, 0:128], pattern=[[-1, 128]],
            compare_op=ALU.is_equal, fill=0.0, base=0, channel_multiplier=1),
            deps=[i0])
        self.tok_const = [c0, c1, c2, c2b, c3, i1, i2]

    def p1_outer_ln(self):
        S, P = self.S, self.P
        A = self.scr
        xfm = P["x_fm"].ap()
        xa0 = A[:, 0:2048]
        xsq = A[:, 2048:4096]
        prev_mm = []
        for k in range(8):
            tl = S.dma("sync", xa0, xfm[128 * k:128 * (k + 1), :],
                       deps=prev_mm + self.tok_const)
            tsq = S.op("scalar", lambda s: s.activation(
                out=A[:, 2048:4096], in_=A[:, 0:2048], func=ACT.Square,
                bias=0.0, scale=1.0), deps=[tl])
            prev_mm = []
            for tc in range(4):
                m1 = S.op("tensor", lambda t, tc=tc, k=k: t.matmul(
                    self.PSB[tc][0:1, :], P["onescol_f"][:],
                    A[:, 512 * tc:512 * (tc + 1)],
                    start=(k == 0), stop=(k == 7)), deps=[tl, tsq])
                m2 = S.op("tensor", lambda t, tc=tc, k=k: t.matmul(
                    self.PSB[4 + tc][0:1, :], P["onescol_f"][:],
                    A[:, 2048 + 512 * tc:2048 + 512 * (tc + 1)],
                    start=(k == 0), stop=(k == 7)), deps=[tsq])
                prev_mm += [m1, m2]
        mrow = A[0:1, 0:2048]
        vrow = A[0:1, 2048:4096]
        ev = []
        for tc in range(4):
            ev.append(S.op("scalar", lambda s, tc=tc: s.activation(
                out=A[0:1, 512 * tc:512 * (tc + 1)],
                in_=self.PSB[tc][0:1, :], func=ACT.Copy, bias=0.0,
                scale=1.0 / D), deps=prev_mm))
            ev.append(S.op("scalar", lambda s, tc=tc: s.activation(
                out=A[0:1, 2048 + 512 * tc:2048 + 512 * (tc + 1)],
                in_=self.PSB[4 + tc][0:1, :], func=ACT.Copy, bias=0.0,
                scale=1.0 / D), deps=prev_mm))
        for i in range(8):
            self.psb_free[i] = list(ev)

        rt1 = A[0:1, 4096:6144]
        rt2 = A[0:1, 6144:8192]
        rrow = A[0:1, 8192:10240]
        r1row = rt2
        o3 = S.op("vector", lambda v: v.tensor_tensor(
            A[0:1, 4096:6144], mrow, mrow, ALU.mult), deps=ev)
        o4 = S.op("vector", lambda v: v.tensor_sub(
            vrow, vrow, A[0:1, 4096:6144]), deps=[o3])

        def emit_rsqrt(dst, srcv, deps):
            a = S.op("scalar", lambda s: s.activation(
                out=rt1, in_=srcv, func=ACT.Sqrt,
                bias=P["eps1"][:, 0:1], scale=1.0), deps=deps + self.tok_const)
            y0 = S.op("vector", lambda v: v.reciprocal(rt2, rt1), deps=[a])
            b1 = S.op("vector", lambda v: v.tensor_scalar_add(
                rt1, srcv, EPS), deps=[y0])
            b2 = S.op("vector", lambda v: v.tensor_tensor(
                rt1, rt1, rt2, ALU.mult), deps=[b1])
            b3 = S.op("vector", lambda v: v.tensor_tensor(
                rt1, rt1, rt2, ALU.mult), deps=[b2])
            b4 = S.op("vector", lambda v: v.tensor_scalar(
                rt1, rt1, -0.5, 1.5, ALU.mult, ALU.add), deps=[b3])
            return S.op("vector", lambda v: v.tensor_tensor(
                dst, rt1, rt2, ALU.mult), deps=[b4])

        o6 = emit_rsqrt(rrow, vrow, [o4])
        o7 = S.op("vector", lambda v: v.tensor_tensor(
            A[0:1, 4096:6144], rrow, rrow, ALU.mult), deps=[o6])
        o8 = S.op("vector", lambda v: v.tensor_tensor(
            vrow, A[0:1, 4096:6144], vrow, ALU.mult), deps=[o7])
        o10 = emit_rsqrt(r1row, vrow, [o8])

        m_bc = A[:, 0:2048]
        r1_bc = A[:, 6144:8192]
        r_bc = A[:, 8192:10240]
        self.r1_bc = r1_bc
        bc_last = [o10]
        for row, dstb in [(mrow, m_bc), (r1row, r1_bc), (rrow, r_bc)]:
            for tc in range(4):
                mm = self.bcast_row_psum(
                    self.PSB[0][:, :], row[0:1, 512 * tc:512 * (tc + 1)],
                    deps=bc_last + self.tok_const + self.psb_free.get(0, []))
                cp = S.op("scalar", lambda s, dstb=dstb, tc=tc: s.activation(
                    out=dstb[:, 512 * tc:512 * (tc + 1)], in_=self.PSB[0][:],
                    func=ACT.Copy, bias=0.0, scale=1.0), deps=[mm])
                self.psb_free[0] = [cp]
                bc_last = [cp]

        xa = A[:, 2048:4096]
        xnf = A[:, 4096:6144]
        xn_bf = self.xnsl
        mxl = self.sbuf("mxl", [128, 8], F32)
        xn_toks = []
        red_toks = []
        prev = list(bc_last)
        for k in range(8):
            tl = S.dma("sync", xa, xfm[128 * k:128 * (k + 1), :], deps=prev)
            s1 = S.op("vector", lambda v: v.scalar_tensor_tensor(
                xnf, m_bc, -1.0, xa, ALU.mult, ALU.add), deps=[tl])
            s2 = S.op("vector", lambda v: v.tensor_tensor(
                xnf, xnf, r_bc, ALU.mult), deps=[s1])
            s3 = S.op("vector", lambda v, k=k: v.tensor_copy(
                xn_bf[:, T * k:T * (k + 1)], xnf), deps=[s2])
            red_toks.append(S.op("vector", lambda v, k=k: v.tensor_reduce(
                P["mxl"][:, k:k + 1], xnf, AXL.X, ALU.max,
                apply_absolute_value=True), deps=[s2]))
            prev = [s3, red_toks[-1]]
            xn_toks.append(s3)
        self.tok_xn = xn_toks
        mx1 = self.sbuf("mx1", [128, 1], F32)
        rt = S.op("vector", lambda v: v.tensor_reduce(
            P["mx1"][:], P["mxl"][:], AXL.X, ALU.max), deps=red_toks)
        mxrow = self.sbuf("mxrow", [1, 128], F32)
        tmm = S.op("tensor", lambda t: t.matmul(
            self.PSB[2][0:1, 0:128], P["mx1"][:], P["id_f"][:],
            start=True, stop=True),
            deps=[rt] + self.tok_const + self.psb_free.get(2, []))
        cpm = S.op("scalar", lambda s: s.activation(
            out=P["mxrow"][:], in_=self.PSB[2][0:1, 0:128], func=ACT.Copy,
            bias=0.0, scale=1.0), deps=[tmm])
        self.psb_free[2] = [cpm]
        mxall = self.sbuf("mxall", [1, 1], F32)
        pr = S.op("vector", lambda v: v.tensor_reduce(
            P["mxall"][:], P["mxrow"][:], AXL.X, ALU.max), deps=[cpm])
        z1 = S.op("vector", lambda v: v.memset(P["ccst"][:], 0.0), deps=[pr])
        z2 = S.op("vector", lambda v: v.tensor_copy(
            P["ccst"][0:1, 0:1], P["mxall"][:]), deps=[z1])
        self.dbg_out("mxl", P["mxl"][:], red_toks)
        self.dbg_out("mx1", P["mx1"][:], [rt])
        self.dbg_out("mxrow", P["mxrow"][:], [cpm])
        self.dbg_out("mxall", P["mxall"][:], [pr])
        self.dbg_out("ccsts1", P["ccst"][:], [z2])
        cct, ccres = self.collective_max(1, [z2], "s1")

        clipx = self.sbuf("clipx_sb", [1, 1], F32)
        tcl = S.dma("sync", clipx[:], P["w_clipx"].ap())
        q1 = S.op("scalar", lambda s: s.activation(
            out=P["clipx_sb"][:], in_=P["clipx_sb"][:], func=ACT.Sigmoid,
            bias=0.0, scale=1.0), deps=[tcl])
        s1s = self.sbuf("s1s", [1, 2], F32)
        q2 = S.op("vector", lambda v, ccres=ccres: v.tensor_tensor(
            P["s1s"][0:1, 0:1], P["clipx_sb"][:], ccres[0:1, 0:1], ALU.mult),
            deps=[q1, cct])
        q3 = S.op("vector", lambda v: v.tensor_scalar_mul(
            P["s1s"][0:1, 0:1], P["s1s"][0:1, 0:1], 1.0 / 127.0), deps=[q2])
        q4 = S.op("vector", lambda v: v.reciprocal(
            P["s1s"][0:1, 1:2], P["s1s"][0:1, 0:1]), deps=[q3])
        s1c = self.sbuf("s1c", [128, 2], F32)
        mm1 = S.op("tensor", lambda t: t.matmul(
            self.PSB[1][:, 0:2], P["ones1_f"][:], P["s1s"][:],
            start=True, stop=True),
            deps=[q4] + self.tok_const + self.psb_free.get(1, []))
        cp1 = S.op("scalar", lambda s: s.activation(
            out=P["s1c"][:], in_=self.PSB[1][:, 0:2], func=ACT.Copy,
            bias=0.0, scale=1.0), deps=[mm1])
        self.psb_free[1] = [cp1]
        self.tok_s1c = cp1

        xq_bf = self.abufC
        self.tok_xq_spill = []
        prevq = list(prev)
        for k in range(8):
            tl = S.dma("sync", xa, xfm[128 * k:128 * (k + 1), :], deps=prevq)
            u1 = S.op("vector", lambda v: v.scalar_tensor_tensor(
                xnf, m_bc, -1.0, xa, ALU.mult, ALU.add), deps=[tl])
            u2 = S.op("vector", lambda v: v.tensor_tensor(
                xnf, xnf, r_bc, ALU.mult), deps=[u1])
            a1 = S.op("vector", lambda v: v.tensor_scalar(
                xnf, xnf, P["s1c"][:, 1:2], RMAGIC, ALU.mult, ALU.add),
                deps=[u2, cp1])
            a2 = S.op("vector", lambda v: v.tensor_scalar(
                xnf, xnf, -RMAGIC, 127.0, ALU.add, ALU.min), deps=[a1])
            a3 = S.op("vector", lambda v, k=k: v.tensor_scalar(
                xq_bf[:, T * k:T * (k + 1)], xnf, -128.0, None, ALU.max),
                deps=[a2] + self.tok_xq_spill[-1:])
            sp = S.dma("sync", P["h_xq"].ap()[128 * k:128 * (k + 1), :],
                       xq_bf[:, T * k:T * (k + 1)], deps=[a3])
            self.tok_xq_spill.append(sp)
            prevq = [a3]

        self.dbg_out("xn", xn_bf[:, 0:T], [xn_toks[0]])
        self.dbg_out("s1", ccres[:], [cct])
        self.dbg_out("xq", xq_bf[:, 7 * T:8 * T], [self.tok_xq_spill[-1]])

    def p2_signnet(self):
        S, P = self.S, self.P
        A = self.scr
        r1_bc = self.r1_bc
        wld = A[:, 0:2048]
        s_bf = self.scrbf[:, 4096:6144]
        bb1row = A[0:1, 4096:6144]

        xxh_off = [0, 1024, 2048, 3072, 4096, 5120, 8192, 9216]

        def xxh_sl(dc, tc):
            o = xxh_off[dc] + 512 * tc
            return A[:, o:o + 512]

        t_g1, _ = self.load_cols("w_ln1g", 8)
        t_b1l, _ = self.load_cols("w_ln1b", 8)
        t_b2, _ = self.load_cols("w_b2", 8)
        t_b1s, _ = self.load_cols("w_b1", 16)
        lnb_bf = self.sbuf("lnb_bf", [128, 8], BF16)
        t_lnbbf = S.op("vector", lambda v: v.tensor_copy(
            P["lnb_bf"][:], P["w_ln1b_c"][:]), deps=[t_b1l])

        w1g = self.wbufA
        w2s = self.wbufB
        bb_mm = []
        prevk = self.tok_xq_spill[-1:]
        for k in range(8):
            tl = S.dma("sync", wld, P["w_W1T"].ap()[128 * k:128 * (k + 1), :],
                       deps=prevk)
            sg = S.op("scalar", lambda s: s.activation(
                out=s_bf, in_=wld, func=ACT.Sign, bias=0.0, scale=1.0),
                deps=[tl])
            bb_k = []
            for tc in range(4):
                bb_k.append(S.op("tensor", lambda t, tc=tc, k=k: t.matmul(
                    self.PSB[4 + tc][0:1, :], P["lnb_bf"][:, k:k + 1],
                    s_bf[:, 512 * tc:512 * (tc + 1)],
                    start=(k == 0), stop=(k == 7)),
                    deps=[sg, t_lnbbf] + self.psb_free.get(4 + tc, [])))
            g1 = S.op("vector", lambda v, k=k: v.tensor_scalar_mul(
                w1g[:, DI * k:DI * (k + 1)], s_bf,
                P["w_ln1g_c"][:, k:k + 1]), deps=[sg, t_g1])
            bb_mm += bb_k
            prevk = [g1] + bb_k
        evb = []
        for tc in range(4):
            evb.append(S.op("scalar", lambda s, tc=tc: s.activation(
                out=bb1row[0:1, 512 * tc:512 * (tc + 1)],
                in_=self.PSB[4 + tc][0:1, :], func=ACT.Copy, bias=0.0,
                scale=1.0), deps=bb_mm))
        for i in range(4, 8):
            self.psb_free[i] = list(evb)
        bb1c = self.sbuf("bb1c", [128, 16], F32)
        prev_c = list(evb) + self.psb_free.get(4, [])
        for ic in range(16):
            mm = S.op("tensor", lambda t, ic=ic: t.matmul(
                self.PSB[4][:, ic:ic + 1],
                bb1row[0:1, 128 * ic:128 * (ic + 1)], P["ones1_f"][0:1, 0:1],
                start=True, stop=True), deps=prev_c + self.tok_const)
            prev_c = [mm]
        cpb0 = S.op("scalar", lambda s: s.activation(
            out=P["bb1c"][:], in_=self.PSB[4][:, 0:16], func=ACT.Copy,
            bias=0.0, scale=1.0), deps=prev_c)
        self.psb_free[4] = [cpb0]
        cpb = S.op("vector", lambda v: v.tensor_add(
            P["bb1c"][:], P["bb1c"][:], P["w_b1_c"][:]), deps=[cpb0, t_b1s])

        w2toks = []
        for ic in range(16):
            tl = S.dma("sync", wld[:, 0:1024],
                       P["w_W2T"].ap()[128 * ic:128 * (ic + 1), :], deps=prevk)
            sg = S.op("scalar", lambda s, ic=ic: s.activation(
                out=w2s[:, D * ic:D * (ic + 1)], in_=wld[:, 0:1024],
                func=ACT.Sign, bias=0.0, scale=1.0), deps=[tl])
            w2toks.append(sg)
            prevk = [sg]

        xn_bf = self.xnsl
        gated_bf = self.abufA
        qx_bf = self.abufC
        gmx = self.sbuf("gmx", [128, 2], F32)
        gmxt = self.sbuf("gmxt", [128, 2], F32)
        grow = self.sbuf("grow", [2, 128], F32)
        grow2 = self.sbuf("grow2", [2, 1], F32)
        gam_sc = self.sbuf("gam_sc", [1, 2], F32)
        gam_c = self.sbuf("gam_c", [128, 2], F32)
        gtmp = self.PSB[5]
        self.tok_qx = []
        w1deps = [S.op("sync", lambda e: e.nop(),
                       deps=prevk + [cpb] + self.tok_xq_spill)]
        prev_gmm = []
        prev_xmm = list(w1deps)
        g_toks = []
        xx_toks = []
        for h in range(2):
            hoff = TH * h
            g_toks = []
            for ic in range(16):
                for tc in range(2):
                    mms = []
                    for k in range(8):
                        mms.append(S.op("tensor",
                            lambda t, ic=ic, tc=tc, k=k, hoff=hoff: t.matmul(
                                self.PSB[tc][:, :],
                                w1g[:, DI * k + 128 * ic:DI * k + 128 * (ic + 1)],
                                xn_bf[:, T * k + hoff + 512 * tc:
                                      T * k + hoff + 512 * (tc + 1)],
                                start=(k == 0), stop=(k == 7)),
                            deps=(w1deps if (ic == 0 and tc == 0) else []) +
                                 self.psb_free.get(tc, []) + self.tok_xn))
                    e1 = S.op("vector", lambda v, tc=tc, hoff=hoff:
                              v.tensor_tensor(
                                  gtmp, self.PSB[tc][:, :],
                                  r1_bc[:, hoff + 512 * tc:hoff + 512 * (tc + 1)],
                                  ALU.mult),
                              deps=mms + self.psb_free.get(5, []))
                    self.psb_free[tc] = [e1]
                    e2 = S.op("vector", lambda v, ic=ic: v.tensor_scalar(
                        gtmp, gtmp, P["bb1c"][:, ic:ic + 1], 1.0,
                        ALU.add, ALU.min), deps=[e2_dep for e2_dep in (e1, cpb)])
                    e3 = S.op("vector", lambda v, ic=ic, tc=tc: v.tensor_scalar(
                        gated_bf[:, 1024 * ic + 512 * tc:
                                 1024 * ic + 512 * (tc + 1)],
                        gtmp, -1.0, None, ALU.max), deps=[e2] + prev_gmm)
                    self.psb_free[5] = [e3]
                    g_toks.append(e3)
            prev_gmm = []
            xx_toks = []
            red_toks = []
            for dc in range(8):
                for tc in range(2):
                    mms = []
                    for ic in range(16):
                        mms.append(S.op("tensor",
                            lambda t, dc=dc, tc=tc, ic=ic: t.matmul(
                                self.PSB[2 + tc][:, :],
                                w2s[:, D * ic + 128 * dc:D * ic + 128 * (dc + 1)],
                                gated_bf[:, 1024 * ic + 512 * tc:
                                         1024 * ic + 512 * (tc + 1)],
                                start=(ic == 0), stop=(ic == 15)),
                            deps=g_toks + w2toks + self.psb_free.get(2 + tc, [])))
                    sl = xxh_sl(dc, tc)
                    e1 = S.op("vector", lambda v, tc=tc, dc=dc, sl=sl:
                              v.tensor_scalar_add(
                                  sl, self.PSB[2 + tc][:, :],
                                  P["w_b2_c"][:, dc:dc + 1]),
                              deps=mms + [t_b2] + prev_xmm)
                    self.psb_free[2 + tc] = [e1]
                    e2 = S.op("vector", lambda v, dc=dc, tc=tc, hoff=hoff, sl=sl:
                              v.tensor_tensor(
                                  sl, sl,
                                  xn_bf[:, T * dc + hoff + 512 * tc:
                                        T * dc + hoff + 512 * (tc + 1)],
                                  ALU.add), deps=[e1])
                    xx_toks.append(e2)
                    r_ = S.op("vector", lambda v, tc=tc, sl=sl: v.tensor_reduce(
                        P["gmxt"][:, tc:tc + 1], sl, AXL.X, ALU.max,
                        apply_absolute_value=True),
                        deps=[e2] + self.tok_dbg)
                    r2_ = S.op("vector", lambda v, dc=dc, tc=tc: (
                        v.tensor_copy(P["gmx"][:, tc:tc + 1],
                                      P["gmxt"][:, tc:tc + 1]) if dc == 0 else
                        v.tensor_max(P["gmx"][:, tc:tc + 1],
                                     P["gmx"][:, tc:tc + 1],
                                     P["gmxt"][:, tc:tc + 1])), deps=[r_])
                    red_toks.append(r2_)
            prev_xmm = []
            self.dbg_out(f"gmx_{h}", P["gmx"][:, 0:2], red_toks)
            tmm = S.op("tensor", lambda t: t.matmul(
                self.PSB[4][0:2, 0:128], P["gmx"][:, 0:2], P["id_f"][:],
                start=True, stop=True),
                deps=red_toks + self.psb_free.get(4, []) + self.tok_const)
            cpg = S.op("scalar", lambda s: s.activation(
                out=P["grow"][:], in_=self.PSB[4][0:2, 0:128], func=ACT.Copy,
                bias=0.0, scale=1.0), deps=[tmm] + self.tok_dbg)
            self.psb_free[4] = [cpg]
            self.dbg_out(f"grow_{h}", P["grow"][:], [cpg])
            rg = S.op("vector", lambda v: v.tensor_reduce(
                P["grow2"][:], P["grow"][:], AXL.X, ALU.max),
                deps=[cpg] + self.tok_dbg)
            z1 = S.op("vector", lambda v: v.memset(P["ccst"][:], 0.0),
                      deps=[rg] + self.tok_cc + self.tok_dbg)
            z2 = S.op("vector", lambda v: v.tensor_copy(
                P["ccst"][0:2, 0:1], P["grow2"][:]), deps=[z1])
            self.dbg_out(f"grow2_{h}", P["grow2"][:], [rg])
            self.dbg_out(f"ccstg_{h}", P["ccst"][:], [z2])
            cct, ccg = self.collective_max(2, [z2], f"gam{h}")
            self.dbg_out(f"ccg_{h}", ccg[:], [cct])
            s1_ = S.op("tensor", lambda t, ccg=ccg: t.matmul(
                self.PSB[4][0:1, 0:2], ccg[0:2, 0:1], P["id_f"][0:2, 0:2],
                start=True, stop=True), deps=[cct] + self.psb_free.get(4, []))
            s2_ = S.op("scalar", lambda s: s.activation(
                out=P["gam_sc"][:], in_=self.PSB[4][0:1, 0:2], func=ACT.Copy,
                bias=EPS, scale=1.0), deps=[s1_])
            self.psb_free[4] = [s2_]
            s3_ = S.op("vector", lambda v: v.reciprocal(
                P["gam_sc"][:], P["gam_sc"][:]), deps=[s2_])
            s4_ = S.op("vector", lambda v: v.tensor_scalar_mul(
                P["gam_sc"][:], P["gam_sc"][:], 128.0), deps=[s3_])
            s5_ = S.op("tensor", lambda t: t.matmul(
                self.PSB[4][:, 0:2], P["ones1_f"][:], P["gam_sc"][:],
                start=True, stop=True), deps=[s4_] + self.psb_free.get(4, []))
            s6_ = S.op("scalar", lambda s: s.activation(
                out=P["gam_c"][:], in_=self.PSB[4][:, 0:2], func=ACT.Copy,
                bias=0.0, scale=1.0), deps=[s5_])
            self.psb_free[4] = [s6_]
            for dc in range(8):
                for tc in range(2):
                    sl = xxh_sl(dc, tc)
                    q1 = S.op("vector", lambda v, sl=sl, tc=tc: v.tensor_scalar(
                        gtmp, sl, P["gam_c"][:, tc:tc + 1], QBE,
                        ALU.mult, ALU.min),
                        deps=[s6_] + self.psb_free.get(5, []))
                    q2 = S.op("vector", lambda v: v.tensor_scalar(
                        gtmp, gtmp, -QBE, None, ALU.max), deps=[q1])
                    qoff = T * dc + hoff + 512 * tc
                    q3 = S.op("vector", lambda v, qoff=qoff: v.tensor_copy(
                        qx_bf[:, qoff:qoff + 512], gtmp), deps=[q2])
                    q4 = S.op("scalar", lambda s, sl=sl: s.activation(
                        out=sl, in_=gtmp, func=ACT.Copy, bias=0.0, scale=1.0),
                        deps=[q2, q3])
                    q5 = S.dma("sync",
                               P["h_qxf"].ap()[128 * dc:128 * (dc + 1),
                                               hoff + 512 * tc:
                                               hoff + 512 * (tc + 1)],
                               sl, deps=[q4])
                    self.psb_free[5] = [q4]
                    prev_xmm.append(q5)
                    prev_gmm.append(q3)
                    self.tok_qx.append(q3)
                    self.tok_qx.append(q5)

        self.dbg_out("gated", gated_bf[:, 0:1024], g_toks[0:1])
        self.dbg_out("xxh", A[:, 0:1024], xx_toks)
        self.dbg_out("qx", qx_bf[:, 0:T], self.tok_qx)
        self.dbg_out("gamc", gam_c[:, 0:2], self.tok_qx)

    def p4_mamba(self):
        S, P = self.S, self.P
        A = self.scr
        Abf = self.scrbf
        qx_bf = self.abufC
        xs_bf = self.abufA
        yg_bf = self.abufA
        wA = self.wbufA
        wB = self.wbufB

        t_g2, _ = self.load_cols("w_ln2g", 8)
        t_b2l, _ = self.load_cols("w_ln2b", 8)
        t_dtb, _ = self.load_cols("w_dtb", 16)
        t_cvb, _ = self.load_cols("w_convB", 16)
        t_dp, _ = self.load_cols("w_Dp", 16)
        lnb2_bf = self.sbuf("lnb2_bf", [128, 8], BF16)
        t_lnb2 = S.op("vector", lambda v: v.tensor_copy(
            P["lnb2_bf"][:], P["w_ln2b_c"][:]), deps=[t_b2l])
        prev_mm = []
        sq_bf = Abf[:, 0:2048]
        for k in range(8):
            tsq = S.op("scalar", lambda s, k=k: s.activation(
                out=sq_bf, in_=qx_bf[:, T * k:T * (k + 1)], func=ACT.Square,
                bias=0.0, scale=1.0), deps=self.tok_qx + prev_mm)
            prev_mm = []
            for tc in range(4):
                m1 = S.op("tensor", lambda t, tc=tc, k=k: t.matmul(
                    self.PSB[tc][0:1, :], P["onescol_b"][:],
                    qx_bf[:, T * k + 512 * tc:T * k + 512 * (tc + 1)],
                    start=(k == 0), stop=(k == 7)),
                    deps=self.tok_qx + self.psb_free.get(tc, []) + self.tok_const)
                m2 = S.op("tensor", lambda t, tc=tc, k=k: t.matmul(
                    self.PSB[4 + tc][0:1, :], P["onescol_b"][:],
                    sq_bf[:, 512 * tc:512 * (tc + 1)],
                    start=(k == 0), stop=(k == 7)),
                    deps=[tsq] + self.psb_free.get(4 + tc, []))
                prev_mm += [m1, m2]
        m2row = A[0:1, 2048:4096]
        v2row = A[0:1, 4096:6144]
        ev = []
        for tc in range(4):
            ev.append(S.op("scalar", lambda s, tc=tc: s.activation(
                out=m2row[0:1, 512 * tc:512 * (tc + 1)],
                in_=self.PSB[tc][0:1, :], func=ACT.Copy, bias=0.0,
                scale=1.0 / D), deps=prev_mm))
            ev.append(S.op("scalar", lambda s, tc=tc: s.activation(
                out=v2row[0:1, 512 * tc:512 * (tc + 1)],
                in_=self.PSB[4 + tc][0:1, :], func=ACT.Copy, bias=0.0,
                scale=1.0 / D), deps=prev_mm))
        for i in range(8):
            self.psb_free[i] = list(ev)
        rt1 = A[0:1, 6144:8192]
        rt2 = A[0:1, 0:2048]
        o3 = S.op("vector", lambda v: v.tensor_tensor(
            rt1, m2row, m2row, ALU.mult), deps=ev)
        o4 = S.op("vector", lambda v: v.tensor_sub(v2row, v2row, rt1),
                  deps=[o3])
        a_ = S.op("scalar", lambda s: s.activation(
            out=rt1, in_=v2row, func=ACT.Sqrt, bias=P["eps1"][:, 0:1],
            scale=1.0), deps=[o4] + self.tok_const)
        y0 = S.op("vector", lambda v: v.reciprocal(rt2, rt1), deps=[a_])
        b1 = S.op("vector", lambda v: v.tensor_scalar_add(rt1, v2row, EPS),
                  deps=[y0])
        b2 = S.op("vector", lambda v: v.tensor_tensor(rt1, rt1, rt2, ALU.mult),
                  deps=[b1])
        b3 = S.op("vector", lambda v: v.tensor_tensor(rt1, rt1, rt2, ALU.mult),
                  deps=[b2])
        b4 = S.op("vector", lambda v: v.tensor_scalar(
            rt1, rt1, -0.5, 1.5, ALU.mult, ALU.add), deps=[b3])
        r2row_ = S.op("vector", lambda v: v.tensor_tensor(
            v2row, rt1, rt2, ALU.mult), deps=[b4])
        r2row = v2row
        r2_bc = A[:, 7808:9856]
        bc_last = [r2row_]
        for tc in range(4):
            mm = self.bcast_row_psum(
                self.PSB[0][:, :], r2row[0:1, 512 * tc:512 * (tc + 1)],
                deps=bc_last + self.tok_const + self.psb_free.get(0, []))
            cp = S.op("scalar", lambda s, tc=tc: s.activation(
                out=r2_bc[:, 512 * tc:512 * (tc + 1)], in_=self.PSB[0][:],
                func=ACT.Copy, bias=0.0, scale=1.0), deps=[mm])
            self.psb_free[0] = [cp]
            bc_last = [cp]

        wld4 = A[:, 0:4096]
        wi_bf = Abf[:, 8192:12288]
        xnf32 = self.xnsl.bitcast(F32)
        crow = xnf32[0:1, 0:4096]
        bbrow = xnf32[0:1, 4096:8192]
        prevk = list(bc_last) + [o3]
        c_mm = []
        wspill = []
        for k in range(8):
            tl = S.dma("sync", wld4, P["w_inpT"].ap()[128 * k:128 * (k + 1), :],
                       deps=prevk)
            sc1 = S.op("vector", lambda v, k=k: v.tensor_scalar_mul(
                wi_bf, wld4, P["w_ln2g_c"][:, k:k + 1]), deps=[tl, t_g2])
            mm_k = []
            for tc in range(8):
                mm_k.append(S.op("tensor", lambda t, tc=tc, k=k: t.matmul(
                    self.PSB[tc][0:1, :], P["onescol_b"][:],
                    wi_bf[:, 512 * tc:512 * (tc + 1)],
                    start=(k == 0), stop=(k == 7)),
                    deps=[sc1] + self.tok_const +
                         (self.psb_free.get(tc, []) if k == 0 else [])))
            sp = S.dma("sync", P["h_winp"].ap()[128 * k:128 * (k + 1), :],
                       wi_bf, deps=mm_k)
            c_mm += mm_k
            wspill.append(sp)
            prevk = [sp]
        evc = []
        for tc in range(8):
            evc.append(S.op("scalar", lambda s, tc=tc, crow=crow: s.activation(
                out=crow[0:1, 512 * tc:512 * (tc + 1)],
                in_=self.PSB[tc][0:1, :], func=ACT.Copy, bias=0.0,
                scale=1.0), deps=c_mm))
        for i in range(8):
            self.psb_free[i] = list(evc)
        raw_bf = Abf[:, 12288:14336]
        bb_mm = []
        prevk2 = list(evc)
        for k in range(8):
            tl = S.dma("sync", wld4, P["w_inpT"].ap()[128 * k:128 * (k + 1), :],
                       deps=prevk2)
            mm_k = []
            for kh in range(2):
                cr = S.op("vector", lambda v, kh=kh: v.tensor_copy(
                    raw_bf, wld4[:, 2048 * kh:2048 * (kh + 1)]),
                    deps=[tl] + mm_k[-4:])
                for tc in range(4):
                    mm_k.append(S.op("tensor", lambda t, tc=tc, k=k, kh=kh:
                        t.matmul(
                            self.PSB[4 * kh + tc][0:1, :],
                            P["lnb2_bf"][:, k:k + 1],
                            raw_bf[:, 512 * tc:512 * (tc + 1)],
                            start=(k == 0), stop=(k == 7)),
                        deps=[cr, t_lnb2] +
                             (self.psb_free.get(4 * kh + tc, [])
                              if k == 0 else [])))
            bb_mm += mm_k
            prevk2 = [mm_k[-1], tl]
        evb = []
        for tc in range(8):
            evb.append(S.op("scalar", lambda s, tc=tc, bbrow=bbrow: s.activation(
                out=bbrow[0:1, 512 * tc:512 * (tc + 1)],
                in_=self.PSB[tc][0:1, :], func=ACT.Copy, bias=0.0,
                scale=1.0), deps=bb_mm))
        for i in range(8):
            self.psb_free[i] = list(evb)
        bb2c = self.sbuf("bb2c", [128, 32], F32)
        prev_c = list(evb)
        for jc in range(32):
            mm = S.op("tensor", lambda t, jc=jc, bbrow=bbrow: t.matmul(
                self.PSB[4][:, jc:jc + 1],
                bbrow[0:1, 128 * jc:128 * (jc + 1)], P["ones1_f"][0:1, 0:1],
                start=True, stop=True), deps=prev_c + self.tok_const)
            prev_c = [mm]
        cpb2 = S.op("scalar", lambda s: s.activation(
            out=P["bb2c"][:], in_=self.PSB[4][:, 0:32], func=ACT.Copy,
            bias=0.0, scale=1.0), deps=prev_c)
        self.psb_free[4] = [cpb2]
        wfold = []
        prevk3 = [cpb2]
        for k in range(8):
            tlw = S.dma("sync", wi_bf, P["h_winp"].ap()[128 * k:128 * (k + 1), :],
                        deps=prevk3 + wspill)
            ftoks = []
            for tc in range(8):
                mm = self.bcast_row_psum(
                    self.PSB[5][:, :], crow[0:1, 512 * tc:512 * (tc + 1)],
                    deps=[tlw] + self.tok_const + evc +
                         self.psb_free.get(5, []))
                f1 = S.op("vector", lambda v, tc=tc: v.scalar_tensor_tensor(
                    wi_bf[:, 512 * tc:512 * (tc + 1)], self.PSB[5][:, :],
                    -1.0 / D, wi_bf[:, 512 * tc:512 * (tc + 1)],
                    ALU.mult, ALU.add), deps=[mm])
                self.psb_free[5] = [f1]
                ftoks.append(f1)
            sp = S.dma("sync", P["h_winp"].ap()[128 * k:128 * (k + 1), :],
                       wi_bf, deps=ftoks)
            wfold.append(sp)
            prevk3 = [sp]

        self.dint("h_woutbf", [DI, D], BF16)
        wobf = []
        prevk2 = list(prevk)
        for k in range(16):
            tl = S.dma("sync", wld4[:, 0:1024],
                       P["w_outpT"].ap()[128 * k:128 * (k + 1), :],
                       deps=prevk2 + evc)
            cv = S.op("vector", lambda v: v.tensor_copy(
                Abf[:, 4096:5120], wld4[:, 0:1024]), deps=[tl])
            sp = S.dma("sync", P["h_woutbf"].ap()[128 * k:128 * (k + 1), :],
                       Abf[:, 4096:5120], deps=[cv])
            wobf.append(sp)
            prevk2 = [sp, cv]

        xnbf = self.xnsl
        xnf32v = self.xnsl.bitcast(F32)
        xpj = xnbf[:, 0:2048]
        dtpj = xnbf[0:64, 2048:4096]
        eoh = xnbf[0:16, 4096:6144]
        dtin_bf = xnbf[0:64, 6144:6656]
        Bbf = xnbf[0:16, 6656:7168]
        Cbf = xnbf[0:16, 7168:7680]
        carry = xnf32v[:, 3840:4096]
        ctx = xnf32v[:, 4096:4144]
        negA = xnf32v[:, 4144:4400]
        convw = xnf32v[:, 4400:4464]
        prep_done = wfold + [cpb2]
        cw_toks = []
        for dt_ in range(16):
            cw_toks.append(S.dma(
                "sync", convw[:, 4 * dt_:4 * (dt_ + 1)],
                P["w_convW"].ap()[128 * dt_:128 * (dt_ + 1), :],
                deps=prep_done))
        na_toks = []
        for dt_ in range(16):
            tl = S.dma("sync", negA[:, 16 * dt_:16 * (dt_ + 1)],
                       P["w_Alog"].ap()[128 * dt_:128 * (dt_ + 1), :],
                       deps=prep_done)
            t1 = S.op("scalar", lambda s, dt_=dt_: s.activation(
                out=negA[:, 16 * dt_:16 * (dt_ + 1)],
                in_=negA[:, 16 * dt_:16 * (dt_ + 1)], func=ACT.Exp,
                bias=0.0, scale=1.0), deps=[tl])
            na_toks.append(S.op("vector", lambda v, dt_=dt_: v.tensor_scalar_mul(
                negA[:, 16 * dt_:16 * (dt_ + 1)],
                negA[:, 16 * dt_:16 * (dt_ + 1)], -1.0), deps=[t1]))
        xpj_toks = []
        for dt_ in range(16):
            tl = S.dma("sync", A[:, 0:128],
                       P["w_xprojT"].ap()[128 * dt_:128 * (dt_ + 1), :],
                       deps=prep_done + xpj_toks[-1:])
            xpj_toks.append(S.op("vector", lambda v, dt_=dt_: v.tensor_copy(
                xpj[:, 128 * dt_:128 * (dt_ + 1)], A[:, 0:128]), deps=[tl]))
        tl = S.dma("sync", A[0:64, 128:2176], P["w_dtprojT"].ap(),
                   deps=prep_done + xpj_toks[-1:])
        dtpj_tok = S.op("vector", lambda v: v.tensor_copy(
            dtpj[:], A[0:64, 128:2176]), deps=[tl])
        e0 = S.op("vector", lambda v: v.memset(A[0:16, 0:2048], 1.0),
                  deps=xpj_toks + [dtpj_tok])
        e1 = S.op("gpsimd", lambda g: g.affine_select(
            eoh[:], A[0:16, 0:2048], pattern=[[1, 16], [0, 128]],
            compare_op=ALU.is_equal, fill=0.0, base=0, channel_multiplier=-1),
            deps=[e0])
        eoh_tok = e1

        xc = A[:, 0:640]
        dAb = [A[:, 640:1152], A[:, 1152:1664]]
        vb = [A[:, 1664:2176], A[:, 2176:2688]]
        hb = A[:, 2688:3200]
        dtb_ = A[:, 3200:3712]
        dtxb = A[:, 3712:4224]
        yb = A[:, 4224:4736]
        ztb = A[:, 4736:5248]
        gb = A[:, 5248:5760]
        pint = A[:, 5760:7808]
        ctx0 = S.op("vector", lambda v: v.memset(ctx[:, :], 0.0),
                    deps=prep_done)
        prev_ch = [ctx0] + wfold + wobf + [eoh_tok, dtpj_tok] + na_toks +             cw_toks + [t_dtb, t_cvb, t_dp]
        tok_mamba_out = []
        for ch in range(4):
            co = TC * ch
            wl = [S.dma("sync", wA[:, 2048 * k:2048 * (k + 1)],
                        P["h_winp"].ap()[128 * k:128 * (k + 1), 0:2048],
                        deps=prev_ch) for k in range(8)]
            xs_toks = []
            for dt_ in range(16):
                mms = []
                for k in range(8):
                    mms.append(S.op("tensor", lambda t, dt_=dt_, k=k, co=co:
                        t.matmul(
                            self.PSB[0][:, :],
                            wA[:, 2048 * k + 128 * dt_:2048 * k + 128 * (dt_ + 1)],
                            qx_bf[:, T * k + co:T * k + co + TC],
                            start=(k == 0), stop=(k == 7)),
                        deps=wl + self.psb_free.get(0, [])))
                x1 = S.op("vector", lambda v, co=co: v.tensor_tensor(
                    xc[:, 3:515], self.PSB[0][:, :], r2_bc[:, co:co + TC],
                    ALU.mult), deps=mms + bc_last)
                self.psb_free[0] = [x1]
                x2 = S.op("vector", lambda v, dt_=dt_: v.tensor_scalar_add(
                    xc[:, 3:515], xc[:, 3:515], P["bb2c"][:, dt_:dt_ + 1]),
                    deps=[x1, cpb2])
                c0 = S.op("vector", lambda v, dt_=dt_: v.tensor_copy(
                    xc[:, 0:3], ctx[:, 3 * dt_:3 * (dt_ + 1)]),
                    deps=[x2])
                cv1 = S.op("vector", lambda v, dt_=dt_: v.tensor_scalar_mul(
                    dtxb, xc[:, 0:512], convw[:, 4 * dt_:4 * dt_ + 1]),
                    deps=[c0])
                cvs = [cv1]
                for kk in range(1, 4):
                    cvs.append(S.op("vector", lambda v, dt_=dt_, kk=kk:
                        v.scalar_tensor_tensor(
                            dtxb, xc[:, kk:512 + kk],
                            convw[:, 4 * dt_ + kk:4 * dt_ + kk + 1], dtxb,
                            ALU.mult, ALU.add), deps=[cvs[-1]]))
                cx = S.op("vector", lambda v, dt_=dt_: v.tensor_copy(
                    ctx[:, 3 * dt_:3 * (dt_ + 1)], xc[:, 512:515]),
                    deps=cvs)
                sl = S.op("scalar", lambda s, dt_=dt_, ch=ch: s.activation(
                    out=xs_bf[:, 512 * dt_:512 * (dt_ + 1)], in_=dtxb,
                    func=ACT.Silu, bias=P["w_convB_c"][:, dt_:dt_ + 1],
                    scale=1.0), deps=cvs + [t_cvb] + tok_mamba_out)
                xs_toks.append(sl)
                prev_ch = [cx, sl]
            xd_mm = []
            for dt_ in range(16):
                xd_mm.append(S.op("tensor", lambda t, dt_=dt_: t.matmul(
                    self.PSB[1][:, :], xpj[:, 128 * dt_:128 * (dt_ + 1)],
                    xs_bf[:, 512 * dt_:512 * (dt_ + 1)],
                    start=(dt_ == 0), stop=(dt_ == 15)),
                    deps=xs_toks + xpj_toks + self.psb_free.get(1, [])))
            exd1 = S.op("scalar", lambda s: s.activation(
                out=dtin_bf[:, :], in_=self.PSB[1][0:64, :], func=ACT.Copy,
                bias=0.0, scale=1.0), deps=xd_mm)
            exd2 = S.op("scalar", lambda s: s.activation(
                out=Bbf[:, :], in_=self.PSB[1][64:80, :], func=ACT.Copy,
                bias=0.0, scale=1.0), deps=xd_mm)
            exd3 = S.op("scalar", lambda s: s.activation(
                out=Cbf[:, :], in_=self.PSB[1][96:112, :], func=ACT.Copy,
                bias=0.0, scale=1.0), deps=xd_mm)
            self.psb_free[1] = [exd1, exd2, exd3]
            wlz = [S.dma("sync", wA[:, 2048 * k:2048 * (k + 1)],
                         P["h_winp"].ap()[128 * k:128 * (k + 1), 2048:4096],
                         deps=xs_toks + xd_mm) for k in range(8)]
            for dt_ in range(16):
                dmm = S.op("tensor", lambda t, dt_=dt_: t.matmul(
                    self.PSB[2][:, :], dtpj[:, 128 * dt_:128 * (dt_ + 1)],
                    dtin_bf[:, :], start=True, stop=True),
                    deps=[exd1, dtpj_tok] + self.psb_free.get(2, []))
                d0 = S.op("vector", lambda v, dt_=dt_: v.tensor_scalar_add(
                    ztb, self.PSB[2][:, :], P["w_dtb_c"][:, dt_:dt_ + 1]),
                    deps=[dmm, t_dtb])
                self.psb_free[2] = [d0]
                d1 = S.op("scalar", lambda s: s.activation(
                    out=gb, in_=ztb, func=ACT.Abs, bias=0.0, scale=1.0),
                    deps=[d0])
                d2 = S.op("scalar", lambda s: s.activation(
                    out=gb, in_=gb, func=ACT.Exp, bias=0.0, scale=-1.0),
                    deps=[d1])
                d3 = S.op("scalar", lambda s: s.activation(
                    out=gb, in_=gb, func=ACT.Ln, bias=1.0, scale=1.0),
                    deps=[d2])
                d4 = S.op("vector", lambda v: v.tensor_scalar_max(
                    dtb_, ztb, 0.0), deps=[d1])
                dact = S.op("vector", lambda v: v.tensor_add(
                    dtb_, dtb_, gb), deps=[d3, d4])
                dtx = S.op("vector", lambda v, dt_=dt_: v.tensor_tensor(
                    dtxb, dtb_, xs_bf[:, 512 * dt_:512 * (dt_ + 1)],
                    ALU.mult), deps=[dact] + xs_toks)
                ytoks = []
                for s_ in range(16):
                    pb = s_ % 2
                    dAx = dAb[pb]
                    vx = vb[pb]
                    da = S.op("scalar", lambda s, s_=s_, dt_=dt_, dAx=dAx:
                        s.activation(
                            out=dAx, in_=dtb_, func=ACT.Exp,
                            scale=negA[:, 16 * dt_ + s_:16 * dt_ + s_ + 1],
                            bias=0.0), deps=[dact] + na_toks + ytoks[-2:])
                    bm = S.op("tensor", lambda t, s_=s_: t.matmul(
                        self.PSB[3][:, :], eoh[:, 128 * s_:128 * (s_ + 1)],
                        Bbf[:, :], start=True, stop=True),
                        deps=[exd2, eoh_tok] + self.psb_free.get(3, []))
                    vv = S.op("vector", lambda v, vx=vx: v.tensor_tensor(
                        vx, dtxb, self.PSB[3][:, :], ALU.mult),
                        deps=[dtx, bm])
                    self.psb_free[3] = [vv]
                    init = 0.0 if ch == 0 else carry[:, 16 * dt_ + s_:
                                                     16 * dt_ + s_ + 1]
                    sc = S.op("vector", lambda v, dAx=dAx, vx=vx, init=init:
                        v.tensor_tensor_scan(hb, dAx, vx, init,
                                             ALU.mult, ALU.add),
                        deps=[da, vv])
                    cs = S.op("vector", lambda v, dt_=dt_, s_=s_: v.tensor_copy(
                        carry[:, 16 * dt_ + s_:16 * dt_ + s_ + 1],
                        hb[:, 511:512]), deps=[sc])
                    cm = S.op("tensor", lambda t, s_=s_: t.matmul(
                        self.PSB[4][:, :], eoh[:, 128 * s_:128 * (s_ + 1)],
                        Cbf[:, :], start=True, stop=True),
                        deps=[exd3, eoh_tok] + self.psb_free.get(4, []))
                    pp = S.op("vector", lambda v, s_=s_: v.tensor_tensor(
                        pint.rearrange("p (l g) -> p l g", g=4)[:, :, s_ % 4],
                        hb, self.PSB[4][:, :], ALU.mult), deps=[sc, cm])
                    self.psb_free[4] = [pp]
                    ytoks.append(pp)
                    if s_ % 4 == 3:
                        grp = s_ // 4
                        if grp == 0:
                            rd = S.op("vector", lambda v: v.tensor_reduce(
                                yb, pint.rearrange("p (l g) -> p l g", g=4),
                                AXL.X, ALU.add), deps=ytoks[-4:])
                        else:
                            rd = S.op("vector", lambda v: v.tensor_reduce(
                                dAb[0], pint.rearrange("p (l g) -> p l g", g=4),
                                AXL.X, ALU.add), deps=ytoks[-4:])
                            rd = S.op("vector", lambda v: v.tensor_add(
                                yb, yb, dAb[0]), deps=[rd])
                        ytoks.append(rd)
                yd = S.op("vector", lambda v, dt_=dt_: v.scalar_tensor_tensor(
                    yb, xs_bf[:, 512 * dt_:512 * (dt_ + 1)],
                    P["w_Dp_c"][:, dt_:dt_ + 1], yb, ALU.mult, ALU.add),
                    deps=ytoks + [t_dp])
                zmm = []
                for k in range(8):
                    zmm.append(S.op("tensor", lambda t, dt_=dt_, k=k, co=co:
                        t.matmul(
                            self.PSB[5][:, :],
                            wA[:, 2048 * k + 128 * dt_:2048 * k + 128 * (dt_ + 1)],
                            qx_bf[:, T * k + co:T * k + co + TC],
                            start=(k == 0), stop=(k == 7)),
                        deps=wlz + self.psb_free.get(5, [])))
                z1 = S.op("vector", lambda v, co=co: v.tensor_tensor(
                    ztb, self.PSB[5][:, :], r2_bc[:, co:co + TC], ALU.mult),
                    deps=zmm)
                self.psb_free[5] = [z1]
                z2 = S.op("scalar", lambda s, dt_=dt_: s.activation(
                    out=gb, in_=ztb, func=ACT.Silu,
                    bias=P["bb2c"][:, 16 + dt_:16 + dt_ + 1], scale=1.0),
                    deps=[z1, cpb2])
                yg = S.op("vector", lambda v, dt_=dt_: v.tensor_tensor(
                    yg_bf[:, 8192 + 512 * dt_:8192 + 512 * (dt_ + 1)],
                    yb, gb, ALU.mult), deps=[yd, z2])
                tok_mamba_out.append(yg)
            wlo = [S.dma("sync", wB[:, 1024 * k:1024 * (k + 1)],
                         P["h_woutbf"].ap()[128 * k:128 * (k + 1), :],
                         deps=wobf + tok_mamba_out[-1:]) for k in range(16)]
            op_toks = []
            for dc in range(8):
                mms = []
                for k in range(16):
                    mms.append(S.op("tensor", lambda t, dc=dc, k=k: t.matmul(
                        self.PSB[6][:, :],
                        wB[:, 1024 * k + 128 * dc:1024 * k + 128 * (dc + 1)],
                        yg_bf[:, 8192 + 512 * k:8192 + 512 * (k + 1)],
                        start=(k == 0), stop=(k == 15)),
                        deps=wlo + tok_mamba_out[-16:] +
                             self.psb_free.get(6, [])))
                tl = S.dma("sync", A[:, 0:512],
                           P["x_fm"].ap()[128 * dc:128 * (dc + 1), co:co + TC],
                           deps=op_toks[-1:] + mms[:1])
                tlq = S.dma("sync", A[:, 512:1024],
                            P["h_qxf"].ap()[128 * dc:128 * (dc + 1), co:co + TC],
                            deps=op_toks[-1:] + mms[:1] + self.tok_qx)
                ad = S.op("vector", lambda v: v.tensor_add(
                    A[:, 0:512], A[:, 0:512], A[:, 512:1024]),
                    deps=[tl, tlq])
                ad2 = S.op("vector", lambda v: v.tensor_add(
                    A[:, 0:512], A[:, 0:512], self.PSB[6][:, :]),
                    deps=[ad] + mms)
                self.psb_free[6] = [ad2]
                sp = S.dma("sync",
                           P["h_xacc"].ap()[128 * dc:128 * (dc + 1), co:co + TC],
                           A[:, 0:512], deps=[ad2])
                op_toks.append(sp)
            prev_ch = op_toks + [cx]
            self.tok_mamba_spill = op_toks

        self.dbg_out("xs", xs_bf[:, 0:8192], xs_toks)
        self.dbg_out("dtc", dtb_, tok_mamba_out)
        self.dbg_out("ygate", yg_bf[:, 8192:16384], tok_mamba_out)
        self.dbg_dram("xacc", P["h_xacc"], [D, T], self.tok_mamba_spill)
        self.dbg_dram("qxf", P["h_qxf"], [D, T],
                      self.tok_mamba_spill + self.tok_qx)


    def p5_ffn(self):
        S, P = self.S, self.P
        A = self.scr
        Abf = self.scrbf
        xq = self.abufC
        psbf = self.PSA.bitcast(BF16)

        def h_sl(c):
            if c < 8:
                return self.abufA[:, 2048 * c:2048 * (c + 1)]
            if c < 16:
                return self.wbufA[:, 2048 * (c - 8):2048 * (c - 7)]
            return self.wbufB[:, 2048 * (c - 16):2048 * (c - 15)]

        last_m = self.tok_mamba_spill
        xq_toks = [S.dma("sync", xq[:, T * k:T * (k + 1)],
                         P["h_xq"].ap()[128 * k:128 * (k + 1), :],
                         deps=last_m + self.tok_xq_spill) for k in range(8)]
        clipd = self.sbuf("clipd_sb", [1, 1], F32)
        tcd = S.dma("sync", clipd[:], P["w_clipd"].ap())
        qd = S.op("scalar", lambda s: s.activation(
            out=P["clipd_sb"][:], in_=P["clipd_sb"][:], func=ACT.Sigmoid,
            bias=0.0, scale=1.0), deps=[tcd])

        wnat = A[:, 0:2816]
        wqbf = [Abf[:, 5632:7680], Abf[:, 7680:9728]]
        wql = [Abf[:, 9728:10752], Abf[:, 10752:11776]]
        sg_ = A[:, 5888:6400]
        up_ = A[:, 6400:6912]
        rcol = self.sbuf("rcol5", [128, 1], F32)
        comb = self.sbuf("comb5", [128, 2], F32)
        hmax = self.sbuf("hmax", [128, 1], F32)
        hmaxt = self.sbuf("hmaxt", [128, 1], F32)
        hm0 = S.op("vector", lambda v: v.memset(P["hmax"][:], 0.0))
        h_toks = []
        prev_w = list(xq_toks)
        prev_h = []
        for c in range(FFC):
            ev_toks = []
            for wi, wdram in [(0, "w_wg"), (1, "w_wu")]:
                tl = S.dma("sync", wnat[:, 0:1024],
                           P[wdram].ap()[128 * c:128 * (c + 1), :],
                           deps=prev_w)
                r0 = S.op("vector", lambda v: v.tensor_reduce(
                    P["rcol5"][:], wnat[:, 0:1024], AXL.X, ALU.max,
                    apply_absolute_value=True), deps=[tl])
                c0 = S.op("vector", lambda v, wi=wi: v.tensor_scalar_mul(
                    P["comb5"][:, wi:wi + 1], P["rcol5"][:], 1.0 / 127.0),
                    deps=[r0])
                c1 = S.op("vector", lambda v, wi=wi: v.tensor_tensor(
                    P["comb5"][:, wi:wi + 1], P["comb5"][:, wi:wi + 1],
                    P["s1c"][:, 0:1], ALU.mult), deps=[c0, self.tok_s1c])
                rr = S.op("vector", lambda v: v.reciprocal(
                    P["rcol5"][:], P["rcol5"][:]), deps=[c0])
                rr2 = S.op("vector", lambda v: v.tensor_scalar_mul(
                    P["rcol5"][:], P["rcol5"][:], 127.0), deps=[rr])
                a1 = S.op("vector", lambda v: v.tensor_scalar(
                    wnat[:, 0:1024], wnat[:, 0:1024], P["rcol5"][:, 0:1],
                    RMAGIC, ALU.mult, ALU.add), deps=[rr2])
                a2 = S.op("vector", lambda v: v.tensor_scalar(
                    wnat[:, 0:1024], wnat[:, 0:1024], -RMAGIC, 127.0,
                    ALU.add, ALU.min), deps=[a1])
                a3 = S.op("vector", lambda v, wi=wi: v.tensor_scalar(
                    wqbf[wi][:, 0:1024], wnat[:, 0:1024],
                    -128.0, None, ALU.max), deps=[a2])
                prev_w = [a3]
                tt = []
                for hc in range(8):
                    tt.append(S.op("tensor", lambda t, hc=hc, wi=wi: t.transpose(
                        psbf[:, 7168 + 128 * hc:7168 + 128 * (hc + 1)],
                        wqbf[wi][:, 128 * hc:128 * (hc + 1)],
                        P["id_bf"][:]),
                        deps=[a3] + self.tok_const + self.psb_free.get(7, [])))
                ev = S.op("scalar", lambda s, wi=wi: s.activation(
                    out=wql[wi][:, 0:1024], in_=psbf[:, 7168:8192],
                    func=ACT.Copy, bias=0.0, scale=1.0), deps=tt)
                self.psb_free[7] = [ev]
                ev_toks.append((ev, c1))
            for tc in range(4):
                mg = []
                mu = []
                for hc in range(8):
                    mg.append(S.op("tensor", lambda t, tc=tc, hc=hc: t.matmul(
                        self.PSB[0][:, :], wql[0][:, 128 * hc:128 * (hc + 1)],
                        xq[:, T * hc + 512 * tc:T * hc + 512 * (tc + 1)],
                        start=(hc == 0), stop=(hc == 7)),
                        deps=[ev_toks[0][0]] + xq_toks +
                             self.psb_free.get(0, [])))
                    mu.append(S.op("tensor", lambda t, tc=tc, hc=hc: t.matmul(
                        self.PSB[1][:, :], wql[1][:, 128 * hc:128 * (hc + 1)],
                        xq[:, T * hc + 512 * tc:T * hc + 512 * (tc + 1)],
                        start=(hc == 0), stop=(hc == 7)),
                        deps=[ev_toks[1][0]] + xq_toks +
                             self.psb_free.get(1, [])))
                e1 = S.op("scalar", lambda s: s.activation(
                    out=sg_, in_=self.PSB[0][:, :], func=ACT.Silu,
                    scale=P["comb5"][:, 0:1], bias=0.0),
                    deps=mg + [ev_toks[0][1]] + prev_h)
                self.psb_free[0] = [e1]
                e2 = S.op("vector", lambda v: v.tensor_scalar_mul(
                    up_, self.PSB[1][:, :], P["comb5"][:, 1:2]),
                    deps=mu + [ev_toks[1][1]] + prev_h)
                self.psb_free[1] = [e2]
                h1 = S.op("vector", lambda v, c=c, tc=tc: v.tensor_tensor(
                    h_sl(c)[:, 512 * tc:512 * (tc + 1)], sg_, up_, ALU.mult),
                    deps=[e1, e2])
                r1_ = S.op("vector", lambda v, c=c, tc=tc: v.tensor_reduce(
                    P["hmaxt"][:], h_sl(c)[:, 512 * tc:512 * (tc + 1)],
                    AXL.X, ALU.max, apply_absolute_value=True), deps=[h1])
                r2_ = S.op("vector", lambda v: v.tensor_max(
                    P["hmax"][:], P["hmax"][:], P["hmaxt"][:]),
                    deps=[r1_, hm0])
                h_toks.append(h1)
                prev_h = [h1, r2_]
            prev_w = prev_w + prev_h

        tmm = S.op("tensor", lambda t: t.matmul(
            self.PSB[2][0:1, 0:128], P["hmax"][:], P["id_f"][:],
            start=True, stop=True),
            deps=prev_h + self.tok_const + self.psb_free.get(2, []))
        cpm = S.op("scalar", lambda s: s.activation(
            out=P["mxrow"][:], in_=self.PSB[2][0:1, 0:128], func=ACT.Copy,
            bias=0.0, scale=1.0), deps=[tmm])
        self.psb_free[2] = [cpm]
        pr = S.op("vector", lambda v: v.tensor_reduce(
            P["mxall"][:], P["mxrow"][:], AXL.X, ALU.max), deps=[cpm])
        z1 = S.op("vector", lambda v: v.memset(P["ccst"][:], 0.0),
                  deps=[pr] + self.tok_cc)
        z2 = S.op("vector", lambda v: v.tensor_copy(
            P["ccst"][0:1, 0:1], P["mxall"][:]), deps=[z1])
        cct, ccs2 = self.collective_max(1, [z2], "s2")
        s2s = self.sbuf("s2s", [1, 2], F32)
        q2 = S.op("vector", lambda v, ccs2=ccs2: v.tensor_tensor(
            P["s2s"][0:1, 0:1], P["clipd_sb"][:], ccs2[0:1, 0:1], ALU.mult),
            deps=[qd, cct])
        q3 = S.op("vector", lambda v: v.tensor_scalar_mul(
            P["s2s"][0:1, 0:1], P["s2s"][0:1, 0:1], 1.0 / 127.0), deps=[q2])
        q4 = S.op("vector", lambda v: v.reciprocal(
            P["s2s"][0:1, 1:2], P["s2s"][0:1, 0:1]), deps=[q3])
        s2c = self.sbuf("s2c", [128, 2], F32)
        mm1 = S.op("tensor", lambda t: t.matmul(
            self.PSB[3][:, 0:2], P["ones1_f"][:], P["s2s"][:],
            start=True, stop=True),
            deps=[q4] + self.tok_const + self.psb_free.get(3, []))
        cp1 = S.op("scalar", lambda s: s.activation(
            out=P["s2c"][:], in_=self.PSB[3][:, 0:2], func=ACT.Copy,
            bias=0.0, scale=1.0), deps=[mm1])
        self.psb_free[3] = [cp1]

        hq_toks = []
        prevq = [cp1] + h_toks
        for c in range(FFC):
            for tc in range(4):
                a1 = S.op("vector", lambda v, c=c, tc=tc: v.tensor_scalar(
                    sg_, h_sl(c)[:, 512 * tc:512 * (tc + 1)],
                    P["s2c"][:, 1:2], RMAGIC, ALU.mult, ALU.add), deps=prevq)
                a2 = S.op("vector", lambda v: v.tensor_scalar(
                    sg_, sg_, -RMAGIC, 127.0, ALU.add, ALU.min), deps=[a1])
                a3 = S.op("vector", lambda v, c=c, tc=tc: v.tensor_scalar(
                    h_sl(c)[:, 512 * tc:512 * (tc + 1)], sg_,
                    -128.0, None, ALU.max), deps=[a2])
                hq_toks.append(a3)
                prevq = [a3]

        wdl = Abf[:, 9728:15360]
        xacc5 = A[:, 7808:8320]
        combd = self.sbuf("combd", [128, 1], F32)
        self.tok_ffn_acc = []
        prev_d = list(hq_toks)
        for dc in range(8):
            tl = S.dma("sync", wnat, P["w_wd"].ap()[128 * dc:128 * (dc + 1), :],
                       deps=prev_d)
            r0 = S.op("vector", lambda v: v.tensor_reduce(
                P["rcol5"][:], wnat, AXL.X, ALU.max,
                apply_absolute_value=True), deps=[tl])
            c0 = S.op("vector", lambda v: v.tensor_scalar_mul(
                P["combd"][:], P["rcol5"][:], 1.0 / 127.0), deps=[r0])
            c1 = S.op("vector", lambda v: v.tensor_tensor(
                P["combd"][:], P["combd"][:], P["s2c"][:, 0:1], ALU.mult),
                deps=[c0, cp1])
            rr = S.op("vector", lambda v: v.reciprocal(
                P["rcol5"][:], P["rcol5"][:]), deps=[c0])
            rr2 = S.op("vector", lambda v: v.tensor_scalar_mul(
                P["rcol5"][:], P["rcol5"][:], 127.0), deps=[rr])
            a1 = S.op("vector", lambda v: v.tensor_scalar(
                wnat, wnat, P["rcol5"][:, 0:1], RMAGIC, ALU.mult, ALU.add),
                deps=[rr2])
            a2 = S.op("vector", lambda v: v.tensor_scalar(
                wnat, wnat, -RMAGIC, 127.0, ALU.add, ALU.min), deps=[a1])
            a3 = S.op("vector", lambda v: v.tensor_scalar(
                Abf[:, 5632:8448], wnat, -128.0, None, ALU.max), deps=[a2])
            ev_all = []
            for b2 in range(3):
                tt = []
                for ic in range(8):
                    gidx = 8 * b2 + ic
                    if gidx >= FFC:
                        break
                    tt.append(S.op("tensor", lambda t, gidx=gidx: t.transpose(
                        psbf[:, 7168 + 128 * (gidx % 8):
                             7168 + 128 * (gidx % 8 + 1)],
                        Abf[:, 5632 + 128 * gidx:5632 + 128 * (gidx + 1)],
                        P["id_bf"][:]),
                        deps=[a3] + self.tok_const + self.psb_free.get(7, [])))
                n = len(tt)
                ev = S.op("scalar", lambda s, b2=b2, n=n: s.activation(
                    out=wdl[:, 1024 * b2:1024 * b2 + 128 * n],
                    in_=psbf[:, 7168:7168 + 128 * n],
                    func=ACT.Copy, bias=0.0, scale=1.0), deps=tt)
                self.psb_free[7] = [ev]
                ev_all.append(ev)
            for tc in range(4):
                mms = []
                for ic in range(FFC):
                    mms.append(S.op("tensor", lambda t, tc=tc, ic=ic: t.matmul(
                        self.PSB[4][:, :],
                        wdl[:, 1024 * (ic // 8) + 128 * (ic % 8):
                            1024 * (ic // 8) + 128 * (ic % 8 + 1)],
                        h_sl(ic)[:, 512 * tc:512 * (tc + 1)],
                        start=(ic == 0), stop=(ic == FFC - 1)),
                        deps=ev_all + hq_toks + self.psb_free.get(4, [])))
                tl2 = S.dma("sync", xacc5,
                            P["h_xacc"].ap()[128 * dc:128 * (dc + 1),
                                             512 * tc:512 * (tc + 1)],
                            deps=self.tok_mamba_spill + self.tok_ffn_acc[-1:])
                ad = S.op("vector", lambda v: v.scalar_tensor_tensor(
                    xacc5, self.PSB[4][:, :], P["combd"][:, 0:1], xacc5,
                    ALU.mult, ALU.add), deps=mms + [tl2, c1])
                self.psb_free[4] = [ad]
                sp = S.dma("sync",
                           P["h_xacc"].ap()[128 * dc:128 * (dc + 1),
                                            512 * tc:512 * (tc + 1)],
                           xacc5, deps=[ad])
                self.tok_ffn_acc.append(sp)
            prev_d = [a3] + self.tok_ffn_acc[-1:]

        self.dbg_out("hq", h_sl(0), hq_toks)
        self.dbg_out("s2", ccs2[:], [cct])
        self.dbg_dram("xacc5", P["h_xacc"], [D, T], self.tok_ffn_acc)

    def p6_attn(self):
        S, P = self.S, self.P
        A = self.scr
        Abf = self.scrbf
        xn1 = self.abufA
        Qb = self.abufC
        Kb = self.wbufA
        Va = self.wbufB
        slopes = alibi_slopes(NH)

        xa = A[:, 0:2048]
        xsq = A[:, 2048:4096]
        prev_mm = list(self.tok_ffn_acc)
        for k in range(8):
            tl = S.dma("sync", xa, P["h_xacc"].ap()[128 * k:128 * (k + 1), :],
                       deps=prev_mm + self.tok_ffn_acc)
            tsq = S.op("scalar", lambda s: s.activation(
                out=xsq, in_=xa, func=ACT.Square, bias=0.0, scale=1.0),
                deps=[tl])
            prev_mm = []
            for tc in range(4):
                m1 = S.op("tensor", lambda t, tc=tc, k=k: t.matmul(
                    self.PSB[tc][0:1, :], P["onescol_f"][:],
                    xa[:, 512 * tc:512 * (tc + 1)],
                    start=(k == 0), stop=(k == 7)),
                    deps=[tl, tsq] + (self.psb_free.get(tc, []) if k == 0 else []))
                m2 = S.op("tensor", lambda t, tc=tc, k=k: t.matmul(
                    self.PSB[4 + tc][0:1, :], P["onescol_f"][:],
                    xsq[:, 512 * tc:512 * (tc + 1)],
                    start=(k == 0), stop=(k == 7)),
                    deps=[tsq] + (self.psb_free.get(4 + tc, []) if k == 0 else []))
                prev_mm += [m1, m2]
        mrow = A[0:1, 0:2048]
        vrow = A[0:1, 2048:4096]
        ev = []
        for tc in range(4):
            ev.append(S.op("scalar", lambda s, tc=tc: s.activation(
                out=mrow[0:1, 512 * tc:512 * (tc + 1)],
                in_=self.PSB[tc][0:1, :], func=ACT.Copy, bias=0.0,
                scale=1.0 / D), deps=prev_mm))
            ev.append(S.op("scalar", lambda s, tc=tc: s.activation(
                out=vrow[0:1, 512 * tc:512 * (tc + 1)],
                in_=self.PSB[4 + tc][0:1, :], func=ACT.Copy, bias=0.0,
                scale=1.0 / D), deps=prev_mm))
        for i in range(8):
            self.psb_free[i] = list(ev)
        rt1 = A[0:1, 4096:6144]
        rt2 = A[0:1, 6144:8192]
        o3 = S.op("vector", lambda v: v.tensor_tensor(
            rt1, mrow, mrow, ALU.mult), deps=ev)
        o4 = S.op("vector", lambda v: v.tensor_sub(vrow, vrow, rt1), deps=[o3])
        a_ = S.op("scalar", lambda s: s.activation(
            out=rt1, in_=vrow, func=ACT.Sqrt, bias=P["eps1"][:, 0:1],
            scale=1.0), deps=[o4] + self.tok_const)
        y0 = S.op("vector", lambda v: v.reciprocal(rt2, rt1), deps=[a_])
        b1 = S.op("vector", lambda v: v.tensor_scalar_add(rt1, vrow, EPS),
                  deps=[y0])
        b2 = S.op("vector", lambda v: v.tensor_tensor(rt1, rt1, rt2, ALU.mult),
                  deps=[b1])
        b3 = S.op("vector", lambda v: v.tensor_tensor(rt1, rt1, rt2, ALU.mult),
                  deps=[b2])
        b4 = S.op("vector", lambda v: v.tensor_scalar(
            rt1, rt1, -0.5, 1.5, ALU.mult, ALU.add), deps=[b3])
        rdone = S.op("vector", lambda v: v.tensor_tensor(
            vrow, rt1, rt2, ALU.mult), deps=[b4])
        rrow = vrow
        m_bc = A[:, 0:2048]
        r_bc = A[:, 2048:4096]
        bc_last = [rdone]
        for row, dstb in [(mrow, m_bc), (rrow, r_bc)]:
            for tc in range(4):
                mm = self.bcast_row_psum(
                    self.PSB[0][:, :], row[0:1, 512 * tc:512 * (tc + 1)],
                    deps=bc_last + self.tok_const + self.psb_free.get(0, []))
                cp = S.op("scalar", lambda s, dstb=dstb, tc=tc: s.activation(
                    out=dstb[:, 512 * tc:512 * (tc + 1)], in_=self.PSB[0][:],
                    func=ACT.Copy, bias=0.0, scale=1.0), deps=[mm])
                self.psb_free[0] = [cp]
                bc_last = [cp]
        xnf = A[:, 4096:6144]
        xn1_toks = []
        prev = list(bc_last)
        for k in range(8):
            tl = S.dma("sync", xa, P["h_xacc"].ap()[128 * k:128 * (k + 1), :],
                       deps=prev)
            s1 = S.op("vector", lambda v: v.scalar_tensor_tensor(
                xnf, m_bc, -1.0, xa, ALU.mult, ALU.add), deps=[tl])
            s2 = S.op("vector", lambda v, k=k: v.tensor_tensor(
                xn1[:, T * k:T * (k + 1)], xnf, r_bc, ALU.mult), deps=[s1])
            prev = [s2]
            xn1_toks.append(s2)
        self.tok_xn1 = xn1_toks

        wstage = Abf[:, 0:8192]

        def cast_w(dram, deps, dst=None):
            dst = wstage if dst is None else dst
            toks = []
            for k in range(8):
                tl = S.dma("sync", A[:, 6144:7168],
                           P[dram].ap()[128 * k:128 * (k + 1), :], deps=deps +
                           toks[-1:])
                toks.append(S.op("vector", lambda v, k=k, dst=dst: v.tensor_copy(
                    dst[:, 1024 * k:1024 * (k + 1)], A[:, 6144:7168]),
                    deps=[tl]))
            return toks

        qk_toks = []
        for dst, wnm in [(Qb, "w_WqT"), (Kb, "w_WkT")]:
            ct = cast_w(wnm, xn1_toks + qk_toks)
            for oc in range(8):
                for tc in range(4):
                    mms = []
                    for k in range(8):
                        mms.append(S.op("tensor",
                            lambda t, oc=oc, tc=tc, k=k: t.matmul(
                                self.PSB[oc % 2][:, :],
                                wstage[:, 1024 * k + 128 * oc:
                                       1024 * k + 128 * (oc + 1)],
                                xn1[:, T * k + 512 * tc:T * k + 512 * (tc + 1)],
                                start=(k == 0), stop=(k == 7)),
                            deps=ct + xn1_toks + self.psb_free.get(oc % 2, [])))
                    e1 = S.op("scalar", lambda s, dst=dst, oc=oc, tc=tc:
                              s.activation(
                                  out=dst[:, 2048 * oc + 512 * tc:
                                          2048 * oc + 512 * (tc + 1)],
                                  in_=self.PSB[oc % 2][:, :], func=ACT.Copy,
                                  bias=0.0, scale=1.0), deps=mms)
                    self.psb_free[oc % 2] = [e1]
                    qk_toks.append(e1)
        ct = cast_w("w_WvT", qk_toks)
        vset = S.op("vector", lambda v: v.memset(
            Va.rearrange("p (tk h o) -> p tk h o", h=NH, o=65)[:, :, :, 64:65],
            1.0), deps=qk_toks)
        v_toks = [vset]
        for tk in range(16):
            for ch in range(2):
                mms = []
                for k in range(8):
                    mms.append(S.op("tensor", lambda t, tk=tk, ch=ch, k=k:
                        t.matmul(
                            self.PSB[2 + ch][:, :],
                            xn1[:, T * k + 128 * tk:T * k + 128 * (tk + 1)],
                            wstage[:, 1024 * k + 512 * ch:
                                   1024 * k + 512 * (ch + 1)],
                            start=(k == 0), stop=(k == 7)),
                        deps=ct + xn1_toks + self.psb_free.get(2 + ch, [])))
                e1 = S.op("scalar", lambda s, tk=tk, ch=ch: s.activation(
                    out=Va.rearrange("p (tk h o) -> p tk h o", h=NH, o=65)
                        [:, tk, 8 * ch:8 * (ch + 1), 0:64],
                    in_=self.PSB[2 + ch][:, :], func=ACT.Copy,
                    bias=0.0, scale=1.0), deps=mms + [vset])
                self.psb_free[2 + ch] = [e1]
                v_toks.append(e1)

        kbase = self.sbuf("kbase", [128, 16], F32)
        kb0 = S.op("gpsimd", lambda g: g.iota(
            P["kbase"][:], pattern=[[128, 16]], base=0, channel_multiplier=1,
            allow_small_or_imprecise_dtypes=True), deps=v_toks)
        masks = Abf[:, 16384:18432]
        maskp = self.sbuf("maskp64", [128, 64], BF16)
        mk = []
        m0 = S.op("vector", lambda v: v.memset(A[:, 9216:9728], 1.0),
                  deps=v_toks + self.tok_mamba_spill)
        for j in range(4):
            mk.append(S.op("gpsimd", lambda g, j=j: g.affine_select(
                masks[:, 512 * j:512 * (j + 1)], A[:, 9216:9728],
                pattern=[[1, 512]], compare_op=ALU.is_ge, fill=0.0,
                base=-128 * j, channel_multiplier=-1), deps=[m0]))
        mk.append(S.op("gpsimd", lambda g: g.affine_select(
            maskp[:], A[:, 9216:9280], pattern=[[1, 64]],
            compare_op=ALU.is_ge, fill=0.0, base=64, channel_multiplier=-1),
            deps=[m0]))
        bias_t = self.sbuf("bias_t", [128, 16], F32)
        ptile = [Abf[:, 18432:19456], Abf[:, 19456:20480]]
        rcrow = self.PSB[7][0:1, 0:512]
        attn_un = Abf[:, 0:16384]

        at_toks = []
        pt_tok = {0: [], 1: []}
        for h in range(NH):
            qB = 64 if h < 2 else (128 if h < 5 else 512)
            nqb = T // qB
            kslice = Kb[64 * (h % 2):64 * (h % 2) + 64, :]
            qslice = Qb[64 * (h % 2):64 * (h % 2) + 64, :]
            for qb in range(nqb):
                qs = qB * qb
                qref = float(qs + qB - 1)
                kb_lo = max(0, (qs - 384) // 128) if qB < 512 else 0
                kb_hi = (qs + qB - 1) // 128
                bt = S.op("vector", lambda v, h=h, qref=qref: v.tensor_scalar(
                    P["bias_t"][:], P["kbase"][:], -qref,
                    float(slopes[h]), ALU.add, ALU.mult),
                    deps=[kb0] + pt_tok[0] + pt_tok[1])
                o_mms = []
                for kb in range(kb_lo, kb_hi + 1):
                    smm = S.op("tensor", lambda t, h=h, qs=qs, qB=qB, kb=kb,
                               kslice=kslice, qslice=qslice: t.matmul(
                        self.PSB[4][:, 0:qB],
                        kslice[:, 2048 * (h // 2) + 128 * kb:
                               2048 * (h // 2) + 128 * (kb + 1)],
                        qslice[:, 2048 * (h // 2) + qs:
                               2048 * (h // 2) + qs + qB],
                        start=True, stop=True),
                        deps=qk_toks + self.psb_free.get(4, []))
                    pb = kb % 2
                    ex = S.op("scalar", lambda s, kb=kb, pb=pb, qB=qB:
                              s.activation(
                                  out=ptile[pb][:, 0:qB],
                                  in_=self.PSB[4][:, 0:qB],
                                  func=ACT.Exp, scale=0.125,
                                  bias=P["bias_t"][:, kb:kb + 1]),
                              deps=[smm, bt] + pt_tok[pb])
                    self.psb_free[4] = [ex]
                    ptok = [ex]
                    if kb == kb_hi and qB < 512:
                        off = qs - 128 * kb
                        msl = (maskp[:, 0:qB] if off == 64 else
                               masks[:, 0:qB])
                        ptok = [S.op("vector", lambda v, pb=pb, qB=qB, msl=msl:
                            v.tensor_tensor(
                                ptile[pb][:, 0:qB], ptile[pb][:, 0:qB],
                                msl, ALU.mult), deps=[ex] + mk)]
                    elif qB == 512 and kb >= 4 * qb:
                        j = kb - 4 * qb
                        ptok = [S.op("vector", lambda v, j=j, pb=pb:
                            v.tensor_tensor(
                                ptile[pb][:, 0:512], ptile[pb][:, 0:512],
                                masks[:, 512 * j:512 * (j + 1)], ALU.mult),
                            deps=[ex] + mk)]
                    omm = S.op("tensor", lambda t, h=h, kb=kb, pb=pb, qB=qB,
                               kb_lo=kb_lo, kb_hi=kb_hi: t.matmul(
                        self.PSB[5][0:65, 0:qB],
                        Va[:, 1040 * kb + 65 * h:1040 * kb + 65 * (h + 1)],
                        ptile[pb][:, 0:qB],
                        start=(kb == kb_lo), stop=(kb == kb_hi)),
                        deps=ptok + v_toks + (self.psb_free.get(5, [])
                                              if kb == kb_lo else []))
                    pt_tok[pb] = [omm]
                    o_mms.append(omm)
                l1 = S.op("scalar", lambda s, qB=qB: s.activation(
                    out=rcrow[0:1, 0:qB], in_=self.PSB[5][64:65, 0:qB],
                    func=ACT.Ln, bias=0.0, scale=1.0),
                    deps=o_mms + self.psb_free.get(7, []))
                self.psb_free[7] = [l1]
                l2 = S.op("scalar", lambda s, qB=qB: s.activation(
                    out=self.xnsl.bitcast(F32)[0:1, 3072:3584][0:1, 0:qB], in_=rcrow[0:1, 0:qB],
                    func=ACT.Exp, bias=0.0, scale=-1.0), deps=[l1])
                bmm = S.op("tensor", lambda t, qB=qB: t.matmul(
                    self.PSB[6][0:64, 0:qB], P["ones1_f"][0:1, 0:64],
                    self.xnsl.bitcast(F32)[0:1, 3072:3584][0:1, 0:qB],
                    start=True, stop=True),
                    deps=[l2] + self.tok_const + self.psb_free.get(6, []))
                asl = attn_un[64 * (h % 2):64 * (h % 2) + 64,
                              2048 * (h // 2) + qs:2048 * (h // 2) + qs + qB]
                ev1 = S.op("scalar", lambda s, asl=asl, qB=qB: s.activation(
                    out=asl, in_=self.PSB[5][0:64, 0:qB], func=ACT.Copy,
                    bias=0.0, scale=1.0), deps=o_mms + [bmm])
                nm = S.op("vector", lambda v, asl=asl, qB=qB: v.tensor_tensor(
                    asl, asl, self.PSB[6][0:64, 0:qB], ALU.mult), deps=[ev1])
                self.psb_free[5] = [ev1]
                self.psb_free[6] = [nm]
                at_toks.append(nm)

        wpst = Qb[:, 0:8192]
        wp = []
        for k in range(8):
            tl = S.dma("sync", A[:, 8192:9216],
                       P["w_WpT"].ap()[128 * k:128 * (k + 1), :],
                       deps=at_toks + wp[-1:])
            wp.append(S.op("vector", lambda v, k=k: v.tensor_copy(
                wpst[:, 1024 * k:1024 * (k + 1)], A[:, 8192:9216]),
                deps=[tl]))
        pr_toks = []
        for dc in range(8):
            for tc in range(4):
                mms = []
                for k in range(8):
                    mms.append(S.op("tensor", lambda t, dc=dc, tc=tc, k=k:
                        t.matmul(
                            self.PSB[0][:, :],
                            wpst[:, 1024 * k + 128 * dc:
                                 1024 * k + 128 * (dc + 1)],
                            attn_un[:, 2048 * k + 512 * tc:
                                    2048 * k + 512 * (tc + 1)],
                            start=(k == 0), stop=(k == 7)),
                        deps=wp + at_toks + self.psb_free.get(0, [])))
                tl = S.dma("sync", A[:, 8704:9216],
                           P["h_xacc"].ap()[128 * dc:128 * (dc + 1),
                                            512 * tc:512 * (tc + 1)],
                           deps=pr_toks[-1:] + self.tok_ffn_acc)
                ad = S.op("vector", lambda v: v.tensor_add(
                    A[:, 8704:9216], A[:, 8704:9216], self.PSB[0][:, :]),
                    deps=[tl] + mms)
                self.psb_free[0] = [ad]
                sp = S.dma("sync",
                           P["h_xacc"].ap()[128 * dc:128 * (dc + 1),
                                            512 * tc:512 * (tc + 1)],
                           A[:, 8704:9216], deps=[ad])
                pr_toks.append(sp)
        self.tok_attn_acc = pr_toks
        self.dbg_out("attnu", self.scrbf[:, 0:2048], at_toks)
        self.dbg_dram("xacc6", P["h_xacc"], [D, T], pr_toks)


    def p7_moe(self):
        S, P = self.S, self.P
        A = self.scr
        Abf = self.scrbf
        xn1 = self.abufA
        hpr = self.abufC
        wst = [Abf[:, 0:8192], Abf[:, 8192:16384]]
        hraw = Abf[:, 16384:17408]
        t_b1, _ = self.load_cols("w_moeb1", 32)
        wc_toks = []
        prev_c = list(self.tok_attn_acc)
        for e in range(NE):
            for k in range(8):
                tl = S.dma("sync", A[:, 0:512],
                           P["w_moe1"].ap()[e, 128 * k:128 * (k + 1), :],
                           deps=prev_c)
                cv = S.op("vector", lambda v: v.tensor_copy(
                    Abf[:, 1024:1536], A[:, 0:512]), deps=[tl])
                sp = S.dma("sync", P["h_moe1"].ap()[e, 128 * k:128 * (k + 1), :],
                           Abf[:, 1024:1536], deps=[cv])
                wc_toks.append(sp)
                prev_c = [sp, cv]
            for ic in range(4):
                tl = S.dma("sync", A[:, 0:1024],
                           P["w_moe2"].ap()[e, 128 * ic:128 * (ic + 1), :],
                           deps=prev_c)
                cv = S.op("vector", lambda v: v.tensor_copy(
                    Abf[:, 2048:3072], A[:, 0:1024]), deps=[tl])
                sp = S.dma("sync", P["h_moe2"].ap()[e, 128 * ic:128 * (ic + 1), :],
                           Abf[:, 2048:3072], deps=[cv])
                wc_toks.append(sp)
                prev_c = [sp, cv]
        gw_toks = []
        for k in range(8):
            tl = S.dma("sync", A[:, 8448:8456],
                       P["w_gw"].ap()[128 * k:128 * (k + 1), :],
                       deps=self.tok_attn_acc + gw_toks[-1:])
            gw_toks.append(S.op("vector", lambda v, k=k: v.tensor_copy(
                self.xnsl[:, 4096:4160][:, 8 * k:8 * (k + 1)], A[:, 8448:8456]), deps=[tl]))

        oh8 = self.xnsl[0:8, 2048:3072]
        z0 = S.op("vector", lambda v: v.memset(A[0:8, 0:1024], 1.0),
                  deps=self.tok_attn_acc)
        zoh = S.op("gpsimd", lambda g: g.affine_select(
            oh8[:], A[0:8, 0:1024], pattern=[[1, 8], [0, 128]],
            compare_op=ALU.is_equal, fill=0.0, base=0, channel_multiplier=-1),
            deps=[z0])

        wr8 = self.xnsl[0:8, 0:2048]
        lg = A[:, 8456:8464]
        pb_ = A[:, 8464:8472]
        col1 = A[:, 8472:8473]
        col2 = A[:, 8473:8474]
        wr_toks = []
        for tk in range(16):
            mms = []
            for k in range(8):
                mms.append(S.op("tensor", lambda t, tk=tk, k=k: t.matmul(
                    self.PSB[0][:, 0:8],
                    xn1[:, T * k + 128 * tk:T * k + 128 * (tk + 1)],
                    self.xnsl[:, 4096:4160][:, 8 * k:8 * (k + 1)],
                    start=(k == 0), stop=(k == 7)),
                    deps=gw_toks + self.tok_xn1 + self.psb_free.get(0, [])))
            g0 = S.op("vector", lambda v: v.tensor_reduce(
                col1, self.PSB[0][:, 0:8], AXL.X, ALU.max), deps=mms)
            g1 = S.op("vector", lambda v: v.tensor_scalar_mul(
                col2, col1, -1.0), deps=[g0])
            g2 = S.op("scalar", lambda s: s.activation(
                out=lg, in_=self.PSB[0][:, 0:8], func=ACT.Exp,
                bias=col2[:, 0:1], scale=1.0), deps=[g1] + mms)
            self.psb_free[0] = [g2]
            g3 = S.op("vector", lambda v: v.tensor_reduce(
                col1, lg, AXL.X, ALU.add), deps=[g2])
            g4 = S.op("vector", lambda v: v.reciprocal(col2, col1), deps=[g3])
            g5 = S.op("vector", lambda v: v.tensor_scalar(
                pb_, lg, col2[:, 0:1], None, ALU.mult), deps=[g4])
            g6 = S.op("vector", lambda v: v.tensor_reduce(
                col1, pb_, AXL.X, ALU.max), deps=[g5])
            g7 = S.op("vector", lambda v: v.tensor_scalar(
                lg, pb_, col1[:, 0:1], None, ALU.is_ge), deps=[g6])
            g8 = S.op("vector", lambda v: v.scalar_tensor_tensor(
                lg, lg, -2.0, pb_, ALU.mult, ALU.add), deps=[g7])
            g9 = S.op("vector", lambda v: v.tensor_reduce(
                col2, lg, AXL.X, ALU.max), deps=[g8])
            ga = S.op("vector", lambda v: v.tensor_scalar(
                lg, pb_, col2[:, 0:1], None, ALU.is_ge), deps=[g9])
            gb = S.op("vector", lambda v: v.tensor_add(
                col1, col1, col2), deps=[ga])
            gc = S.op("vector", lambda v: v.tensor_scalar(
                lg, lg, col1[:, 0:1], None, ALU.mult), deps=[gb])
            gd = S.op("vector", lambda v: v.tensor_copy(
                self.xnsl[:, 4160:4168][:], lg), deps=[gc])
            tmm = S.op("tensor", lambda t: t.transpose(
                self.PSA.bitcast(BF16)[0:8, 2048:2176], self.xnsl[:, 4160:4168][:],
                P["id_bf"][:]), deps=[gd] + self.tok_const +
                self.psb_free.get(1, []))
            ev = S.op("scalar", lambda s, tk=tk: s.activation(
                out=wr8[:, 128 * tk:128 * (tk + 1)],
                in_=self.PSA.bitcast(BF16)[0:8, 2048:2176],
                func=ACT.Copy, bias=0.0, scale=1.0), deps=[tmm])
            self.psb_free[1] = [ev]
            wr_toks.append(ev)

        b2t = self.xnsl[0:8, 3072:4096]
        tb2a = S.dma("sync", A[0:8, 7168:8192], P["w_moeb2"].ap(),
                     deps=self.tok_attn_acc)
        tok_b2t = S.op("vector", lambda v: v.tensor_copy(
            b2t[:], A[0:8, 7168:8192]), deps=[tb2a])
        out_toks = []
        for tr in range(4):
            to = 512 * tr
            h_toks = []
            prev_e = list(wr_toks) + wc_toks[-1:] + out_toks[-1:]
            for e in range(NE):
                st = wst[0]
                wl = []
                for k in range(8):
                    wl.append(S.dma("sync", st[:, 512 * k:512 * (k + 1)],
                                    P["h_moe1"].ap()[e, 128 * k:128 * (k + 1), :],
                                    deps=prev_e))
                emms = []
                for ic in range(4):
                    mms = []
                    for k in range(8):
                        mms.append(S.op("tensor",
                            lambda t, ic=ic, k=k, st=st, to=to: t.matmul(
                                self.PSB[2][:, :],
                                st[:, 512 * k + 128 * ic:512 * k + 128 * (ic + 1)],
                                xn1[:, T * k + to:T * k + to + 512],
                                start=(k == 0), stop=(k == 7)),
                            deps=wl + self.tok_xn1 + self.psb_free.get(2, [])))
                    ge = S.op("scalar", lambda s, e=e, ic=ic: s.activation(
                        out=hraw[:, 0:512], in_=self.PSB[2][:, :],
                        func=ACT.Gelu, bias=P["w_moeb1_c"][:, 4 * e + ic:
                                              4 * e + ic + 1], scale=1.0),
                        deps=mms + [t_b1])
                    self.psb_free[2] = [ge]
                    bmm = S.op("tensor", lambda t, e=e, tr=tr: t.matmul(
                        self.PSB[3][:, :], oh8[:, 128 * e:128 * (e + 1)],
                        wr8[:, 512 * tr:512 * (tr + 1)],
                        start=True, stop=True),
                        deps=[zoh] + wr_toks + self.psb_free.get(3, []))
                    hp = S.op("vector", lambda v, e=e, ic=ic: v.tensor_tensor(
                        hpr[:, 2048 * e + 512 * ic:2048 * e + 512 * (ic + 1)],
                        hraw[:, 0:512], self.PSB[3][:, :], ALU.mult),
                        deps=[ge, bmm])
                    self.psb_free[3] = [hp]
                    h_toks.append(hp)
                    emms += mms
                prev_e = emms
            mm_all = []
            prev_e2 = list(h_toks)
            for e in range(NE):
                st = wst[1]
                wl = [S.dma("sync", st[:, 1024 * ic:1024 * (ic + 1)],
                            P["h_moe2"].ap()[e, 128 * ic:128 * (ic + 1), :],
                            deps=prev_e2) for ic in range(4)]
                emms = []
                for dc in range(8):
                    for ic in range(4):
                        emms.append(S.op("tensor",
                            lambda t, e=e, dc=dc, ic=ic, st=st: t.matmul(
                                self.PSB[dc][:, :],
                                st[:, 1024 * ic + 128 * dc:
                                   1024 * ic + 128 * (dc + 1)],
                                hpr[:, 2048 * e + 512 * ic:
                                    2048 * e + 512 * (ic + 1)],
                                start=(e == 0 and ic == 0), stop=False),
                            deps=wl + h_toks +
                                 (self.psb_free.get(dc, []) if e == 0 and
                                  ic == 0 else [])))
                mm_all += emms
                prev_e2 = emms
            for dc in range(8):
                fmm = S.op("tensor", lambda t, dc=dc, tr=tr: t.matmul(
                    self.PSB[dc][:, :], b2t[:, 128 * dc:128 * (dc + 1)],
                    wr8[:, 512 * tr:512 * (tr + 1)],
                    start=False, stop=True),
                    deps=mm_all + [tok_b2t] + wr_toks)
                tl = S.dma("sync", A[:, 8704:9216],
                           P["h_xacc"].ap()[128 * dc:128 * (dc + 1),
                                            to:to + 512],
                           deps=self.tok_attn_acc + out_toks[-1:])
                ad = S.op("vector", lambda v, dc=dc: v.tensor_add(
                    A[:, 8704:9216], A[:, 8704:9216], self.PSB[dc][:, :]),
                    deps=[tl, fmm])
                self.psb_free[dc] = [ad]
                sp = S.dma("sync",
                           P["out_fm"].ap()[128 * dc:128 * (dc + 1),
                                            to:to + 512],
                           A[:, 8704:9216], deps=[ad])
                out_toks.append(sp)
        self.final_deps += out_toks


_BUILD_CACHE = {}


def _get_nc():
    if "nc" not in _BUILD_CACHE:
        _BUILD_CACHE["nc"] = Builder().build()
    return _BUILD_CACHE["nc"]


def kernel(**inputs):
    w, xs = prep_inputs(inputs)
    nc = _get_nc()
    in_maps = [dict(w, x_fm=xs[c]) for c in range(8)]
    res = run_bass_kernel_spmd(nc, in_maps, core_ids=list(range(8)))
    out = np.stack([r["out_fm"].T for r in res.results], axis=0)
    return np.ascontiguousarray(out.astype(np.float32))


# version 03
# speedup vs baseline: 1.0534x; 1.0534x over previous
import sys
import numpy as np
import concourse.bass as bass
import concourse.mybir as mybir
from concourse.bass_utils import run_bass_kernel_spmd

F32 = mybir.dt.float32
BF16 = mybir.dt.bfloat16
ALU = mybir.AluOpType
ACT = mybir.ActivationFunctionType
AXL = mybir.AxisListType

T = 2048
TH = T // 2
TC = 512
D = 1024
DI = 2048
DS = 16
DTR = 64
NH = 16
DH = 64
NE = 8
EC = 512
FF = 2816
FFC = FF // 128
EPS = 1e-5
RMAGIC = float(np.float32(1.5 * 2.0**23))
QBE = float(np.float32(128.0) - np.float32(1e-5))
ENGINES = ("tensor", "vector", "scalar", "gpsimd", "sync")


class Tok:
    __slots__ = ("eng", "idx")

    def __init__(self, eng, idx):
        self.eng, self.idx = eng, idx


class Sched:

    def __init__(self, nc):
        self.nc = nc
        self.ops = {e: [] for e in ENGINES}
        self.sem = {e: nc.alloc_semaphore(f"sem_{e}") for e in ENGINES}
        self.dsem = {e: nc.alloc_semaphore(f"dsem_{e}") for e in ENGINES}
        self.fence_buf = nc.alloc_sbuf_tensor("sched_fence", [128, 4],
                                              mybir.dt.float32)
        self.junk_sem = nc.alloc_semaphore("sem_junk")

    def op(self, eng, fn, deps=()):
        self.ops[eng].append([fn, list(deps), False, False,
                              sys._getframe(1).f_lineno])
        return Tok(eng, len(self.ops[eng]) - 1)

    def dma(self, eng, out, in_, deps=(), **kw):
        assert out is not None and in_ is not None
        fn = lambda e, out=out, in_=in_, kw=kw: e.dma_start(out=out, in_=in_, **kw)
        self.ops[eng].append([fn, list(deps), False, True,
                              sys._getframe(1).f_lineno])
        return Tok(eng, len(self.ops[eng]) - 1)

    def finalize(self, block):
        for e in ENGINES:
            for rec in self.ops[e]:
                if rec[3]:
                    rec[2] = True
                for t in rec[1]:
                    self.ops[t.eng][t.idx][2] = True
        self.incval = {e: [] for e in ENGINES}
        self.dincval = {e: [] for e in ENGINES}
        for e in ENGINES:
            c = dc = 0
            for rec in self.ops[e]:
                if rec[2]:
                    if rec[3]:
                        dc += 16
                    else:
                        c += 1
                self.incval[e].append(c)
                self.dincval[e].append(dc)

        GAPCOL = {"vector": 0, "scalar": 1, "gpsimd": 2}

        def emitter(eng_name):
            def run(eng):
                waited = {}
                empos = {}
                pos = 0

                def gap_op():
                    col = GAPCOL[eng_name]
                    fb = self.fence_buf[:, col:col + 1]
                    if eng_name == "scalar":
                        g = eng.activation(
                            out=fb, in_=fb,
                            func=mybir.ActivationFunctionType.Copy,
                            bias=0.0, scale=1.0)
                    else:
                        g = eng.memset(fb, 0.0)
                    g.then_inc(self.junk_sem, 1)
                    return g

                for i, (fn, deps, inc, is_dma, _ln) in enumerate(
                        self.ops[eng_name]):
                    for t in deps:
                        t_isdma = self.ops[t.eng][t.idx][3]
                        if (t.eng == eng_name and not t_isdma
                                and eng_name not in GAPCOL):
                            continue
                        if t_isdma:
                            sem, val, key = (self.dsem[t.eng],
                                             self.dincval[t.eng][t.idx],
                                             ("d", t.eng))
                        else:
                            sem, val, key = (self.sem[t.eng],
                                             self.incval[t.eng][t.idx],
                                             ("c", t.eng))
                        if waited.get(key, -1) < val:
                            eng.wait_ge(sem, val)
                            waited[key] = val
                    inst = fn(eng)
                    empos[i] = pos
                    pos += 1
                    if inc:
                        if eng_name == "vector" and not is_dma:
                            try:
                                small = any(int(o.ap[0][1]) < 128
                                            for o in inst.ins.outs)
                            except Exception:
                                small = True
                            if small:
                                inst = eng.memset(
                                    self.fence_buf[:, 0:1], 0.0)
                                pos += 1
                        inst.then_inc(self.dsem[eng_name] if is_dma
                                      else self.sem[eng_name],
                                      16 if is_dma else 1)
            return run

        for name in ENGINES:
            if self.ops[name]:
                getattr(block, name)(emitter(name))



def _t(a):
    return np.ascontiguousarray(np.asarray(a, dtype=np.float32).T)


def _c(a, shape=None):
    a = np.asarray(a, dtype=np.float32)
    if shape is not None:
        a = a.reshape(shape)
    return np.ascontiguousarray(a)


def _v(a):
    return np.ascontiguousarray(np.asarray(a, np.float32).reshape(-1, 128))


def prep_inputs(inp):
    w = {}
    w["w_ln1g"] = _v(inp["L0_attn_ln1_g"])
    w["w_ln1b"] = _v(inp["L0_attn_ln1_b"])
    w["w_ln2g"] = _v(inp["L0_attn_ln2_g"])
    w["w_ln2b"] = _v(inp["L0_attn_ln2_b"])
    w["w_W1T"] = _t(inp["L0_attn_W1"])
    w["w_b1"] = _v(inp["L0_attn_b1"])
    w["w_W2T"] = _t(inp["L0_attn_W2"])
    w["w_b2"] = _v(inp["L0_attn_b2"])
    w["w_inpT"] = _t(inp["L0_attn_in_proj_w"])
    w["w_convW"] = _c(inp["L0_attn_conv_w"], (DI, 4))
    w["w_convB"] = _v(inp["L0_attn_conv_b"])
    xp = np.asarray(inp["L0_attn_x_proj_w"], np.float32)
    xpp = np.zeros((128, DI), np.float32)
    xpp[0:64] = xp[0:64]
    xpp[64:80] = xp[64:80]
    xpp[96:112] = xp[80:96]
    w["w_xprojT"] = _t(xpp)
    w["w_dtprojT"] = _t(inp["L0_attn_dt_proj_w"])
    w["w_dtb"] = _v(inp["L0_attn_dt_proj_b"])
    w["w_Alog"] = _c(inp["L0_attn_A_log"])
    w["w_Dp"] = _v(inp["L0_attn_D_param"])
    w["w_outpT"] = _t(inp["L0_attn_out_proj_w"])
    w["w_clipx"] = _c(inp["L0_ffn_clip_x"], (1, 1))
    w["w_clipd"] = _c(inp["L0_ffn_clip_down"], (1, 1))
    w["w_wg"] = _c(inp["L0_ffn_w_gate"])
    w["w_wu"] = _c(inp["L0_ffn_w_up"])
    w["w_wd"] = _c(inp["L0_ffn_w_down"])
    Wa = np.asarray(inp["L1_attn_W_attn"], dtype=np.float32)
    w["w_WqT"] = _t(Wa[0:D])
    w["w_WkT"] = _t(Wa[D:2 * D])
    w["w_WvT"] = _t(Wa[2 * D:3 * D])
    w["w_WpT"] = _t(inp["L1_attn_W_proj"])
    w["w_gw"] = _c(inp["L1_ffn_gate_w"])
    w["w_moe1"] = _c(inp["L1_ffn_w1"])
    w["w_moeb1"] = _c(inp["L1_ffn_b1"], (NE * EC // 128, 128))
    w["w_moe2"] = _c(inp["L1_ffn_w2"])
    w["w_moeb2"] = _c(inp["L1_ffn_b2"], (NE * D // 128, 128))
    xs = [np.ascontiguousarray(np.asarray(inp["L0_attn_x"][c], np.float32).T)
          for c in range(8)]
    return w, xs


def alibi_slopes(n=16):
    import math
    start = 2.0 ** (-(2.0 ** -(math.log2(n) - 3)))
    return [start * start ** i for i in range(n)]


DRAM_SPECS = [
    ("w_ln1g", [8, 128]), ("w_ln1b", [8, 128]), ("w_ln2g", [8, 128]),
    ("w_ln2b", [8, 128]), ("w_W1T", [D, DI]), ("w_b1", [16, 128]),
    ("w_W2T", [DI, D]), ("w_b2", [8, 128]), ("w_inpT", [D, 2 * DI]),
    ("w_convW", [DI, 4]), ("w_convB", [16, 128]), ("w_xprojT", [DI, 128]),
    ("w_dtprojT", [DTR, DI]), ("w_dtb", [16, 128]), ("w_Alog", [DI, DS]),
    ("w_Dp", [16, 128]), ("w_outpT", [DI, D]), ("w_clipx", [1, 1]),
    ("w_clipd", [1, 1]), ("w_wg", [FF, D]), ("w_wu", [FF, D]),
    ("w_wd", [D, FF]), ("w_WqT", [D, D]), ("w_WkT", [D, D]),
    ("w_WvT", [D, D]), ("w_WpT", [D, D]), ("w_gw", [D, NE]),
    ("w_moe1", [NE, D, EC]), ("w_moeb1", [NE * EC // 128, 128]),
    ("w_moe2", [NE, EC, D]), ("w_moeb2", [NE * D // 128, 128]),
]


class Builder:
    def __init__(self, dbg=(), stop=None, drc=False):
        self.dbg = set(dbg) if dbg else set()
        self.stop = stop
        nc = bass.Bass("TRN2", target_bir_lowering=False,
                       detect_race_conditions=drc)
        self.nc = nc
        self.P = {}
        self.final_deps = []
        self.dbg_last = []
        self.tok_dbg = []
        self.din("x_fm", [D, T])
        for n, s in DRAM_SPECS:
            self.din(n, s)
        self.dout("out_fm", [D, T])
        self.dint("h_xacc", [D, T])
        self.dint("h_xq", [D, T], BF16)
        self.dint("h_qxf", [D, T])
        self.dint("h_winp", [D, 2 * DI], BF16)
        self.dint("h_moe1", [NE, D, EC], BF16)
        self.dint("h_moe2", [NE, EC, D], BF16)
        self.dint("cc_in", [4, 16])
        self.dint("cc_out", [4, 16])
        self.S = Sched(nc)
        self.psb_free = {}
        self.tok_cc = []
        self.PSA = nc.alloc_psum_tensor("psall", [128, 4096], F32)
        self.PSB = [self.PSA[:, 512 * i:512 * (i + 1)] for i in range(8)]
        self.scr = nc.alloc_sbuf_tensor("scr", [128, 10240], F32)
        self.scrbf = self.scr.bitcast(BF16)
        self.xnsl = nc.alloc_sbuf_tensor("xnsl", [128, 16384], BF16)
        self.abufA = nc.alloc_sbuf_tensor("abufA", [128, 16384], BF16)
        self.abufC = nc.alloc_sbuf_tensor("abufC", [128, 16384], BF16)
        self.wbufA = nc.alloc_sbuf_tensor("wbufA", [128, 16384], BF16)
        self.wbufB = nc.alloc_sbuf_tensor("wbufB", [128, 16640], BF16)

    def din(self, name, shape, dtype=F32):
        self.P[name] = self.nc.dram_tensor(name, shape, dtype, kind="ExternalInput")

    def dout(self, name, shape, dtype=F32):
        self.P[name] = self.nc.dram_tensor(name, shape, dtype, kind="ExternalOutput")

    def dint(self, name, shape, dtype=F32):
        self.P[name] = self.nc.dram_tensor(name, shape, dtype)

    def sbuf(self, name, shape, dtype=F32):
        self.P[name] = self.nc.alloc_sbuf_tensor(name, shape, dtype)
        return self.P[name]

    def dbg_out(self, tag, ap, deps):
        if tag not in self.dbg:
            return []
        t = self.nc.dram_tensor("dbg_" + tag, list(ap.shape), ap.dtype,
                                kind="ExternalOutput")
        tok = self.S.dma("sync", t.ap(), ap, deps=deps)
        self.final_deps.append(tok)
        self.tok_dbg.append(tok)
        return [tok]

    def dbg_dram(self, tag, dram, shape, deps):
        if tag not in self.dbg:
            return
        S = self.S
        t = self.nc.dram_tensor("dbg_" + tag, list(shape), F32,
                                kind="ExternalOutput")
        stg = self.scr[:, 0:shape[1]]
        last = list(deps) + self.dbg_last
        for r0 in range(0, shape[0], 128):
            l1 = S.dma("sync", stg, dram.ap()[r0:r0 + 128, :], deps=last)
            l2 = S.dma("sync", t.ap()[r0:r0 + 128, :], stg, deps=[l1])
            last = [l2]
        self.dbg_last = last
        self.final_deps.append(last[0])

    def load_cols(self, dram_name, n, psb=3):
        S, P = self.S, self.P
        if "lc_row" not in P:
            self.sbuf("lc_row", [32, 128], F32)
            self.lc_last = []
        row = P["lc_row"]
        col = self.sbuf(dram_name + "_c", [128, n], F32)
        t0 = S.dma("sync", row[0:n, :], P[dram_name].ap(), deps=self.lc_last)
        t1 = S.op("tensor", lambda t, n=n: t.matmul(
            self.PSB[psb][:, 0:n], row[0:n, :], P["id_f"][0:n, 0:n],
            start=True, stop=True),
            deps=[t0] + self.tok_const + self.psb_free.get(psb, []))
        t2 = S.op("scalar", lambda s, n=n: s.activation(
            out=col[:], in_=self.PSB[psb][:, 0:n], func=ACT.Copy,
            bias=0.0, scale=1.0), deps=[t1])
        self.psb_free[psb] = [t2]
        self.lc_last = [t2]
        return t2, col

    def bcast_row_psum(self, psum_ap, row_ap, deps, bf16=False):
        ones = self.P["ones1_bf" if bf16 else "ones1_f"]
        return self.S.op("tensor", lambda t, o=psum_ap, r=row_ap, on=ones:
                         t.matmul(o, on[:], r, start=True, stop=True),
                         deps=deps)

    def collective_max(self, n, deps, name):
        S, P = self.S, self.P
        res = self.sbuf(f"ccres_{name}", [4, 16], F32)
        t1 = S.dma("sync", P["cc_in"].ap(), P["ccst"][:], deps=deps)
        tok_cc = S.op("gpsimd", lambda g: g.collective_compute(
            "AllReduce", ALU.max, replica_groups=[list(range(8))],
            ins=[P["cc_in"].ap().opt()], outs=[P["cc_out"].ap().opt()]),
            deps=[t1])
        t2 = S.dma("sync", res[:], P["cc_out"].ap(), deps=[tok_cc])
        self.tok_cc = [t2]
        return t2, res

    def build(self):
        self.p0_constants()
        self.p1_outer_ln()
        if self.stop == "p1":
            return self.done()
        self.p2_signnet()
        if self.stop == "p2":
            return self.done()
        self.p4_mamba()
        if self.stop == "p4":
            return self.done()
        self.p5_ffn()
        if self.stop == "p5":
            return self.done()
        self.p6_attn()
        if self.stop == "p6":
            return self.done()
        self.p7_moe()
        return self.done()

    def done(self):
        S = self.S
        if self.final_deps:
            S.op("sync", lambda e: e.nop(), deps=self.final_deps)
        with self.nc.Block() as block:
            S.finalize(block)
        return self.nc

    def p0_constants(self):
        S, P = self.S, self.P
        self.sbuf("ones1_bf", [1, 128], BF16)
        self.sbuf("ones1_f", [1, 128], F32)
        self.sbuf("onescol_f", [128, 1], F32)
        self.sbuf("onescol_b", [128, 1], BF16)
        self.sbuf("eps1", [1, 1], F32)
        self.sbuf("id_bf", [128, 128], BF16)
        self.sbuf("id_f", [128, 128], F32)
        self.sbuf("ccst", [4, 16], F32)
        c0 = S.op("vector", lambda v: v.memset(P["ones1_bf"][:], 1.0))
        c1 = S.op("vector", lambda v: v.memset(P["ones1_f"][:], 1.0))
        c2 = S.op("vector", lambda v: v.memset(P["onescol_f"][:], 1.0))
        c2b = S.op("vector", lambda v: v.memset(P["onescol_b"][:], 1.0))
        c3 = S.op("vector", lambda v: v.memset(P["eps1"][:], EPS))
        i0 = S.op("vector", lambda v: v.memset(self.scr[:, 0:128], 1.0))
        i1 = S.op("gpsimd", lambda g: g.affine_select(
            P["id_bf"][:], self.scr[:, 0:128], pattern=[[-1, 128]],
            compare_op=ALU.is_equal, fill=0.0, base=0, channel_multiplier=1),
            deps=[i0])
        i2 = S.op("gpsimd", lambda g: g.affine_select(
            P["id_f"][:], self.scr[:, 0:128], pattern=[[-1, 128]],
            compare_op=ALU.is_equal, fill=0.0, base=0, channel_multiplier=1),
            deps=[i0])
        self.tok_const = [c0, c1, c2, c2b, c3, i1, i2]

    def p1_outer_ln(self):
        S, P = self.S, self.P
        A = self.scr
        xfm = P["x_fm"].ap()
        xa0 = A[:, 0:2048]
        xsq = A[:, 2048:4096]
        prev_mm = []
        for k in range(8):
            tl = S.dma("sync", xa0, xfm[128 * k:128 * (k + 1), :],
                       deps=prev_mm + self.tok_const)
            tsq = S.op("scalar", lambda s: s.activation(
                out=A[:, 2048:4096], in_=A[:, 0:2048], func=ACT.Square,
                bias=0.0, scale=1.0), deps=[tl])
            prev_mm = []
            for tc in range(4):
                m1 = S.op("tensor", lambda t, tc=tc, k=k: t.matmul(
                    self.PSB[tc][0:1, :], P["onescol_f"][:],
                    A[:, 512 * tc:512 * (tc + 1)],
                    start=(k == 0), stop=(k == 7)), deps=[tl, tsq])
                m2 = S.op("tensor", lambda t, tc=tc, k=k: t.matmul(
                    self.PSB[4 + tc][0:1, :], P["onescol_f"][:],
                    A[:, 2048 + 512 * tc:2048 + 512 * (tc + 1)],
                    start=(k == 0), stop=(k == 7)), deps=[tsq])
                prev_mm += [m1, m2]
        mrow = A[0:1, 0:2048]
        vrow = A[0:1, 2048:4096]
        ev = []
        for tc in range(4):
            ev.append(S.op("scalar", lambda s, tc=tc: s.activation(
                out=A[0:1, 512 * tc:512 * (tc + 1)],
                in_=self.PSB[tc][0:1, :], func=ACT.Copy, bias=0.0,
                scale=1.0 / D), deps=prev_mm))
            ev.append(S.op("scalar", lambda s, tc=tc: s.activation(
                out=A[0:1, 2048 + 512 * tc:2048 + 512 * (tc + 1)],
                in_=self.PSB[4 + tc][0:1, :], func=ACT.Copy, bias=0.0,
                scale=1.0 / D), deps=prev_mm))
        for i in range(8):
            self.psb_free[i] = list(ev)

        rt1 = A[0:1, 4096:6144]
        rt2 = A[0:1, 6144:8192]
        rrow = A[0:1, 8192:10240]
        r1row = rt2
        o3 = S.op("vector", lambda v: v.tensor_tensor(
            A[0:1, 4096:6144], mrow, mrow, ALU.mult), deps=ev)
        o4 = S.op("vector", lambda v: v.tensor_sub(
            vrow, vrow, A[0:1, 4096:6144]), deps=[o3])

        def emit_rsqrt(dst, srcv, deps):
            a = S.op("scalar", lambda s: s.activation(
                out=rt1, in_=srcv, func=ACT.Sqrt,
                bias=P["eps1"][:, 0:1], scale=1.0), deps=deps + self.tok_const)
            y0 = S.op("vector", lambda v: v.reciprocal(rt2, rt1), deps=[a])
            b1 = S.op("vector", lambda v: v.tensor_scalar_add(
                rt1, srcv, EPS), deps=[y0])
            b2 = S.op("vector", lambda v: v.tensor_tensor(
                rt1, rt1, rt2, ALU.mult), deps=[b1])
            b3 = S.op("vector", lambda v: v.tensor_tensor(
                rt1, rt1, rt2, ALU.mult), deps=[b2])
            b4 = S.op("vector", lambda v: v.tensor_scalar(
                rt1, rt1, -0.5, 1.5, ALU.mult, ALU.add), deps=[b3])
            return S.op("vector", lambda v: v.tensor_tensor(
                dst, rt1, rt2, ALU.mult), deps=[b4])

        o6 = emit_rsqrt(rrow, vrow, [o4])
        o7 = S.op("vector", lambda v: v.tensor_tensor(
            A[0:1, 4096:6144], rrow, rrow, ALU.mult), deps=[o6])
        o8 = S.op("vector", lambda v: v.tensor_tensor(
            vrow, A[0:1, 4096:6144], vrow, ALU.mult), deps=[o7])
        o10 = emit_rsqrt(r1row, vrow, [o8])

        m_bc = A[:, 0:2048]
        r1_bc = A[:, 6144:8192]
        r_bc = A[:, 8192:10240]
        self.r1_bc = r1_bc
        bc_last = [o10]
        for row, dstb in [(mrow, m_bc), (r1row, r1_bc), (rrow, r_bc)]:
            for tc in range(4):
                mm = self.bcast_row_psum(
                    self.PSB[0][:, :], row[0:1, 512 * tc:512 * (tc + 1)],
                    deps=bc_last + self.tok_const + self.psb_free.get(0, []))
                cp = S.op("scalar", lambda s, dstb=dstb, tc=tc: s.activation(
                    out=dstb[:, 512 * tc:512 * (tc + 1)], in_=self.PSB[0][:],
                    func=ACT.Copy, bias=0.0, scale=1.0), deps=[mm])
                self.psb_free[0] = [cp]
                bc_last = [cp]

        xa = A[:, 2048:4096]
        xnf = A[:, 4096:6144]
        xn_bf = self.xnsl
        mxl = self.sbuf("mxl", [128, 8], F32)
        xn_toks = []
        red_toks = []
        prev = list(bc_last)
        for k in range(8):
            tl = S.dma("sync", xa, xfm[128 * k:128 * (k + 1), :], deps=prev)
            s1 = S.op("vector", lambda v: v.scalar_tensor_tensor(
                xnf, m_bc, -1.0, xa, ALU.mult, ALU.add), deps=[tl])
            s2 = S.op("vector", lambda v: v.tensor_tensor(
                xnf, xnf, r_bc, ALU.mult), deps=[s1])
            s3 = S.op("vector", lambda v, k=k: v.tensor_copy(
                xn_bf[:, T * k:T * (k + 1)], xnf), deps=[s2])
            red_toks.append(S.op("vector", lambda v, k=k: v.tensor_reduce(
                P["mxl"][:, k:k + 1], xnf, AXL.X, ALU.max,
                apply_absolute_value=True), deps=[s2]))
            prev = [s3, red_toks[-1]]
            xn_toks.append(s3)
        self.tok_xn = xn_toks
        mx1 = self.sbuf("mx1", [128, 1], F32)
        rt = S.op("vector", lambda v: v.tensor_reduce(
            P["mx1"][:], P["mxl"][:], AXL.X, ALU.max), deps=red_toks)
        mxrow = self.sbuf("mxrow", [1, 128], F32)
        tmm = S.op("tensor", lambda t: t.matmul(
            self.PSB[2][0:1, 0:128], P["mx1"][:], P["id_f"][:],
            start=True, stop=True),
            deps=[rt] + self.tok_const + self.psb_free.get(2, []))
        cpm = S.op("scalar", lambda s: s.activation(
            out=P["mxrow"][:], in_=self.PSB[2][0:1, 0:128], func=ACT.Copy,
            bias=0.0, scale=1.0), deps=[tmm])
        self.psb_free[2] = [cpm]
        mxall = self.sbuf("mxall", [1, 1], F32)
        pr = S.op("vector", lambda v: v.tensor_reduce(
            P["mxall"][:], P["mxrow"][:], AXL.X, ALU.max), deps=[cpm])
        z1 = S.op("vector", lambda v: v.memset(P["ccst"][:], 0.0), deps=[pr])
        z2 = S.op("vector", lambda v: v.tensor_copy(
            P["ccst"][0:1, 0:1], P["mxall"][:]), deps=[z1])
        self.dbg_out("mxl", P["mxl"][:], red_toks)
        self.dbg_out("mx1", P["mx1"][:], [rt])
        self.dbg_out("mxrow", P["mxrow"][:], [cpm])
        self.dbg_out("mxall", P["mxall"][:], [pr])
        self.dbg_out("ccsts1", P["ccst"][:], [z2])
        cct, ccres = self.collective_max(1, [z2], "s1")

        clipx = self.sbuf("clipx_sb", [1, 1], F32)
        tcl = S.dma("sync", clipx[:], P["w_clipx"].ap())
        q1 = S.op("scalar", lambda s: s.activation(
            out=P["clipx_sb"][:], in_=P["clipx_sb"][:], func=ACT.Sigmoid,
            bias=0.0, scale=1.0), deps=[tcl])
        s1s = self.sbuf("s1s", [1, 2], F32)
        q2 = S.op("vector", lambda v, ccres=ccres: v.tensor_tensor(
            P["s1s"][0:1, 0:1], P["clipx_sb"][:], ccres[0:1, 0:1], ALU.mult),
            deps=[q1, cct])
        q3 = S.op("vector", lambda v: v.tensor_scalar_mul(
            P["s1s"][0:1, 0:1], P["s1s"][0:1, 0:1], 1.0 / 127.0), deps=[q2])
        q4 = S.op("vector", lambda v: v.reciprocal(
            P["s1s"][0:1, 1:2], P["s1s"][0:1, 0:1]), deps=[q3])
        s1c = self.sbuf("s1c", [128, 2], F32)
        mm1 = S.op("tensor", lambda t: t.matmul(
            self.PSB[1][:, 0:2], P["ones1_f"][:], P["s1s"][:],
            start=True, stop=True),
            deps=[q4] + self.tok_const + self.psb_free.get(1, []))
        cp1 = S.op("scalar", lambda s: s.activation(
            out=P["s1c"][:], in_=self.PSB[1][:, 0:2], func=ACT.Copy,
            bias=0.0, scale=1.0), deps=[mm1])
        self.psb_free[1] = [cp1]
        self.tok_s1c = cp1

        xq_bf = self.abufC
        self.tok_xq_spill = []
        prevq = list(prev)
        for k in range(8):
            tl = S.dma("sync", xa, xfm[128 * k:128 * (k + 1), :], deps=prevq)
            u1 = S.op("vector", lambda v: v.scalar_tensor_tensor(
                xnf, m_bc, -1.0, xa, ALU.mult, ALU.add), deps=[tl])
            u2 = S.op("vector", lambda v: v.tensor_tensor(
                xnf, xnf, r_bc, ALU.mult), deps=[u1])
            a1 = S.op("vector", lambda v: v.tensor_scalar(
                xnf, xnf, P["s1c"][:, 1:2], RMAGIC, ALU.mult, ALU.add),
                deps=[u2, cp1])
            a2 = S.op("vector", lambda v: v.tensor_scalar(
                xnf, xnf, -RMAGIC, 127.0, ALU.add, ALU.min), deps=[a1])
            a3 = S.op("vector", lambda v, k=k: v.tensor_scalar(
                xq_bf[:, T * k:T * (k + 1)], xnf, -128.0, None, ALU.max),
                deps=[a2] + self.tok_xq_spill[-1:])
            sp = S.dma("sync", P["h_xq"].ap()[128 * k:128 * (k + 1), :],
                       xq_bf[:, T * k:T * (k + 1)], deps=[a3])
            self.tok_xq_spill.append(sp)
            prevq = [a3]

        self.dbg_out("xn", xn_bf[:, 0:T], [xn_toks[0]])
        self.dbg_out("s1", ccres[:], [cct])
        self.dbg_out("xq", xq_bf[:, 7 * T:8 * T], [self.tok_xq_spill[-1]])

    def p2_signnet(self):
        S, P = self.S, self.P
        A = self.scr
        r1_bc = self.r1_bc
        wld = A[:, 0:2048]
        s_bf = self.scrbf[:, 4096:6144]
        bb1row = A[0:1, 4096:6144]

        xxh_off = [0, 1024, 2048, 3072, 4096, 5120, 8192, 9216]

        def xxh_sl(dc, tc):
            o = xxh_off[dc] + 512 * tc
            return A[:, o:o + 512]

        t_g1, _ = self.load_cols("w_ln1g", 8)
        t_b1l, _ = self.load_cols("w_ln1b", 8)
        t_b2, _ = self.load_cols("w_b2", 8)
        t_b1s, _ = self.load_cols("w_b1", 16)
        lnb_bf = self.sbuf("lnb_bf", [128, 8], BF16)
        t_lnbbf = S.op("vector", lambda v: v.tensor_copy(
            P["lnb_bf"][:], P["w_ln1b_c"][:]), deps=[t_b1l])

        w1g = self.wbufA
        w2s = self.wbufB
        bb_mm = []
        prevk = self.tok_xq_spill[-1:]
        for k in range(8):
            tl = S.dma("sync", wld, P["w_W1T"].ap()[128 * k:128 * (k + 1), :],
                       deps=prevk)
            sg = S.op("scalar", lambda s: s.activation(
                out=s_bf, in_=wld, func=ACT.Sign, bias=0.0, scale=1.0),
                deps=[tl])
            bb_k = []
            for tc in range(4):
                bb_k.append(S.op("tensor", lambda t, tc=tc, k=k: t.matmul(
                    self.PSB[4 + tc][0:1, :], P["lnb_bf"][:, k:k + 1],
                    s_bf[:, 512 * tc:512 * (tc + 1)],
                    start=(k == 0), stop=(k == 7)),
                    deps=[sg, t_lnbbf] + self.psb_free.get(4 + tc, [])))
            g1 = S.op("vector", lambda v, k=k: v.tensor_scalar_mul(
                w1g[:, DI * k:DI * (k + 1)], s_bf,
                P["w_ln1g_c"][:, k:k + 1]), deps=[sg, t_g1])
            bb_mm += bb_k
            prevk = [g1] + bb_k
        evb = []
        for tc in range(4):
            evb.append(S.op("scalar", lambda s, tc=tc: s.activation(
                out=bb1row[0:1, 512 * tc:512 * (tc + 1)],
                in_=self.PSB[4 + tc][0:1, :], func=ACT.Copy, bias=0.0,
                scale=1.0), deps=bb_mm))
        for i in range(4, 8):
            self.psb_free[i] = list(evb)
        bb1c = self.sbuf("bb1c", [128, 16], F32)
        prev_c = list(evb) + self.psb_free.get(4, [])
        for ic in range(16):
            mm = S.op("tensor", lambda t, ic=ic: t.matmul(
                self.PSB[4][:, ic:ic + 1],
                bb1row[0:1, 128 * ic:128 * (ic + 1)], P["ones1_f"][0:1, 0:1],
                start=True, stop=True), deps=prev_c + self.tok_const)
            prev_c = [mm]
        cpb0 = S.op("scalar", lambda s: s.activation(
            out=P["bb1c"][:], in_=self.PSB[4][:, 0:16], func=ACT.Copy,
            bias=0.0, scale=1.0), deps=prev_c)
        self.psb_free[4] = [cpb0]
        cpb = S.op("vector", lambda v: v.tensor_add(
            P["bb1c"][:], P["bb1c"][:], P["w_b1_c"][:]), deps=[cpb0, t_b1s])

        w2toks = []
        for ic in range(16):
            tl = S.dma("sync", wld[:, 0:1024],
                       P["w_W2T"].ap()[128 * ic:128 * (ic + 1), :], deps=prevk)
            sg = S.op("scalar", lambda s, ic=ic: s.activation(
                out=w2s[:, D * ic:D * (ic + 1)], in_=wld[:, 0:1024],
                func=ACT.Sign, bias=0.0, scale=1.0), deps=[tl])
            w2toks.append(sg)
            prevk = [sg]

        xn_bf = self.xnsl
        gated_bf = self.abufA
        qx_bf = self.abufC
        gmx = self.sbuf("gmx", [128, 2], F32)
        gmxt = self.sbuf("gmxt", [128, 2], F32)
        grow = self.sbuf("grow", [2, 128], F32)
        grow2 = self.sbuf("grow2", [2, 1], F32)
        gam_sc = self.sbuf("gam_sc", [1, 2], F32)
        gam_c = self.sbuf("gam_c", [128, 2], F32)
        gtmp = self.PSB[5]
        self.tok_qx = []
        w1deps = [S.op("sync", lambda e: e.nop(),
                       deps=prevk + [cpb] + self.tok_xq_spill)]
        prev_gmm = []
        prev_xmm = list(w1deps)
        g_toks = []
        xx_toks = []
        for h in range(2):
            hoff = TH * h
            g_toks = []
            for ic in range(16):
                for tc in range(2):
                    mms = []
                    for k in range(8):
                        mms.append(S.op("tensor",
                            lambda t, ic=ic, tc=tc, k=k, hoff=hoff: t.matmul(
                                self.PSB[tc][:, :],
                                w1g[:, DI * k + 128 * ic:DI * k + 128 * (ic + 1)],
                                xn_bf[:, T * k + hoff + 512 * tc:
                                      T * k + hoff + 512 * (tc + 1)],
                                start=(k == 0), stop=(k == 7)),
                            deps=(w1deps if (ic == 0 and tc == 0) else []) +
                                 self.psb_free.get(tc, []) + self.tok_xn))
                    e1 = S.op("vector", lambda v, tc=tc, hoff=hoff:
                              v.tensor_tensor(
                                  gtmp, self.PSB[tc][:, :],
                                  r1_bc[:, hoff + 512 * tc:hoff + 512 * (tc + 1)],
                                  ALU.mult),
                              deps=mms + self.psb_free.get(5, []))
                    self.psb_free[tc] = [e1]
                    e2 = S.op("vector", lambda v, ic=ic: v.tensor_scalar(
                        gtmp, gtmp, P["bb1c"][:, ic:ic + 1], 1.0,
                        ALU.add, ALU.min), deps=[e2_dep for e2_dep in (e1, cpb)])
                    e3 = S.op("vector", lambda v, ic=ic, tc=tc: v.tensor_scalar(
                        gated_bf[:, 1024 * ic + 512 * tc:
                                 1024 * ic + 512 * (tc + 1)],
                        gtmp, -1.0, None, ALU.max), deps=[e2] + prev_gmm)
                    self.psb_free[5] = [e3]
                    g_toks.append(e3)
            prev_gmm = []
            xx_toks = []
            red_toks = []
            for dc in range(8):
                for tc in range(2):
                    mms = []
                    for ic in range(16):
                        mms.append(S.op("tensor",
                            lambda t, dc=dc, tc=tc, ic=ic: t.matmul(
                                self.PSB[2 + tc][:, :],
                                w2s[:, D * ic + 128 * dc:D * ic + 128 * (dc + 1)],
                                gated_bf[:, 1024 * ic + 512 * tc:
                                         1024 * ic + 512 * (tc + 1)],
                                start=(ic == 0), stop=(ic == 15)),
                            deps=g_toks + w2toks + self.psb_free.get(2 + tc, [])))
                    sl = xxh_sl(dc, tc)
                    e1 = S.op("vector", lambda v, tc=tc, dc=dc, sl=sl:
                              v.tensor_scalar_add(
                                  sl, self.PSB[2 + tc][:, :],
                                  P["w_b2_c"][:, dc:dc + 1]),
                              deps=mms + [t_b2] + prev_xmm)
                    self.psb_free[2 + tc] = [e1]
                    e2 = S.op("vector", lambda v, dc=dc, tc=tc, hoff=hoff, sl=sl:
                              v.tensor_tensor(
                                  sl, sl,
                                  xn_bf[:, T * dc + hoff + 512 * tc:
                                        T * dc + hoff + 512 * (tc + 1)],
                                  ALU.add), deps=[e1])
                    xx_toks.append(e2)
                    r_ = S.op("vector", lambda v, tc=tc, sl=sl: v.tensor_reduce(
                        P["gmxt"][:, tc:tc + 1], sl, AXL.X, ALU.max,
                        apply_absolute_value=True),
                        deps=[e2] + self.tok_dbg)
                    r2_ = S.op("vector", lambda v, dc=dc, tc=tc: (
                        v.tensor_copy(P["gmx"][:, tc:tc + 1],
                                      P["gmxt"][:, tc:tc + 1]) if dc == 0 else
                        v.tensor_max(P["gmx"][:, tc:tc + 1],
                                     P["gmx"][:, tc:tc + 1],
                                     P["gmxt"][:, tc:tc + 1])), deps=[r_])
                    red_toks.append(r2_)
            prev_xmm = []
            self.dbg_out(f"gmx_{h}", P["gmx"][:, 0:2], red_toks)
            tmm = S.op("tensor", lambda t: t.matmul(
                self.PSB[4][0:2, 0:128], P["gmx"][:, 0:2], P["id_f"][:],
                start=True, stop=True),
                deps=red_toks + self.psb_free.get(4, []) + self.tok_const)
            cpg = S.op("scalar", lambda s: s.activation(
                out=P["grow"][:], in_=self.PSB[4][0:2, 0:128], func=ACT.Copy,
                bias=0.0, scale=1.0), deps=[tmm] + self.tok_dbg)
            self.psb_free[4] = [cpg]
            self.dbg_out(f"grow_{h}", P["grow"][:], [cpg])
            rg = S.op("vector", lambda v: v.tensor_reduce(
                P["grow2"][:], P["grow"][:], AXL.X, ALU.max),
                deps=[cpg] + self.tok_dbg)
            z1 = S.op("vector", lambda v: v.memset(P["ccst"][:], 0.0),
                      deps=[rg] + self.tok_cc + self.tok_dbg)
            z2 = S.op("vector", lambda v: v.tensor_copy(
                P["ccst"][0:2, 0:1], P["grow2"][:]), deps=[z1])
            self.dbg_out(f"grow2_{h}", P["grow2"][:], [rg])
            self.dbg_out(f"ccstg_{h}", P["ccst"][:], [z2])
            cct, ccg = self.collective_max(2, [z2], f"gam{h}")
            self.dbg_out(f"ccg_{h}", ccg[:], [cct])
            s1_ = S.op("tensor", lambda t, ccg=ccg: t.matmul(
                self.PSB[4][0:1, 0:2], ccg[0:2, 0:1], P["id_f"][0:2, 0:2],
                start=True, stop=True), deps=[cct] + self.psb_free.get(4, []))
            s2_ = S.op("scalar", lambda s: s.activation(
                out=P["gam_sc"][:], in_=self.PSB[4][0:1, 0:2], func=ACT.Copy,
                bias=EPS, scale=1.0), deps=[s1_])
            self.psb_free[4] = [s2_]
            s3_ = S.op("vector", lambda v: v.reciprocal(
                P["gam_sc"][:], P["gam_sc"][:]), deps=[s2_])
            s4_ = S.op("vector", lambda v: v.tensor_scalar_mul(
                P["gam_sc"][:], P["gam_sc"][:], 128.0), deps=[s3_])
            s5_ = S.op("tensor", lambda t: t.matmul(
                self.PSB[4][:, 0:2], P["ones1_f"][:], P["gam_sc"][:],
                start=True, stop=True), deps=[s4_] + self.psb_free.get(4, []))
            s6_ = S.op("scalar", lambda s: s.activation(
                out=P["gam_c"][:], in_=self.PSB[4][:, 0:2], func=ACT.Copy,
                bias=0.0, scale=1.0), deps=[s5_])
            self.psb_free[4] = [s6_]
            for dc in range(8):
                for tc in range(2):
                    sl = xxh_sl(dc, tc)
                    q1 = S.op("vector", lambda v, sl=sl, tc=tc: v.tensor_scalar(
                        gtmp, sl, P["gam_c"][:, tc:tc + 1], QBE,
                        ALU.mult, ALU.min),
                        deps=[s6_] + self.psb_free.get(5, []))
                    q2 = S.op("vector", lambda v: v.tensor_scalar(
                        gtmp, gtmp, -QBE, None, ALU.max), deps=[q1])
                    qoff = T * dc + hoff + 512 * tc
                    q3 = S.op("vector", lambda v, qoff=qoff: v.tensor_copy(
                        qx_bf[:, qoff:qoff + 512], gtmp), deps=[q2])
                    q4 = S.op("scalar", lambda s, sl=sl: s.activation(
                        out=sl, in_=gtmp, func=ACT.Copy, bias=0.0, scale=1.0),
                        deps=[q2, q3])
                    q5 = S.dma("sync",
                               P["h_qxf"].ap()[128 * dc:128 * (dc + 1),
                                               hoff + 512 * tc:
                                               hoff + 512 * (tc + 1)],
                               sl, deps=[q4])
                    self.psb_free[5] = [q4]
                    prev_xmm.append(q5)
                    prev_gmm.append(q3)
                    self.tok_qx.append(q3)
                    self.tok_qx.append(q5)

        self.dbg_out("gated", gated_bf[:, 0:1024], g_toks[0:1])
        self.dbg_out("xxh", A[:, 0:1024], xx_toks)
        self.dbg_out("qx", qx_bf[:, 0:T], self.tok_qx)
        self.dbg_out("gamc", gam_c[:, 0:2], self.tok_qx)

    def p4_mamba(self):
        S, P = self.S, self.P
        A = self.scr
        Abf = self.scrbf
        qx_bf = self.abufC
        xs_bf = self.abufA
        yg_bf = self.abufA
        wA = self.wbufA
        wB = self.wbufB

        t_g2, _ = self.load_cols("w_ln2g", 8)
        t_b2l, _ = self.load_cols("w_ln2b", 8)
        t_dtb, _ = self.load_cols("w_dtb", 16)
        t_cvb, _ = self.load_cols("w_convB", 16)
        t_dp, _ = self.load_cols("w_Dp", 16)
        lnb2_bf = self.sbuf("lnb2_bf", [128, 8], BF16)
        t_lnb2 = S.op("vector", lambda v: v.tensor_copy(
            P["lnb2_bf"][:], P["w_ln2b_c"][:]), deps=[t_b2l])
        prev_mm = []
        sq_bf = Abf[:, 0:2048]
        for k in range(8):
            tsq = S.op("scalar", lambda s, k=k: s.activation(
                out=sq_bf, in_=qx_bf[:, T * k:T * (k + 1)], func=ACT.Square,
                bias=0.0, scale=1.0), deps=self.tok_qx + prev_mm)
            prev_mm = []
            for tc in range(4):
                m1 = S.op("tensor", lambda t, tc=tc, k=k: t.matmul(
                    self.PSB[tc][0:1, :], P["onescol_b"][:],
                    qx_bf[:, T * k + 512 * tc:T * k + 512 * (tc + 1)],
                    start=(k == 0), stop=(k == 7)),
                    deps=self.tok_qx + self.psb_free.get(tc, []) + self.tok_const)
                m2 = S.op("tensor", lambda t, tc=tc, k=k: t.matmul(
                    self.PSB[4 + tc][0:1, :], P["onescol_b"][:],
                    sq_bf[:, 512 * tc:512 * (tc + 1)],
                    start=(k == 0), stop=(k == 7)),
                    deps=[tsq] + self.psb_free.get(4 + tc, []))
                prev_mm += [m1, m2]
        m2row = A[0:1, 2048:4096]
        v2row = A[0:1, 4096:6144]
        ev = []
        for tc in range(4):
            ev.append(S.op("scalar", lambda s, tc=tc: s.activation(
                out=m2row[0:1, 512 * tc:512 * (tc + 1)],
                in_=self.PSB[tc][0:1, :], func=ACT.Copy, bias=0.0,
                scale=1.0 / D), deps=prev_mm))
            ev.append(S.op("scalar", lambda s, tc=tc: s.activation(
                out=v2row[0:1, 512 * tc:512 * (tc + 1)],
                in_=self.PSB[4 + tc][0:1, :], func=ACT.Copy, bias=0.0,
                scale=1.0 / D), deps=prev_mm))
        for i in range(8):
            self.psb_free[i] = list(ev)
        rt1 = A[0:1, 6144:8192]
        rt2 = A[0:1, 0:2048]
        o3 = S.op("vector", lambda v: v.tensor_tensor(
            rt1, m2row, m2row, ALU.mult), deps=ev)
        o4 = S.op("vector", lambda v: v.tensor_sub(v2row, v2row, rt1),
                  deps=[o3])
        a_ = S.op("scalar", lambda s: s.activation(
            out=rt1, in_=v2row, func=ACT.Sqrt, bias=P["eps1"][:, 0:1],
            scale=1.0), deps=[o4] + self.tok_const)
        y0 = S.op("vector", lambda v: v.reciprocal(rt2, rt1), deps=[a_])
        b1 = S.op("vector", lambda v: v.tensor_scalar_add(rt1, v2row, EPS),
                  deps=[y0])
        b2 = S.op("vector", lambda v: v.tensor_tensor(rt1, rt1, rt2, ALU.mult),
                  deps=[b1])
        b3 = S.op("vector", lambda v: v.tensor_tensor(rt1, rt1, rt2, ALU.mult),
                  deps=[b2])
        b4 = S.op("vector", lambda v: v.tensor_scalar(
            rt1, rt1, -0.5, 1.5, ALU.mult, ALU.add), deps=[b3])
        r2row_ = S.op("vector", lambda v: v.tensor_tensor(
            v2row, rt1, rt2, ALU.mult), deps=[b4])
        r2row = v2row
        r2_bc = A[:, 7808:9856]
        bc_last = [r2row_]
        for tc in range(4):
            mm = self.bcast_row_psum(
                self.PSB[0][:, :], r2row[0:1, 512 * tc:512 * (tc + 1)],
                deps=bc_last + self.tok_const + self.psb_free.get(0, []))
            cp = S.op("scalar", lambda s, tc=tc: s.activation(
                out=r2_bc[:, 512 * tc:512 * (tc + 1)], in_=self.PSB[0][:],
                func=ACT.Copy, bias=0.0, scale=1.0), deps=[mm])
            self.psb_free[0] = [cp]
            bc_last = [cp]

        wld4 = A[:, 0:4096]
        wi_bf = Abf[:, 8192:12288]
        xnf32 = self.xnsl.bitcast(F32)
        crow = xnf32[0:1, 0:4096]
        bbrow = xnf32[0:1, 4096:8192]
        prevk = list(bc_last) + [o3]
        c_mm = []
        wspill = []
        for k in range(8):
            tl = S.dma("sync", wld4, P["w_inpT"].ap()[128 * k:128 * (k + 1), :],
                       deps=prevk)
            sc1 = S.op("vector", lambda v, k=k: v.tensor_scalar_mul(
                wi_bf, wld4, P["w_ln2g_c"][:, k:k + 1]), deps=[tl, t_g2])
            mm_k = []
            for tc in range(8):
                mm_k.append(S.op("tensor", lambda t, tc=tc, k=k: t.matmul(
                    self.PSB[tc][0:1, :], P["onescol_b"][:],
                    wi_bf[:, 512 * tc:512 * (tc + 1)],
                    start=(k == 0), stop=(k == 7)),
                    deps=[sc1] + self.tok_const +
                         (self.psb_free.get(tc, []) if k == 0 else [])))
            sp = S.dma("sync", P["h_winp"].ap()[128 * k:128 * (k + 1), :],
                       wi_bf, deps=mm_k)
            c_mm += mm_k
            wspill.append(sp)
            prevk = [sp]
        evc = []
        for tc in range(8):
            evc.append(S.op("scalar", lambda s, tc=tc, crow=crow: s.activation(
                out=crow[0:1, 512 * tc:512 * (tc + 1)],
                in_=self.PSB[tc][0:1, :], func=ACT.Copy, bias=0.0,
                scale=1.0), deps=c_mm))
        for i in range(8):
            self.psb_free[i] = list(evc)
        raw_bf = Abf[:, 12288:14336]
        bb_mm = []
        prevk2 = list(evc)
        for k in range(8):
            tl = S.dma("sync", wld4, P["w_inpT"].ap()[128 * k:128 * (k + 1), :],
                       deps=prevk2)
            mm_k = []
            for kh in range(2):
                cr = S.op("vector", lambda v, kh=kh: v.tensor_copy(
                    raw_bf, wld4[:, 2048 * kh:2048 * (kh + 1)]),
                    deps=[tl] + mm_k[-4:])
                for tc in range(4):
                    mm_k.append(S.op("tensor", lambda t, tc=tc, k=k, kh=kh:
                        t.matmul(
                            self.PSB[4 * kh + tc][0:1, :],
                            P["lnb2_bf"][:, k:k + 1],
                            raw_bf[:, 512 * tc:512 * (tc + 1)],
                            start=(k == 0), stop=(k == 7)),
                        deps=[cr, t_lnb2] +
                             (self.psb_free.get(4 * kh + tc, [])
                              if k == 0 else [])))
            bb_mm += mm_k
            prevk2 = [mm_k[-1], tl]
        evb = []
        for tc in range(8):
            evb.append(S.op("scalar", lambda s, tc=tc, bbrow=bbrow: s.activation(
                out=bbrow[0:1, 512 * tc:512 * (tc + 1)],
                in_=self.PSB[tc][0:1, :], func=ACT.Copy, bias=0.0,
                scale=1.0), deps=bb_mm))
        for i in range(8):
            self.psb_free[i] = list(evb)
        bb2c = self.sbuf("bb2c", [128, 32], F32)
        prev_c = list(evb)
        for jc in range(32):
            mm = S.op("tensor", lambda t, jc=jc, bbrow=bbrow: t.matmul(
                self.PSB[4][:, jc:jc + 1],
                bbrow[0:1, 128 * jc:128 * (jc + 1)], P["ones1_f"][0:1, 0:1],
                start=True, stop=True), deps=prev_c + self.tok_const)
            prev_c = [mm]
        cpb2 = S.op("scalar", lambda s: s.activation(
            out=P["bb2c"][:], in_=self.PSB[4][:, 0:32], func=ACT.Copy,
            bias=0.0, scale=1.0), deps=prev_c)
        self.psb_free[4] = [cpb2]
        wfold = []
        prevk3 = [cpb2]
        for k in range(8):
            tlw = S.dma("sync", wi_bf, P["h_winp"].ap()[128 * k:128 * (k + 1), :],
                        deps=prevk3 + wspill)
            ftoks = []
            for tc in range(8):
                mm = self.bcast_row_psum(
                    self.PSB[5][:, :], crow[0:1, 512 * tc:512 * (tc + 1)],
                    deps=[tlw] + self.tok_const + evc +
                         self.psb_free.get(5, []))
                f1 = S.op("vector", lambda v, tc=tc: v.scalar_tensor_tensor(
                    wi_bf[:, 512 * tc:512 * (tc + 1)], self.PSB[5][:, :],
                    -1.0 / D, wi_bf[:, 512 * tc:512 * (tc + 1)],
                    ALU.mult, ALU.add), deps=[mm])
                self.psb_free[5] = [f1]
                ftoks.append(f1)
            sp = S.dma("sync", P["h_winp"].ap()[128 * k:128 * (k + 1), :],
                       wi_bf, deps=ftoks)
            wfold.append(sp)
            prevk3 = [sp]

        self.dint("h_woutbf", [DI, D], BF16)
        wobf = []
        prevk2 = list(prevk)
        for k in range(16):
            tl = S.dma("sync", wld4[:, 0:1024],
                       P["w_outpT"].ap()[128 * k:128 * (k + 1), :],
                       deps=prevk2 + evc)
            cv = S.op("vector", lambda v: v.tensor_copy(
                Abf[:, 4096:5120], wld4[:, 0:1024]), deps=[tl])
            sp = S.dma("sync", P["h_woutbf"].ap()[128 * k:128 * (k + 1), :],
                       Abf[:, 4096:5120], deps=[cv])
            wobf.append(sp)
            prevk2 = [sp, cv]

        xnbf = self.xnsl
        xnf32v = self.xnsl.bitcast(F32)
        xpj = xnbf[:, 0:2048]
        dtpj = xnbf[0:64, 2048:4096]
        eoh = xnbf[0:16, 4096:6144]
        dtin_bf = xnbf[0:64, 6144:6656]
        Bbf = xnbf[0:16, 6656:7168]
        Cbf = xnbf[0:16, 7168:7680]
        carry = xnf32v[:, 3840:4096]
        ctx = xnf32v[:, 4096:4144]
        negA = xnf32v[:, 4144:4400]
        convw = xnf32v[:, 4400:4464]
        prep_done = wfold + [cpb2]
        cw_toks = []
        for dt_ in range(16):
            cw_toks.append(S.dma(
                "sync", convw[:, 4 * dt_:4 * (dt_ + 1)],
                P["w_convW"].ap()[128 * dt_:128 * (dt_ + 1), :],
                deps=prep_done))
        na_toks = []
        for dt_ in range(16):
            tl = S.dma("sync", negA[:, 16 * dt_:16 * (dt_ + 1)],
                       P["w_Alog"].ap()[128 * dt_:128 * (dt_ + 1), :],
                       deps=prep_done)
            t1 = S.op("scalar", lambda s, dt_=dt_: s.activation(
                out=negA[:, 16 * dt_:16 * (dt_ + 1)],
                in_=negA[:, 16 * dt_:16 * (dt_ + 1)], func=ACT.Exp,
                bias=0.0, scale=1.0), deps=[tl])
            na_toks.append(S.op("vector", lambda v, dt_=dt_: v.tensor_scalar_mul(
                negA[:, 16 * dt_:16 * (dt_ + 1)],
                negA[:, 16 * dt_:16 * (dt_ + 1)], -1.0), deps=[t1]))
        xpj_toks = []
        for dt_ in range(16):
            tl = S.dma("sync", A[:, 0:128],
                       P["w_xprojT"].ap()[128 * dt_:128 * (dt_ + 1), :],
                       deps=prep_done + xpj_toks[-1:])
            xpj_toks.append(S.op("vector", lambda v, dt_=dt_: v.tensor_copy(
                xpj[:, 128 * dt_:128 * (dt_ + 1)], A[:, 0:128]), deps=[tl]))
        tl = S.dma("sync", A[0:64, 128:2176], P["w_dtprojT"].ap(),
                   deps=prep_done + xpj_toks[-1:])
        dtpj_tok = S.op("vector", lambda v: v.tensor_copy(
            dtpj[:], A[0:64, 128:2176]), deps=[tl])
        e0 = S.op("vector", lambda v: v.memset(A[0:16, 0:2048], 1.0),
                  deps=xpj_toks + [dtpj_tok])
        e1 = S.op("gpsimd", lambda g: g.affine_select(
            eoh[:], A[0:16, 0:2048], pattern=[[1, 16], [0, 128]],
            compare_op=ALU.is_equal, fill=0.0, base=0, channel_multiplier=-1),
            deps=[e0])
        eoh_tok = e1

        xc = A[:, 0:640]
        dAb = [A[:, 640:1152], A[:, 1152:1664]]
        vb = [A[:, 1664:2176], A[:, 2176:2688]]
        hb = A[:, 2688:3200]
        dtb_ = A[:, 3200:3712]
        dtxb = A[:, 3712:4224]
        yb = A[:, 4224:4736]
        ztb = A[:, 4736:5248]
        gb = A[:, 5248:5760]
        pint = A[:, 5760:7808]
        ctx0 = S.op("vector", lambda v: v.memset(ctx[:, :], 0.0),
                    deps=prep_done)
        prev_ch = [ctx0] + wfold + wobf + [eoh_tok, dtpj_tok] + na_toks +             cw_toks + [t_dtb, t_cvb, t_dp]
        tok_mamba_out = []
        for ch in range(4):
            co = TC * ch
            wl = [S.dma("sync", wA[:, 2048 * k:2048 * (k + 1)],
                        P["h_winp"].ap()[128 * k:128 * (k + 1), 0:2048],
                        deps=prev_ch) for k in range(8)]
            xs_toks = []
            for dt_ in range(16):
                mms = []
                for k in range(8):
                    mms.append(S.op("tensor", lambda t, dt_=dt_, k=k, co=co:
                        t.matmul(
                            self.PSB[0][:, :],
                            wA[:, 2048 * k + 128 * dt_:2048 * k + 128 * (dt_ + 1)],
                            qx_bf[:, T * k + co:T * k + co + TC],
                            start=(k == 0), stop=(k == 7)),
                        deps=wl + self.psb_free.get(0, [])))
                x1 = S.op("vector", lambda v, co=co: v.tensor_tensor(
                    xc[:, 3:515], self.PSB[0][:, :], r2_bc[:, co:co + TC],
                    ALU.mult), deps=mms + bc_last)
                self.psb_free[0] = [x1]
                x2 = S.op("vector", lambda v, dt_=dt_: v.tensor_scalar_add(
                    xc[:, 3:515], xc[:, 3:515], P["bb2c"][:, dt_:dt_ + 1]),
                    deps=[x1, cpb2])
                c0 = S.op("vector", lambda v, dt_=dt_: v.tensor_copy(
                    xc[:, 0:3], ctx[:, 3 * dt_:3 * (dt_ + 1)]),
                    deps=[x2])
                cv1 = S.op("vector", lambda v, dt_=dt_: v.tensor_scalar_mul(
                    dtxb, xc[:, 0:512], convw[:, 4 * dt_:4 * dt_ + 1]),
                    deps=[c0])
                cvs = [cv1]
                for kk in range(1, 4):
                    cvs.append(S.op("vector", lambda v, dt_=dt_, kk=kk:
                        v.scalar_tensor_tensor(
                            dtxb, xc[:, kk:512 + kk],
                            convw[:, 4 * dt_ + kk:4 * dt_ + kk + 1], dtxb,
                            ALU.mult, ALU.add), deps=[cvs[-1]]))
                cx = S.op("vector", lambda v, dt_=dt_: v.tensor_copy(
                    ctx[:, 3 * dt_:3 * (dt_ + 1)], xc[:, 512:515]),
                    deps=cvs)
                sl = S.op("scalar", lambda s, dt_=dt_, ch=ch: s.activation(
                    out=xs_bf[:, 512 * dt_:512 * (dt_ + 1)], in_=dtxb,
                    func=ACT.Silu, bias=P["w_convB_c"][:, dt_:dt_ + 1],
                    scale=1.0), deps=cvs + [t_cvb] + tok_mamba_out)
                xs_toks.append(sl)
                prev_ch = [cx, sl]
            xd_mm = []
            for dt_ in range(16):
                xd_mm.append(S.op("tensor", lambda t, dt_=dt_: t.matmul(
                    self.PSB[1][:, :], xpj[:, 128 * dt_:128 * (dt_ + 1)],
                    xs_bf[:, 512 * dt_:512 * (dt_ + 1)],
                    start=(dt_ == 0), stop=(dt_ == 15)),
                    deps=xs_toks + xpj_toks + self.psb_free.get(1, [])))
            exd1 = S.op("scalar", lambda s: s.activation(
                out=dtin_bf[:, :], in_=self.PSB[1][0:64, :], func=ACT.Copy,
                bias=0.0, scale=1.0), deps=xd_mm)
            exd2 = S.op("scalar", lambda s: s.activation(
                out=Bbf[:, :], in_=self.PSB[1][64:80, :], func=ACT.Copy,
                bias=0.0, scale=1.0), deps=xd_mm)
            exd3 = S.op("scalar", lambda s: s.activation(
                out=Cbf[:, :], in_=self.PSB[1][96:112, :], func=ACT.Copy,
                bias=0.0, scale=1.0), deps=xd_mm)
            self.psb_free[1] = [exd1, exd2, exd3]
            wlz = [S.dma("sync", wA[:, 2048 * k:2048 * (k + 1)],
                         P["h_winp"].ap()[128 * k:128 * (k + 1), 2048:4096],
                         deps=xs_toks + xd_mm) for k in range(8)]
            for dt_ in range(16):
                dmm = S.op("tensor", lambda t, dt_=dt_: t.matmul(
                    self.PSB[2][:, :], dtpj[:, 128 * dt_:128 * (dt_ + 1)],
                    dtin_bf[:, :], start=True, stop=True),
                    deps=[exd1, dtpj_tok] + self.psb_free.get(2, []))
                d0 = S.op("vector", lambda v, dt_=dt_: v.tensor_scalar_add(
                    ztb, self.PSB[2][:, :], P["w_dtb_c"][:, dt_:dt_ + 1]),
                    deps=[dmm, t_dtb])
                self.psb_free[2] = [d0]
                d1 = S.op("scalar", lambda s: s.activation(
                    out=gb, in_=ztb, func=ACT.Abs, bias=0.0, scale=1.0),
                    deps=[d0])
                d2 = S.op("scalar", lambda s: s.activation(
                    out=gb, in_=gb, func=ACT.Exp, bias=0.0, scale=-1.0),
                    deps=[d1])
                d3 = S.op("scalar", lambda s: s.activation(
                    out=gb, in_=gb, func=ACT.Ln, bias=1.0, scale=1.0),
                    deps=[d2])
                d4 = S.op("vector", lambda v: v.tensor_scalar_max(
                    dtb_, ztb, 0.0), deps=[d1])
                dact = S.op("vector", lambda v: v.tensor_add(
                    dtb_, dtb_, gb), deps=[d3, d4])
                dtx = S.op("vector", lambda v, dt_=dt_: v.tensor_tensor(
                    dtxb, dtb_, xs_bf[:, 512 * dt_:512 * (dt_ + 1)],
                    ALU.mult), deps=[dact] + xs_toks)
                ytoks = []
                BMB = [3, 1]
                CMB = [4, 7]
                for s_ in range(16):
                    pb = s_ % 2
                    dAx = dAb[pb]
                    vx = vb[pb]
                    bb_ = BMB[pb]
                    cb_ = CMB[pb]
                    da = S.op("scalar", lambda s, s_=s_, dt_=dt_, dAx=dAx:
                        s.activation(
                            out=dAx, in_=dtb_, func=ACT.Exp,
                            scale=negA[:, 16 * dt_ + s_:16 * dt_ + s_ + 1],
                            bias=0.0), deps=[dact] + na_toks + ytoks[-2:])
                    bm = S.op("tensor", lambda t, s_=s_, bb_=bb_: t.matmul(
                        self.PSB[bb_][:, :], eoh[:, 128 * s_:128 * (s_ + 1)],
                        Bbf[:, :], start=True, stop=True),
                        deps=[exd2, eoh_tok] + self.psb_free.get(bb_, []))
                    vv = S.op("vector", lambda v, vx=vx, bb_=bb_:
                        v.tensor_tensor(
                            vx, dtxb, self.PSB[bb_][:, :], ALU.mult),
                        deps=[dtx, bm])
                    self.psb_free[bb_] = [vv]
                    init = 0.0 if ch == 0 else carry[:, 16 * dt_ + s_:
                                                     16 * dt_ + s_ + 1]
                    sc = S.op("vector", lambda v, dAx=dAx, vx=vx, init=init:
                        v.tensor_tensor_scan(hb, dAx, vx, init,
                                             ALU.mult, ALU.add),
                        deps=[da, vv])
                    cs = S.op("vector", lambda v, dt_=dt_, s_=s_: v.tensor_copy(
                        carry[:, 16 * dt_ + s_:16 * dt_ + s_ + 1],
                        hb[:, 511:512]), deps=[sc])
                    cm = S.op("tensor", lambda t, s_=s_, cb_=cb_: t.matmul(
                        self.PSB[cb_][:, :], eoh[:, 128 * s_:128 * (s_ + 1)],
                        Cbf[:, :], start=True, stop=True),
                        deps=[exd3, eoh_tok] + self.psb_free.get(cb_, []))
                    pp = S.op("vector", lambda v, s_=s_, cb_=cb_:
                        v.tensor_tensor(
                            pint[:, 512 * (s_ % 4):512 * (s_ % 4 + 1)],
                            hb, self.PSB[cb_][:, :], ALU.mult),
                        deps=[sc, cm])
                    self.psb_free[cb_] = [pp]
                    ytoks.append(pp)
                    if s_ % 4 == 3:
                        grp = s_ // 4
                        p4s = ytoks[-4:]
                        if grp == 0:
                            rd = S.op("vector", lambda v: v.tensor_add(
                                yb, pint[:, 0:512], pint[:, 512:1024]),
                                deps=p4s)
                        else:
                            rd = S.op("vector", lambda v: v.tensor_add(
                                yb, yb, pint[:, 0:512]), deps=p4s)
                            rd = S.op("vector", lambda v: v.tensor_add(
                                yb, yb, pint[:, 512:1024]), deps=[rd])
                        rd = S.op("vector", lambda v: v.tensor_add(
                            yb, yb, pint[:, 1024:1536]), deps=[rd])
                        rd = S.op("vector", lambda v: v.tensor_add(
                            yb, yb, pint[:, 1536:2048]), deps=[rd])
                        ytoks.append(rd)
                yd = S.op("vector", lambda v, dt_=dt_: v.scalar_tensor_tensor(
                    yb, xs_bf[:, 512 * dt_:512 * (dt_ + 1)],
                    P["w_Dp_c"][:, dt_:dt_ + 1], yb, ALU.mult, ALU.add),
                    deps=ytoks + [t_dp])
                zmm = []
                for k in range(8):
                    zmm.append(S.op("tensor", lambda t, dt_=dt_, k=k, co=co:
                        t.matmul(
                            self.PSB[5][:, :],
                            wA[:, 2048 * k + 128 * dt_:2048 * k + 128 * (dt_ + 1)],
                            qx_bf[:, T * k + co:T * k + co + TC],
                            start=(k == 0), stop=(k == 7)),
                        deps=wlz + self.psb_free.get(5, [])))
                z1 = S.op("vector", lambda v, co=co: v.tensor_tensor(
                    ztb, self.PSB[5][:, :], r2_bc[:, co:co + TC], ALU.mult),
                    deps=zmm)
                self.psb_free[5] = [z1]
                z2 = S.op("scalar", lambda s, dt_=dt_: s.activation(
                    out=gb, in_=ztb, func=ACT.Silu,
                    bias=P["bb2c"][:, 16 + dt_:16 + dt_ + 1], scale=1.0),
                    deps=[z1, cpb2])
                yg = S.op("vector", lambda v, dt_=dt_: v.tensor_tensor(
                    yg_bf[:, 8192 + 512 * dt_:8192 + 512 * (dt_ + 1)],
                    yb, gb, ALU.mult), deps=[yd, z2])
                tok_mamba_out.append(yg)
            wlo = [S.dma("sync", wB[:, 1024 * k:1024 * (k + 1)],
                         P["h_woutbf"].ap()[128 * k:128 * (k + 1), :],
                         deps=wobf + tok_mamba_out[-1:]) for k in range(16)]
            op_toks = []
            for dc in range(8):
                mms = []
                for k in range(16):
                    mms.append(S.op("tensor", lambda t, dc=dc, k=k: t.matmul(
                        self.PSB[6][:, :],
                        wB[:, 1024 * k + 128 * dc:1024 * k + 128 * (dc + 1)],
                        yg_bf[:, 8192 + 512 * k:8192 + 512 * (k + 1)],
                        start=(k == 0), stop=(k == 15)),
                        deps=wlo + tok_mamba_out[-16:] +
                             self.psb_free.get(6, [])))
                tl = S.dma("sync", A[:, 0:512],
                           P["x_fm"].ap()[128 * dc:128 * (dc + 1), co:co + TC],
                           deps=op_toks[-1:] + mms[:1])
                tlq = S.dma("sync", A[:, 512:1024],
                            P["h_qxf"].ap()[128 * dc:128 * (dc + 1), co:co + TC],
                            deps=op_toks[-1:] + mms[:1] + self.tok_qx)
                ad = S.op("vector", lambda v: v.tensor_add(
                    A[:, 0:512], A[:, 0:512], A[:, 512:1024]),
                    deps=[tl, tlq])
                ad2 = S.op("vector", lambda v: v.tensor_add(
                    A[:, 0:512], A[:, 0:512], self.PSB[6][:, :]),
                    deps=[ad] + mms)
                self.psb_free[6] = [ad2]
                sp = S.dma("sync",
                           P["h_xacc"].ap()[128 * dc:128 * (dc + 1), co:co + TC],
                           A[:, 0:512], deps=[ad2])
                op_toks.append(sp)
            prev_ch = op_toks + [cx]
            self.tok_mamba_spill = op_toks

        self.dbg_out("xs", xs_bf[:, 0:8192], xs_toks)
        self.dbg_out("dtc", dtb_, tok_mamba_out)
        self.dbg_out("ygate", yg_bf[:, 8192:16384], tok_mamba_out)
        self.dbg_dram("xacc", P["h_xacc"], [D, T], self.tok_mamba_spill)
        self.dbg_dram("qxf", P["h_qxf"], [D, T],
                      self.tok_mamba_spill + self.tok_qx)


    def p5_ffn(self):
        S, P = self.S, self.P
        A = self.scr
        Abf = self.scrbf
        xq = self.abufC
        psbf = self.PSA.bitcast(BF16)

        def h_sl(c):
            if c < 8:
                return self.abufA[:, 2048 * c:2048 * (c + 1)]
            if c < 16:
                return self.wbufA[:, 2048 * (c - 8):2048 * (c - 7)]
            return self.wbufB[:, 2048 * (c - 16):2048 * (c - 15)]

        last_m = self.tok_mamba_spill
        xq_toks = [S.dma("sync", xq[:, T * k:T * (k + 1)],
                         P["h_xq"].ap()[128 * k:128 * (k + 1), :],
                         deps=last_m + self.tok_xq_spill) for k in range(8)]
        clipd = self.sbuf("clipd_sb", [1, 1], F32)
        tcd = S.dma("sync", clipd[:], P["w_clipd"].ap())
        qd = S.op("scalar", lambda s: s.activation(
            out=P["clipd_sb"][:], in_=P["clipd_sb"][:], func=ACT.Sigmoid,
            bias=0.0, scale=1.0), deps=[tcd])

        wnat = A[:, 0:2816]
        wqbf = [Abf[:, 5632:7680], Abf[:, 7680:9728]]
        wql = [Abf[:, 9728:10752], Abf[:, 10752:11776]]
        sg_ = A[:, 5888:6400]
        up_ = A[:, 6400:6912]
        rcol = self.sbuf("rcol5", [128, 1], F32)
        comb = self.sbuf("comb5", [128, 2], F32)
        hmax = self.sbuf("hmax", [128, 1], F32)
        hmaxt = self.sbuf("hmaxt", [128, 1], F32)
        hm0 = S.op("vector", lambda v: v.memset(P["hmax"][:], 0.0))
        h_toks = []
        prev_w = list(xq_toks)
        prev_h = []
        for c in range(FFC):
            ev_toks = []
            for wi, wdram in [(0, "w_wg"), (1, "w_wu")]:
                tl = S.dma("sync", wnat[:, 0:1024],
                           P[wdram].ap()[128 * c:128 * (c + 1), :],
                           deps=prev_w)
                r0 = S.op("vector", lambda v: v.tensor_reduce(
                    P["rcol5"][:], wnat[:, 0:1024], AXL.X, ALU.max,
                    apply_absolute_value=True), deps=[tl])
                c0 = S.op("vector", lambda v, wi=wi: v.tensor_scalar_mul(
                    P["comb5"][:, wi:wi + 1], P["rcol5"][:], 1.0 / 127.0),
                    deps=[r0])
                c1 = S.op("vector", lambda v, wi=wi: v.tensor_tensor(
                    P["comb5"][:, wi:wi + 1], P["comb5"][:, wi:wi + 1],
                    P["s1c"][:, 0:1], ALU.mult), deps=[c0, self.tok_s1c])
                rr = S.op("vector", lambda v: v.reciprocal(
                    P["rcol5"][:], P["rcol5"][:]), deps=[c0])
                rr2 = S.op("vector", lambda v: v.tensor_scalar_mul(
                    P["rcol5"][:], P["rcol5"][:], 127.0), deps=[rr])
                a1 = S.op("vector", lambda v: v.tensor_scalar(
                    wnat[:, 0:1024], wnat[:, 0:1024], P["rcol5"][:, 0:1],
                    RMAGIC, ALU.mult, ALU.add), deps=[rr2])
                a2 = S.op("vector", lambda v: v.tensor_scalar(
                    wnat[:, 0:1024], wnat[:, 0:1024], -RMAGIC, 127.0,
                    ALU.add, ALU.min), deps=[a1])
                a3 = S.op("vector", lambda v, wi=wi: v.tensor_scalar(
                    wqbf[wi][:, 0:1024], wnat[:, 0:1024],
                    -128.0, None, ALU.max), deps=[a2])
                prev_w = [a3]
                tt = []
                for hc in range(8):
                    tt.append(S.op("tensor", lambda t, hc=hc, wi=wi: t.transpose(
                        psbf[:, 7168 + 128 * hc:7168 + 128 * (hc + 1)],
                        wqbf[wi][:, 128 * hc:128 * (hc + 1)],
                        P["id_bf"][:]),
                        deps=[a3] + self.tok_const + self.psb_free.get(7, [])))
                ev = S.op("scalar", lambda s, wi=wi: s.activation(
                    out=wql[wi][:, 0:1024], in_=psbf[:, 7168:8192],
                    func=ACT.Copy, bias=0.0, scale=1.0), deps=tt)
                self.psb_free[7] = [ev]
                ev_toks.append((ev, c1))
            for tc in range(4):
                mg = []
                mu = []
                for hc in range(8):
                    mg.append(S.op("tensor", lambda t, tc=tc, hc=hc: t.matmul(
                        self.PSB[0][:, :], wql[0][:, 128 * hc:128 * (hc + 1)],
                        xq[:, T * hc + 512 * tc:T * hc + 512 * (tc + 1)],
                        start=(hc == 0), stop=(hc == 7)),
                        deps=[ev_toks[0][0]] + xq_toks +
                             self.psb_free.get(0, [])))
                    mu.append(S.op("tensor", lambda t, tc=tc, hc=hc: t.matmul(
                        self.PSB[1][:, :], wql[1][:, 128 * hc:128 * (hc + 1)],
                        xq[:, T * hc + 512 * tc:T * hc + 512 * (tc + 1)],
                        start=(hc == 0), stop=(hc == 7)),
                        deps=[ev_toks[1][0]] + xq_toks +
                             self.psb_free.get(1, [])))
                e1 = S.op("scalar", lambda s: s.activation(
                    out=sg_, in_=self.PSB[0][:, :], func=ACT.Silu,
                    scale=P["comb5"][:, 0:1], bias=0.0),
                    deps=mg + [ev_toks[0][1]] + prev_h)
                self.psb_free[0] = [e1]
                e2 = S.op("vector", lambda v: v.tensor_scalar_mul(
                    up_, self.PSB[1][:, :], P["comb5"][:, 1:2]),
                    deps=mu + [ev_toks[1][1]] + prev_h)
                self.psb_free[1] = [e2]
                h1 = S.op("vector", lambda v, c=c, tc=tc: v.tensor_tensor(
                    h_sl(c)[:, 512 * tc:512 * (tc + 1)], sg_, up_, ALU.mult),
                    deps=[e1, e2])
                r1_ = S.op("vector", lambda v, c=c, tc=tc: v.tensor_reduce(
                    P["hmaxt"][:], h_sl(c)[:, 512 * tc:512 * (tc + 1)],
                    AXL.X, ALU.max, apply_absolute_value=True), deps=[h1])
                r2_ = S.op("vector", lambda v: v.tensor_max(
                    P["hmax"][:], P["hmax"][:], P["hmaxt"][:]),
                    deps=[r1_, hm0])
                h_toks.append(h1)
                prev_h = [h1, r2_]
            prev_w = prev_w + prev_h

        tmm = S.op("tensor", lambda t: t.matmul(
            self.PSB[2][0:1, 0:128], P["hmax"][:], P["id_f"][:],
            start=True, stop=True),
            deps=prev_h + self.tok_const + self.psb_free.get(2, []))
        cpm = S.op("scalar", lambda s: s.activation(
            out=P["mxrow"][:], in_=self.PSB[2][0:1, 0:128], func=ACT.Copy,
            bias=0.0, scale=1.0), deps=[tmm])
        self.psb_free[2] = [cpm]
        pr = S.op("vector", lambda v: v.tensor_reduce(
            P["mxall"][:], P["mxrow"][:], AXL.X, ALU.max), deps=[cpm])
        z1 = S.op("vector", lambda v: v.memset(P["ccst"][:], 0.0),
                  deps=[pr] + self.tok_cc)
        z2 = S.op("vector", lambda v: v.tensor_copy(
            P["ccst"][0:1, 0:1], P["mxall"][:]), deps=[z1])
        cct, ccs2 = self.collective_max(1, [z2], "s2")
        s2s = self.sbuf("s2s", [1, 2], F32)
        q2 = S.op("vector", lambda v, ccs2=ccs2: v.tensor_tensor(
            P["s2s"][0:1, 0:1], P["clipd_sb"][:], ccs2[0:1, 0:1], ALU.mult),
            deps=[qd, cct])
        q3 = S.op("vector", lambda v: v.tensor_scalar_mul(
            P["s2s"][0:1, 0:1], P["s2s"][0:1, 0:1], 1.0 / 127.0), deps=[q2])
        q4 = S.op("vector", lambda v: v.reciprocal(
            P["s2s"][0:1, 1:2], P["s2s"][0:1, 0:1]), deps=[q3])
        s2c = self.sbuf("s2c", [128, 2], F32)
        mm1 = S.op("tensor", lambda t: t.matmul(
            self.PSB[3][:, 0:2], P["ones1_f"][:], P["s2s"][:],
            start=True, stop=True),
            deps=[q4] + self.tok_const + self.psb_free.get(3, []))
        cp1 = S.op("scalar", lambda s: s.activation(
            out=P["s2c"][:], in_=self.PSB[3][:, 0:2], func=ACT.Copy,
            bias=0.0, scale=1.0), deps=[mm1])
        self.psb_free[3] = [cp1]

        hq_toks = []
        prevq = [cp1] + h_toks
        for c in range(FFC):
            for tc in range(4):
                a1 = S.op("vector", lambda v, c=c, tc=tc: v.tensor_scalar(
                    sg_, h_sl(c)[:, 512 * tc:512 * (tc + 1)],
                    P["s2c"][:, 1:2], RMAGIC, ALU.mult, ALU.add), deps=prevq)
                a2 = S.op("vector", lambda v: v.tensor_scalar(
                    sg_, sg_, -RMAGIC, 127.0, ALU.add, ALU.min), deps=[a1])
                a3 = S.op("vector", lambda v, c=c, tc=tc: v.tensor_scalar(
                    h_sl(c)[:, 512 * tc:512 * (tc + 1)], sg_,
                    -128.0, None, ALU.max), deps=[a2])
                hq_toks.append(a3)
                prevq = [a3]

        wdl = Abf[:, 9728:15360]
        xacc5 = A[:, 7808:8320]
        combd = self.sbuf("combd", [128, 1], F32)
        self.tok_ffn_acc = []
        prev_d = list(hq_toks)
        for dc in range(8):
            tl = S.dma("sync", wnat, P["w_wd"].ap()[128 * dc:128 * (dc + 1), :],
                       deps=prev_d)
            r0 = S.op("vector", lambda v: v.tensor_reduce(
                P["rcol5"][:], wnat, AXL.X, ALU.max,
                apply_absolute_value=True), deps=[tl])
            c0 = S.op("vector", lambda v: v.tensor_scalar_mul(
                P["combd"][:], P["rcol5"][:], 1.0 / 127.0), deps=[r0])
            c1 = S.op("vector", lambda v: v.tensor_tensor(
                P["combd"][:], P["combd"][:], P["s2c"][:, 0:1], ALU.mult),
                deps=[c0, cp1])
            rr = S.op("vector", lambda v: v.reciprocal(
                P["rcol5"][:], P["rcol5"][:]), deps=[c0])
            rr2 = S.op("vector", lambda v: v.tensor_scalar_mul(
                P["rcol5"][:], P["rcol5"][:], 127.0), deps=[rr])
            a1 = S.op("vector", lambda v: v.tensor_scalar(
                wnat, wnat, P["rcol5"][:, 0:1], RMAGIC, ALU.mult, ALU.add),
                deps=[rr2])
            a2 = S.op("vector", lambda v: v.tensor_scalar(
                wnat, wnat, -RMAGIC, 127.0, ALU.add, ALU.min), deps=[a1])
            a3 = S.op("vector", lambda v: v.tensor_scalar(
                Abf[:, 5632:8448], wnat, -128.0, None, ALU.max), deps=[a2])
            ev_all = []
            for b2 in range(3):
                tt = []
                for ic in range(8):
                    gidx = 8 * b2 + ic
                    if gidx >= FFC:
                        break
                    tt.append(S.op("tensor", lambda t, gidx=gidx: t.transpose(
                        psbf[:, 7168 + 128 * (gidx % 8):
                             7168 + 128 * (gidx % 8 + 1)],
                        Abf[:, 5632 + 128 * gidx:5632 + 128 * (gidx + 1)],
                        P["id_bf"][:]),
                        deps=[a3] + self.tok_const + self.psb_free.get(7, [])))
                n = len(tt)
                ev = S.op("scalar", lambda s, b2=b2, n=n: s.activation(
                    out=wdl[:, 1024 * b2:1024 * b2 + 128 * n],
                    in_=psbf[:, 7168:7168 + 128 * n],
                    func=ACT.Copy, bias=0.0, scale=1.0), deps=tt)
                self.psb_free[7] = [ev]
                ev_all.append(ev)
            for tc in range(4):
                mms = []
                for ic in range(FFC):
                    mms.append(S.op("tensor", lambda t, tc=tc, ic=ic: t.matmul(
                        self.PSB[4][:, :],
                        wdl[:, 1024 * (ic // 8) + 128 * (ic % 8):
                            1024 * (ic // 8) + 128 * (ic % 8 + 1)],
                        h_sl(ic)[:, 512 * tc:512 * (tc + 1)],
                        start=(ic == 0), stop=(ic == FFC - 1)),
                        deps=ev_all + hq_toks + self.psb_free.get(4, [])))
                tl2 = S.dma("sync", xacc5,
                            P["h_xacc"].ap()[128 * dc:128 * (dc + 1),
                                             512 * tc:512 * (tc + 1)],
                            deps=self.tok_mamba_spill + self.tok_ffn_acc[-1:])
                ad = S.op("vector", lambda v: v.scalar_tensor_tensor(
                    xacc5, self.PSB[4][:, :], P["combd"][:, 0:1], xacc5,
                    ALU.mult, ALU.add), deps=mms + [tl2, c1])
                self.psb_free[4] = [ad]
                sp = S.dma("sync",
                           P["h_xacc"].ap()[128 * dc:128 * (dc + 1),
                                            512 * tc:512 * (tc + 1)],
                           xacc5, deps=[ad])
                self.tok_ffn_acc.append(sp)
            prev_d = [a3] + self.tok_ffn_acc[-1:]

        self.dbg_out("hq", h_sl(0), hq_toks)
        self.dbg_out("s2", ccs2[:], [cct])
        self.dbg_dram("xacc5", P["h_xacc"], [D, T], self.tok_ffn_acc)

    def p6_attn(self):
        S, P = self.S, self.P
        A = self.scr
        Abf = self.scrbf
        xn1 = self.abufA
        Qb = self.abufC
        Kb = self.wbufA
        Va = self.wbufB
        slopes = alibi_slopes(NH)

        xa = A[:, 0:2048]
        xsq = A[:, 2048:4096]
        prev_mm = list(self.tok_ffn_acc)
        for k in range(8):
            tl = S.dma("sync", xa, P["h_xacc"].ap()[128 * k:128 * (k + 1), :],
                       deps=prev_mm + self.tok_ffn_acc)
            tsq = S.op("scalar", lambda s: s.activation(
                out=xsq, in_=xa, func=ACT.Square, bias=0.0, scale=1.0),
                deps=[tl])
            prev_mm = []
            for tc in range(4):
                m1 = S.op("tensor", lambda t, tc=tc, k=k: t.matmul(
                    self.PSB[tc][0:1, :], P["onescol_f"][:],
                    xa[:, 512 * tc:512 * (tc + 1)],
                    start=(k == 0), stop=(k == 7)),
                    deps=[tl, tsq] + (self.psb_free.get(tc, []) if k == 0 else []))
                m2 = S.op("tensor", lambda t, tc=tc, k=k: t.matmul(
                    self.PSB[4 + tc][0:1, :], P["onescol_f"][:],
                    xsq[:, 512 * tc:512 * (tc + 1)],
                    start=(k == 0), stop=(k == 7)),
                    deps=[tsq] + (self.psb_free.get(4 + tc, []) if k == 0 else []))
                prev_mm += [m1, m2]
        mrow = A[0:1, 0:2048]
        vrow = A[0:1, 2048:4096]
        ev = []
        for tc in range(4):
            ev.append(S.op("scalar", lambda s, tc=tc: s.activation(
                out=mrow[0:1, 512 * tc:512 * (tc + 1)],
                in_=self.PSB[tc][0:1, :], func=ACT.Copy, bias=0.0,
                scale=1.0 / D), deps=prev_mm))
            ev.append(S.op("scalar", lambda s, tc=tc: s.activation(
                out=vrow[0:1, 512 * tc:512 * (tc + 1)],
                in_=self.PSB[4 + tc][0:1, :], func=ACT.Copy, bias=0.0,
                scale=1.0 / D), deps=prev_mm))
        for i in range(8):
            self.psb_free[i] = list(ev)
        rt1 = A[0:1, 4096:6144]
        rt2 = A[0:1, 6144:8192]
        o3 = S.op("vector", lambda v: v.tensor_tensor(
            rt1, mrow, mrow, ALU.mult), deps=ev)
        o4 = S.op("vector", lambda v: v.tensor_sub(vrow, vrow, rt1), deps=[o3])
        a_ = S.op("scalar", lambda s: s.activation(
            out=rt1, in_=vrow, func=ACT.Sqrt, bias=P["eps1"][:, 0:1],
            scale=1.0), deps=[o4] + self.tok_const)
        y0 = S.op("vector", lambda v: v.reciprocal(rt2, rt1), deps=[a_])
        b1 = S.op("vector", lambda v: v.tensor_scalar_add(rt1, vrow, EPS),
                  deps=[y0])
        b2 = S.op("vector", lambda v: v.tensor_tensor(rt1, rt1, rt2, ALU.mult),
                  deps=[b1])
        b3 = S.op("vector", lambda v: v.tensor_tensor(rt1, rt1, rt2, ALU.mult),
                  deps=[b2])
        b4 = S.op("vector", lambda v: v.tensor_scalar(
            rt1, rt1, -0.5, 1.5, ALU.mult, ALU.add), deps=[b3])
        rdone = S.op("vector", lambda v: v.tensor_tensor(
            vrow, rt1, rt2, ALU.mult), deps=[b4])
        rrow = vrow
        m_bc = A[:, 0:2048]
        r_bc = A[:, 2048:4096]
        bc_last = [rdone]
        for row, dstb in [(mrow, m_bc), (rrow, r_bc)]:
            for tc in range(4):
                mm = self.bcast_row_psum(
                    self.PSB[0][:, :], row[0:1, 512 * tc:512 * (tc + 1)],
                    deps=bc_last + self.tok_const + self.psb_free.get(0, []))
                cp = S.op("scalar", lambda s, dstb=dstb, tc=tc: s.activation(
                    out=dstb[:, 512 * tc:512 * (tc + 1)], in_=self.PSB[0][:],
                    func=ACT.Copy, bias=0.0, scale=1.0), deps=[mm])
                self.psb_free[0] = [cp]
                bc_last = [cp]
        xnf = A[:, 4096:6144]
        xn1_toks = []
        prev = list(bc_last)
        for k in range(8):
            tl = S.dma("sync", xa, P["h_xacc"].ap()[128 * k:128 * (k + 1), :],
                       deps=prev)
            s1 = S.op("vector", lambda v: v.scalar_tensor_tensor(
                xnf, m_bc, -1.0, xa, ALU.mult, ALU.add), deps=[tl])
            s2 = S.op("vector", lambda v, k=k: v.tensor_tensor(
                xn1[:, T * k:T * (k + 1)], xnf, r_bc, ALU.mult), deps=[s1])
            prev = [s2]
            xn1_toks.append(s2)
        self.tok_xn1 = xn1_toks

        wstage = Abf[:, 0:8192]

        def cast_w(dram, deps, dst=None):
            dst = wstage if dst is None else dst
            toks = []
            for k in range(8):
                tl = S.dma("sync", A[:, 6144:7168],
                           P[dram].ap()[128 * k:128 * (k + 1), :], deps=deps +
                           toks[-1:])
                toks.append(S.op("vector", lambda v, k=k, dst=dst: v.tensor_copy(
                    dst[:, 1024 * k:1024 * (k + 1)], A[:, 6144:7168]),
                    deps=[tl]))
            return toks

        qk_toks = []
        for dst, wnm in [(Qb, "w_WqT"), (Kb, "w_WkT")]:
            ct = cast_w(wnm, xn1_toks + qk_toks)
            for oc in range(8):
                for tc in range(4):
                    mms = []
                    for k in range(8):
                        mms.append(S.op("tensor",
                            lambda t, oc=oc, tc=tc, k=k: t.matmul(
                                self.PSB[oc % 2][:, :],
                                wstage[:, 1024 * k + 128 * oc:
                                       1024 * k + 128 * (oc + 1)],
                                xn1[:, T * k + 512 * tc:T * k + 512 * (tc + 1)],
                                start=(k == 0), stop=(k == 7)),
                            deps=ct + xn1_toks + self.psb_free.get(oc % 2, [])))
                    e1 = S.op("scalar", lambda s, dst=dst, oc=oc, tc=tc:
                              s.activation(
                                  out=dst[:, 2048 * oc + 512 * tc:
                                          2048 * oc + 512 * (tc + 1)],
                                  in_=self.PSB[oc % 2][:, :], func=ACT.Copy,
                                  bias=0.0, scale=1.0), deps=mms)
                    self.psb_free[oc % 2] = [e1]
                    qk_toks.append(e1)
        ct = cast_w("w_WvT", qk_toks)
        vset = S.op("vector", lambda v: v.memset(
            Va.rearrange("p (tk h o) -> p tk h o", h=NH, o=65)[:, :, :, 64:65],
            1.0), deps=qk_toks)
        v_toks = [vset]
        for tk in range(16):
            for ch in range(2):
                mms = []
                for k in range(8):
                    mms.append(S.op("tensor", lambda t, tk=tk, ch=ch, k=k:
                        t.matmul(
                            self.PSB[2 + ch][:, :],
                            xn1[:, T * k + 128 * tk:T * k + 128 * (tk + 1)],
                            wstage[:, 1024 * k + 512 * ch:
                                   1024 * k + 512 * (ch + 1)],
                            start=(k == 0), stop=(k == 7)),
                        deps=ct + xn1_toks + self.psb_free.get(2 + ch, [])))
                e1 = S.op("scalar", lambda s, tk=tk, ch=ch: s.activation(
                    out=Va.rearrange("p (tk h o) -> p tk h o", h=NH, o=65)
                        [:, tk, 8 * ch:8 * (ch + 1), 0:64],
                    in_=self.PSB[2 + ch][:, :], func=ACT.Copy,
                    bias=0.0, scale=1.0), deps=mms + [vset])
                self.psb_free[2 + ch] = [e1]
                v_toks.append(e1)

        kbase = self.sbuf("kbase", [128, 16], F32)
        kb0 = S.op("gpsimd", lambda g: g.iota(
            P["kbase"][:], pattern=[[128, 16]], base=0, channel_multiplier=1,
            allow_small_or_imprecise_dtypes=True), deps=v_toks)
        masks = Abf[:, 16384:18432]
        maskp = self.sbuf("maskp64", [128, 64], BF16)
        mk = []
        m0 = S.op("vector", lambda v: v.memset(A[:, 9216:9728], 1.0),
                  deps=v_toks + self.tok_mamba_spill)
        for j in range(4):
            mk.append(S.op("gpsimd", lambda g, j=j: g.affine_select(
                masks[:, 512 * j:512 * (j + 1)], A[:, 9216:9728],
                pattern=[[1, 512]], compare_op=ALU.is_ge, fill=0.0,
                base=-128 * j, channel_multiplier=-1), deps=[m0]))
        mk.append(S.op("gpsimd", lambda g: g.affine_select(
            maskp[:], A[:, 9216:9280], pattern=[[1, 64]],
            compare_op=ALU.is_ge, fill=0.0, base=64, channel_multiplier=-1),
            deps=[m0]))
        bias_t = self.sbuf("bias_t", [128, 16], F32)
        ptile = [Abf[:, 18432:19456], Abf[:, 19456:20480]]
        rcrow = self.PSB[7][0:1, 0:512]
        attn_un = Abf[:, 0:16384]

        at_toks = []
        pt_tok = {0: [], 1: []}
        for h in range(NH):
            qB = 64 if h < 2 else (128 if h < 5 else 512)
            nqb = T // qB
            kslice = Kb[64 * (h % 2):64 * (h % 2) + 64, :]
            qslice = Qb[64 * (h % 2):64 * (h % 2) + 64, :]
            for qb in range(nqb):
                qs = qB * qb
                qref = float(qs + qB - 1)
                kb_lo = max(0, (qs - 384) // 128) if qB < 512 else 0
                kb_hi = (qs + qB - 1) // 128
                bt = S.op("vector", lambda v, h=h, qref=qref: v.tensor_scalar(
                    P["bias_t"][:], P["kbase"][:], -qref,
                    float(slopes[h]), ALU.add, ALU.mult),
                    deps=[kb0] + pt_tok[0] + pt_tok[1])
                o_mms = []
                for kb in range(kb_lo, kb_hi + 1):
                    smm = S.op("tensor", lambda t, h=h, qs=qs, qB=qB, kb=kb,
                               kslice=kslice, qslice=qslice: t.matmul(
                        self.PSB[4][:, 0:qB],
                        kslice[:, 2048 * (h // 2) + 128 * kb:
                               2048 * (h // 2) + 128 * (kb + 1)],
                        qslice[:, 2048 * (h // 2) + qs:
                               2048 * (h // 2) + qs + qB],
                        start=True, stop=True),
                        deps=qk_toks + self.psb_free.get(4, []))
                    pb = kb % 2
                    ex = S.op("scalar", lambda s, kb=kb, pb=pb, qB=qB:
                              s.activation(
                                  out=ptile[pb][:, 0:qB],
                                  in_=self.PSB[4][:, 0:qB],
                                  func=ACT.Exp, scale=0.125,
                                  bias=P["bias_t"][:, kb:kb + 1]),
                              deps=[smm, bt] + pt_tok[pb])
                    self.psb_free[4] = [ex]
                    ptok = [ex]
                    if kb == kb_hi and qB < 512:
                        off = qs - 128 * kb
                        msl = (maskp[:, 0:qB] if off == 64 else
                               masks[:, 0:qB])
                        ptok = [S.op("vector", lambda v, pb=pb, qB=qB, msl=msl:
                            v.tensor_tensor(
                                ptile[pb][:, 0:qB], ptile[pb][:, 0:qB],
                                msl, ALU.mult), deps=[ex] + mk)]
                    elif qB == 512 and kb >= 4 * qb:
                        j = kb - 4 * qb
                        ptok = [S.op("vector", lambda v, j=j, pb=pb:
                            v.tensor_tensor(
                                ptile[pb][:, 0:512], ptile[pb][:, 0:512],
                                masks[:, 512 * j:512 * (j + 1)], ALU.mult),
                            deps=[ex] + mk)]
                    omm = S.op("tensor", lambda t, h=h, kb=kb, pb=pb, qB=qB,
                               kb_lo=kb_lo, kb_hi=kb_hi: t.matmul(
                        self.PSB[5][0:65, 0:qB],
                        Va[:, 1040 * kb + 65 * h:1040 * kb + 65 * (h + 1)],
                        ptile[pb][:, 0:qB],
                        start=(kb == kb_lo), stop=(kb == kb_hi)),
                        deps=ptok + v_toks + (self.psb_free.get(5, [])
                                              if kb == kb_lo else []))
                    pt_tok[pb] = [omm]
                    o_mms.append(omm)
                l1 = S.op("scalar", lambda s, qB=qB: s.activation(
                    out=rcrow[0:1, 0:qB], in_=self.PSB[5][64:65, 0:qB],
                    func=ACT.Ln, bias=0.0, scale=1.0),
                    deps=o_mms + self.psb_free.get(7, []))
                self.psb_free[7] = [l1]
                l2 = S.op("scalar", lambda s, qB=qB: s.activation(
                    out=self.xnsl.bitcast(F32)[0:1, 3072:3584][0:1, 0:qB], in_=rcrow[0:1, 0:qB],
                    func=ACT.Exp, bias=0.0, scale=-1.0), deps=[l1])
                bmm = S.op("tensor", lambda t, qB=qB: t.matmul(
                    self.PSB[6][0:64, 0:qB], P["ones1_f"][0:1, 0:64],
                    self.xnsl.bitcast(F32)[0:1, 3072:3584][0:1, 0:qB],
                    start=True, stop=True),
                    deps=[l2] + self.tok_const + self.psb_free.get(6, []))
                asl = attn_un[64 * (h % 2):64 * (h % 2) + 64,
                              2048 * (h // 2) + qs:2048 * (h // 2) + qs + qB]
                ev1 = S.op("scalar", lambda s, asl=asl, qB=qB: s.activation(
                    out=asl, in_=self.PSB[5][0:64, 0:qB], func=ACT.Copy,
                    bias=0.0, scale=1.0), deps=o_mms + [bmm])
                nm = S.op("vector", lambda v, asl=asl, qB=qB: v.tensor_tensor(
                    asl, asl, self.PSB[6][0:64, 0:qB], ALU.mult), deps=[ev1])
                self.psb_free[5] = [ev1]
                self.psb_free[6] = [nm]
                at_toks.append(nm)

        wpst = Qb[:, 0:8192]
        wp = []
        for k in range(8):
            tl = S.dma("sync", A[:, 8192:9216],
                       P["w_WpT"].ap()[128 * k:128 * (k + 1), :],
                       deps=at_toks + wp[-1:])
            wp.append(S.op("vector", lambda v, k=k: v.tensor_copy(
                wpst[:, 1024 * k:1024 * (k + 1)], A[:, 8192:9216]),
                deps=[tl]))
        pr_toks = []
        for dc in range(8):
            for tc in range(4):
                mms = []
                for k in range(8):
                    mms.append(S.op("tensor", lambda t, dc=dc, tc=tc, k=k:
                        t.matmul(
                            self.PSB[0][:, :],
                            wpst[:, 1024 * k + 128 * dc:
                                 1024 * k + 128 * (dc + 1)],
                            attn_un[:, 2048 * k + 512 * tc:
                                    2048 * k + 512 * (tc + 1)],
                            start=(k == 0), stop=(k == 7)),
                        deps=wp + at_toks + self.psb_free.get(0, [])))
                tl = S.dma("sync", A[:, 8704:9216],
                           P["h_xacc"].ap()[128 * dc:128 * (dc + 1),
                                            512 * tc:512 * (tc + 1)],
                           deps=pr_toks[-1:] + self.tok_ffn_acc)
                ad = S.op("vector", lambda v: v.tensor_add(
                    A[:, 8704:9216], A[:, 8704:9216], self.PSB[0][:, :]),
                    deps=[tl] + mms)
                self.psb_free[0] = [ad]
                sp = S.dma("sync",
                           P["h_xacc"].ap()[128 * dc:128 * (dc + 1),
                                            512 * tc:512 * (tc + 1)],
                           A[:, 8704:9216], deps=[ad])
                pr_toks.append(sp)
        self.tok_attn_acc = pr_toks
        self.dbg_out("attnu", self.scrbf[:, 0:2048], at_toks)
        self.dbg_dram("xacc6", P["h_xacc"], [D, T], pr_toks)


    def p7_moe(self):
        S, P = self.S, self.P
        A = self.scr
        Abf = self.scrbf
        xn1 = self.abufA
        hpr = self.abufC
        wst = [Abf[:, 0:8192], Abf[:, 8192:16384]]
        hraw = Abf[:, 16384:17408]
        t_b1, _ = self.load_cols("w_moeb1", 32)
        wc_toks = []
        prev_c = list(self.tok_attn_acc)
        for e in range(NE):
            for k in range(8):
                tl = S.dma("sync", A[:, 0:512],
                           P["w_moe1"].ap()[e, 128 * k:128 * (k + 1), :],
                           deps=prev_c)
                cv = S.op("vector", lambda v: v.tensor_copy(
                    Abf[:, 1024:1536], A[:, 0:512]), deps=[tl])
                sp = S.dma("sync", P["h_moe1"].ap()[e, 128 * k:128 * (k + 1), :],
                           Abf[:, 1024:1536], deps=[cv])
                wc_toks.append(sp)
                prev_c = [sp, cv]
            for ic in range(4):
                tl = S.dma("sync", A[:, 0:1024],
                           P["w_moe2"].ap()[e, 128 * ic:128 * (ic + 1), :],
                           deps=prev_c)
                cv = S.op("vector", lambda v: v.tensor_copy(
                    Abf[:, 2048:3072], A[:, 0:1024]), deps=[tl])
                sp = S.dma("sync", P["h_moe2"].ap()[e, 128 * ic:128 * (ic + 1), :],
                           Abf[:, 2048:3072], deps=[cv])
                wc_toks.append(sp)
                prev_c = [sp, cv]
        gw_toks = []
        for k in range(8):
            tl = S.dma("sync", A[:, 8448:8456],
                       P["w_gw"].ap()[128 * k:128 * (k + 1), :],
                       deps=self.tok_attn_acc + gw_toks[-1:])
            gw_toks.append(S.op("vector", lambda v, k=k: v.tensor_copy(
                self.xnsl[:, 4096:4160][:, 8 * k:8 * (k + 1)], A[:, 8448:8456]), deps=[tl]))

        oh8 = self.xnsl[0:8, 2048:3072]
        z0 = S.op("vector", lambda v: v.memset(A[0:8, 0:1024], 1.0),
                  deps=self.tok_attn_acc)
        zoh = S.op("gpsimd", lambda g: g.affine_select(
            oh8[:], A[0:8, 0:1024], pattern=[[1, 8], [0, 128]],
            compare_op=ALU.is_equal, fill=0.0, base=0, channel_multiplier=-1),
            deps=[z0])

        wr8 = self.xnsl[0:8, 0:2048]
        lg = A[:, 8456:8464]
        pb_ = A[:, 8464:8472]
        col1 = A[:, 8472:8473]
        col2 = A[:, 8473:8474]
        wr_toks = []
        for tk in range(16):
            mms = []
            for k in range(8):
                mms.append(S.op("tensor", lambda t, tk=tk, k=k: t.matmul(
                    self.PSB[0][:, 0:8],
                    xn1[:, T * k + 128 * tk:T * k + 128 * (tk + 1)],
                    self.xnsl[:, 4096:4160][:, 8 * k:8 * (k + 1)],
                    start=(k == 0), stop=(k == 7)),
                    deps=gw_toks + self.tok_xn1 + self.psb_free.get(0, [])))
            g0 = S.op("vector", lambda v: v.tensor_reduce(
                col1, self.PSB[0][:, 0:8], AXL.X, ALU.max), deps=mms)
            g1 = S.op("vector", lambda v: v.tensor_scalar_mul(
                col2, col1, -1.0), deps=[g0])
            g2 = S.op("scalar", lambda s: s.activation(
                out=lg, in_=self.PSB[0][:, 0:8], func=ACT.Exp,
                bias=col2[:, 0:1], scale=1.0), deps=[g1] + mms)
            self.psb_free[0] = [g2]
            g3 = S.op("vector", lambda v: v.tensor_reduce(
                col1, lg, AXL.X, ALU.add), deps=[g2])
            g4 = S.op("vector", lambda v: v.reciprocal(col2, col1), deps=[g3])
            g5 = S.op("vector", lambda v: v.tensor_scalar(
                pb_, lg, col2[:, 0:1], None, ALU.mult), deps=[g4])
            g6 = S.op("vector", lambda v: v.tensor_reduce(
                col1, pb_, AXL.X, ALU.max), deps=[g5])
            g7 = S.op("vector", lambda v: v.tensor_scalar(
                lg, pb_, col1[:, 0:1], None, ALU.is_ge), deps=[g6])
            g8 = S.op("vector", lambda v: v.scalar_tensor_tensor(
                lg, lg, -2.0, pb_, ALU.mult, ALU.add), deps=[g7])
            g9 = S.op("vector", lambda v: v.tensor_reduce(
                col2, lg, AXL.X, ALU.max), deps=[g8])
            ga = S.op("vector", lambda v: v.tensor_scalar(
                lg, pb_, col2[:, 0:1], None, ALU.is_ge), deps=[g9])
            gb = S.op("vector", lambda v: v.tensor_add(
                col1, col1, col2), deps=[ga])
            gc = S.op("vector", lambda v: v.tensor_scalar(
                lg, lg, col1[:, 0:1], None, ALU.mult), deps=[gb])
            gd = S.op("vector", lambda v: v.tensor_copy(
                self.xnsl[:, 4160:4168][:], lg), deps=[gc])
            tmm = S.op("tensor", lambda t: t.transpose(
                self.PSA.bitcast(BF16)[0:8, 2048:2176], self.xnsl[:, 4160:4168][:],
                P["id_bf"][:]), deps=[gd] + self.tok_const +
                self.psb_free.get(1, []))
            ev = S.op("scalar", lambda s, tk=tk: s.activation(
                out=wr8[:, 128 * tk:128 * (tk + 1)],
                in_=self.PSA.bitcast(BF16)[0:8, 2048:2176],
                func=ACT.Copy, bias=0.0, scale=1.0), deps=[tmm])
            self.psb_free[1] = [ev]
            wr_toks.append(ev)

        b2t = self.xnsl[0:8, 3072:4096]
        tb2a = S.dma("sync", A[0:8, 7168:8192], P["w_moeb2"].ap(),
                     deps=self.tok_attn_acc)
        tok_b2t = S.op("vector", lambda v: v.tensor_copy(
            b2t[:], A[0:8, 7168:8192]), deps=[tb2a])
        out_toks = []
        for tr in range(4):
            to = 512 * tr
            h_toks = []
            prev_init = list(wr_toks) + wc_toks[-1:] + out_toks[-1:]
            prevbuf = {0: list(prev_init), 1: list(prev_init)}
            for e in range(NE):
                st = wst[e % 2]
                wl = []
                for k in range(8):
                    wl.append(S.dma("sync", st[:, 512 * k:512 * (k + 1)],
                                    P["h_moe1"].ap()[e, 128 * k:128 * (k + 1), :],
                                    deps=prevbuf[e % 2]))
                emms = []
                for ic in range(4):
                    mms = []
                    for k in range(8):
                        mms.append(S.op("tensor",
                            lambda t, ic=ic, k=k, st=st, to=to: t.matmul(
                                self.PSB[2][:, :],
                                st[:, 512 * k + 128 * ic:512 * k + 128 * (ic + 1)],
                                xn1[:, T * k + to:T * k + to + 512],
                                start=(k == 0), stop=(k == 7)),
                            deps=wl + self.tok_xn1 + self.psb_free.get(2, [])))
                    ge = S.op("scalar", lambda s, e=e, ic=ic: s.activation(
                        out=hraw[:, 0:512], in_=self.PSB[2][:, :],
                        func=ACT.Gelu, bias=P["w_moeb1_c"][:, 4 * e + ic:
                                              4 * e + ic + 1], scale=1.0),
                        deps=mms + [t_b1])
                    self.psb_free[2] = [ge]
                    bmm = S.op("tensor", lambda t, e=e, tr=tr: t.matmul(
                        self.PSB[3][:, :], oh8[:, 128 * e:128 * (e + 1)],
                        wr8[:, 512 * tr:512 * (tr + 1)],
                        start=True, stop=True),
                        deps=[zoh] + wr_toks + self.psb_free.get(3, []))
                    hp = S.op("vector", lambda v, e=e, ic=ic: v.tensor_tensor(
                        hpr[:, 2048 * e + 512 * ic:2048 * e + 512 * (ic + 1)],
                        hraw[:, 0:512], self.PSB[3][:, :], ALU.mult),
                        deps=[ge, bmm])
                    self.psb_free[3] = [hp]
                    h_toks.append(hp)
                    emms += mms
                prevbuf[e % 2] = emms
            mm_all = []
            prevbuf2 = {0: list(h_toks), 1: list(h_toks)}
            for e in range(NE):
                st = wst[e % 2]
                wl = [S.dma("sync", st[:, 1024 * ic:1024 * (ic + 1)],
                            P["h_moe2"].ap()[e, 128 * ic:128 * (ic + 1), :],
                            deps=prevbuf2[e % 2]) for ic in range(4)]
                emms = []
                for dc in range(8):
                    for ic in range(4):
                        emms.append(S.op("tensor",
                            lambda t, e=e, dc=dc, ic=ic, st=st: t.matmul(
                                self.PSB[dc][:, :],
                                st[:, 1024 * ic + 128 * dc:
                                   1024 * ic + 128 * (dc + 1)],
                                hpr[:, 2048 * e + 512 * ic:
                                    2048 * e + 512 * (ic + 1)],
                                start=(e == 0 and ic == 0), stop=False),
                            deps=wl + h_toks +
                                 (self.psb_free.get(dc, []) if e == 0 and
                                  ic == 0 else [])))
                mm_all += emms
                prevbuf2[e % 2] = emms
            for dc in range(8):
                fmm = S.op("tensor", lambda t, dc=dc, tr=tr: t.matmul(
                    self.PSB[dc][:, :], b2t[:, 128 * dc:128 * (dc + 1)],
                    wr8[:, 512 * tr:512 * (tr + 1)],
                    start=False, stop=True),
                    deps=mm_all + [tok_b2t] + wr_toks)
                tl = S.dma("sync", A[:, 8704:9216],
                           P["h_xacc"].ap()[128 * dc:128 * (dc + 1),
                                            to:to + 512],
                           deps=self.tok_attn_acc + out_toks[-1:])
                ad = S.op("vector", lambda v, dc=dc: v.tensor_add(
                    A[:, 8704:9216], A[:, 8704:9216], self.PSB[dc][:, :]),
                    deps=[tl, fmm])
                self.psb_free[dc] = [ad]
                sp = S.dma("sync",
                           P["out_fm"].ap()[128 * dc:128 * (dc + 1),
                                            to:to + 512],
                           A[:, 8704:9216], deps=[ad])
                out_toks.append(sp)
        self.final_deps += out_toks


_BUILD_CACHE = {}


def _get_nc():
    if "nc" not in _BUILD_CACHE:
        _BUILD_CACHE["nc"] = Builder().build()
    return _BUILD_CACHE["nc"]


def kernel(**inputs):
    w, xs = prep_inputs(inputs)
    nc = _get_nc()
    in_maps = [dict(w, x_fm=xs[c]) for c in range(8)]
    res = run_bass_kernel_spmd(nc, in_maps, core_ids=list(range(8)))
    out = np.stack([r["out_fm"].T for r in res.results], axis=0)
    return np.ascontiguousarray(out.astype(np.float32))


# version 06
# speedup vs baseline: 1.1285x; 1.0712x over previous
import sys
import numpy as np
import concourse.bass as bass
import concourse.mybir as mybir
from concourse.bass_utils import run_bass_kernel_spmd

F32 = mybir.dt.float32
BF16 = mybir.dt.bfloat16
ALU = mybir.AluOpType
ACT = mybir.ActivationFunctionType
AXL = mybir.AxisListType

T = 2048
TH = T // 2
TC = 512
D = 1024
DI = 2048
DS = 16
DTR = 64
NH = 16
DH = 64
NE = 8
EC = 512
FF = 2816
FFC = FF // 128
EPS = 1e-5
RMAGIC = float(np.float32(1.5 * 2.0**23))
QBE = float(np.float32(128.0) - np.float32(1e-5))
ENGINES = ("tensor", "vector", "scalar", "gpsimd", "sync")


class Tok:
    __slots__ = ("eng", "idx")

    def __init__(self, eng, idx):
        self.eng, self.idx = eng, idx


class Sched:

    def __init__(self, nc):
        self.nc = nc
        self.ops = {e: [] for e in ENGINES}
        self.sem = {e: nc.alloc_semaphore(f"sem_{e}") for e in ENGINES}
        self.dsem = {e: nc.alloc_semaphore(f"dsem_{e}") for e in ENGINES}
        self.fence_buf = nc.alloc_sbuf_tensor("sched_fence", [128, 4],
                                              mybir.dt.float32)
        self.junk_sem = nc.alloc_semaphore("sem_junk")

    def op(self, eng, fn, deps=()):
        self.ops[eng].append([fn, list(deps), False, False,
                              sys._getframe(1).f_lineno])
        return Tok(eng, len(self.ops[eng]) - 1)

    def dma(self, eng, out, in_, deps=(), **kw):
        assert out is not None and in_ is not None
        fn = lambda e, out=out, in_=in_, kw=kw: e.dma_start(out=out, in_=in_, **kw)
        self.ops[eng].append([fn, list(deps), False, True,
                              sys._getframe(1).f_lineno])
        return Tok(eng, len(self.ops[eng]) - 1)

    def finalize(self, block):
        for e in ENGINES:
            for rec in self.ops[e]:
                if rec[3]:
                    rec[2] = True
                for t in rec[1]:
                    self.ops[t.eng][t.idx][2] = True
        self.incval = {e: [] for e in ENGINES}
        self.dincval = {e: [] for e in ENGINES}
        for e in ENGINES:
            c = dc = 0
            for rec in self.ops[e]:
                if rec[2]:
                    if rec[3]:
                        dc += 16
                    else:
                        c += 1
                self.incval[e].append(c)
                self.dincval[e].append(dc)

        GAPCOL = {"vector": 0, "scalar": 1, "gpsimd": 2}

        def emitter(eng_name):
            def run(eng):
                waited = {}
                empos = {}
                pos = 0

                def gap_op():
                    col = GAPCOL[eng_name]
                    fb = self.fence_buf[:, col:col + 1]
                    if eng_name == "scalar":
                        g = eng.activation(
                            out=fb, in_=fb,
                            func=mybir.ActivationFunctionType.Copy,
                            bias=0.0, scale=1.0)
                    else:
                        g = eng.memset(fb, 0.0)
                    g.then_inc(self.junk_sem, 1)
                    return g

                for i, (fn, deps, inc, is_dma, _ln) in enumerate(
                        self.ops[eng_name]):
                    for t in deps:
                        t_isdma = self.ops[t.eng][t.idx][3]
                        if (t.eng == eng_name and not t_isdma
                                and eng_name not in GAPCOL):
                            continue
                        if t_isdma:
                            sem, val, key = (self.dsem[t.eng],
                                             self.dincval[t.eng][t.idx],
                                             ("d", t.eng))
                        else:
                            sem, val, key = (self.sem[t.eng],
                                             self.incval[t.eng][t.idx],
                                             ("c", t.eng))
                        if waited.get(key, -1) < val:
                            eng.wait_ge(sem, val)
                            waited[key] = val
                    inst = fn(eng)
                    empos[i] = pos
                    pos += 1
                    if inc:
                        if eng_name == "vector" and not is_dma:
                            try:
                                small = any(int(o.ap[0][1]) < 128
                                            for o in inst.ins.outs)
                            except Exception:
                                small = True
                            if small:
                                inst = eng.memset(
                                    self.fence_buf[:, 0:1], 0.0)
                                pos += 1
                        inst.then_inc(self.dsem[eng_name] if is_dma
                                      else self.sem[eng_name],
                                      16 if is_dma else 1)
            return run

        for name in ENGINES:
            if self.ops[name]:
                getattr(block, name)(emitter(name))



def _t(a):
    return np.ascontiguousarray(np.asarray(a, dtype=np.float32).T)


def _c(a, shape=None):
    a = np.asarray(a, dtype=np.float32)
    if shape is not None:
        a = a.reshape(shape)
    return np.ascontiguousarray(a)


def _v(a):
    return np.ascontiguousarray(np.asarray(a, np.float32).reshape(-1, 128))


def prep_inputs(inp):
    w = {}
    w["w_ln1g"] = _v(inp["L0_attn_ln1_g"])
    w["w_ln1b"] = _v(inp["L0_attn_ln1_b"])
    w["w_ln2g"] = _v(inp["L0_attn_ln2_g"])
    w["w_ln2b"] = _v(inp["L0_attn_ln2_b"])
    w["w_W1T"] = _t(inp["L0_attn_W1"])
    w["w_b1"] = _v(inp["L0_attn_b1"])
    w["w_W2T"] = _t(inp["L0_attn_W2"])
    w["w_b2"] = _v(inp["L0_attn_b2"])
    w["w_inpT"] = _t(inp["L0_attn_in_proj_w"])
    w["w_convW"] = _c(inp["L0_attn_conv_w"], (DI, 4))
    w["w_convB"] = _v(inp["L0_attn_conv_b"])
    xp = np.asarray(inp["L0_attn_x_proj_w"], np.float32)
    xpp = np.zeros((128, DI), np.float32)
    xpp[0:64] = xp[0:64]
    xpp[64:80] = xp[64:80]
    xpp[96:112] = xp[80:96]
    w["w_xprojT"] = _t(xpp)
    w["w_dtprojT"] = _t(inp["L0_attn_dt_proj_w"])
    w["w_dtb"] = _v(inp["L0_attn_dt_proj_b"])
    w["w_Alog"] = _c(inp["L0_attn_A_log"])
    w["w_Dp"] = _v(inp["L0_attn_D_param"])
    w["w_outpT"] = _t(inp["L0_attn_out_proj_w"])
    w["w_clipx"] = _c(inp["L0_ffn_clip_x"], (1, 1))
    w["w_clipd"] = _c(inp["L0_ffn_clip_down"], (1, 1))
    w["w_wg"] = _c(inp["L0_ffn_w_gate"])
    w["w_wu"] = _c(inp["L0_ffn_w_up"])
    w["w_wd"] = _c(inp["L0_ffn_w_down"])
    Wa = np.asarray(inp["L1_attn_W_attn"], dtype=np.float32)
    w["w_WqT"] = _t(Wa[0:D])
    w["w_WkT"] = _t(Wa[D:2 * D])
    w["w_WvT"] = _t(Wa[2 * D:3 * D])
    w["w_WpT"] = _t(inp["L1_attn_W_proj"])
    w["w_gw"] = _c(inp["L1_ffn_gate_w"])
    w["w_moe1"] = _c(inp["L1_ffn_w1"])
    w["w_moeb1"] = _c(inp["L1_ffn_b1"], (NE * EC // 128, 128))
    w["w_moe2"] = _c(inp["L1_ffn_w2"])
    w["w_moeb2"] = _c(inp["L1_ffn_b2"], (NE * D // 128, 128))
    xs = [np.ascontiguousarray(np.asarray(inp["L0_attn_x"][c], np.float32).T)
          for c in range(8)]
    return w, xs


def alibi_slopes(n=16):
    import math
    start = 2.0 ** (-(2.0 ** -(math.log2(n) - 3)))
    return [start * start ** i for i in range(n)]


DRAM_SPECS = [
    ("w_ln1g", [8, 128]), ("w_ln1b", [8, 128]), ("w_ln2g", [8, 128]),
    ("w_ln2b", [8, 128]), ("w_W1T", [D, DI]), ("w_b1", [16, 128]),
    ("w_W2T", [DI, D]), ("w_b2", [8, 128]), ("w_inpT", [D, 2 * DI]),
    ("w_convW", [DI, 4]), ("w_convB", [16, 128]), ("w_xprojT", [DI, 128]),
    ("w_dtprojT", [DTR, DI]), ("w_dtb", [16, 128]), ("w_Alog", [DI, DS]),
    ("w_Dp", [16, 128]), ("w_outpT", [DI, D]), ("w_clipx", [1, 1]),
    ("w_clipd", [1, 1]), ("w_wg", [FF, D]), ("w_wu", [FF, D]),
    ("w_wd", [D, FF]), ("w_WqT", [D, D]), ("w_WkT", [D, D]),
    ("w_WvT", [D, D]), ("w_WpT", [D, D]), ("w_gw", [D, NE]),
    ("w_moe1", [NE, D, EC]), ("w_moeb1", [NE * EC // 128, 128]),
    ("w_moe2", [NE, EC, D]), ("w_moeb2", [NE * D // 128, 128]),
]


class Builder:
    def __init__(self, dbg=(), stop=None, drc=False):
        self.dbg = set(dbg) if dbg else set()
        self.stop = stop
        nc = bass.Bass("TRN2", target_bir_lowering=False,
                       detect_race_conditions=drc)
        self.nc = nc
        self.P = {}
        self.final_deps = []
        self.dbg_last = []
        self.tok_dbg = []
        self.din("x_fm", [D, T])
        for n, s in DRAM_SPECS:
            self.din(n, s)
        self.dout("out_fm", [D, T])
        self.dint("h_xacc", [D, T])
        self.dint("h_xq", [D, T], BF16)
        self.dint("h_qxf", [D, T])
        self.dint("h_winp", [D, 2 * DI], BF16)
        self.dint("h_moe1", [NE, D, EC], BF16)
        self.dint("h_moe2", [NE, EC, D], BF16)
        self.dint("cc_in", [4, 16])
        self.dint("cc_out", [4, 16])
        self.S = Sched(nc)
        self.psb_free = {}
        self.tok_cc = []
        self.PSA = nc.alloc_psum_tensor("psall", [128, 4096], F32)
        self.PSB = [self.PSA[:, 512 * i:512 * (i + 1)] for i in range(8)]
        self.scr = nc.alloc_sbuf_tensor("scr", [128, 10240], F32)
        self.scrbf = self.scr.bitcast(BF16)
        self.xnsl = nc.alloc_sbuf_tensor("xnsl", [128, 16384], BF16)
        self.abufA = nc.alloc_sbuf_tensor("abufA", [128, 16384], BF16)
        self.abufC = nc.alloc_sbuf_tensor("abufC", [128, 16384], BF16)
        self.wbufA = nc.alloc_sbuf_tensor("wbufA", [128, 16384], BF16)
        self.wbufB = nc.alloc_sbuf_tensor("wbufB", [128, 16640], BF16)

    def din(self, name, shape, dtype=F32):
        self.P[name] = self.nc.dram_tensor(name, shape, dtype, kind="ExternalInput")

    def dout(self, name, shape, dtype=F32):
        self.P[name] = self.nc.dram_tensor(name, shape, dtype, kind="ExternalOutput")

    def dint(self, name, shape, dtype=F32):
        self.P[name] = self.nc.dram_tensor(name, shape, dtype)

    def sbuf(self, name, shape, dtype=F32):
        self.P[name] = self.nc.alloc_sbuf_tensor(name, shape, dtype)
        return self.P[name]

    def dbg_out(self, tag, ap, deps):
        if tag not in self.dbg:
            return []
        t = self.nc.dram_tensor("dbg_" + tag, list(ap.shape), ap.dtype,
                                kind="ExternalOutput")
        tok = self.S.dma("sync", t.ap(), ap, deps=deps)
        self.final_deps.append(tok)
        self.tok_dbg.append(tok)
        return [tok]

    def dbg_dram(self, tag, dram, shape, deps):
        if tag not in self.dbg:
            return
        S = self.S
        t = self.nc.dram_tensor("dbg_" + tag, list(shape), F32,
                                kind="ExternalOutput")
        stg = self.scr[:, 0:shape[1]]
        last = list(deps) + self.dbg_last
        for r0 in range(0, shape[0], 128):
            l1 = S.dma("sync", stg, dram.ap()[r0:r0 + 128, :], deps=last)
            l2 = S.dma("sync", t.ap()[r0:r0 + 128, :], stg, deps=[l1])
            last = [l2]
        self.dbg_last = last
        self.final_deps.append(last[0])

    def load_cols(self, dram_name, n, psb=3):
        S, P = self.S, self.P
        if "lc_row" not in P:
            self.sbuf("lc_row", [32, 128], F32)
            self.lc_last = []
        row = P["lc_row"]
        col = self.sbuf(dram_name + "_c", [128, n], F32)
        t0 = S.dma("sync", row[0:n, :], P[dram_name].ap(), deps=self.lc_last)
        t1 = S.op("tensor", lambda t, n=n: t.matmul(
            self.PSB[psb][:, 0:n], row[0:n, :], P["id_f"][0:n, 0:n],
            start=True, stop=True),
            deps=[t0] + self.tok_const + self.psb_free.get(psb, []))
        t2 = S.op("scalar", lambda s, n=n: s.activation(
            out=col[:], in_=self.PSB[psb][:, 0:n], func=ACT.Copy,
            bias=0.0, scale=1.0), deps=[t1])
        self.psb_free[psb] = [t2]
        self.lc_last = [t2]
        return t2, col

    def bcast_row_psum(self, psum_ap, row_ap, deps, bf16=False):
        ones = self.P["ones1_bf" if bf16 else "ones1_f"]
        return self.S.op("tensor", lambda t, o=psum_ap, r=row_ap, on=ones:
                         t.matmul(o, on[:], r, start=True, stop=True),
                         deps=deps)

    def collective_max(self, n, deps, name):
        S, P = self.S, self.P
        res = self.sbuf(f"ccres_{name}", [4, 16], F32)
        t1 = S.dma("sync", P["cc_in"].ap(), P["ccst"][:], deps=deps)
        tok_cc = S.op("gpsimd", lambda g: g.collective_compute(
            "AllReduce", ALU.max, replica_groups=[list(range(8))],
            ins=[P["cc_in"].ap().opt()], outs=[P["cc_out"].ap().opt()]),
            deps=[t1])
        t2 = S.dma("sync", res[:], P["cc_out"].ap(), deps=[tok_cc])
        self.tok_cc = [t2]
        return t2, res

    def build(self):
        self.p0_constants()
        self.p1_outer_ln()
        if self.stop == "p1":
            return self.done()
        self.p2_signnet()
        if self.stop == "p2":
            return self.done()
        self.p4_mamba()
        if self.stop == "p4":
            return self.done()
        self.p5_ffn()
        if self.stop == "p5":
            return self.done()
        self.p6_attn()
        if self.stop == "p6":
            return self.done()
        self.p7_moe()
        return self.done()

    def done(self):
        S = self.S
        if self.final_deps:
            S.op("sync", lambda e: e.nop(), deps=self.final_deps)
        with self.nc.Block() as block:
            S.finalize(block)
        return self.nc

    def p0_constants(self):
        S, P = self.S, self.P
        self.sbuf("ones1_bf", [1, 128], BF16)
        self.sbuf("ones1_f", [1, 128], F32)
        self.sbuf("onescol_f", [128, 1], F32)
        self.sbuf("onescol_b", [128, 1], BF16)
        self.sbuf("eps1", [1, 1], F32)
        self.sbuf("id_bf", [128, 128], BF16)
        self.sbuf("id_f", [128, 128], F32)
        self.sbuf("ccst", [4, 16], F32)
        c0 = S.op("vector", lambda v: v.memset(P["ones1_bf"][:], 1.0))
        c1 = S.op("vector", lambda v: v.memset(P["ones1_f"][:], 1.0))
        c2 = S.op("vector", lambda v: v.memset(P["onescol_f"][:], 1.0))
        c2b = S.op("vector", lambda v: v.memset(P["onescol_b"][:], 1.0))
        c3 = S.op("vector", lambda v: v.memset(P["eps1"][:], EPS))
        i0 = S.op("vector", lambda v: v.memset(self.scr[:, 0:128], 1.0))
        i1 = S.op("gpsimd", lambda g: g.affine_select(
            P["id_bf"][:], self.scr[:, 0:128], pattern=[[-1, 128]],
            compare_op=ALU.is_equal, fill=0.0, base=0, channel_multiplier=1),
            deps=[i0])
        i2 = S.op("gpsimd", lambda g: g.affine_select(
            P["id_f"][:], self.scr[:, 0:128], pattern=[[-1, 128]],
            compare_op=ALU.is_equal, fill=0.0, base=0, channel_multiplier=1),
            deps=[i0])
        self.tok_const = [c0, c1, c2, c2b, c3, i1, i2]

    def p1_outer_ln(self):
        S, P = self.S, self.P
        A = self.scr
        xfm = P["x_fm"].ap()
        xa0 = A[:, 0:2048]
        xsq = A[:, 2048:4096]
        prev_mm = []
        for k in range(8):
            tl = S.dma("sync", xa0, xfm[128 * k:128 * (k + 1), :],
                       deps=prev_mm + self.tok_const)
            tsq = S.op("scalar", lambda s: s.activation(
                out=A[:, 2048:4096], in_=A[:, 0:2048], func=ACT.Square,
                bias=0.0, scale=1.0), deps=[tl])
            prev_mm = []
            for tc in range(4):
                m1 = S.op("tensor", lambda t, tc=tc, k=k: t.matmul(
                    self.PSB[tc][0:1, :], P["onescol_f"][:],
                    A[:, 512 * tc:512 * (tc + 1)],
                    start=(k == 0), stop=(k == 7)), deps=[tl, tsq])
                m2 = S.op("tensor", lambda t, tc=tc, k=k: t.matmul(
                    self.PSB[4 + tc][0:1, :], P["onescol_f"][:],
                    A[:, 2048 + 512 * tc:2048 + 512 * (tc + 1)],
                    start=(k == 0), stop=(k == 7)), deps=[tsq])
                prev_mm += [m1, m2]
        mrow = A[0:1, 0:2048]
        vrow = A[0:1, 2048:4096]
        ev = []
        for tc in range(4):
            ev.append(S.op("scalar", lambda s, tc=tc: s.activation(
                out=A[0:1, 512 * tc:512 * (tc + 1)],
                in_=self.PSB[tc][0:1, :], func=ACT.Copy, bias=0.0,
                scale=1.0 / D), deps=prev_mm))
            ev.append(S.op("scalar", lambda s, tc=tc: s.activation(
                out=A[0:1, 2048 + 512 * tc:2048 + 512 * (tc + 1)],
                in_=self.PSB[4 + tc][0:1, :], func=ACT.Copy, bias=0.0,
                scale=1.0 / D), deps=prev_mm))
        for i in range(8):
            self.psb_free[i] = list(ev)

        rt1 = A[0:1, 4096:6144]
        rt2 = A[0:1, 6144:8192]
        rrow = A[0:1, 8192:10240]
        r1row = rt2
        o3 = S.op("vector", lambda v: v.tensor_tensor(
            A[0:1, 4096:6144], mrow, mrow, ALU.mult), deps=ev)
        o4 = S.op("vector", lambda v: v.tensor_sub(
            vrow, vrow, A[0:1, 4096:6144]), deps=[o3])

        def emit_rsqrt(dst, srcv, deps):
            a = S.op("scalar", lambda s: s.activation(
                out=rt1, in_=srcv, func=ACT.Sqrt,
                bias=P["eps1"][:, 0:1], scale=1.0), deps=deps + self.tok_const)
            y0 = S.op("vector", lambda v: v.reciprocal(rt2, rt1), deps=[a])
            b1 = S.op("vector", lambda v: v.tensor_scalar_add(
                rt1, srcv, EPS), deps=[y0])
            b2 = S.op("vector", lambda v: v.tensor_tensor(
                rt1, rt1, rt2, ALU.mult), deps=[b1])
            b3 = S.op("vector", lambda v: v.tensor_tensor(
                rt1, rt1, rt2, ALU.mult), deps=[b2])
            b4 = S.op("vector", lambda v: v.tensor_scalar(
                rt1, rt1, -0.5, 1.5, ALU.mult, ALU.add), deps=[b3])
            return S.op("vector", lambda v: v.tensor_tensor(
                dst, rt1, rt2, ALU.mult), deps=[b4])

        o6 = emit_rsqrt(rrow, vrow, [o4])
        o7 = S.op("vector", lambda v: v.tensor_tensor(
            A[0:1, 4096:6144], rrow, rrow, ALU.mult), deps=[o6])
        o8 = S.op("vector", lambda v: v.tensor_tensor(
            vrow, A[0:1, 4096:6144], vrow, ALU.mult), deps=[o7])
        o10 = emit_rsqrt(r1row, vrow, [o8])

        m_bc = A[:, 0:2048]
        r1_bc = A[:, 6144:8192]
        r_bc = A[:, 8192:10240]
        self.r1_bc = r1_bc
        bc_last = [o10]
        for row, dstb in [(mrow, m_bc), (r1row, r1_bc), (rrow, r_bc)]:
            for tc in range(4):
                mm = self.bcast_row_psum(
                    self.PSB[0][:, :], row[0:1, 512 * tc:512 * (tc + 1)],
                    deps=bc_last + self.tok_const + self.psb_free.get(0, []))
                cp = S.op("scalar", lambda s, dstb=dstb, tc=tc: s.activation(
                    out=dstb[:, 512 * tc:512 * (tc + 1)], in_=self.PSB[0][:],
                    func=ACT.Copy, bias=0.0, scale=1.0), deps=[mm])
                self.psb_free[0] = [cp]
                bc_last = [cp]

        xa = A[:, 2048:4096]
        xnf = A[:, 4096:6144]
        xn_bf = self.xnsl
        mxl = self.sbuf("mxl", [128, 8], F32)
        xn_toks = []
        red_toks = []
        prev = list(bc_last)
        for k in range(8):
            tl = S.dma("sync", xa, xfm[128 * k:128 * (k + 1), :], deps=prev)
            s1 = S.op("vector", lambda v: v.scalar_tensor_tensor(
                xnf, m_bc, -1.0, xa, ALU.mult, ALU.add), deps=[tl])
            s2 = S.op("vector", lambda v: v.tensor_tensor(
                xnf, xnf, r_bc, ALU.mult), deps=[s1])
            s3 = S.op("vector", lambda v, k=k: v.tensor_copy(
                xn_bf[:, T * k:T * (k + 1)], xnf), deps=[s2])
            red_toks.append(S.op("vector", lambda v, k=k: v.tensor_reduce(
                P["mxl"][:, k:k + 1], xnf, AXL.X, ALU.max,
                apply_absolute_value=True), deps=[s2]))
            prev = [s3, red_toks[-1]]
            xn_toks.append(s3)
        self.tok_xn = xn_toks
        mx1 = self.sbuf("mx1", [128, 1], F32)
        rt = S.op("vector", lambda v: v.tensor_reduce(
            P["mx1"][:], P["mxl"][:], AXL.X, ALU.max), deps=red_toks)
        mxrow = self.sbuf("mxrow", [1, 128], F32)
        tmm = S.op("tensor", lambda t: t.matmul(
            self.PSB[2][0:1, 0:128], P["mx1"][:], P["id_f"][:],
            start=True, stop=True),
            deps=[rt] + self.tok_const + self.psb_free.get(2, []))
        cpm = S.op("scalar", lambda s: s.activation(
            out=P["mxrow"][:], in_=self.PSB[2][0:1, 0:128], func=ACT.Copy,
            bias=0.0, scale=1.0), deps=[tmm])
        self.psb_free[2] = [cpm]
        mxall = self.sbuf("mxall", [1, 1], F32)
        pr = S.op("vector", lambda v: v.tensor_reduce(
            P["mxall"][:], P["mxrow"][:], AXL.X, ALU.max), deps=[cpm])
        z1 = S.op("vector", lambda v: v.memset(P["ccst"][:], 0.0), deps=[pr])
        z2 = S.op("vector", lambda v: v.tensor_copy(
            P["ccst"][0:1, 0:1], P["mxall"][:]), deps=[z1])
        self.dbg_out("mxl", P["mxl"][:], red_toks)
        self.dbg_out("mx1", P["mx1"][:], [rt])
        self.dbg_out("mxrow", P["mxrow"][:], [cpm])
        self.dbg_out("mxall", P["mxall"][:], [pr])
        self.dbg_out("ccsts1", P["ccst"][:], [z2])
        cct, ccres = self.collective_max(1, [z2], "s1")

        clipx = self.sbuf("clipx_sb", [1, 1], F32)
        tcl = S.dma("sync", clipx[:], P["w_clipx"].ap())
        q1 = S.op("scalar", lambda s: s.activation(
            out=P["clipx_sb"][:], in_=P["clipx_sb"][:], func=ACT.Sigmoid,
            bias=0.0, scale=1.0), deps=[tcl])
        s1s = self.sbuf("s1s", [1, 2], F32)
        q2 = S.op("vector", lambda v, ccres=ccres: v.tensor_tensor(
            P["s1s"][0:1, 0:1], P["clipx_sb"][:], ccres[0:1, 0:1], ALU.mult),
            deps=[q1, cct])
        q3 = S.op("vector", lambda v: v.tensor_scalar_mul(
            P["s1s"][0:1, 0:1], P["s1s"][0:1, 0:1], 1.0 / 127.0), deps=[q2])
        q4 = S.op("vector", lambda v: v.reciprocal(
            P["s1s"][0:1, 1:2], P["s1s"][0:1, 0:1]), deps=[q3])
        s1c = self.sbuf("s1c", [128, 2], F32)
        mm1 = S.op("tensor", lambda t: t.matmul(
            self.PSB[1][:, 0:2], P["ones1_f"][:], P["s1s"][:],
            start=True, stop=True),
            deps=[q4] + self.tok_const + self.psb_free.get(1, []))
        cp1 = S.op("scalar", lambda s: s.activation(
            out=P["s1c"][:], in_=self.PSB[1][:, 0:2], func=ACT.Copy,
            bias=0.0, scale=1.0), deps=[mm1])
        self.psb_free[1] = [cp1]
        self.tok_s1c = cp1

        xq_bf = self.abufC
        self.tok_xq_spill = []
        prevq = list(prev)
        for k in range(8):
            tl = S.dma("sync", xa, xfm[128 * k:128 * (k + 1), :], deps=prevq)
            u1 = S.op("vector", lambda v: v.scalar_tensor_tensor(
                xnf, m_bc, -1.0, xa, ALU.mult, ALU.add), deps=[tl])
            u2 = S.op("vector", lambda v: v.tensor_tensor(
                xnf, xnf, r_bc, ALU.mult), deps=[u1])
            a1 = S.op("vector", lambda v: v.tensor_scalar(
                xnf, xnf, P["s1c"][:, 1:2], RMAGIC, ALU.mult, ALU.add),
                deps=[u2, cp1])
            a2 = S.op("vector", lambda v: v.tensor_scalar(
                xnf, xnf, -RMAGIC, 127.0, ALU.add, ALU.min), deps=[a1])
            a3 = S.op("vector", lambda v, k=k: v.tensor_scalar(
                xq_bf[:, T * k:T * (k + 1)], xnf, -128.0, None, ALU.max),
                deps=[a2] + self.tok_xq_spill[-1:])
            sp = S.dma("sync", P["h_xq"].ap()[128 * k:128 * (k + 1), :],
                       xq_bf[:, T * k:T * (k + 1)], deps=[a3])
            self.tok_xq_spill.append(sp)
            prevq = [a3]

        self.dbg_out("xn", xn_bf[:, 0:T], [xn_toks[0]])
        self.dbg_out("s1", ccres[:], [cct])
        self.dbg_out("xq", xq_bf[:, 7 * T:8 * T], [self.tok_xq_spill[-1]])

    def p2_signnet(self):
        S, P = self.S, self.P
        A = self.scr
        r1_bc = self.r1_bc
        wld = A[:, 0:2048]
        s_bf = self.scrbf[:, 4096:6144]
        bb1row = A[0:1, 4096:6144]

        xxh_off = [0, 1024, 2048, 3072, 4096, 5120, 8192, 9216]

        def xxh_sl(dc, tc):
            o = xxh_off[dc] + 512 * tc
            return A[:, o:o + 512]

        t_g1, _ = self.load_cols("w_ln1g", 8)
        t_b1l, _ = self.load_cols("w_ln1b", 8)
        t_b2, _ = self.load_cols("w_b2", 8)
        t_b1s, _ = self.load_cols("w_b1", 16)
        lnb_bf = self.sbuf("lnb_bf", [128, 8], BF16)
        t_lnbbf = S.op("vector", lambda v: v.tensor_copy(
            P["lnb_bf"][:], P["w_ln1b_c"][:]), deps=[t_b1l])

        w1g = self.wbufA
        w2s = self.wbufB
        bb_mm = []
        prevk = self.tok_xq_spill[-1:]
        for k in range(8):
            tl = S.dma("sync", wld, P["w_W1T"].ap()[128 * k:128 * (k + 1), :],
                       deps=prevk)
            sg = S.op("scalar", lambda s: s.activation(
                out=s_bf, in_=wld, func=ACT.Sign, bias=0.0, scale=1.0),
                deps=[tl])
            bb_k = []
            for tc in range(4):
                bb_k.append(S.op("tensor", lambda t, tc=tc, k=k: t.matmul(
                    self.PSB[4 + tc][0:1, :], P["lnb_bf"][:, k:k + 1],
                    s_bf[:, 512 * tc:512 * (tc + 1)],
                    start=(k == 0), stop=(k == 7)),
                    deps=[sg, t_lnbbf] + self.psb_free.get(4 + tc, [])))
            g1 = S.op("vector", lambda v, k=k: v.tensor_scalar_mul(
                w1g[:, DI * k:DI * (k + 1)], s_bf,
                P["w_ln1g_c"][:, k:k + 1]), deps=[sg, t_g1])
            bb_mm += bb_k
            prevk = [g1] + bb_k
        evb = []
        for tc in range(4):
            evb.append(S.op("scalar", lambda s, tc=tc: s.activation(
                out=bb1row[0:1, 512 * tc:512 * (tc + 1)],
                in_=self.PSB[4 + tc][0:1, :], func=ACT.Copy, bias=0.0,
                scale=1.0), deps=bb_mm))
        for i in range(4, 8):
            self.psb_free[i] = list(evb)
        bb1c = self.sbuf("bb1c", [128, 16], F32)
        prev_c = list(evb) + self.psb_free.get(4, [])
        for ic in range(16):
            mm = S.op("tensor", lambda t, ic=ic: t.matmul(
                self.PSB[4][:, ic:ic + 1],
                bb1row[0:1, 128 * ic:128 * (ic + 1)], P["ones1_f"][0:1, 0:1],
                start=True, stop=True), deps=prev_c + self.tok_const)
            prev_c = [mm]
        cpb0 = S.op("scalar", lambda s: s.activation(
            out=P["bb1c"][:], in_=self.PSB[4][:, 0:16], func=ACT.Copy,
            bias=0.0, scale=1.0), deps=prev_c)
        self.psb_free[4] = [cpb0]
        cpb = S.op("vector", lambda v: v.tensor_add(
            P["bb1c"][:], P["bb1c"][:], P["w_b1_c"][:]), deps=[cpb0, t_b1s])

        w2toks = []
        for ic in range(16):
            tl = S.dma("sync", wld[:, 0:1024],
                       P["w_W2T"].ap()[128 * ic:128 * (ic + 1), :], deps=prevk)
            sg = S.op("scalar", lambda s, ic=ic: s.activation(
                out=w2s[:, D * ic:D * (ic + 1)], in_=wld[:, 0:1024],
                func=ACT.Sign, bias=0.0, scale=1.0), deps=[tl])
            w2toks.append(sg)
            prevk = [sg]

        xn_bf = self.xnsl
        gated_bf = self.abufA
        qx_bf = self.abufC
        gmx = self.sbuf("gmx", [128, 2], F32)
        gmxt = self.sbuf("gmxt", [128, 2], F32)
        grow = self.sbuf("grow", [2, 128], F32)
        grow2 = self.sbuf("grow2", [2, 1], F32)
        gam_sc = self.sbuf("gam_sc", [1, 2], F32)
        gam_c = self.sbuf("gam_c", [128, 2], F32)
        gtmp = self.PSB[5]
        self.tok_qx = []
        w1deps = [S.op("sync", lambda e: e.nop(),
                       deps=prevk + [cpb] + self.tok_xq_spill)]
        prev_gmm = []
        prev_xmm = list(w1deps)
        g_toks = []
        xx_toks = []
        for h in range(2):
            hoff = TH * h
            g_toks = []
            for ic in range(16):
                for tc in range(2):
                    mms = []
                    for k in range(8):
                        mms.append(S.op("tensor",
                            lambda t, ic=ic, tc=tc, k=k, hoff=hoff: t.matmul(
                                self.PSB[tc][:, :],
                                w1g[:, DI * k + 128 * ic:DI * k + 128 * (ic + 1)],
                                xn_bf[:, T * k + hoff + 512 * tc:
                                      T * k + hoff + 512 * (tc + 1)],
                                start=(k == 0), stop=(k == 7)),
                            deps=(w1deps if (ic == 0 and tc == 0) else []) +
                                 self.psb_free.get(tc, []) + self.tok_xn))
                    e1 = S.op("vector", lambda v, tc=tc, hoff=hoff:
                              v.tensor_tensor(
                                  gtmp, self.PSB[tc][:, :],
                                  r1_bc[:, hoff + 512 * tc:hoff + 512 * (tc + 1)],
                                  ALU.mult),
                              deps=mms + self.psb_free.get(5, []))
                    self.psb_free[tc] = [e1]
                    e2 = S.op("vector", lambda v, ic=ic: v.tensor_scalar(
                        gtmp, gtmp, P["bb1c"][:, ic:ic + 1], 1.0,
                        ALU.add, ALU.min), deps=[e2_dep for e2_dep in (e1, cpb)])
                    e3 = S.op("vector", lambda v, ic=ic, tc=tc: v.tensor_scalar(
                        gated_bf[:, 1024 * ic + 512 * tc:
                                 1024 * ic + 512 * (tc + 1)],
                        gtmp, -1.0, None, ALU.max), deps=[e2] + prev_gmm)
                    self.psb_free[5] = [e3]
                    g_toks.append(e3)
            prev_gmm = []
            xx_toks = []
            red_toks = []
            for dc in range(8):
                for tc in range(2):
                    mms = []
                    for ic in range(16):
                        mms.append(S.op("tensor",
                            lambda t, dc=dc, tc=tc, ic=ic: t.matmul(
                                self.PSB[2 + tc][:, :],
                                w2s[:, D * ic + 128 * dc:D * ic + 128 * (dc + 1)],
                                gated_bf[:, 1024 * ic + 512 * tc:
                                         1024 * ic + 512 * (tc + 1)],
                                start=(ic == 0), stop=(ic == 15)),
                            deps=g_toks + w2toks + self.psb_free.get(2 + tc, [])))
                    sl = xxh_sl(dc, tc)
                    e1 = S.op("vector", lambda v, tc=tc, dc=dc, sl=sl:
                              v.tensor_scalar_add(
                                  sl, self.PSB[2 + tc][:, :],
                                  P["w_b2_c"][:, dc:dc + 1]),
                              deps=mms + [t_b2] + prev_xmm)
                    self.psb_free[2 + tc] = [e1]
                    e2 = S.op("vector", lambda v, dc=dc, tc=tc, hoff=hoff, sl=sl:
                              v.tensor_tensor(
                                  sl, sl,
                                  xn_bf[:, T * dc + hoff + 512 * tc:
                                        T * dc + hoff + 512 * (tc + 1)],
                                  ALU.add), deps=[e1])
                    xx_toks.append(e2)
                    r_ = S.op("vector", lambda v, tc=tc, sl=sl: v.tensor_reduce(
                        P["gmxt"][:, tc:tc + 1], sl, AXL.X, ALU.max,
                        apply_absolute_value=True),
                        deps=[e2] + self.tok_dbg)
                    r2_ = S.op("vector", lambda v, dc=dc, tc=tc: (
                        v.tensor_copy(P["gmx"][:, tc:tc + 1],
                                      P["gmxt"][:, tc:tc + 1]) if dc == 0 else
                        v.tensor_max(P["gmx"][:, tc:tc + 1],
                                     P["gmx"][:, tc:tc + 1],
                                     P["gmxt"][:, tc:tc + 1])), deps=[r_])
                    red_toks.append(r2_)
            prev_xmm = []
            self.dbg_out(f"gmx_{h}", P["gmx"][:, 0:2], red_toks)
            tmm = S.op("tensor", lambda t: t.matmul(
                self.PSB[4][0:2, 0:128], P["gmx"][:, 0:2], P["id_f"][:],
                start=True, stop=True),
                deps=red_toks + self.psb_free.get(4, []) + self.tok_const)
            cpg = S.op("scalar", lambda s: s.activation(
                out=P["grow"][:], in_=self.PSB[4][0:2, 0:128], func=ACT.Copy,
                bias=0.0, scale=1.0), deps=[tmm] + self.tok_dbg)
            self.psb_free[4] = [cpg]
            self.dbg_out(f"grow_{h}", P["grow"][:], [cpg])
            rg = S.op("vector", lambda v: v.tensor_reduce(
                P["grow2"][:], P["grow"][:], AXL.X, ALU.max),
                deps=[cpg] + self.tok_dbg)
            z1 = S.op("vector", lambda v: v.memset(P["ccst"][:], 0.0),
                      deps=[rg] + self.tok_cc + self.tok_dbg)
            z2 = S.op("vector", lambda v: v.tensor_copy(
                P["ccst"][0:2, 0:1], P["grow2"][:]), deps=[z1])
            self.dbg_out(f"grow2_{h}", P["grow2"][:], [rg])
            self.dbg_out(f"ccstg_{h}", P["ccst"][:], [z2])
            cct, ccg = self.collective_max(2, [z2], f"gam{h}")
            self.dbg_out(f"ccg_{h}", ccg[:], [cct])
            s1_ = S.op("tensor", lambda t, ccg=ccg: t.matmul(
                self.PSB[4][0:1, 0:2], ccg[0:2, 0:1], P["id_f"][0:2, 0:2],
                start=True, stop=True), deps=[cct] + self.psb_free.get(4, []))
            s2_ = S.op("scalar", lambda s: s.activation(
                out=P["gam_sc"][:], in_=self.PSB[4][0:1, 0:2], func=ACT.Copy,
                bias=EPS, scale=1.0), deps=[s1_])
            self.psb_free[4] = [s2_]
            s3_ = S.op("vector", lambda v: v.reciprocal(
                P["gam_sc"][:], P["gam_sc"][:]), deps=[s2_])
            s4_ = S.op("vector", lambda v: v.tensor_scalar_mul(
                P["gam_sc"][:], P["gam_sc"][:], 128.0), deps=[s3_])
            s5_ = S.op("tensor", lambda t: t.matmul(
                self.PSB[4][:, 0:2], P["ones1_f"][:], P["gam_sc"][:],
                start=True, stop=True), deps=[s4_] + self.psb_free.get(4, []))
            s6_ = S.op("scalar", lambda s: s.activation(
                out=P["gam_c"][:], in_=self.PSB[4][:, 0:2], func=ACT.Copy,
                bias=0.0, scale=1.0), deps=[s5_])
            self.psb_free[4] = [s6_]
            for dc in range(8):
                for tc in range(2):
                    sl = xxh_sl(dc, tc)
                    q1 = S.op("vector", lambda v, sl=sl, tc=tc: v.tensor_scalar(
                        gtmp, sl, P["gam_c"][:, tc:tc + 1], QBE,
                        ALU.mult, ALU.min),
                        deps=[s6_] + self.psb_free.get(5, []))
                    q2 = S.op("vector", lambda v: v.tensor_scalar(
                        gtmp, gtmp, -QBE, None, ALU.max), deps=[q1])
                    qoff = T * dc + hoff + 512 * tc
                    q3 = S.op("vector", lambda v, qoff=qoff: v.tensor_copy(
                        qx_bf[:, qoff:qoff + 512], gtmp), deps=[q2])
                    q4 = S.op("scalar", lambda s, sl=sl: s.activation(
                        out=sl, in_=gtmp, func=ACT.Copy, bias=0.0, scale=1.0),
                        deps=[q2, q3])
                    q5 = S.dma("sync",
                               P["h_qxf"].ap()[128 * dc:128 * (dc + 1),
                                               hoff + 512 * tc:
                                               hoff + 512 * (tc + 1)],
                               sl, deps=[q4])
                    self.psb_free[5] = [q4]
                    prev_xmm.append(q5)
                    prev_gmm.append(q3)
                    self.tok_qx.append(q3)
                    self.tok_qx.append(q5)

        self.dbg_out("gated", gated_bf[:, 0:1024], g_toks[0:1])
        self.dbg_out("xxh", A[:, 0:1024], xx_toks)
        self.dbg_out("qx", qx_bf[:, 0:T], self.tok_qx)
        self.dbg_out("gamc", gam_c[:, 0:2], self.tok_qx)

    def p4_mamba(self):
        S, P = self.S, self.P
        A = self.scr
        Abf = self.scrbf
        qx_bf = self.abufC
        xs_bf = self.abufA
        yg_bf = self.abufA
        wA = self.wbufA
        wB = self.wbufB

        t_g2, _ = self.load_cols("w_ln2g", 8)
        t_b2l, _ = self.load_cols("w_ln2b", 8)
        t_dtb, _ = self.load_cols("w_dtb", 16)
        t_cvb, _ = self.load_cols("w_convB", 16)
        t_dp, _ = self.load_cols("w_Dp", 16)
        lnb2_bf = self.sbuf("lnb2_bf", [128, 8], BF16)
        t_lnb2 = S.op("vector", lambda v: v.tensor_copy(
            P["lnb2_bf"][:], P["w_ln2b_c"][:]), deps=[t_b2l])
        prev_mm = []
        sq_bf = Abf[:, 0:2048]
        for k in range(8):
            tsq = S.op("scalar", lambda s, k=k: s.activation(
                out=sq_bf, in_=qx_bf[:, T * k:T * (k + 1)], func=ACT.Square,
                bias=0.0, scale=1.0), deps=self.tok_qx + prev_mm)
            prev_mm = []
            for tc in range(4):
                m1 = S.op("tensor", lambda t, tc=tc, k=k: t.matmul(
                    self.PSB[tc][0:1, :], P["onescol_b"][:],
                    qx_bf[:, T * k + 512 * tc:T * k + 512 * (tc + 1)],
                    start=(k == 0), stop=(k == 7)),
                    deps=self.tok_qx + self.psb_free.get(tc, []) + self.tok_const)
                m2 = S.op("tensor", lambda t, tc=tc, k=k: t.matmul(
                    self.PSB[4 + tc][0:1, :], P["onescol_b"][:],
                    sq_bf[:, 512 * tc:512 * (tc + 1)],
                    start=(k == 0), stop=(k == 7)),
                    deps=[tsq] + self.psb_free.get(4 + tc, []))
                prev_mm += [m1, m2]
        m2row = A[0:1, 2048:4096]
        v2row = A[0:1, 4096:6144]
        ev = []
        for tc in range(4):
            ev.append(S.op("scalar", lambda s, tc=tc: s.activation(
                out=m2row[0:1, 512 * tc:512 * (tc + 1)],
                in_=self.PSB[tc][0:1, :], func=ACT.Copy, bias=0.0,
                scale=1.0 / D), deps=prev_mm))
            ev.append(S.op("scalar", lambda s, tc=tc: s.activation(
                out=v2row[0:1, 512 * tc:512 * (tc + 1)],
                in_=self.PSB[4 + tc][0:1, :], func=ACT.Copy, bias=0.0,
                scale=1.0 / D), deps=prev_mm))
        for i in range(8):
            self.psb_free[i] = list(ev)
        rt1 = A[0:1, 6144:8192]
        rt2 = A[0:1, 0:2048]
        o3 = S.op("vector", lambda v: v.tensor_tensor(
            rt1, m2row, m2row, ALU.mult), deps=ev)
        o4 = S.op("vector", lambda v: v.tensor_sub(v2row, v2row, rt1),
                  deps=[o3])
        a_ = S.op("scalar", lambda s: s.activation(
            out=rt1, in_=v2row, func=ACT.Sqrt, bias=P["eps1"][:, 0:1],
            scale=1.0), deps=[o4] + self.tok_const)
        y0 = S.op("vector", lambda v: v.reciprocal(rt2, rt1), deps=[a_])
        b1 = S.op("vector", lambda v: v.tensor_scalar_add(rt1, v2row, EPS),
                  deps=[y0])
        b2 = S.op("vector", lambda v: v.tensor_tensor(rt1, rt1, rt2, ALU.mult),
                  deps=[b1])
        b3 = S.op("vector", lambda v: v.tensor_tensor(rt1, rt1, rt2, ALU.mult),
                  deps=[b2])
        b4 = S.op("vector", lambda v: v.tensor_scalar(
            rt1, rt1, -0.5, 1.5, ALU.mult, ALU.add), deps=[b3])
        r2row_ = S.op("vector", lambda v: v.tensor_tensor(
            v2row, rt1, rt2, ALU.mult), deps=[b4])
        r2row = v2row
        r2_bc = A[:, 7808:9856]
        bc_last = [r2row_]
        for tc in range(4):
            mm = self.bcast_row_psum(
                self.PSB[0][:, :], r2row[0:1, 512 * tc:512 * (tc + 1)],
                deps=bc_last + self.tok_const + self.psb_free.get(0, []))
            cp = S.op("scalar", lambda s, tc=tc: s.activation(
                out=r2_bc[:, 512 * tc:512 * (tc + 1)], in_=self.PSB[0][:],
                func=ACT.Copy, bias=0.0, scale=1.0), deps=[mm])
            self.psb_free[0] = [cp]
            bc_last = [cp]

        wld4 = A[:, 0:4096]
        wi_bf = Abf[:, 8192:12288]
        xnf32 = self.xnsl.bitcast(F32)
        crow = xnf32[0:1, 0:4096]
        bbrow = xnf32[0:1, 4096:8192]
        prevk = list(bc_last) + [o3]
        c_mm = []
        wspill = []
        for k in range(8):
            tl = S.dma("sync", wld4, P["w_inpT"].ap()[128 * k:128 * (k + 1), :],
                       deps=prevk)
            sc1 = S.op("vector", lambda v, k=k: v.tensor_scalar_mul(
                wi_bf, wld4, P["w_ln2g_c"][:, k:k + 1]), deps=[tl, t_g2])
            mm_k = []
            for tc in range(8):
                mm_k.append(S.op("tensor", lambda t, tc=tc, k=k: t.matmul(
                    self.PSB[tc][0:1, :], P["onescol_b"][:],
                    wi_bf[:, 512 * tc:512 * (tc + 1)],
                    start=(k == 0), stop=(k == 7)),
                    deps=[sc1] + self.tok_const +
                         (self.psb_free.get(tc, []) if k == 0 else [])))
            sp = S.dma("sync", P["h_winp"].ap()[128 * k:128 * (k + 1), :],
                       wi_bf, deps=mm_k)
            c_mm += mm_k
            wspill.append(sp)
            prevk = [sp]
        evc = []
        for tc in range(8):
            evc.append(S.op("scalar", lambda s, tc=tc, crow=crow: s.activation(
                out=crow[0:1, 512 * tc:512 * (tc + 1)],
                in_=self.PSB[tc][0:1, :], func=ACT.Copy, bias=0.0,
                scale=1.0), deps=c_mm))
        for i in range(8):
            self.psb_free[i] = list(evc)
        raw_bf = Abf[:, 12288:14336]
        bb_mm = []
        prevk2 = list(evc)
        for k in range(8):
            tl = S.dma("sync", wld4, P["w_inpT"].ap()[128 * k:128 * (k + 1), :],
                       deps=prevk2)
            mm_k = []
            for kh in range(2):
                cr = S.op("vector", lambda v, kh=kh: v.tensor_copy(
                    raw_bf, wld4[:, 2048 * kh:2048 * (kh + 1)]),
                    deps=[tl] + mm_k[-4:])
                for tc in range(4):
                    mm_k.append(S.op("tensor", lambda t, tc=tc, k=k, kh=kh:
                        t.matmul(
                            self.PSB[4 * kh + tc][0:1, :],
                            P["lnb2_bf"][:, k:k + 1],
                            raw_bf[:, 512 * tc:512 * (tc + 1)],
                            start=(k == 0), stop=(k == 7)),
                        deps=[cr, t_lnb2] +
                             (self.psb_free.get(4 * kh + tc, [])
                              if k == 0 else [])))
            bb_mm += mm_k
            prevk2 = [mm_k[-1], tl]
        evb = []
        for tc in range(8):
            evb.append(S.op("scalar", lambda s, tc=tc, bbrow=bbrow: s.activation(
                out=bbrow[0:1, 512 * tc:512 * (tc + 1)],
                in_=self.PSB[tc][0:1, :], func=ACT.Copy, bias=0.0,
                scale=1.0), deps=bb_mm))
        for i in range(8):
            self.psb_free[i] = list(evb)
        bb2c = self.sbuf("bb2c", [128, 32], F32)
        prev_c = list(evb)
        for jc in range(32):
            mm = S.op("tensor", lambda t, jc=jc, bbrow=bbrow: t.matmul(
                self.PSB[4][:, jc:jc + 1],
                bbrow[0:1, 128 * jc:128 * (jc + 1)], P["ones1_f"][0:1, 0:1],
                start=True, stop=True), deps=prev_c + self.tok_const)
            prev_c = [mm]
        cpb2 = S.op("scalar", lambda s: s.activation(
            out=P["bb2c"][:], in_=self.PSB[4][:, 0:32], func=ACT.Copy,
            bias=0.0, scale=1.0), deps=prev_c)
        self.psb_free[4] = [cpb2]
        wfold = []
        prevk3 = [cpb2]
        for k in range(8):
            tlw = S.dma("sync", wi_bf, P["h_winp"].ap()[128 * k:128 * (k + 1), :],
                        deps=prevk3 + wspill)
            ftoks = []
            for tc in range(8):
                mm = self.bcast_row_psum(
                    self.PSB[5][:, :], crow[0:1, 512 * tc:512 * (tc + 1)],
                    deps=[tlw] + self.tok_const + evc +
                         self.psb_free.get(5, []))
                f1 = S.op("vector", lambda v, tc=tc: v.scalar_tensor_tensor(
                    wi_bf[:, 512 * tc:512 * (tc + 1)], self.PSB[5][:, :],
                    -1.0 / D, wi_bf[:, 512 * tc:512 * (tc + 1)],
                    ALU.mult, ALU.add), deps=[mm])
                self.psb_free[5] = [f1]
                ftoks.append(f1)
            sp = S.dma("sync", P["h_winp"].ap()[128 * k:128 * (k + 1), :],
                       wi_bf, deps=ftoks)
            wfold.append(sp)
            prevk3 = [sp]

        self.dint("h_woutbf", [DI, D], BF16)
        wobf = []
        prevk2 = list(prevk)
        for k in range(16):
            tl = S.dma("sync", wld4[:, 0:1024],
                       P["w_outpT"].ap()[128 * k:128 * (k + 1), :],
                       deps=prevk2 + evc)
            cv = S.op("vector", lambda v: v.tensor_copy(
                Abf[:, 4096:5120], wld4[:, 0:1024]), deps=[tl])
            sp = S.dma("sync", P["h_woutbf"].ap()[128 * k:128 * (k + 1), :],
                       Abf[:, 4096:5120], deps=[cv])
            wobf.append(sp)
            prevk2 = [sp, cv]

        xnbf = self.xnsl
        xnf32v = self.xnsl.bitcast(F32)
        xpj = xnbf[:, 0:2048]
        dtpj = xnbf[0:64, 2048:4096]
        eoh = xnbf[0:16, 4096:6144]
        dtin_bf = xnbf[0:64, 6144:6656]
        Bbf = xnbf[0:16, 6656:7168]
        Cbf = xnbf[0:16, 7168:7680]
        carry = xnf32v[:, 3840:4096]
        ctx = xnf32v[:, 4096:4144]
        negA = xnf32v[:, 4144:4400]
        convw = xnf32v[:, 4400:4464]
        prep_done = wfold + [cpb2]
        cw_toks = []
        for dt_ in range(16):
            cw_toks.append(S.dma(
                "sync", convw[:, 4 * dt_:4 * (dt_ + 1)],
                P["w_convW"].ap()[128 * dt_:128 * (dt_ + 1), :],
                deps=prep_done))
        na_toks = []
        for dt_ in range(16):
            tl = S.dma("sync", negA[:, 16 * dt_:16 * (dt_ + 1)],
                       P["w_Alog"].ap()[128 * dt_:128 * (dt_ + 1), :],
                       deps=prep_done)
            t1 = S.op("scalar", lambda s, dt_=dt_: s.activation(
                out=negA[:, 16 * dt_:16 * (dt_ + 1)],
                in_=negA[:, 16 * dt_:16 * (dt_ + 1)], func=ACT.Exp,
                bias=0.0, scale=1.0), deps=[tl])
            na_toks.append(S.op("vector", lambda v, dt_=dt_: v.tensor_scalar_mul(
                negA[:, 16 * dt_:16 * (dt_ + 1)],
                negA[:, 16 * dt_:16 * (dt_ + 1)], -1.0), deps=[t1]))
        xpj_toks = []
        for dt_ in range(16):
            tl = S.dma("sync", A[:, 0:128],
                       P["w_xprojT"].ap()[128 * dt_:128 * (dt_ + 1), :],
                       deps=prep_done + xpj_toks[-1:])
            xpj_toks.append(S.op("vector", lambda v, dt_=dt_: v.tensor_copy(
                xpj[:, 128 * dt_:128 * (dt_ + 1)], A[:, 0:128]), deps=[tl]))
        tl = S.dma("sync", A[0:64, 128:2176], P["w_dtprojT"].ap(),
                   deps=prep_done + xpj_toks[-1:])
        dtpj_tok = S.op("vector", lambda v: v.tensor_copy(
            dtpj[:], A[0:64, 128:2176]), deps=[tl])
        e0 = S.op("vector", lambda v: v.memset(A[0:16, 0:2048], 1.0),
                  deps=xpj_toks + [dtpj_tok])
        e1 = S.op("gpsimd", lambda g: g.affine_select(
            eoh[:], A[0:16, 0:2048], pattern=[[1, 16], [0, 128]],
            compare_op=ALU.is_equal, fill=0.0, base=0, channel_multiplier=-1),
            deps=[e0])
        eoh_tok = e1

        xc = A[:, 0:640]
        dAb = [A[:, 640:1152], A[:, 1152:1664]]
        vb = [A[:, 1664:2176], A[:, 2176:2688]]
        hb = A[:, 2688:3200]
        dtb_ = A[:, 3200:3712]
        dtxb = A[:, 3712:4224]
        yb = A[:, 4224:4736]
        ztb = A[:, 4736:5248]
        gb = A[:, 5248:5760]
        pint = A[:, 5760:7808]
        ctx0 = S.op("vector", lambda v: v.memset(ctx[:, :], 0.0),
                    deps=prep_done)
        prev_ch = [ctx0] + wfold + wobf + [eoh_tok, dtpj_tok] + na_toks +             cw_toks + [t_dtb, t_cvb, t_dp]
        tok_mamba_out = []
        for ch in range(4):
            co = TC * ch
            wl = [S.dma("sync", wA[:, 2048 * k:2048 * (k + 1)],
                        P["h_winp"].ap()[128 * k:128 * (k + 1), 0:2048],
                        deps=prev_ch) for k in range(8)]
            xs_toks = []
            xcs = [xc, A[:, 640:1280]]
            cts = [dtxb, A[:, 1664:2176]]
            IPB = [0, 5]
            for dp in range(8):
                mm2 = {}
                xce = {}
                cte = {}
                toks = {}
                for j in (0, 1):
                    dt_ = 2 * dp + j
                    mms = []
                    for k in range(8):
                        mms.append(S.op("tensor",
                            lambda t, dt_=dt_, k=k, co=co, pb_=IPB[j]:
                            t.matmul(
                                self.PSB[pb_][:, :],
                                wA[:, 2048 * k + 128 * dt_:
                                   2048 * k + 128 * (dt_ + 1)],
                                qx_bf[:, T * k + co:T * k + co + TC],
                                start=(k == 0), stop=(k == 7)),
                            deps=wl + self.psb_free.get(IPB[j], [])))
                    mm2[j] = mms
                    xce[j] = xcs[j]
                    cte[j] = cts[j]
                for j in (0, 1):
                    toks[j] = S.op("vector",
                        lambda v, co=co, xcx=xce[j], pb_=IPB[j]:
                        v.tensor_tensor(
                            xcx[:, 3:515], self.PSB[pb_][:, :],
                            r2_bc[:, co:co + TC], ALU.mult),
                        deps=mm2[j] + bc_last)
                    self.psb_free[IPB[j]] = [toks[j]]
                for j in (0, 1):
                    toks[j] = S.op("vector",
                        lambda v, dt_=2 * dp + j, xcx=xce[j]:
                        v.tensor_scalar_add(
                            xcx[:, 3:515], xcx[:, 3:515],
                            P["bb2c"][:, dt_:dt_ + 1]),
                        deps=[toks[j], cpb2])
                for j in (0, 1):
                    toks[j] = S.op("vector",
                        lambda v, dt_=2 * dp + j, xcx=xce[j]: v.tensor_copy(
                            xcx[:, 0:3], ctx[:, 3 * dt_:3 * (dt_ + 1)]),
                        deps=[toks[j]])
                for kk in range(4):
                    for j in (0, 1):
                        dt_ = 2 * dp + j
                        if kk == 0:
                            toks[j] = S.op("vector",
                                lambda v, dt_=dt_, xcx=xce[j], ctm=cte[j]:
                                v.tensor_scalar_mul(
                                    ctm, xcx[:, 0:512],
                                    convw[:, 4 * dt_:4 * dt_ + 1]),
                                deps=[toks[j]])
                        else:
                            toks[j] = S.op("vector",
                                lambda v, dt_=dt_, kk=kk, xcx=xce[j],
                                ctm=cte[j]:
                                v.scalar_tensor_tensor(
                                    ctm, xcx[:, kk:512 + kk],
                                    convw[:, 4 * dt_ + kk:4 * dt_ + kk + 1],
                                    ctm, ALU.mult, ALU.add),
                                deps=[toks[j]])
                for j in (0, 1):
                    dt_ = 2 * dp + j
                    cx_t = S.op("vector",
                        lambda v, dt_=dt_, xcx=xce[j]: v.tensor_copy(
                            ctx[:, 3 * dt_:3 * (dt_ + 1)], xcx[:, 512:515]),
                        deps=[toks[j]])
                    sl = S.op("scalar",
                        lambda s, dt_=dt_, ctm=cte[j]: s.activation(
                            out=xs_bf[:, 512 * dt_:512 * (dt_ + 1)],
                            in_=ctm, func=ACT.Silu,
                            bias=P["w_convB_c"][:, dt_:dt_ + 1],
                            scale=1.0),
                        deps=[toks[j], t_cvb] + tok_mamba_out)
                    xs_toks.append(sl)
                    prev_ch = [cx_t, sl]
            xd_mm = []
            for dt_ in range(16):
                xd_mm.append(S.op("tensor", lambda t, dt_=dt_: t.matmul(
                    self.PSB[1][:, :], xpj[:, 128 * dt_:128 * (dt_ + 1)],
                    xs_bf[:, 512 * dt_:512 * (dt_ + 1)],
                    start=(dt_ == 0), stop=(dt_ == 15)),
                    deps=xs_toks + xpj_toks + self.psb_free.get(1, [])))
            exd1 = S.op("scalar", lambda s: s.activation(
                out=dtin_bf[:, :], in_=self.PSB[1][0:64, :], func=ACT.Copy,
                bias=0.0, scale=1.0), deps=xd_mm)
            exd2 = S.op("scalar", lambda s: s.activation(
                out=Bbf[:, :], in_=self.PSB[1][64:80, :], func=ACT.Copy,
                bias=0.0, scale=1.0), deps=xd_mm)
            exd3 = S.op("scalar", lambda s: s.activation(
                out=Cbf[:, :], in_=self.PSB[1][96:112, :], func=ACT.Copy,
                bias=0.0, scale=1.0), deps=xd_mm)
            self.psb_free[1] = [exd1, exd2, exd3]
            wlz = [S.dma("sync", wA[:, 2048 * k:2048 * (k + 1)],
                         P["h_winp"].ap()[128 * k:128 * (k + 1), 2048:4096],
                         deps=xs_toks + xd_mm) for k in range(8)]
            for dt_ in range(16):
                dmm = S.op("tensor", lambda t, dt_=dt_: t.matmul(
                    self.PSB[2][:, :], dtpj[:, 128 * dt_:128 * (dt_ + 1)],
                    dtin_bf[:, :], start=True, stop=True),
                    deps=[exd1, dtpj_tok] + self.psb_free.get(2, []))
                d0 = S.op("vector", lambda v, dt_=dt_: v.tensor_scalar_add(
                    ztb, self.PSB[2][:, :], P["w_dtb_c"][:, dt_:dt_ + 1]),
                    deps=[dmm, t_dtb])
                self.psb_free[2] = [d0]
                d1 = S.op("scalar", lambda s: s.activation(
                    out=gb, in_=ztb, func=ACT.Abs, bias=0.0, scale=1.0),
                    deps=[d0])
                d2 = S.op("scalar", lambda s: s.activation(
                    out=gb, in_=gb, func=ACT.Exp, bias=0.0, scale=-1.0),
                    deps=[d1])
                d3 = S.op("scalar", lambda s: s.activation(
                    out=gb, in_=gb, func=ACT.Ln, bias=1.0, scale=1.0),
                    deps=[d2])
                d4 = S.op("vector", lambda v: v.tensor_scalar_max(
                    dtb_, ztb, 0.0), deps=[d1])
                dact = S.op("vector", lambda v: v.tensor_add(
                    dtb_, dtb_, gb), deps=[d3, d4])
                dtx = S.op("vector", lambda v, dt_=dt_: v.tensor_tensor(
                    dtxb, dtb_, xs_bf[:, 512 * dt_:512 * (dt_ + 1)],
                    ALU.mult), deps=[dact] + xs_toks)
                ytoks = []
                pptoks = []
                BMB = [3, 1]
                CMB = [4, 7]
                hbx = [hb, xc[:, 0:512]]
                for sp in range(8):
                    das = {}
                    vvs = {}
                    scs = {}
                    for pb in (0, 1):
                        s_ = 2 * sp + pb
                        das[pb] = S.op("scalar",
                            lambda s, s_=s_, dt_=dt_, dAx=dAb[pb]:
                            s.activation(
                                out=dAx, in_=dtb_, func=ACT.Exp,
                                scale=negA[:, 16 * dt_ + s_:
                                           16 * dt_ + s_ + 1],
                                bias=0.0),
                            deps=[dact] + na_toks + pptoks[-2:])
                    for pb in (0, 1):
                        s_ = 2 * sp + pb
                        bb_ = BMB[pb]
                        bm = S.op("tensor", lambda t, s_=s_, bb_=bb_:
                            t.matmul(
                                self.PSB[bb_][:, :],
                                eoh[:, 128 * s_:128 * (s_ + 1)],
                                Bbf[:, :], start=True, stop=True),
                            deps=[exd2, eoh_tok] + self.psb_free.get(bb_, []))
                        vvs[pb] = S.op("vector",
                            lambda v, vx=vb[pb], bb_=bb_: v.tensor_tensor(
                                vx, dtxb, self.PSB[bb_][:, :], ALU.mult),
                            deps=[dtx, bm])
                        self.psb_free[bb_] = [vvs[pb]]
                    for pb in (0, 1):
                        s_ = 2 * sp + pb
                        init = 0.0 if ch == 0 else carry[:, 16 * dt_ + s_:
                                                         16 * dt_ + s_ + 1]
                        scs[pb] = S.op("vector",
                            lambda v, dAx=dAb[pb], vx=vb[pb], hx=hbx[pb],
                            init=init:
                            v.tensor_tensor_scan(hx, dAx, vx, init,
                                                 ALU.mult, ALU.add),
                            deps=[das[pb], vvs[pb]])
                    for pb in (0, 1):
                        s_ = 2 * sp + pb
                        S.op("vector", lambda v, dt_=dt_, s_=s_, hx=hbx[pb]:
                             v.tensor_copy(
                                 carry[:, 16 * dt_ + s_:16 * dt_ + s_ + 1],
                                 hx[:, 511:512]), deps=[scs[pb]])
                    for pb in (0, 1):
                        s_ = 2 * sp + pb
                        cb_ = CMB[pb]
                        cm = S.op("tensor", lambda t, s_=s_, cb_=cb_:
                            t.matmul(
                                self.PSB[cb_][:, :],
                                eoh[:, 128 * s_:128 * (s_ + 1)],
                                Cbf[:, :], start=True, stop=True),
                            deps=[exd3, eoh_tok] + self.psb_free.get(cb_, []))
                        pp = S.op("vector", lambda v, s_=s_, cb_=cb_,
                                  hx=hbx[pb]: v.tensor_tensor(
                                pint[:, 512 * (s_ % 4):512 * (s_ % 4 + 1)],
                                hx, self.PSB[cb_][:, :], ALU.mult),
                            deps=[scs[pb], cm])
                        self.psb_free[cb_] = [pp]
                        pptoks.append(pp)
                    if sp % 2 == 1:
                        grp = sp // 2
                        p4s = pptoks[-4:]
                        if grp == 0:
                            rd = S.op("vector", lambda v: v.tensor_add(
                                yb, pint[:, 0:512], pint[:, 512:1024]),
                                deps=p4s)
                        else:
                            rd = S.op("vector", lambda v: v.tensor_add(
                                yb, yb, pint[:, 0:512]), deps=p4s)
                            rd = S.op("vector", lambda v: v.tensor_add(
                                yb, yb, pint[:, 512:1024]), deps=[rd])
                        rd = S.op("vector", lambda v: v.tensor_add(
                            yb, yb, pint[:, 1024:1536]), deps=[rd])
                        rd = S.op("vector", lambda v: v.tensor_add(
                            yb, yb, pint[:, 1536:2048]), deps=[rd])
                        ytoks.append(rd)
                yd = S.op("vector", lambda v, dt_=dt_: v.scalar_tensor_tensor(
                    yb, xs_bf[:, 512 * dt_:512 * (dt_ + 1)],
                    P["w_Dp_c"][:, dt_:dt_ + 1], yb, ALU.mult, ALU.add),
                    deps=ytoks + [t_dp])
                zmm = []
                for k in range(8):
                    zmm.append(S.op("tensor", lambda t, dt_=dt_, k=k, co=co:
                        t.matmul(
                            self.PSB[5][:, :],
                            wA[:, 2048 * k + 128 * dt_:2048 * k + 128 * (dt_ + 1)],
                            qx_bf[:, T * k + co:T * k + co + TC],
                            start=(k == 0), stop=(k == 7)),
                        deps=wlz + self.psb_free.get(5, [])))
                z1 = S.op("vector", lambda v, co=co: v.tensor_tensor(
                    ztb, self.PSB[5][:, :], r2_bc[:, co:co + TC], ALU.mult),
                    deps=zmm)
                self.psb_free[5] = [z1]
                z2 = S.op("scalar", lambda s, dt_=dt_: s.activation(
                    out=gb, in_=ztb, func=ACT.Silu,
                    bias=P["bb2c"][:, 16 + dt_:16 + dt_ + 1], scale=1.0),
                    deps=[z1, cpb2])
                yg = S.op("vector", lambda v, dt_=dt_: v.tensor_tensor(
                    yg_bf[:, 8192 + 512 * dt_:8192 + 512 * (dt_ + 1)],
                    yb, gb, ALU.mult), deps=[yd, z2])
                tok_mamba_out.append(yg)
            wlo = [S.dma("sync", wB[:, 1024 * k:1024 * (k + 1)],
                         P["h_woutbf"].ap()[128 * k:128 * (k + 1), :],
                         deps=wobf + tok_mamba_out[-1:]) for k in range(16)]
            op_toks = []
            for dc in range(8):
                mms = []
                for k in range(16):
                    mms.append(S.op("tensor", lambda t, dc=dc, k=k: t.matmul(
                        self.PSB[6][:, :],
                        wB[:, 1024 * k + 128 * dc:1024 * k + 128 * (dc + 1)],
                        yg_bf[:, 8192 + 512 * k:8192 + 512 * (k + 1)],
                        start=(k == 0), stop=(k == 15)),
                        deps=wlo + tok_mamba_out[-16:] +
                             self.psb_free.get(6, [])))
                tl = S.dma("sync", A[:, 0:512],
                           P["x_fm"].ap()[128 * dc:128 * (dc + 1), co:co + TC],
                           deps=op_toks[-1:] + mms[:1])
                tlq = S.dma("sync", A[:, 512:1024],
                            P["h_qxf"].ap()[128 * dc:128 * (dc + 1), co:co + TC],
                            deps=op_toks[-1:] + mms[:1] + self.tok_qx)
                ad = S.op("vector", lambda v: v.tensor_add(
                    A[:, 0:512], A[:, 0:512], A[:, 512:1024]),
                    deps=[tl, tlq])
                ad2 = S.op("vector", lambda v: v.tensor_add(
                    A[:, 0:512], A[:, 0:512], self.PSB[6][:, :]),
                    deps=[ad] + mms)
                self.psb_free[6] = [ad2]
                sp = S.dma("sync",
                           P["h_xacc"].ap()[128 * dc:128 * (dc + 1), co:co + TC],
                           A[:, 0:512], deps=[ad2])
                op_toks.append(sp)
            prev_ch = op_toks + prev_ch
            self.tok_mamba_spill = op_toks

        self.dbg_out("xs", xs_bf[:, 0:8192], xs_toks)
        self.dbg_out("dtc", dtb_, tok_mamba_out)
        self.dbg_out("ygate", yg_bf[:, 8192:16384], tok_mamba_out)
        self.dbg_dram("xacc", P["h_xacc"], [D, T], self.tok_mamba_spill)
        self.dbg_dram("qxf", P["h_qxf"], [D, T],
                      self.tok_mamba_spill + self.tok_qx)


    def p5_ffn(self):
        S, P = self.S, self.P
        A = self.scr
        Abf = self.scrbf
        xq = self.abufC
        psbf = self.PSA.bitcast(BF16)

        def h_sl(c):
            if c < 8:
                return self.abufA[:, 2048 * c:2048 * (c + 1)]
            if c < 16:
                return self.wbufA[:, 2048 * (c - 8):2048 * (c - 7)]
            return self.wbufB[:, 2048 * (c - 16):2048 * (c - 15)]

        last_m = self.tok_mamba_spill
        xq_toks = [S.dma("sync", xq[:, T * k:T * (k + 1)],
                         P["h_xq"].ap()[128 * k:128 * (k + 1), :],
                         deps=last_m + self.tok_xq_spill) for k in range(8)]
        clipd = self.sbuf("clipd_sb", [1, 1], F32)
        tcd = S.dma("sync", clipd[:], P["w_clipd"].ap())
        qd = S.op("scalar", lambda s: s.activation(
            out=P["clipd_sb"][:], in_=P["clipd_sb"][:], func=ACT.Sigmoid,
            bias=0.0, scale=1.0), deps=[tcd])

        wnat = A[:, 0:2816]
        wqbf = [Abf[:, 5632:7680], Abf[:, 7680:9728]]
        wql = [Abf[:, 9728:10752], Abf[:, 10752:11776]]
        sg_ = A[:, 5888:6400]
        up_ = A[:, 6400:6912]
        rcol = self.sbuf("rcol5", [128, 1], F32)
        comb = self.sbuf("comb5", [128, 2], F32)
        hmax = self.sbuf("hmax", [128, 1], F32)
        hmaxt = self.sbuf("hmaxt", [128, 1], F32)
        hm0 = S.op("vector", lambda v: v.memset(P["hmax"][:], 0.0))
        h_toks = []
        prev_w = list(xq_toks)
        prev_h = []
        for c in range(FFC):
            ev_toks = []
            for wi, wdram in [(0, "w_wg"), (1, "w_wu")]:
                tl = S.dma("sync", wnat[:, 0:1024],
                           P[wdram].ap()[128 * c:128 * (c + 1), :],
                           deps=prev_w)
                r0 = S.op("vector", lambda v: v.tensor_reduce(
                    P["rcol5"][:], wnat[:, 0:1024], AXL.X, ALU.max,
                    apply_absolute_value=True), deps=[tl])
                c0 = S.op("vector", lambda v, wi=wi: v.tensor_scalar_mul(
                    P["comb5"][:, wi:wi + 1], P["rcol5"][:], 1.0 / 127.0),
                    deps=[r0])
                c1 = S.op("vector", lambda v, wi=wi: v.tensor_tensor(
                    P["comb5"][:, wi:wi + 1], P["comb5"][:, wi:wi + 1],
                    P["s1c"][:, 0:1], ALU.mult), deps=[c0, self.tok_s1c])
                rr = S.op("vector", lambda v: v.reciprocal(
                    P["rcol5"][:], P["rcol5"][:]), deps=[c0])
                rr2 = S.op("vector", lambda v: v.tensor_scalar_mul(
                    P["rcol5"][:], P["rcol5"][:], 127.0), deps=[rr])
                a1 = S.op("vector", lambda v: v.tensor_scalar(
                    wnat[:, 0:1024], wnat[:, 0:1024], P["rcol5"][:, 0:1],
                    RMAGIC, ALU.mult, ALU.add), deps=[rr2])
                a2 = S.op("vector", lambda v: v.tensor_scalar(
                    wnat[:, 0:1024], wnat[:, 0:1024], -RMAGIC, 127.0,
                    ALU.add, ALU.min), deps=[a1])
                a3 = S.op("vector", lambda v, wi=wi: v.tensor_scalar(
                    wqbf[wi][:, 0:1024], wnat[:, 0:1024],
                    -128.0, None, ALU.max), deps=[a2])
                prev_w = [a3]
                tt = []
                for hc in range(8):
                    tt.append(S.op("tensor", lambda t, hc=hc, wi=wi: t.transpose(
                        psbf[:, 7168 + 128 * hc:7168 + 128 * (hc + 1)],
                        wqbf[wi][:, 128 * hc:128 * (hc + 1)],
                        P["id_bf"][:]),
                        deps=[a3] + self.tok_const + self.psb_free.get(7, [])))
                ev = S.op("scalar", lambda s, wi=wi: s.activation(
                    out=wql[wi][:, 0:1024], in_=psbf[:, 7168:8192],
                    func=ACT.Copy, bias=0.0, scale=1.0), deps=tt)
                self.psb_free[7] = [ev]
                ev_toks.append((ev, c1))
            for tc in range(4):
                mg = []
                mu = []
                for hc in range(8):
                    mg.append(S.op("tensor", lambda t, tc=tc, hc=hc: t.matmul(
                        self.PSB[0][:, :], wql[0][:, 128 * hc:128 * (hc + 1)],
                        xq[:, T * hc + 512 * tc:T * hc + 512 * (tc + 1)],
                        start=(hc == 0), stop=(hc == 7)),
                        deps=[ev_toks[0][0]] + xq_toks +
                             self.psb_free.get(0, [])))
                    mu.append(S.op("tensor", lambda t, tc=tc, hc=hc: t.matmul(
                        self.PSB[1][:, :], wql[1][:, 128 * hc:128 * (hc + 1)],
                        xq[:, T * hc + 512 * tc:T * hc + 512 * (tc + 1)],
                        start=(hc == 0), stop=(hc == 7)),
                        deps=[ev_toks[1][0]] + xq_toks +
                             self.psb_free.get(1, [])))
                e1 = S.op("scalar", lambda s: s.activation(
                    out=sg_, in_=self.PSB[0][:, :], func=ACT.Silu,
                    scale=P["comb5"][:, 0:1], bias=0.0),
                    deps=mg + [ev_toks[0][1]] + prev_h)
                self.psb_free[0] = [e1]
                e2 = S.op("vector", lambda v: v.tensor_scalar_mul(
                    up_, self.PSB[1][:, :], P["comb5"][:, 1:2]),
                    deps=mu + [ev_toks[1][1]] + prev_h)
                self.psb_free[1] = [e2]
                h1 = S.op("vector", lambda v, c=c, tc=tc: v.tensor_tensor(
                    h_sl(c)[:, 512 * tc:512 * (tc + 1)], sg_, up_, ALU.mult),
                    deps=[e1, e2])
                r1_ = S.op("vector", lambda v, c=c, tc=tc: v.tensor_reduce(
                    P["hmaxt"][:], h_sl(c)[:, 512 * tc:512 * (tc + 1)],
                    AXL.X, ALU.max, apply_absolute_value=True), deps=[h1])
                r2_ = S.op("vector", lambda v: v.tensor_max(
                    P["hmax"][:], P["hmax"][:], P["hmaxt"][:]),
                    deps=[r1_, hm0])
                h_toks.append(h1)
                prev_h = [h1, r2_]
            prev_w = prev_w + prev_h

        tmm = S.op("tensor", lambda t: t.matmul(
            self.PSB[2][0:1, 0:128], P["hmax"][:], P["id_f"][:],
            start=True, stop=True),
            deps=prev_h + self.tok_const + self.psb_free.get(2, []))
        cpm = S.op("scalar", lambda s: s.activation(
            out=P["mxrow"][:], in_=self.PSB[2][0:1, 0:128], func=ACT.Copy,
            bias=0.0, scale=1.0), deps=[tmm])
        self.psb_free[2] = [cpm]
        pr = S.op("vector", lambda v: v.tensor_reduce(
            P["mxall"][:], P["mxrow"][:], AXL.X, ALU.max), deps=[cpm])
        z1 = S.op("vector", lambda v: v.memset(P["ccst"][:], 0.0),
                  deps=[pr] + self.tok_cc)
        z2 = S.op("vector", lambda v: v.tensor_copy(
            P["ccst"][0:1, 0:1], P["mxall"][:]), deps=[z1])
        cct, ccs2 = self.collective_max(1, [z2], "s2")
        s2s = self.sbuf("s2s", [1, 2], F32)
        q2 = S.op("vector", lambda v, ccs2=ccs2: v.tensor_tensor(
            P["s2s"][0:1, 0:1], P["clipd_sb"][:], ccs2[0:1, 0:1], ALU.mult),
            deps=[qd, cct])
        q3 = S.op("vector", lambda v: v.tensor_scalar_mul(
            P["s2s"][0:1, 0:1], P["s2s"][0:1, 0:1], 1.0 / 127.0), deps=[q2])
        q4 = S.op("vector", lambda v: v.reciprocal(
            P["s2s"][0:1, 1:2], P["s2s"][0:1, 0:1]), deps=[q3])
        s2c = self.sbuf("s2c", [128, 2], F32)
        mm1 = S.op("tensor", lambda t: t.matmul(
            self.PSB[3][:, 0:2], P["ones1_f"][:], P["s2s"][:],
            start=True, stop=True),
            deps=[q4] + self.tok_const + self.psb_free.get(3, []))
        cp1 = S.op("scalar", lambda s: s.activation(
            out=P["s2c"][:], in_=self.PSB[3][:, 0:2], func=ACT.Copy,
            bias=0.0, scale=1.0), deps=[mm1])
        self.psb_free[3] = [cp1]

        hq_toks = []
        prevq = [cp1] + h_toks
        for c in range(FFC):
            for tc in range(4):
                a1 = S.op("vector", lambda v, c=c, tc=tc: v.tensor_scalar(
                    sg_, h_sl(c)[:, 512 * tc:512 * (tc + 1)],
                    P["s2c"][:, 1:2], RMAGIC, ALU.mult, ALU.add), deps=prevq)
                a2 = S.op("vector", lambda v: v.tensor_scalar(
                    sg_, sg_, -RMAGIC, 127.0, ALU.add, ALU.min), deps=[a1])
                a3 = S.op("vector", lambda v, c=c, tc=tc: v.tensor_scalar(
                    h_sl(c)[:, 512 * tc:512 * (tc + 1)], sg_,
                    -128.0, None, ALU.max), deps=[a2])
                hq_toks.append(a3)
                prevq = [a3]

        wdl = Abf[:, 9728:15360]
        xacc5 = A[:, 7808:8320]
        combd = self.sbuf("combd", [128, 1], F32)
        self.tok_ffn_acc = []
        prev_d = list(hq_toks)
        for dc in range(8):
            tl = S.dma("sync", wnat, P["w_wd"].ap()[128 * dc:128 * (dc + 1), :],
                       deps=prev_d)
            r0 = S.op("vector", lambda v: v.tensor_reduce(
                P["rcol5"][:], wnat, AXL.X, ALU.max,
                apply_absolute_value=True), deps=[tl])
            c0 = S.op("vector", lambda v: v.tensor_scalar_mul(
                P["combd"][:], P["rcol5"][:], 1.0 / 127.0), deps=[r0])
            c1 = S.op("vector", lambda v: v.tensor_tensor(
                P["combd"][:], P["combd"][:], P["s2c"][:, 0:1], ALU.mult),
                deps=[c0, cp1])
            rr = S.op("vector", lambda v: v.reciprocal(
                P["rcol5"][:], P["rcol5"][:]), deps=[c0])
            rr2 = S.op("vector", lambda v: v.tensor_scalar_mul(
                P["rcol5"][:], P["rcol5"][:], 127.0), deps=[rr])
            a1 = S.op("vector", lambda v: v.tensor_scalar(
                wnat, wnat, P["rcol5"][:, 0:1], RMAGIC, ALU.mult, ALU.add),
                deps=[rr2])
            a2 = S.op("vector", lambda v: v.tensor_scalar(
                wnat, wnat, -RMAGIC, 127.0, ALU.add, ALU.min), deps=[a1])
            a3 = S.op("vector", lambda v: v.tensor_scalar(
                Abf[:, 5632:8448], wnat, -128.0, None, ALU.max), deps=[a2])
            ev_all = []
            for b2 in range(3):
                tt = []
                for ic in range(8):
                    gidx = 8 * b2 + ic
                    if gidx >= FFC:
                        break
                    tt.append(S.op("tensor", lambda t, gidx=gidx: t.transpose(
                        psbf[:, 7168 + 128 * (gidx % 8):
                             7168 + 128 * (gidx % 8 + 1)],
                        Abf[:, 5632 + 128 * gidx:5632 + 128 * (gidx + 1)],
                        P["id_bf"][:]),
                        deps=[a3] + self.tok_const + self.psb_free.get(7, [])))
                n = len(tt)
                ev = S.op("scalar", lambda s, b2=b2, n=n: s.activation(
                    out=wdl[:, 1024 * b2:1024 * b2 + 128 * n],
                    in_=psbf[:, 7168:7168 + 128 * n],
                    func=ACT.Copy, bias=0.0, scale=1.0), deps=tt)
                self.psb_free[7] = [ev]
                ev_all.append(ev)
            for tc in range(4):
                mms = []
                for ic in range(FFC):
                    mms.append(S.op("tensor", lambda t, tc=tc, ic=ic: t.matmul(
                        self.PSB[4][:, :],
                        wdl[:, 1024 * (ic // 8) + 128 * (ic % 8):
                            1024 * (ic // 8) + 128 * (ic % 8 + 1)],
                        h_sl(ic)[:, 512 * tc:512 * (tc + 1)],
                        start=(ic == 0), stop=(ic == FFC - 1)),
                        deps=ev_all + hq_toks + self.psb_free.get(4, [])))
                tl2 = S.dma("sync", xacc5,
                            P["h_xacc"].ap()[128 * dc:128 * (dc + 1),
                                             512 * tc:512 * (tc + 1)],
                            deps=self.tok_mamba_spill + self.tok_ffn_acc[-1:])
                ad = S.op("vector", lambda v: v.scalar_tensor_tensor(
                    xacc5, self.PSB[4][:, :], P["combd"][:, 0:1], xacc5,
                    ALU.mult, ALU.add), deps=mms + [tl2, c1])
                self.psb_free[4] = [ad]
                sp = S.dma("sync",
                           P["h_xacc"].ap()[128 * dc:128 * (dc + 1),
                                            512 * tc:512 * (tc + 1)],
                           xacc5, deps=[ad])
                self.tok_ffn_acc.append(sp)
            prev_d = [a3] + self.tok_ffn_acc[-1:]

        self.dbg_out("hq", h_sl(0), hq_toks)
        self.dbg_out("s2", ccs2[:], [cct])
        self.dbg_dram("xacc5", P["h_xacc"], [D, T], self.tok_ffn_acc)

    def p6_attn(self):
        S, P = self.S, self.P
        A = self.scr
        Abf = self.scrbf
        xn1 = self.abufA
        Qb = self.abufC
        Kb = self.wbufA
        Va = self.wbufB
        slopes = alibi_slopes(NH)

        xa = A[:, 0:2048]
        xsq = A[:, 2048:4096]
        prev_mm = list(self.tok_ffn_acc)
        for k in range(8):
            tl = S.dma("sync", xa, P["h_xacc"].ap()[128 * k:128 * (k + 1), :],
                       deps=prev_mm + self.tok_ffn_acc)
            tsq = S.op("scalar", lambda s: s.activation(
                out=xsq, in_=xa, func=ACT.Square, bias=0.0, scale=1.0),
                deps=[tl])
            prev_mm = []
            for tc in range(4):
                m1 = S.op("tensor", lambda t, tc=tc, k=k: t.matmul(
                    self.PSB[tc][0:1, :], P["onescol_f"][:],
                    xa[:, 512 * tc:512 * (tc + 1)],
                    start=(k == 0), stop=(k == 7)),
                    deps=[tl, tsq] + (self.psb_free.get(tc, []) if k == 0 else []))
                m2 = S.op("tensor", lambda t, tc=tc, k=k: t.matmul(
                    self.PSB[4 + tc][0:1, :], P["onescol_f"][:],
                    xsq[:, 512 * tc:512 * (tc + 1)],
                    start=(k == 0), stop=(k == 7)),
                    deps=[tsq] + (self.psb_free.get(4 + tc, []) if k == 0 else []))
                prev_mm += [m1, m2]
        mrow = A[0:1, 0:2048]
        vrow = A[0:1, 2048:4096]
        ev = []
        for tc in range(4):
            ev.append(S.op("scalar", lambda s, tc=tc: s.activation(
                out=mrow[0:1, 512 * tc:512 * (tc + 1)],
                in_=self.PSB[tc][0:1, :], func=ACT.Copy, bias=0.0,
                scale=1.0 / D), deps=prev_mm))
            ev.append(S.op("scalar", lambda s, tc=tc: s.activation(
                out=vrow[0:1, 512 * tc:512 * (tc + 1)],
                in_=self.PSB[4 + tc][0:1, :], func=ACT.Copy, bias=0.0,
                scale=1.0 / D), deps=prev_mm))
        for i in range(8):
            self.psb_free[i] = list(ev)
        rt1 = A[0:1, 4096:6144]
        rt2 = A[0:1, 6144:8192]
        o3 = S.op("vector", lambda v: v.tensor_tensor(
            rt1, mrow, mrow, ALU.mult), deps=ev)
        o4 = S.op("vector", lambda v: v.tensor_sub(vrow, vrow, rt1), deps=[o3])
        a_ = S.op("scalar", lambda s: s.activation(
            out=rt1, in_=vrow, func=ACT.Sqrt, bias=P["eps1"][:, 0:1],
            scale=1.0), deps=[o4] + self.tok_const)
        y0 = S.op("vector", lambda v: v.reciprocal(rt2, rt1), deps=[a_])
        b1 = S.op("vector", lambda v: v.tensor_scalar_add(rt1, vrow, EPS),
                  deps=[y0])
        b2 = S.op("vector", lambda v: v.tensor_tensor(rt1, rt1, rt2, ALU.mult),
                  deps=[b1])
        b3 = S.op("vector", lambda v: v.tensor_tensor(rt1, rt1, rt2, ALU.mult),
                  deps=[b2])
        b4 = S.op("vector", lambda v: v.tensor_scalar(
            rt1, rt1, -0.5, 1.5, ALU.mult, ALU.add), deps=[b3])
        rdone = S.op("vector", lambda v: v.tensor_tensor(
            vrow, rt1, rt2, ALU.mult), deps=[b4])
        rrow = vrow
        m_bc = A[:, 0:2048]
        r_bc = A[:, 2048:4096]
        bc_last = [rdone]
        for row, dstb in [(mrow, m_bc), (rrow, r_bc)]:
            for tc in range(4):
                mm = self.bcast_row_psum(
                    self.PSB[0][:, :], row[0:1, 512 * tc:512 * (tc + 1)],
                    deps=bc_last + self.tok_const + self.psb_free.get(0, []))
                cp = S.op("scalar", lambda s, dstb=dstb, tc=tc: s.activation(
                    out=dstb[:, 512 * tc:512 * (tc + 1)], in_=self.PSB[0][:],
                    func=ACT.Copy, bias=0.0, scale=1.0), deps=[mm])
                self.psb_free[0] = [cp]
                bc_last = [cp]
        xnf = A[:, 4096:6144]
        xn1_toks = []
        prev = list(bc_last)
        for k in range(8):
            tl = S.dma("sync", xa, P["h_xacc"].ap()[128 * k:128 * (k + 1), :],
                       deps=prev)
            s1 = S.op("vector", lambda v: v.scalar_tensor_tensor(
                xnf, m_bc, -1.0, xa, ALU.mult, ALU.add), deps=[tl])
            s2 = S.op("vector", lambda v, k=k: v.tensor_tensor(
                xn1[:, T * k:T * (k + 1)], xnf, r_bc, ALU.mult), deps=[s1])
            prev = [s2]
            xn1_toks.append(s2)
        self.tok_xn1 = xn1_toks

        wstage = Abf[:, 0:8192]

        def cast_w(dram, deps, dst=None):
            dst = wstage if dst is None else dst
            toks = []
            for k in range(8):
                tl = S.dma("sync", A[:, 6144:7168],
                           P[dram].ap()[128 * k:128 * (k + 1), :], deps=deps +
                           toks[-1:])
                toks.append(S.op("vector", lambda v, k=k, dst=dst: v.tensor_copy(
                    dst[:, 1024 * k:1024 * (k + 1)], A[:, 6144:7168]),
                    deps=[tl]))
            return toks

        qk_toks = []
        for dst, wnm in [(Qb, "w_WqT"), (Kb, "w_WkT")]:
            ct = cast_w(wnm, xn1_toks + qk_toks)
            for oc in range(8):
                for tc in range(4):
                    mms = []
                    for k in range(8):
                        mms.append(S.op("tensor",
                            lambda t, oc=oc, tc=tc, k=k: t.matmul(
                                self.PSB[oc % 2][:, :],
                                wstage[:, 1024 * k + 128 * oc:
                                       1024 * k + 128 * (oc + 1)],
                                xn1[:, T * k + 512 * tc:T * k + 512 * (tc + 1)],
                                start=(k == 0), stop=(k == 7)),
                            deps=ct + xn1_toks + self.psb_free.get(oc % 2, [])))
                    e1 = S.op("scalar", lambda s, dst=dst, oc=oc, tc=tc:
                              s.activation(
                                  out=dst[:, 2048 * oc + 512 * tc:
                                          2048 * oc + 512 * (tc + 1)],
                                  in_=self.PSB[oc % 2][:, :], func=ACT.Copy,
                                  bias=0.0, scale=1.0), deps=mms)
                    self.psb_free[oc % 2] = [e1]
                    qk_toks.append(e1)
        ct = cast_w("w_WvT", qk_toks)
        vset = S.op("vector", lambda v: v.memset(
            Va.rearrange("p (tk h o) -> p tk h o", h=NH, o=65)[:, :, :, 64:65],
            1.0), deps=qk_toks)
        v_toks = [vset]
        for tk in range(16):
            for ch in range(2):
                mms = []
                for k in range(8):
                    mms.append(S.op("tensor", lambda t, tk=tk, ch=ch, k=k:
                        t.matmul(
                            self.PSB[2 + ch][:, :],
                            xn1[:, T * k + 128 * tk:T * k + 128 * (tk + 1)],
                            wstage[:, 1024 * k + 512 * ch:
                                   1024 * k + 512 * (ch + 1)],
                            start=(k == 0), stop=(k == 7)),
                        deps=ct + xn1_toks + self.psb_free.get(2 + ch, [])))
                e1 = S.op("scalar", lambda s, tk=tk, ch=ch: s.activation(
                    out=Va.rearrange("p (tk h o) -> p tk h o", h=NH, o=65)
                        [:, tk, 8 * ch:8 * (ch + 1), 0:64],
                    in_=self.PSB[2 + ch][:, :], func=ACT.Copy,
                    bias=0.0, scale=1.0), deps=mms + [vset])
                self.psb_free[2 + ch] = [e1]
                v_toks.append(e1)

        kbase = self.sbuf("kbase", [128, 16], F32)
        kb0 = S.op("gpsimd", lambda g: g.iota(
            P["kbase"][:], pattern=[[128, 16]], base=0, channel_multiplier=1,
            allow_small_or_imprecise_dtypes=True), deps=v_toks)
        masks = Abf[:, 16384:18432]
        maskp = self.sbuf("maskp64", [128, 64], BF16)
        mk = []
        m0 = S.op("vector", lambda v: v.memset(A[:, 9216:9728], 1.0),
                  deps=v_toks + self.tok_mamba_spill)
        for j in range(4):
            mk.append(S.op("gpsimd", lambda g, j=j: g.affine_select(
                masks[:, 512 * j:512 * (j + 1)], A[:, 9216:9728],
                pattern=[[1, 512]], compare_op=ALU.is_ge, fill=0.0,
                base=-128 * j, channel_multiplier=-1), deps=[m0]))
        mk.append(S.op("gpsimd", lambda g: g.affine_select(
            maskp[:], A[:, 9216:9280], pattern=[[1, 64]],
            compare_op=ALU.is_ge, fill=0.0, base=64, channel_multiplier=-1),
            deps=[m0]))
        bias_t = self.sbuf("bias_t", [128, 16], F32)
        ptile = [Abf[:, 18432:19456], Abf[:, 19456:20480]]
        rcrow = self.PSB[7][0:1, 0:512]
        attn_un = Abf[:, 0:16384]

        at_toks = []
        pt_tok = {0: [], 1: []}
        for h in range(NH):
            qB = 64 if h < 2 else (128 if h < 5 else 512)
            nqb = T // qB
            kslice = Kb[64 * (h % 2):64 * (h % 2) + 64, :]
            qslice = Qb[64 * (h % 2):64 * (h % 2) + 64, :]
            for qb in range(nqb):
                qs = qB * qb
                qref = float(qs + qB - 1)
                kb_lo = max(0, (qs - 384) // 128) if qB < 512 else 0
                kb_hi = (qs + qB - 1) // 128
                bt = S.op("vector", lambda v, h=h, qref=qref: v.tensor_scalar(
                    P["bias_t"][:], P["kbase"][:], -qref,
                    float(slopes[h]), ALU.add, ALU.mult),
                    deps=[kb0] + pt_tok[0] + pt_tok[1])
                o_mms = []
                for kb in range(kb_lo, kb_hi + 1):
                    smm = S.op("tensor", lambda t, h=h, qs=qs, qB=qB, kb=kb,
                               kslice=kslice, qslice=qslice: t.matmul(
                        self.PSB[4][:, 0:qB],
                        kslice[:, 2048 * (h // 2) + 128 * kb:
                               2048 * (h // 2) + 128 * (kb + 1)],
                        qslice[:, 2048 * (h // 2) + qs:
                               2048 * (h // 2) + qs + qB],
                        start=True, stop=True),
                        deps=qk_toks + self.psb_free.get(4, []))
                    pb = kb % 2
                    ex = S.op("scalar", lambda s, kb=kb, pb=pb, qB=qB:
                              s.activation(
                                  out=ptile[pb][:, 0:qB],
                                  in_=self.PSB[4][:, 0:qB],
                                  func=ACT.Exp, scale=0.125,
                                  bias=P["bias_t"][:, kb:kb + 1]),
                              deps=[smm, bt] + pt_tok[pb])
                    self.psb_free[4] = [ex]
                    ptok = [ex]
                    if kb == kb_hi and qB < 512:
                        off = qs - 128 * kb
                        msl = (maskp[:, 0:qB] if off == 64 else
                               masks[:, 0:qB])
                        ptok = [S.op("vector", lambda v, pb=pb, qB=qB, msl=msl:
                            v.tensor_tensor(
                                ptile[pb][:, 0:qB], ptile[pb][:, 0:qB],
                                msl, ALU.mult), deps=[ex] + mk)]
                    elif qB == 512 and kb >= 4 * qb:
                        j = kb - 4 * qb
                        ptok = [S.op("vector", lambda v, j=j, pb=pb:
                            v.tensor_tensor(
                                ptile[pb][:, 0:512], ptile[pb][:, 0:512],
                                masks[:, 512 * j:512 * (j + 1)], ALU.mult),
                            deps=[ex] + mk)]
                    omm = S.op("tensor", lambda t, h=h, kb=kb, pb=pb, qB=qB,
                               kb_lo=kb_lo, kb_hi=kb_hi: t.matmul(
                        self.PSB[5][0:65, 0:qB],
                        Va[:, 1040 * kb + 65 * h:1040 * kb + 65 * (h + 1)],
                        ptile[pb][:, 0:qB],
                        start=(kb == kb_lo), stop=(kb == kb_hi)),
                        deps=ptok + v_toks + (self.psb_free.get(5, [])
                                              if kb == kb_lo else []))
                    pt_tok[pb] = [omm]
                    o_mms.append(omm)
                l1 = S.op("scalar", lambda s, qB=qB: s.activation(
                    out=rcrow[0:1, 0:qB], in_=self.PSB[5][64:65, 0:qB],
                    func=ACT.Ln, bias=0.0, scale=1.0),
                    deps=o_mms + self.psb_free.get(7, []))
                self.psb_free[7] = [l1]
                l2 = S.op("scalar", lambda s, qB=qB: s.activation(
                    out=self.xnsl.bitcast(F32)[0:1, 3072:3584][0:1, 0:qB], in_=rcrow[0:1, 0:qB],
                    func=ACT.Exp, bias=0.0, scale=-1.0), deps=[l1])
                bmm = S.op("tensor", lambda t, qB=qB: t.matmul(
                    self.PSB[6][0:64, 0:qB], P["ones1_f"][0:1, 0:64],
                    self.xnsl.bitcast(F32)[0:1, 3072:3584][0:1, 0:qB],
                    start=True, stop=True),
                    deps=[l2] + self.tok_const + self.psb_free.get(6, []))
                asl = attn_un[64 * (h % 2):64 * (h % 2) + 64,
                              2048 * (h // 2) + qs:2048 * (h // 2) + qs + qB]
                ev1 = S.op("scalar", lambda s, asl=asl, qB=qB: s.activation(
                    out=asl, in_=self.PSB[5][0:64, 0:qB], func=ACT.Copy,
                    bias=0.0, scale=1.0), deps=o_mms + [bmm])
                nm = S.op("vector", lambda v, asl=asl, qB=qB: v.tensor_tensor(
                    asl, asl, self.PSB[6][0:64, 0:qB], ALU.mult), deps=[ev1])
                self.psb_free[5] = [ev1]
                self.psb_free[6] = [nm]
                at_toks.append(nm)

        wpst = Qb[:, 0:8192]
        wp = []
        for k in range(8):
            tl = S.dma("sync", A[:, 8192:9216],
                       P["w_WpT"].ap()[128 * k:128 * (k + 1), :],
                       deps=at_toks + wp[-1:])
            wp.append(S.op("vector", lambda v, k=k: v.tensor_copy(
                wpst[:, 1024 * k:1024 * (k + 1)], A[:, 8192:9216]),
                deps=[tl]))
        pr_toks = []
        for dc in range(8):
            for tc in range(4):
                mms = []
                for k in range(8):
                    mms.append(S.op("tensor", lambda t, dc=dc, tc=tc, k=k:
                        t.matmul(
                            self.PSB[0][:, :],
                            wpst[:, 1024 * k + 128 * dc:
                                 1024 * k + 128 * (dc + 1)],
                            attn_un[:, 2048 * k + 512 * tc:
                                    2048 * k + 512 * (tc + 1)],
                            start=(k == 0), stop=(k == 7)),
                        deps=wp + at_toks + self.psb_free.get(0, [])))
                tl = S.dma("sync", A[:, 8704:9216],
                           P["h_xacc"].ap()[128 * dc:128 * (dc + 1),
                                            512 * tc:512 * (tc + 1)],
                           deps=pr_toks[-1:] + self.tok_ffn_acc)
                ad = S.op("vector", lambda v: v.tensor_add(
                    A[:, 8704:9216], A[:, 8704:9216], self.PSB[0][:, :]),
                    deps=[tl] + mms)
                self.psb_free[0] = [ad]
                sp = S.dma("sync",
                           P["h_xacc"].ap()[128 * dc:128 * (dc + 1),
                                            512 * tc:512 * (tc + 1)],
                           A[:, 8704:9216], deps=[ad])
                pr_toks.append(sp)
        self.tok_attn_acc = pr_toks
        self.dbg_out("attnu", self.scrbf[:, 0:2048], at_toks)
        self.dbg_dram("xacc6", P["h_xacc"], [D, T], pr_toks)


    def p7_moe(self):
        S, P = self.S, self.P
        A = self.scr
        Abf = self.scrbf
        xn1 = self.abufA
        hpr = self.abufC
        wst = [Abf[:, 0:8192], Abf[:, 8192:16384]]
        hraw = Abf[:, 16384:17408]
        t_b1, _ = self.load_cols("w_moeb1", 32)
        wc_toks = []
        prev_c = list(self.tok_attn_acc)
        for e in range(NE):
            for k in range(8):
                tl = S.dma("sync", A[:, 0:512],
                           P["w_moe1"].ap()[e, 128 * k:128 * (k + 1), :],
                           deps=prev_c)
                cv = S.op("vector", lambda v: v.tensor_copy(
                    Abf[:, 1024:1536], A[:, 0:512]), deps=[tl])
                sp = S.dma("sync", P["h_moe1"].ap()[e, 128 * k:128 * (k + 1), :],
                           Abf[:, 1024:1536], deps=[cv])
                wc_toks.append(sp)
                prev_c = [sp, cv]
            for ic in range(4):
                tl = S.dma("sync", A[:, 0:1024],
                           P["w_moe2"].ap()[e, 128 * ic:128 * (ic + 1), :],
                           deps=prev_c)
                cv = S.op("vector", lambda v: v.tensor_copy(
                    Abf[:, 2048:3072], A[:, 0:1024]), deps=[tl])
                sp = S.dma("sync", P["h_moe2"].ap()[e, 128 * ic:128 * (ic + 1), :],
                           Abf[:, 2048:3072], deps=[cv])
                wc_toks.append(sp)
                prev_c = [sp, cv]
        gw_toks = []
        for k in range(8):
            tl = S.dma("sync", A[:, 8448:8456],
                       P["w_gw"].ap()[128 * k:128 * (k + 1), :],
                       deps=self.tok_attn_acc + gw_toks[-1:])
            gw_toks.append(S.op("vector", lambda v, k=k: v.tensor_copy(
                self.xnsl[:, 4096:4160][:, 8 * k:8 * (k + 1)], A[:, 8448:8456]), deps=[tl]))

        oh8 = self.xnsl[0:8, 2048:3072]
        z0 = S.op("vector", lambda v: v.memset(A[0:8, 0:1024], 1.0),
                  deps=self.tok_attn_acc)
        zoh = S.op("gpsimd", lambda g: g.affine_select(
            oh8[:], A[0:8, 0:1024], pattern=[[1, 8], [0, 128]],
            compare_op=ALU.is_equal, fill=0.0, base=0, channel_multiplier=-1),
            deps=[z0])

        wr8 = self.xnsl[0:8, 0:2048]
        lg = A[:, 8456:8464]
        pb_ = A[:, 8464:8472]
        col1 = A[:, 8472:8473]
        col2 = A[:, 8473:8474]
        wr_toks = []
        for tk in range(16):
            mms = []
            for k in range(8):
                mms.append(S.op("tensor", lambda t, tk=tk, k=k: t.matmul(
                    self.PSB[0][:, 0:8],
                    xn1[:, T * k + 128 * tk:T * k + 128 * (tk + 1)],
                    self.xnsl[:, 4096:4160][:, 8 * k:8 * (k + 1)],
                    start=(k == 0), stop=(k == 7)),
                    deps=gw_toks + self.tok_xn1 + self.psb_free.get(0, [])))
            g0 = S.op("vector", lambda v: v.tensor_reduce(
                col1, self.PSB[0][:, 0:8], AXL.X, ALU.max), deps=mms)
            g1 = S.op("vector", lambda v: v.tensor_scalar_mul(
                col2, col1, -1.0), deps=[g0])
            g2 = S.op("scalar", lambda s: s.activation(
                out=lg, in_=self.PSB[0][:, 0:8], func=ACT.Exp,
                bias=col2[:, 0:1], scale=1.0), deps=[g1] + mms)
            self.psb_free[0] = [g2]
            g3 = S.op("vector", lambda v: v.tensor_reduce(
                col1, lg, AXL.X, ALU.add), deps=[g2])
            g4 = S.op("vector", lambda v: v.reciprocal(col2, col1), deps=[g3])
            g5 = S.op("vector", lambda v: v.tensor_scalar(
                pb_, lg, col2[:, 0:1], None, ALU.mult), deps=[g4])
            g6 = S.op("vector", lambda v: v.tensor_reduce(
                col1, pb_, AXL.X, ALU.max), deps=[g5])
            g7 = S.op("vector", lambda v: v.tensor_scalar(
                lg, pb_, col1[:, 0:1], None, ALU.is_ge), deps=[g6])
            g8 = S.op("vector", lambda v: v.scalar_tensor_tensor(
                lg, lg, -2.0, pb_, ALU.mult, ALU.add), deps=[g7])
            g9 = S.op("vector", lambda v: v.tensor_reduce(
                col2, lg, AXL.X, ALU.max), deps=[g8])
            ga = S.op("vector", lambda v: v.tensor_scalar(
                lg, pb_, col2[:, 0:1], None, ALU.is_ge), deps=[g9])
            gb = S.op("vector", lambda v: v.tensor_add(
                col1, col1, col2), deps=[ga])
            gc = S.op("vector", lambda v: v.tensor_scalar(
                lg, lg, col1[:, 0:1], None, ALU.mult), deps=[gb])
            gd = S.op("vector", lambda v: v.tensor_copy(
                self.xnsl[:, 4160:4168][:], lg), deps=[gc])
            tmm = S.op("tensor", lambda t: t.transpose(
                self.PSA.bitcast(BF16)[0:8, 2048:2176], self.xnsl[:, 4160:4168][:],
                P["id_bf"][:]), deps=[gd] + self.tok_const +
                self.psb_free.get(1, []))
            ev = S.op("scalar", lambda s, tk=tk: s.activation(
                out=wr8[:, 128 * tk:128 * (tk + 1)],
                in_=self.PSA.bitcast(BF16)[0:8, 2048:2176],
                func=ACT.Copy, bias=0.0, scale=1.0), deps=[tmm])
            self.psb_free[1] = [ev]
            wr_toks.append(ev)

        b2t = self.xnsl[0:8, 3072:4096]
        tb2a = S.dma("sync", A[0:8, 7168:8192], P["w_moeb2"].ap(),
                     deps=self.tok_attn_acc)
        tok_b2t = S.op("vector", lambda v: v.tensor_copy(
            b2t[:], A[0:8, 7168:8192]), deps=[tb2a])
        out_toks = []
        for tr in range(4):
            to = 512 * tr
            h_toks = []
            prev_init = list(wr_toks) + wc_toks[-1:] + out_toks[-1:]
            prevbuf = {0: list(prev_init), 1: list(prev_init)}
            for e in range(NE):
                st = wst[e % 2]
                wl = []
                for k in range(8):
                    wl.append(S.dma("sync", st[:, 512 * k:512 * (k + 1)],
                                    P["h_moe1"].ap()[e, 128 * k:128 * (k + 1), :],
                                    deps=prevbuf[e % 2]))
                emms = []
                for ic in range(4):
                    mms = []
                    for k in range(8):
                        mms.append(S.op("tensor",
                            lambda t, ic=ic, k=k, st=st, to=to: t.matmul(
                                self.PSB[2][:, :],
                                st[:, 512 * k + 128 * ic:512 * k + 128 * (ic + 1)],
                                xn1[:, T * k + to:T * k + to + 512],
                                start=(k == 0), stop=(k == 7)),
                            deps=wl + self.tok_xn1 + self.psb_free.get(2, [])))
                    ge = S.op("scalar", lambda s, e=e, ic=ic: s.activation(
                        out=hraw[:, 0:512], in_=self.PSB[2][:, :],
                        func=ACT.Gelu, bias=P["w_moeb1_c"][:, 4 * e + ic:
                                              4 * e + ic + 1], scale=1.0),
                        deps=mms + [t_b1])
                    self.psb_free[2] = [ge]
                    bmm = S.op("tensor", lambda t, e=e, tr=tr: t.matmul(
                        self.PSB[3][:, :], oh8[:, 128 * e:128 * (e + 1)],
                        wr8[:, 512 * tr:512 * (tr + 1)],
                        start=True, stop=True),
                        deps=[zoh] + wr_toks + self.psb_free.get(3, []))
                    hp = S.op("vector", lambda v, e=e, ic=ic: v.tensor_tensor(
                        hpr[:, 2048 * e + 512 * ic:2048 * e + 512 * (ic + 1)],
                        hraw[:, 0:512], self.PSB[3][:, :], ALU.mult),
                        deps=[ge, bmm])
                    self.psb_free[3] = [hp]
                    h_toks.append(hp)
                    emms += mms
                prevbuf[e % 2] = emms
            mm_all = []
            prevbuf2 = {0: list(h_toks), 1: list(h_toks)}
            for e in range(NE):
                st = wst[e % 2]
                wl = [S.dma("sync", st[:, 1024 * ic:1024 * (ic + 1)],
                            P["h_moe2"].ap()[e, 128 * ic:128 * (ic + 1), :],
                            deps=prevbuf2[e % 2]) for ic in range(4)]
                emms = []
                for dc in range(8):
                    for ic in range(4):
                        emms.append(S.op("tensor",
                            lambda t, e=e, dc=dc, ic=ic, st=st: t.matmul(
                                self.PSB[dc][:, :],
                                st[:, 1024 * ic + 128 * dc:
                                   1024 * ic + 128 * (dc + 1)],
                                hpr[:, 2048 * e + 512 * ic:
                                    2048 * e + 512 * (ic + 1)],
                                start=(e == 0 and ic == 0), stop=False),
                            deps=wl + h_toks +
                                 (self.psb_free.get(dc, []) if e == 0 and
                                  ic == 0 else [])))
                mm_all += emms
                prevbuf2[e % 2] = emms
            for dc in range(8):
                fmm = S.op("tensor", lambda t, dc=dc, tr=tr: t.matmul(
                    self.PSB[dc][:, :], b2t[:, 128 * dc:128 * (dc + 1)],
                    wr8[:, 512 * tr:512 * (tr + 1)],
                    start=False, stop=True),
                    deps=mm_all + [tok_b2t] + wr_toks)
                tl = S.dma("sync", A[:, 8704:9216],
                           P["h_xacc"].ap()[128 * dc:128 * (dc + 1),
                                            to:to + 512],
                           deps=self.tok_attn_acc + out_toks[-1:])
                ad = S.op("vector", lambda v, dc=dc: v.tensor_add(
                    A[:, 8704:9216], A[:, 8704:9216], self.PSB[dc][:, :]),
                    deps=[tl, fmm])
                self.psb_free[dc] = [ad]
                sp = S.dma("sync",
                           P["out_fm"].ap()[128 * dc:128 * (dc + 1),
                                            to:to + 512],
                           A[:, 8704:9216], deps=[ad])
                out_toks.append(sp)
        self.final_deps += out_toks


_BUILD_CACHE = {}


def _get_nc():
    if "nc" not in _BUILD_CACHE:
        _BUILD_CACHE["nc"] = Builder().build()
    return _BUILD_CACHE["nc"]


def kernel(**inputs):
    w, xs = prep_inputs(inputs)
    nc = _get_nc()
    in_maps = [dict(w, x_fm=xs[c]) for c in range(8)]
    res = run_bass_kernel_spmd(nc, in_maps, core_ids=list(range(8)))
    out = np.stack([r["out_fm"].T for r in res.results], axis=0)
    return np.ascontiguousarray(out.astype(np.float32))


# version 07
# speedup vs baseline: 1.1312x; 1.0024x over previous
import sys
import numpy as np
import concourse.bass as bass
import concourse.mybir as mybir
from concourse.bass_utils import run_bass_kernel_spmd

F32 = mybir.dt.float32
BF16 = mybir.dt.bfloat16
ALU = mybir.AluOpType
ACT = mybir.ActivationFunctionType
AXL = mybir.AxisListType

T = 2048
TH = T // 2
TC = 512
D = 1024
DI = 2048
DS = 16
DTR = 64
NH = 16
DH = 64
NE = 8
EC = 512
FF = 2816
FFC = FF // 128
EPS = 1e-5
RMAGIC = float(np.float32(1.5 * 2.0**23))
QBE = float(np.float32(128.0) - np.float32(1e-5))
ENGINES = ("tensor", "vector", "scalar", "gpsimd", "sync")


class Tok:
    __slots__ = ("eng", "idx")

    def __init__(self, eng, idx):
        self.eng, self.idx = eng, idx


class Sched:

    def __init__(self, nc):
        self.nc = nc
        self.ops = {e: [] for e in ENGINES}
        self.sem = {e: nc.alloc_semaphore(f"sem_{e}") for e in ENGINES}
        self.dsem = {e: nc.alloc_semaphore(f"dsem_{e}") for e in ENGINES}
        self.fence_buf = nc.alloc_sbuf_tensor("sched_fence", [128, 4],
                                              mybir.dt.float32)
        self.junk_sem = nc.alloc_semaphore("sem_junk")

    def op(self, eng, fn, deps=()):
        self.ops[eng].append([fn, list(deps), False, False,
                              sys._getframe(1).f_lineno])
        return Tok(eng, len(self.ops[eng]) - 1)

    def dma(self, eng, out, in_, deps=(), **kw):
        assert out is not None and in_ is not None
        fn = lambda e, out=out, in_=in_, kw=kw: e.dma_start(out=out, in_=in_, **kw)
        self.ops[eng].append([fn, list(deps), False, True,
                              sys._getframe(1).f_lineno])
        return Tok(eng, len(self.ops[eng]) - 1)

    def finalize(self, block):
        for e in ENGINES:
            for rec in self.ops[e]:
                if rec[3]:
                    rec[2] = True
                for t in rec[1]:
                    self.ops[t.eng][t.idx][2] = True
        self.incval = {e: [] for e in ENGINES}
        self.dincval = {e: [] for e in ENGINES}
        for e in ENGINES:
            c = dc = 0
            for rec in self.ops[e]:
                if rec[2]:
                    if rec[3]:
                        dc += 16
                    else:
                        c += 1
                self.incval[e].append(c)
                self.dincval[e].append(dc)

        GAPCOL = {"vector": 0, "scalar": 1, "gpsimd": 2}

        def emitter(eng_name):
            def run(eng):
                waited = {}
                empos = {}
                pos = 0

                def gap_op():
                    col = GAPCOL[eng_name]
                    fb = self.fence_buf[:, col:col + 1]
                    if eng_name == "scalar":
                        g = eng.activation(
                            out=fb, in_=fb,
                            func=mybir.ActivationFunctionType.Copy,
                            bias=0.0, scale=1.0)
                    else:
                        g = eng.memset(fb, 0.0)
                    g.then_inc(self.junk_sem, 1)
                    return g

                for i, (fn, deps, inc, is_dma, _ln) in enumerate(
                        self.ops[eng_name]):
                    for t in deps:
                        t_isdma = self.ops[t.eng][t.idx][3]
                        if (t.eng == eng_name and not t_isdma
                                and eng_name not in GAPCOL):
                            continue
                        if t_isdma:
                            sem, val, key = (self.dsem[t.eng],
                                             self.dincval[t.eng][t.idx],
                                             ("d", t.eng))
                        else:
                            sem, val, key = (self.sem[t.eng],
                                             self.incval[t.eng][t.idx],
                                             ("c", t.eng))
                        if waited.get(key, -1) < val:
                            eng.wait_ge(sem, val)
                            waited[key] = val
                    inst = fn(eng)
                    empos[i] = pos
                    pos += 1
                    if inc:
                        if eng_name == "vector" and not is_dma:
                            try:
                                small = any(int(o.ap[0][1]) < 128
                                            for o in inst.ins.outs)
                            except Exception:
                                small = True
                            if small:
                                inst = eng.memset(
                                    self.fence_buf[:, 0:1], 0.0)
                                pos += 1
                        inst.then_inc(self.dsem[eng_name] if is_dma
                                      else self.sem[eng_name],
                                      16 if is_dma else 1)
            return run

        for name in ENGINES:
            if self.ops[name]:
                getattr(block, name)(emitter(name))



def _t(a):
    return np.ascontiguousarray(np.asarray(a, dtype=np.float32).T)


def _c(a, shape=None):
    a = np.asarray(a, dtype=np.float32)
    if shape is not None:
        a = a.reshape(shape)
    return np.ascontiguousarray(a)


def _v(a):
    return np.ascontiguousarray(np.asarray(a, np.float32).reshape(-1, 128))


def prep_inputs(inp):
    w = {}
    w["w_ln1g"] = _v(inp["L0_attn_ln1_g"])
    w["w_ln1b"] = _v(inp["L0_attn_ln1_b"])
    w["w_ln2g"] = _v(inp["L0_attn_ln2_g"])
    w["w_ln2b"] = _v(inp["L0_attn_ln2_b"])
    w["w_W1T"] = _t(inp["L0_attn_W1"])
    w["w_b1"] = _v(inp["L0_attn_b1"])
    w["w_W2T"] = _t(inp["L0_attn_W2"])
    w["w_b2"] = _v(inp["L0_attn_b2"])
    w["w_inpT"] = _t(inp["L0_attn_in_proj_w"])
    w["w_convW"] = _c(inp["L0_attn_conv_w"], (DI, 4))
    w["w_convB"] = _v(inp["L0_attn_conv_b"])
    xp = np.asarray(inp["L0_attn_x_proj_w"], np.float32)
    xpp = np.zeros((128, DI), np.float32)
    xpp[0:64] = xp[0:64]
    xpp[64:80] = xp[64:80]
    xpp[96:112] = xp[80:96]
    w["w_xprojT"] = _t(xpp)
    w["w_dtprojT"] = _t(inp["L0_attn_dt_proj_w"])
    w["w_dtb"] = _v(inp["L0_attn_dt_proj_b"])
    w["w_Alog"] = _c(inp["L0_attn_A_log"])
    w["w_Dp"] = _v(inp["L0_attn_D_param"])
    w["w_outpT"] = _t(inp["L0_attn_out_proj_w"])
    w["w_clipx"] = _c(inp["L0_ffn_clip_x"], (1, 1))
    w["w_clipd"] = _c(inp["L0_ffn_clip_down"], (1, 1))
    w["w_wg"] = _c(inp["L0_ffn_w_gate"])
    w["w_wu"] = _c(inp["L0_ffn_w_up"])
    w["w_wd"] = _c(inp["L0_ffn_w_down"])
    Wa = np.asarray(inp["L1_attn_W_attn"], dtype=np.float32)
    w["w_WqT"] = _t(Wa[0:D])
    w["w_WkT"] = _t(Wa[D:2 * D])
    w["w_WvT"] = _t(Wa[2 * D:3 * D])
    w["w_WpT"] = _t(inp["L1_attn_W_proj"])
    w["w_gw"] = _c(inp["L1_ffn_gate_w"])
    w["w_moe1"] = _c(inp["L1_ffn_w1"])
    w["w_moeb1"] = _c(inp["L1_ffn_b1"], (NE * EC // 128, 128))
    w["w_moe2"] = _c(inp["L1_ffn_w2"])
    w["w_moeb2"] = _c(inp["L1_ffn_b2"], (NE * D // 128, 128))
    xs = [np.ascontiguousarray(np.asarray(inp["L0_attn_x"][c], np.float32).T)
          for c in range(8)]
    return w, xs


def alibi_slopes(n=16):
    import math
    start = 2.0 ** (-(2.0 ** -(math.log2(n) - 3)))
    return [start * start ** i for i in range(n)]


DRAM_SPECS = [
    ("w_ln1g", [8, 128]), ("w_ln1b", [8, 128]), ("w_ln2g", [8, 128]),
    ("w_ln2b", [8, 128]), ("w_W1T", [D, DI]), ("w_b1", [16, 128]),
    ("w_W2T", [DI, D]), ("w_b2", [8, 128]), ("w_inpT", [D, 2 * DI]),
    ("w_convW", [DI, 4]), ("w_convB", [16, 128]), ("w_xprojT", [DI, 128]),
    ("w_dtprojT", [DTR, DI]), ("w_dtb", [16, 128]), ("w_Alog", [DI, DS]),
    ("w_Dp", [16, 128]), ("w_outpT", [DI, D]), ("w_clipx", [1, 1]),
    ("w_clipd", [1, 1]), ("w_wg", [FF, D]), ("w_wu", [FF, D]),
    ("w_wd", [D, FF]), ("w_WqT", [D, D]), ("w_WkT", [D, D]),
    ("w_WvT", [D, D]), ("w_WpT", [D, D]), ("w_gw", [D, NE]),
    ("w_moe1", [NE, D, EC]), ("w_moeb1", [NE * EC // 128, 128]),
    ("w_moe2", [NE, EC, D]), ("w_moeb2", [NE * D // 128, 128]),
]


class Builder:
    def __init__(self, dbg=(), stop=None, drc=False):
        self.dbg = set(dbg) if dbg else set()
        self.stop = stop
        nc = bass.Bass("TRN2", target_bir_lowering=False,
                       detect_race_conditions=drc)
        self.nc = nc
        self.P = {}
        self.final_deps = []
        self.dbg_last = []
        self.tok_dbg = []
        self.din("x_fm", [D, T])
        for n, s in DRAM_SPECS:
            self.din(n, s)
        self.dout("out_fm", [D, T])
        self.dint("h_xacc", [D, T])
        self.dint("h_xq", [D, T], BF16)
        self.dint("h_qxf", [D, T])
        self.dint("h_winp", [D, 2 * DI], BF16)
        self.dint("h_moe1", [NE, D, EC], BF16)
        self.dint("h_moe2", [NE, EC, D], BF16)
        self.dint("cc_in", [4, 16])
        self.dint("cc_out", [4, 16])
        self.S = Sched(nc)
        self.psb_free = {}
        self.tok_cc = []
        self.PSA = nc.alloc_psum_tensor("psall", [128, 4096], F32)
        self.PSB = [self.PSA[:, 512 * i:512 * (i + 1)] for i in range(8)]
        self.scr = nc.alloc_sbuf_tensor("scr", [128, 10240], F32)
        self.scrbf = self.scr.bitcast(BF16)
        self.xnsl = nc.alloc_sbuf_tensor("xnsl", [128, 16384], BF16)
        self.abufA = nc.alloc_sbuf_tensor("abufA", [128, 16384], BF16)
        self.abufC = nc.alloc_sbuf_tensor("abufC", [128, 16384], BF16)
        self.wbufA = nc.alloc_sbuf_tensor("wbufA", [128, 16384], BF16)
        self.wbufB = nc.alloc_sbuf_tensor("wbufB", [128, 16640], BF16)

    def din(self, name, shape, dtype=F32):
        self.P[name] = self.nc.dram_tensor(name, shape, dtype, kind="ExternalInput")

    def dout(self, name, shape, dtype=F32):
        self.P[name] = self.nc.dram_tensor(name, shape, dtype, kind="ExternalOutput")

    def dint(self, name, shape, dtype=F32):
        self.P[name] = self.nc.dram_tensor(name, shape, dtype)

    def sbuf(self, name, shape, dtype=F32):
        self.P[name] = self.nc.alloc_sbuf_tensor(name, shape, dtype)
        return self.P[name]

    def dbg_out(self, tag, ap, deps):
        if tag not in self.dbg:
            return []
        t = self.nc.dram_tensor("dbg_" + tag, list(ap.shape), ap.dtype,
                                kind="ExternalOutput")
        tok = self.S.dma("sync", t.ap(), ap, deps=deps)
        self.final_deps.append(tok)
        self.tok_dbg.append(tok)
        return [tok]

    def dbg_dram(self, tag, dram, shape, deps):
        if tag not in self.dbg:
            return
        S = self.S
        t = self.nc.dram_tensor("dbg_" + tag, list(shape), F32,
                                kind="ExternalOutput")
        stg = self.scr[:, 0:shape[1]]
        last = list(deps) + self.dbg_last
        for r0 in range(0, shape[0], 128):
            l1 = S.dma("sync", stg, dram.ap()[r0:r0 + 128, :], deps=last)
            l2 = S.dma("sync", t.ap()[r0:r0 + 128, :], stg, deps=[l1])
            last = [l2]
        self.dbg_last = last
        self.final_deps.append(last[0])

    def load_cols(self, dram_name, n, psb=3):
        S, P = self.S, self.P
        if "lc_row" not in P:
            self.sbuf("lc_row", [32, 128], F32)
            self.lc_last = []
        row = P["lc_row"]
        col = self.sbuf(dram_name + "_c", [128, n], F32)
        t0 = S.dma("sync", row[0:n, :], P[dram_name].ap(), deps=self.lc_last)
        t1 = S.op("tensor", lambda t, n=n: t.matmul(
            self.PSB[psb][:, 0:n], row[0:n, :], P["id_f"][0:n, 0:n],
            start=True, stop=True),
            deps=[t0] + self.tok_const + self.psb_free.get(psb, []))
        t2 = S.op("scalar", lambda s, n=n: s.activation(
            out=col[:], in_=self.PSB[psb][:, 0:n], func=ACT.Copy,
            bias=0.0, scale=1.0), deps=[t1])
        self.psb_free[psb] = [t2]
        self.lc_last = [t2]
        return t2, col

    def bcast_row_psum(self, psum_ap, row_ap, deps, bf16=False):
        ones = self.P["ones1_bf" if bf16 else "ones1_f"]
        return self.S.op("tensor", lambda t, o=psum_ap, r=row_ap, on=ones:
                         t.matmul(o, on[:], r, start=True, stop=True),
                         deps=deps)

    def collective_max(self, n, deps, name):
        S, P = self.S, self.P
        res = self.sbuf(f"ccres_{name}", [4, 16], F32)
        t1 = S.dma("sync", P["cc_in"].ap(), P["ccst"][:], deps=deps)
        tok_cc = S.op("gpsimd", lambda g: g.collective_compute(
            "AllReduce", ALU.max, replica_groups=[list(range(8))],
            ins=[P["cc_in"].ap().opt()], outs=[P["cc_out"].ap().opt()]),
            deps=[t1])
        t2 = S.dma("sync", res[:], P["cc_out"].ap(), deps=[tok_cc])
        self.tok_cc = [t2]
        return t2, res

    def build(self):
        self.p0_constants()
        self.p1_outer_ln()
        if self.stop == "p1":
            return self.done()
        self.p2_signnet()
        if self.stop == "p2":
            return self.done()
        self.p4_mamba()
        if self.stop == "p4":
            return self.done()
        self.p5_ffn()
        if self.stop == "p5":
            return self.done()
        self.p6_attn()
        if self.stop == "p6":
            return self.done()
        self.p7_moe()
        return self.done()

    def done(self):
        S = self.S
        if self.final_deps:
            S.op("sync", lambda e: e.nop(), deps=self.final_deps)
        with self.nc.Block() as block:
            S.finalize(block)
        return self.nc

    def p0_constants(self):
        S, P = self.S, self.P
        self.sbuf("ones1_bf", [1, 128], BF16)
        self.sbuf("ones1_f", [1, 128], F32)
        self.sbuf("onescol_f", [128, 1], F32)
        self.sbuf("onescol_b", [128, 1], BF16)
        self.sbuf("eps1", [1, 1], F32)
        self.sbuf("id_bf", [128, 128], BF16)
        self.sbuf("id_f", [128, 128], F32)
        self.sbuf("ccst", [4, 16], F32)
        c0 = S.op("vector", lambda v: v.memset(P["ones1_bf"][:], 1.0))
        c1 = S.op("vector", lambda v: v.memset(P["ones1_f"][:], 1.0))
        c2 = S.op("vector", lambda v: v.memset(P["onescol_f"][:], 1.0))
        c2b = S.op("vector", lambda v: v.memset(P["onescol_b"][:], 1.0))
        c3 = S.op("vector", lambda v: v.memset(P["eps1"][:], EPS))
        i0 = S.op("vector", lambda v: v.memset(self.scr[:, 0:128], 1.0))
        i1 = S.op("gpsimd", lambda g: g.affine_select(
            P["id_bf"][:], self.scr[:, 0:128], pattern=[[-1, 128]],
            compare_op=ALU.is_equal, fill=0.0, base=0, channel_multiplier=1),
            deps=[i0])
        i2 = S.op("gpsimd", lambda g: g.affine_select(
            P["id_f"][:], self.scr[:, 0:128], pattern=[[-1, 128]],
            compare_op=ALU.is_equal, fill=0.0, base=0, channel_multiplier=1),
            deps=[i0])
        self.tok_const = [c0, c1, c2, c2b, c3, i1, i2]

    def p1_outer_ln(self):
        S, P = self.S, self.P
        A = self.scr
        xfm = P["x_fm"].ap()
        xa0 = A[:, 0:2048]
        xsq = A[:, 2048:4096]
        prev_mm = []
        for k in range(8):
            tl = S.dma("sync", xa0, xfm[128 * k:128 * (k + 1), :],
                       deps=prev_mm + self.tok_const)
            tsq = S.op("scalar", lambda s: s.activation(
                out=A[:, 2048:4096], in_=A[:, 0:2048], func=ACT.Square,
                bias=0.0, scale=1.0), deps=[tl])
            prev_mm = []
            for tc in range(4):
                m1 = S.op("tensor", lambda t, tc=tc, k=k: t.matmul(
                    self.PSB[tc][0:1, :], P["onescol_f"][:],
                    A[:, 512 * tc:512 * (tc + 1)],
                    start=(k == 0), stop=(k == 7)), deps=[tl, tsq])
                m2 = S.op("tensor", lambda t, tc=tc, k=k: t.matmul(
                    self.PSB[4 + tc][0:1, :], P["onescol_f"][:],
                    A[:, 2048 + 512 * tc:2048 + 512 * (tc + 1)],
                    start=(k == 0), stop=(k == 7)), deps=[tsq])
                prev_mm += [m1, m2]
        mrow = A[0:1, 0:2048]
        vrow = A[0:1, 2048:4096]
        ev = []
        for tc in range(4):
            ev.append(S.op("scalar", lambda s, tc=tc: s.activation(
                out=A[0:1, 512 * tc:512 * (tc + 1)],
                in_=self.PSB[tc][0:1, :], func=ACT.Copy, bias=0.0,
                scale=1.0 / D), deps=prev_mm))
            ev.append(S.op("scalar", lambda s, tc=tc: s.activation(
                out=A[0:1, 2048 + 512 * tc:2048 + 512 * (tc + 1)],
                in_=self.PSB[4 + tc][0:1, :], func=ACT.Copy, bias=0.0,
                scale=1.0 / D), deps=prev_mm))
        for i in range(8):
            self.psb_free[i] = list(ev)

        rt1 = A[0:1, 4096:6144]
        rt2 = A[0:1, 6144:8192]
        rrow = A[0:1, 8192:10240]
        r1row = rt2
        o3 = S.op("vector", lambda v: v.tensor_tensor(
            A[0:1, 4096:6144], mrow, mrow, ALU.mult), deps=ev)
        o4 = S.op("vector", lambda v: v.tensor_sub(
            vrow, vrow, A[0:1, 4096:6144]), deps=[o3])

        def emit_rsqrt(dst, srcv, deps):
            a = S.op("scalar", lambda s: s.activation(
                out=rt1, in_=srcv, func=ACT.Sqrt,
                bias=P["eps1"][:, 0:1], scale=1.0), deps=deps + self.tok_const)
            y0 = S.op("vector", lambda v: v.reciprocal(rt2, rt1), deps=[a])
            b1 = S.op("vector", lambda v: v.tensor_scalar_add(
                rt1, srcv, EPS), deps=[y0])
            b2 = S.op("vector", lambda v: v.tensor_tensor(
                rt1, rt1, rt2, ALU.mult), deps=[b1])
            b3 = S.op("vector", lambda v: v.tensor_tensor(
                rt1, rt1, rt2, ALU.mult), deps=[b2])
            b4 = S.op("vector", lambda v: v.tensor_scalar(
                rt1, rt1, -0.5, 1.5, ALU.mult, ALU.add), deps=[b3])
            return S.op("vector", lambda v: v.tensor_tensor(
                dst, rt1, rt2, ALU.mult), deps=[b4])

        o6 = emit_rsqrt(rrow, vrow, [o4])
        o7 = S.op("vector", lambda v: v.tensor_tensor(
            A[0:1, 4096:6144], rrow, rrow, ALU.mult), deps=[o6])
        o8 = S.op("vector", lambda v: v.tensor_tensor(
            vrow, A[0:1, 4096:6144], vrow, ALU.mult), deps=[o7])
        o10 = emit_rsqrt(r1row, vrow, [o8])

        m_bc = A[:, 0:2048]
        r1_bc = A[:, 6144:8192]
        r_bc = A[:, 8192:10240]
        self.r1_bc = r1_bc
        bc_last = [o10]
        for row, dstb in [(mrow, m_bc), (r1row, r1_bc), (rrow, r_bc)]:
            for tc in range(4):
                mm = self.bcast_row_psum(
                    self.PSB[0][:, :], row[0:1, 512 * tc:512 * (tc + 1)],
                    deps=bc_last + self.tok_const + self.psb_free.get(0, []))
                cp = S.op("scalar", lambda s, dstb=dstb, tc=tc: s.activation(
                    out=dstb[:, 512 * tc:512 * (tc + 1)], in_=self.PSB[0][:],
                    func=ACT.Copy, bias=0.0, scale=1.0), deps=[mm])
                self.psb_free[0] = [cp]
                bc_last = [cp]

        xa = A[:, 2048:4096]
        xnf = A[:, 4096:6144]
        xn_bf = self.xnsl
        mxl = self.sbuf("mxl", [128, 8], F32)
        xn_toks = []
        red_toks = []
        prev = list(bc_last)
        for k in range(8):
            tl = S.dma("sync", xa, xfm[128 * k:128 * (k + 1), :], deps=prev)
            s1 = S.op("vector", lambda v: v.scalar_tensor_tensor(
                xnf, m_bc, -1.0, xa, ALU.mult, ALU.add), deps=[tl])
            s2 = S.op("vector", lambda v: v.tensor_tensor(
                xnf, xnf, r_bc, ALU.mult), deps=[s1])
            s3 = S.op("vector", lambda v, k=k: v.tensor_copy(
                xn_bf[:, T * k:T * (k + 1)], xnf), deps=[s2])
            red_toks.append(S.op("vector", lambda v, k=k: v.tensor_reduce(
                P["mxl"][:, k:k + 1], xnf, AXL.X, ALU.max,
                apply_absolute_value=True), deps=[s2]))
            prev = [s3, red_toks[-1]]
            xn_toks.append(s3)
        self.tok_xn = xn_toks
        mx1 = self.sbuf("mx1", [128, 1], F32)
        rt = S.op("vector", lambda v: v.tensor_reduce(
            P["mx1"][:], P["mxl"][:], AXL.X, ALU.max), deps=red_toks)
        mxrow = self.sbuf("mxrow", [1, 128], F32)
        tmm = S.op("tensor", lambda t: t.matmul(
            self.PSB[2][0:1, 0:128], P["mx1"][:], P["id_f"][:],
            start=True, stop=True),
            deps=[rt] + self.tok_const + self.psb_free.get(2, []))
        cpm = S.op("scalar", lambda s: s.activation(
            out=P["mxrow"][:], in_=self.PSB[2][0:1, 0:128], func=ACT.Copy,
            bias=0.0, scale=1.0), deps=[tmm])
        self.psb_free[2] = [cpm]
        mxall = self.sbuf("mxall", [1, 1], F32)
        pr = S.op("vector", lambda v: v.tensor_reduce(
            P["mxall"][:], P["mxrow"][:], AXL.X, ALU.max), deps=[cpm])
        z1 = S.op("vector", lambda v: v.memset(P["ccst"][:], 0.0), deps=[pr])
        z2 = S.op("vector", lambda v: v.tensor_copy(
            P["ccst"][0:1, 0:1], P["mxall"][:]), deps=[z1])
        self.dbg_out("mxl", P["mxl"][:], red_toks)
        self.dbg_out("mx1", P["mx1"][:], [rt])
        self.dbg_out("mxrow", P["mxrow"][:], [cpm])
        self.dbg_out("mxall", P["mxall"][:], [pr])
        self.dbg_out("ccsts1", P["ccst"][:], [z2])
        cct, ccres = self.collective_max(1, [z2], "s1")

        clipx = self.sbuf("clipx_sb", [1, 1], F32)
        tcl = S.dma("sync", clipx[:], P["w_clipx"].ap())
        q1 = S.op("scalar", lambda s: s.activation(
            out=P["clipx_sb"][:], in_=P["clipx_sb"][:], func=ACT.Sigmoid,
            bias=0.0, scale=1.0), deps=[tcl])
        s1s = self.sbuf("s1s", [1, 2], F32)
        q2 = S.op("vector", lambda v, ccres=ccres: v.tensor_tensor(
            P["s1s"][0:1, 0:1], P["clipx_sb"][:], ccres[0:1, 0:1], ALU.mult),
            deps=[q1, cct])
        q3 = S.op("vector", lambda v: v.tensor_scalar_mul(
            P["s1s"][0:1, 0:1], P["s1s"][0:1, 0:1], 1.0 / 127.0), deps=[q2])
        q4 = S.op("vector", lambda v: v.reciprocal(
            P["s1s"][0:1, 1:2], P["s1s"][0:1, 0:1]), deps=[q3])
        s1c = self.sbuf("s1c", [128, 2], F32)
        mm1 = S.op("tensor", lambda t: t.matmul(
            self.PSB[1][:, 0:2], P["ones1_f"][:], P["s1s"][:],
            start=True, stop=True),
            deps=[q4] + self.tok_const + self.psb_free.get(1, []))
        cp1 = S.op("scalar", lambda s: s.activation(
            out=P["s1c"][:], in_=self.PSB[1][:, 0:2], func=ACT.Copy,
            bias=0.0, scale=1.0), deps=[mm1])
        self.psb_free[1] = [cp1]
        self.tok_s1c = cp1

        xq_bf = self.abufC
        self.tok_xq_spill = []
        prevq = list(prev)
        for k in range(8):
            tl = S.dma("sync", xa, xfm[128 * k:128 * (k + 1), :], deps=prevq)
            u1 = S.op("vector", lambda v: v.scalar_tensor_tensor(
                xnf, m_bc, -1.0, xa, ALU.mult, ALU.add), deps=[tl])
            u2 = S.op("vector", lambda v: v.tensor_tensor(
                xnf, xnf, r_bc, ALU.mult), deps=[u1])
            a1 = S.op("vector", lambda v: v.tensor_scalar(
                xnf, xnf, P["s1c"][:, 1:2], RMAGIC, ALU.mult, ALU.add),
                deps=[u2, cp1])
            a2 = S.op("vector", lambda v: v.tensor_scalar(
                xnf, xnf, -RMAGIC, 127.0, ALU.add, ALU.min), deps=[a1])
            a3 = S.op("vector", lambda v, k=k: v.tensor_scalar(
                xq_bf[:, T * k:T * (k + 1)], xnf, -128.0, None, ALU.max),
                deps=[a2] + self.tok_xq_spill[-1:])
            sp = S.dma("sync", P["h_xq"].ap()[128 * k:128 * (k + 1), :],
                       xq_bf[:, T * k:T * (k + 1)], deps=[a3])
            self.tok_xq_spill.append(sp)
            prevq = [a3]

        self.dbg_out("xn", xn_bf[:, 0:T], [xn_toks[0]])
        self.dbg_out("s1", ccres[:], [cct])
        self.dbg_out("xq", xq_bf[:, 7 * T:8 * T], [self.tok_xq_spill[-1]])

    def p2_signnet(self):
        S, P = self.S, self.P
        A = self.scr
        r1_bc = self.r1_bc
        wld = A[:, 0:2048]
        s_bf = self.scrbf[:, 4096:6144]
        bb1row = A[0:1, 4096:6144]

        xxh_off = [0, 1024, 2048, 3072, 4096, 5120, 8192, 9216]

        def xxh_sl(dc, tc):
            o = xxh_off[dc] + 512 * tc
            return A[:, o:o + 512]

        t_g1, _ = self.load_cols("w_ln1g", 8)
        t_b1l, _ = self.load_cols("w_ln1b", 8)
        t_b2, _ = self.load_cols("w_b2", 8)
        t_b1s, _ = self.load_cols("w_b1", 16)
        lnb_bf = self.sbuf("lnb_bf", [128, 8], BF16)
        t_lnbbf = S.op("vector", lambda v: v.tensor_copy(
            P["lnb_bf"][:], P["w_ln1b_c"][:]), deps=[t_b1l])

        w1g = self.wbufA
        w2s = self.wbufB
        bb_mm = []
        prevk = self.tok_xq_spill[-1:]
        for k in range(8):
            tl = S.dma("sync", wld, P["w_W1T"].ap()[128 * k:128 * (k + 1), :],
                       deps=prevk)
            sg = S.op("scalar", lambda s: s.activation(
                out=s_bf, in_=wld, func=ACT.Sign, bias=0.0, scale=1.0),
                deps=[tl])
            bb_k = []
            for tc in range(4):
                bb_k.append(S.op("tensor", lambda t, tc=tc, k=k: t.matmul(
                    self.PSB[4 + tc][0:1, :], P["lnb_bf"][:, k:k + 1],
                    s_bf[:, 512 * tc:512 * (tc + 1)],
                    start=(k == 0), stop=(k == 7)),
                    deps=[sg, t_lnbbf] + self.psb_free.get(4 + tc, [])))
            g1 = S.op("vector", lambda v, k=k: v.tensor_scalar_mul(
                w1g[:, DI * k:DI * (k + 1)], s_bf,
                P["w_ln1g_c"][:, k:k + 1]), deps=[sg, t_g1])
            bb_mm += bb_k
            prevk = [g1] + bb_k
        evb = []
        for tc in range(4):
            evb.append(S.op("scalar", lambda s, tc=tc: s.activation(
                out=bb1row[0:1, 512 * tc:512 * (tc + 1)],
                in_=self.PSB[4 + tc][0:1, :], func=ACT.Copy, bias=0.0,
                scale=1.0), deps=bb_mm))
        for i in range(4, 8):
            self.psb_free[i] = list(evb)
        bb1c = self.sbuf("bb1c", [128, 16], F32)
        prev_c = list(evb) + self.psb_free.get(4, [])
        for ic in range(16):
            mm = S.op("tensor", lambda t, ic=ic: t.matmul(
                self.PSB[4][:, ic:ic + 1],
                bb1row[0:1, 128 * ic:128 * (ic + 1)], P["ones1_f"][0:1, 0:1],
                start=True, stop=True), deps=prev_c + self.tok_const)
            prev_c = [mm]
        cpb0 = S.op("scalar", lambda s: s.activation(
            out=P["bb1c"][:], in_=self.PSB[4][:, 0:16], func=ACT.Copy,
            bias=0.0, scale=1.0), deps=prev_c)
        self.psb_free[4] = [cpb0]
        cpb = S.op("vector", lambda v: v.tensor_add(
            P["bb1c"][:], P["bb1c"][:], P["w_b1_c"][:]), deps=[cpb0, t_b1s])

        w2toks = []
        for ic in range(16):
            tl = S.dma("sync", wld[:, 0:1024],
                       P["w_W2T"].ap()[128 * ic:128 * (ic + 1), :], deps=prevk)
            sg = S.op("scalar", lambda s, ic=ic: s.activation(
                out=w2s[:, D * ic:D * (ic + 1)], in_=wld[:, 0:1024],
                func=ACT.Sign, bias=0.0, scale=1.0), deps=[tl])
            w2toks.append(sg)
            prevk = [sg]

        xn_bf = self.xnsl
        gated_bf = self.abufA
        qx_bf = self.abufC
        gmx = self.sbuf("gmx", [128, 2], F32)
        gmxt = self.sbuf("gmxt", [128, 2], F32)
        grow = self.sbuf("grow", [2, 128], F32)
        grow2 = self.sbuf("grow2", [2, 1], F32)
        gam_sc = self.sbuf("gam_sc", [1, 2], F32)
        gam_c = self.sbuf("gam_c", [128, 2], F32)
        gtmp = self.PSB[5]
        self.tok_qx = []
        w1deps = [S.op("sync", lambda e: e.nop(),
                       deps=prevk + [cpb] + self.tok_xq_spill)]
        prev_gmm = []
        prev_xmm = list(w1deps)
        g_toks = []
        xx_toks = []
        for h in range(2):
            hoff = TH * h
            g_toks = []
            for ic in range(16):
                for tc in range(2):
                    mms = []
                    for k in range(8):
                        mms.append(S.op("tensor",
                            lambda t, ic=ic, tc=tc, k=k, hoff=hoff: t.matmul(
                                self.PSB[tc][:, :],
                                w1g[:, DI * k + 128 * ic:DI * k + 128 * (ic + 1)],
                                xn_bf[:, T * k + hoff + 512 * tc:
                                      T * k + hoff + 512 * (tc + 1)],
                                start=(k == 0), stop=(k == 7)),
                            deps=(w1deps if (ic == 0 and tc == 0) else []) +
                                 self.psb_free.get(tc, []) + self.tok_xn))
                    e1 = S.op("vector", lambda v, tc=tc, hoff=hoff:
                              v.tensor_tensor(
                                  gtmp, self.PSB[tc][:, :],
                                  r1_bc[:, hoff + 512 * tc:hoff + 512 * (tc + 1)],
                                  ALU.mult),
                              deps=mms + self.psb_free.get(5, []))
                    self.psb_free[tc] = [e1]
                    e2 = S.op("vector", lambda v, ic=ic: v.tensor_scalar(
                        gtmp, gtmp, P["bb1c"][:, ic:ic + 1], 1.0,
                        ALU.add, ALU.min), deps=[e2_dep for e2_dep in (e1, cpb)])
                    e3 = S.op("vector", lambda v, ic=ic, tc=tc: v.tensor_scalar(
                        gated_bf[:, 1024 * ic + 512 * tc:
                                 1024 * ic + 512 * (tc + 1)],
                        gtmp, -1.0, None, ALU.max), deps=[e2] + prev_gmm)
                    self.psb_free[5] = [e3]
                    g_toks.append(e3)
            prev_gmm = []
            xx_toks = []
            red_toks = []
            for dc in range(8):
                for tc in range(2):
                    mms = []
                    for ic in range(16):
                        mms.append(S.op("tensor",
                            lambda t, dc=dc, tc=tc, ic=ic: t.matmul(
                                self.PSB[2 + tc][:, :],
                                w2s[:, D * ic + 128 * dc:D * ic + 128 * (dc + 1)],
                                gated_bf[:, 1024 * ic + 512 * tc:
                                         1024 * ic + 512 * (tc + 1)],
                                start=(ic == 0), stop=(ic == 15)),
                            deps=g_toks + w2toks + self.psb_free.get(2 + tc, [])))
                    sl = xxh_sl(dc, tc)
                    e1 = S.op("vector", lambda v, tc=tc, dc=dc, sl=sl:
                              v.tensor_scalar_add(
                                  sl, self.PSB[2 + tc][:, :],
                                  P["w_b2_c"][:, dc:dc + 1]),
                              deps=mms + [t_b2] + prev_xmm)
                    self.psb_free[2 + tc] = [e1]
                    e2 = S.op("vector", lambda v, dc=dc, tc=tc, hoff=hoff, sl=sl:
                              v.tensor_tensor(
                                  sl, sl,
                                  xn_bf[:, T * dc + hoff + 512 * tc:
                                        T * dc + hoff + 512 * (tc + 1)],
                                  ALU.add), deps=[e1])
                    xx_toks.append(e2)
                    r_ = S.op("vector", lambda v, tc=tc, sl=sl: v.tensor_reduce(
                        P["gmxt"][:, tc:tc + 1], sl, AXL.X, ALU.max,
                        apply_absolute_value=True),
                        deps=[e2] + self.tok_dbg)
                    r2_ = S.op("vector", lambda v, dc=dc, tc=tc: (
                        v.tensor_copy(P["gmx"][:, tc:tc + 1],
                                      P["gmxt"][:, tc:tc + 1]) if dc == 0 else
                        v.tensor_max(P["gmx"][:, tc:tc + 1],
                                     P["gmx"][:, tc:tc + 1],
                                     P["gmxt"][:, tc:tc + 1])), deps=[r_])
                    red_toks.append(r2_)
            prev_xmm = []
            self.dbg_out(f"gmx_{h}", P["gmx"][:, 0:2], red_toks)
            tmm = S.op("tensor", lambda t: t.matmul(
                self.PSB[4][0:2, 0:128], P["gmx"][:, 0:2], P["id_f"][:],
                start=True, stop=True),
                deps=red_toks + self.psb_free.get(4, []) + self.tok_const)
            cpg = S.op("scalar", lambda s: s.activation(
                out=P["grow"][:], in_=self.PSB[4][0:2, 0:128], func=ACT.Copy,
                bias=0.0, scale=1.0), deps=[tmm] + self.tok_dbg)
            self.psb_free[4] = [cpg]
            self.dbg_out(f"grow_{h}", P["grow"][:], [cpg])
            rg = S.op("vector", lambda v: v.tensor_reduce(
                P["grow2"][:], P["grow"][:], AXL.X, ALU.max),
                deps=[cpg] + self.tok_dbg)
            z1 = S.op("vector", lambda v: v.memset(P["ccst"][:], 0.0),
                      deps=[rg] + self.tok_cc + self.tok_dbg)
            z2 = S.op("vector", lambda v: v.tensor_copy(
                P["ccst"][0:2, 0:1], P["grow2"][:]), deps=[z1])
            self.dbg_out(f"grow2_{h}", P["grow2"][:], [rg])
            self.dbg_out(f"ccstg_{h}", P["ccst"][:], [z2])
            cct, ccg = self.collective_max(2, [z2], f"gam{h}")
            self.dbg_out(f"ccg_{h}", ccg[:], [cct])
            s1_ = S.op("tensor", lambda t, ccg=ccg: t.matmul(
                self.PSB[4][0:1, 0:2], ccg[0:2, 0:1], P["id_f"][0:2, 0:2],
                start=True, stop=True), deps=[cct] + self.psb_free.get(4, []))
            s2_ = S.op("scalar", lambda s: s.activation(
                out=P["gam_sc"][:], in_=self.PSB[4][0:1, 0:2], func=ACT.Copy,
                bias=EPS, scale=1.0), deps=[s1_])
            self.psb_free[4] = [s2_]
            s3_ = S.op("vector", lambda v: v.reciprocal(
                P["gam_sc"][:], P["gam_sc"][:]), deps=[s2_])
            s4_ = S.op("vector", lambda v: v.tensor_scalar_mul(
                P["gam_sc"][:], P["gam_sc"][:], 128.0), deps=[s3_])
            s5_ = S.op("tensor", lambda t: t.matmul(
                self.PSB[4][:, 0:2], P["ones1_f"][:], P["gam_sc"][:],
                start=True, stop=True), deps=[s4_] + self.psb_free.get(4, []))
            s6_ = S.op("scalar", lambda s: s.activation(
                out=P["gam_c"][:], in_=self.PSB[4][:, 0:2], func=ACT.Copy,
                bias=0.0, scale=1.0), deps=[s5_])
            self.psb_free[4] = [s6_]
            for dc in range(8):
                for tc in range(2):
                    sl = xxh_sl(dc, tc)
                    q1 = S.op("vector", lambda v, sl=sl, tc=tc: v.tensor_scalar(
                        gtmp, sl, P["gam_c"][:, tc:tc + 1], QBE,
                        ALU.mult, ALU.min),
                        deps=[s6_] + self.psb_free.get(5, []))
                    q2 = S.op("vector", lambda v: v.tensor_scalar(
                        gtmp, gtmp, -QBE, None, ALU.max), deps=[q1])
                    qoff = T * dc + hoff + 512 * tc
                    q3 = S.op("vector", lambda v, qoff=qoff: v.tensor_copy(
                        qx_bf[:, qoff:qoff + 512], gtmp), deps=[q2])
                    q4 = S.op("scalar", lambda s, sl=sl: s.activation(
                        out=sl, in_=gtmp, func=ACT.Copy, bias=0.0, scale=1.0),
                        deps=[q2, q3])
                    q5 = S.dma("sync",
                               P["h_qxf"].ap()[128 * dc:128 * (dc + 1),
                                               hoff + 512 * tc:
                                               hoff + 512 * (tc + 1)],
                               sl, deps=[q4])
                    self.psb_free[5] = [q4]
                    prev_xmm.append(q5)
                    prev_gmm.append(q3)
                    self.tok_qx.append(q3)
                    self.tok_qx.append(q5)

        self.dbg_out("gated", gated_bf[:, 0:1024], g_toks[0:1])
        self.dbg_out("xxh", A[:, 0:1024], xx_toks)
        self.dbg_out("qx", qx_bf[:, 0:T], self.tok_qx)
        self.dbg_out("gamc", gam_c[:, 0:2], self.tok_qx)

    def p4_mamba(self):
        S, P = self.S, self.P
        A = self.scr
        Abf = self.scrbf
        qx_bf = self.abufC
        xs_bf = self.abufA
        yg_bf = self.abufA
        wA = self.wbufA
        wB = self.wbufB

        t_g2, _ = self.load_cols("w_ln2g", 8)
        t_b2l, _ = self.load_cols("w_ln2b", 8)
        t_dtb, _ = self.load_cols("w_dtb", 16)
        t_cvb, _ = self.load_cols("w_convB", 16)
        t_dp, _ = self.load_cols("w_Dp", 16)
        lnb2_bf = self.sbuf("lnb2_bf", [128, 8], BF16)
        t_lnb2 = S.op("vector", lambda v: v.tensor_copy(
            P["lnb2_bf"][:], P["w_ln2b_c"][:]), deps=[t_b2l])
        prev_mm = []
        sq_bf = Abf[:, 0:2048]
        for k in range(8):
            tsq = S.op("scalar", lambda s, k=k: s.activation(
                out=sq_bf, in_=qx_bf[:, T * k:T * (k + 1)], func=ACT.Square,
                bias=0.0, scale=1.0), deps=self.tok_qx + prev_mm)
            prev_mm = []
            for tc in range(4):
                m1 = S.op("tensor", lambda t, tc=tc, k=k: t.matmul(
                    self.PSB[tc][0:1, :], P["onescol_b"][:],
                    qx_bf[:, T * k + 512 * tc:T * k + 512 * (tc + 1)],
                    start=(k == 0), stop=(k == 7)),
                    deps=self.tok_qx + self.psb_free.get(tc, []) + self.tok_const)
                m2 = S.op("tensor", lambda t, tc=tc, k=k: t.matmul(
                    self.PSB[4 + tc][0:1, :], P["onescol_b"][:],
                    sq_bf[:, 512 * tc:512 * (tc + 1)],
                    start=(k == 0), stop=(k == 7)),
                    deps=[tsq] + self.psb_free.get(4 + tc, []))
                prev_mm += [m1, m2]
        m2row = A[0:1, 2048:4096]
        v2row = A[0:1, 4096:6144]
        ev = []
        for tc in range(4):
            ev.append(S.op("scalar", lambda s, tc=tc: s.activation(
                out=m2row[0:1, 512 * tc:512 * (tc + 1)],
                in_=self.PSB[tc][0:1, :], func=ACT.Copy, bias=0.0,
                scale=1.0 / D), deps=prev_mm))
            ev.append(S.op("scalar", lambda s, tc=tc: s.activation(
                out=v2row[0:1, 512 * tc:512 * (tc + 1)],
                in_=self.PSB[4 + tc][0:1, :], func=ACT.Copy, bias=0.0,
                scale=1.0 / D), deps=prev_mm))
        for i in range(8):
            self.psb_free[i] = list(ev)
        rt1 = A[0:1, 6144:8192]
        rt2 = A[0:1, 0:2048]
        o3 = S.op("vector", lambda v: v.tensor_tensor(
            rt1, m2row, m2row, ALU.mult), deps=ev)
        o4 = S.op("vector", lambda v: v.tensor_sub(v2row, v2row, rt1),
                  deps=[o3])
        a_ = S.op("scalar", lambda s: s.activation(
            out=rt1, in_=v2row, func=ACT.Sqrt, bias=P["eps1"][:, 0:1],
            scale=1.0), deps=[o4] + self.tok_const)
        y0 = S.op("vector", lambda v: v.reciprocal(rt2, rt1), deps=[a_])
        b1 = S.op("vector", lambda v: v.tensor_scalar_add(rt1, v2row, EPS),
                  deps=[y0])
        b2 = S.op("vector", lambda v: v.tensor_tensor(rt1, rt1, rt2, ALU.mult),
                  deps=[b1])
        b3 = S.op("vector", lambda v: v.tensor_tensor(rt1, rt1, rt2, ALU.mult),
                  deps=[b2])
        b4 = S.op("vector", lambda v: v.tensor_scalar(
            rt1, rt1, -0.5, 1.5, ALU.mult, ALU.add), deps=[b3])
        r2row_ = S.op("vector", lambda v: v.tensor_tensor(
            v2row, rt1, rt2, ALU.mult), deps=[b4])
        r2row = v2row
        r2_bc = A[:, 7808:9856]
        bc_last = [r2row_]
        for tc in range(4):
            mm = self.bcast_row_psum(
                self.PSB[0][:, :], r2row[0:1, 512 * tc:512 * (tc + 1)],
                deps=bc_last + self.tok_const + self.psb_free.get(0, []))
            cp = S.op("scalar", lambda s, tc=tc: s.activation(
                out=r2_bc[:, 512 * tc:512 * (tc + 1)], in_=self.PSB[0][:],
                func=ACT.Copy, bias=0.0, scale=1.0), deps=[mm])
            self.psb_free[0] = [cp]
            bc_last = [cp]

        wld4 = A[:, 0:4096]
        wi_bf = Abf[:, 8192:12288]
        xnf32 = self.xnsl.bitcast(F32)
        crow = xnf32[0:1, 0:4096]
        bbrow = xnf32[0:1, 4096:8192]
        prevk = list(bc_last) + [o3]
        c_mm = []
        wspill = []
        for k in range(8):
            tl = S.dma("sync", wld4, P["w_inpT"].ap()[128 * k:128 * (k + 1), :],
                       deps=prevk)
            sc1 = S.op("vector", lambda v, k=k: v.tensor_scalar_mul(
                wi_bf, wld4, P["w_ln2g_c"][:, k:k + 1]), deps=[tl, t_g2])
            mm_k = []
            for tc in range(8):
                mm_k.append(S.op("tensor", lambda t, tc=tc, k=k: t.matmul(
                    self.PSB[tc][0:1, :], P["onescol_b"][:],
                    wi_bf[:, 512 * tc:512 * (tc + 1)],
                    start=(k == 0), stop=(k == 7)),
                    deps=[sc1] + self.tok_const +
                         (self.psb_free.get(tc, []) if k == 0 else [])))
            sp = S.dma("sync", P["h_winp"].ap()[128 * k:128 * (k + 1), :],
                       wi_bf, deps=mm_k)
            c_mm += mm_k
            wspill.append(sp)
            prevk = [sp]
        evc = []
        for tc in range(8):
            evc.append(S.op("scalar", lambda s, tc=tc, crow=crow: s.activation(
                out=crow[0:1, 512 * tc:512 * (tc + 1)],
                in_=self.PSB[tc][0:1, :], func=ACT.Copy, bias=0.0,
                scale=1.0), deps=c_mm))
        for i in range(8):
            self.psb_free[i] = list(evc)
        raw_bf = Abf[:, 12288:14336]
        bb_mm = []
        prevk2 = list(evc)
        for k in range(8):
            tl = S.dma("sync", wld4, P["w_inpT"].ap()[128 * k:128 * (k + 1), :],
                       deps=prevk2)
            mm_k = []
            for kh in range(2):
                cr = S.op("vector", lambda v, kh=kh: v.tensor_copy(
                    raw_bf, wld4[:, 2048 * kh:2048 * (kh + 1)]),
                    deps=[tl] + mm_k[-4:])
                for tc in range(4):
                    mm_k.append(S.op("tensor", lambda t, tc=tc, k=k, kh=kh:
                        t.matmul(
                            self.PSB[4 * kh + tc][0:1, :],
                            P["lnb2_bf"][:, k:k + 1],
                            raw_bf[:, 512 * tc:512 * (tc + 1)],
                            start=(k == 0), stop=(k == 7)),
                        deps=[cr, t_lnb2] +
                             (self.psb_free.get(4 * kh + tc, [])
                              if k == 0 else [])))
            bb_mm += mm_k
            prevk2 = [mm_k[-1], tl]
        evb = []
        for tc in range(8):
            evb.append(S.op("scalar", lambda s, tc=tc, bbrow=bbrow: s.activation(
                out=bbrow[0:1, 512 * tc:512 * (tc + 1)],
                in_=self.PSB[tc][0:1, :], func=ACT.Copy, bias=0.0,
                scale=1.0), deps=bb_mm))
        for i in range(8):
            self.psb_free[i] = list(evb)
        bb2c = self.sbuf("bb2c", [128, 32], F32)
        prev_c = list(evb)
        for jc in range(32):
            mm = S.op("tensor", lambda t, jc=jc, bbrow=bbrow: t.matmul(
                self.PSB[4][:, jc:jc + 1],
                bbrow[0:1, 128 * jc:128 * (jc + 1)], P["ones1_f"][0:1, 0:1],
                start=True, stop=True), deps=prev_c + self.tok_const)
            prev_c = [mm]
        cpb2 = S.op("scalar", lambda s: s.activation(
            out=P["bb2c"][:], in_=self.PSB[4][:, 0:32], func=ACT.Copy,
            bias=0.0, scale=1.0), deps=prev_c)
        self.psb_free[4] = [cpb2]
        wfold = []
        prevk3 = [cpb2]
        for k in range(8):
            tlw = S.dma("sync", wi_bf, P["h_winp"].ap()[128 * k:128 * (k + 1), :],
                        deps=prevk3 + wspill)
            ftoks = []
            for tc in range(8):
                mm = self.bcast_row_psum(
                    self.PSB[5][:, :], crow[0:1, 512 * tc:512 * (tc + 1)],
                    deps=[tlw] + self.tok_const + evc +
                         self.psb_free.get(5, []))
                f1 = S.op("vector", lambda v, tc=tc: v.scalar_tensor_tensor(
                    wi_bf[:, 512 * tc:512 * (tc + 1)], self.PSB[5][:, :],
                    -1.0 / D, wi_bf[:, 512 * tc:512 * (tc + 1)],
                    ALU.mult, ALU.add), deps=[mm])
                self.psb_free[5] = [f1]
                ftoks.append(f1)
            sp = S.dma("sync", P["h_winp"].ap()[128 * k:128 * (k + 1), :],
                       wi_bf, deps=ftoks)
            wfold.append(sp)
            prevk3 = [sp]

        self.dint("h_woutbf", [DI, D], BF16)
        wobf = []
        prevk2 = list(prevk)
        for k in range(16):
            tl = S.dma("sync", wld4[:, 0:1024],
                       P["w_outpT"].ap()[128 * k:128 * (k + 1), :],
                       deps=prevk2 + evc)
            cv = S.op("vector", lambda v: v.tensor_copy(
                Abf[:, 4096:5120], wld4[:, 0:1024]), deps=[tl])
            sp = S.dma("sync", P["h_woutbf"].ap()[128 * k:128 * (k + 1), :],
                       Abf[:, 4096:5120], deps=[cv])
            wobf.append(sp)
            prevk2 = [sp, cv]

        xnbf = self.xnsl
        xnf32v = self.xnsl.bitcast(F32)
        xpj = xnbf[:, 0:2048]
        dtpj = xnbf[0:64, 2048:4096]
        eoh = xnbf[0:16, 4096:6144]
        dtin_bf = xnbf[0:64, 6144:6656]
        Bbf = xnbf[0:16, 6656:7168]
        Cbf = xnbf[0:16, 7168:7680]
        carry = xnf32v[:, 3840:4096]
        ctx = xnf32v[:, 4096:4144]
        negA = xnf32v[:, 4144:4400]
        convw = xnf32v[:, 4400:4464]
        prep_done = wfold + [cpb2]
        cw_toks = []
        for dt_ in range(16):
            cw_toks.append(S.dma(
                "sync", convw[:, 4 * dt_:4 * (dt_ + 1)],
                P["w_convW"].ap()[128 * dt_:128 * (dt_ + 1), :],
                deps=prep_done))
        na_toks = []
        for dt_ in range(16):
            tl = S.dma("sync", negA[:, 16 * dt_:16 * (dt_ + 1)],
                       P["w_Alog"].ap()[128 * dt_:128 * (dt_ + 1), :],
                       deps=prep_done)
            t1 = S.op("scalar", lambda s, dt_=dt_: s.activation(
                out=negA[:, 16 * dt_:16 * (dt_ + 1)],
                in_=negA[:, 16 * dt_:16 * (dt_ + 1)], func=ACT.Exp,
                bias=0.0, scale=1.0), deps=[tl])
            na_toks.append(S.op("vector", lambda v, dt_=dt_: v.tensor_scalar_mul(
                negA[:, 16 * dt_:16 * (dt_ + 1)],
                negA[:, 16 * dt_:16 * (dt_ + 1)], -1.0), deps=[t1]))
        xpj_toks = []
        for dt_ in range(16):
            tl = S.dma("sync", A[:, 0:128],
                       P["w_xprojT"].ap()[128 * dt_:128 * (dt_ + 1), :],
                       deps=prep_done + xpj_toks[-1:])
            xpj_toks.append(S.op("vector", lambda v, dt_=dt_: v.tensor_copy(
                xpj[:, 128 * dt_:128 * (dt_ + 1)], A[:, 0:128]), deps=[tl]))
        tl = S.dma("sync", A[0:64, 128:2176], P["w_dtprojT"].ap(),
                   deps=prep_done + xpj_toks[-1:])
        dtpj_tok = S.op("vector", lambda v: v.tensor_copy(
            dtpj[:], A[0:64, 128:2176]), deps=[tl])
        e0 = S.op("vector", lambda v: v.memset(A[0:16, 0:2048], 1.0),
                  deps=xpj_toks + [dtpj_tok])
        e1 = S.op("gpsimd", lambda g: g.affine_select(
            eoh[:], A[0:16, 0:2048], pattern=[[1, 16], [0, 128]],
            compare_op=ALU.is_equal, fill=0.0, base=0, channel_multiplier=-1),
            deps=[e0])
        eoh_tok = e1

        xc = A[:, 0:640]
        dAb = [A[:, 640:1152], A[:, 1152:1664]]
        vb = [A[:, 1664:2176], A[:, 2176:2688]]
        hb = A[:, 2688:3200]
        dtb_ = A[:, 3200:3712]
        dtxb = A[:, 3712:4224]
        yb = A[:, 4224:4736]
        ztb = A[:, 4736:5248]
        gb = A[:, 5248:5760]
        pint = A[:, 5760:7808]
        ctx0 = S.op("vector", lambda v: v.memset(ctx[:, :], 0.0),
                    deps=prep_done)
        prev_ch = [ctx0] + wfold + wobf + [eoh_tok, dtpj_tok] + na_toks +             cw_toks + [t_dtb, t_cvb, t_dp]
        tok_mamba_out = []
        for ch in range(4):
            co = TC * ch
            wl = [S.dma("sync", wA[:, 2048 * k:2048 * (k + 1)],
                        P["h_winp"].ap()[128 * k:128 * (k + 1), 0:2048],
                        deps=prev_ch) for k in range(8)]
            xs_toks = []
            xcs = [xc, A[:, 640:1280]]
            cts = [dtxb, A[:, 1664:2176]]
            IPB = [0, 5]
            for dp in range(8):
                mm2 = {}
                xce = {}
                cte = {}
                toks = {}
                for j in (0, 1):
                    dt_ = 2 * dp + j
                    mms = []
                    for k in range(8):
                        mms.append(S.op("tensor",
                            lambda t, dt_=dt_, k=k, co=co, pb_=IPB[j]:
                            t.matmul(
                                self.PSB[pb_][:, :],
                                wA[:, 2048 * k + 128 * dt_:
                                   2048 * k + 128 * (dt_ + 1)],
                                qx_bf[:, T * k + co:T * k + co + TC],
                                start=(k == 0), stop=(k == 7)),
                            deps=wl + self.psb_free.get(IPB[j], [])))
                    mm2[j] = mms
                    xce[j] = xcs[j]
                    cte[j] = cts[j]
                for j in (0, 1):
                    toks[j] = S.op("vector",
                        lambda v, co=co, xcx=xce[j], pb_=IPB[j]:
                        v.tensor_tensor(
                            xcx[:, 3:515], self.PSB[pb_][:, :],
                            r2_bc[:, co:co + TC], ALU.mult),
                        deps=mm2[j] + bc_last)
                    self.psb_free[IPB[j]] = [toks[j]]
                for j in (0, 1):
                    toks[j] = S.op("vector",
                        lambda v, dt_=2 * dp + j, xcx=xce[j]:
                        v.tensor_scalar_add(
                            xcx[:, 3:515], xcx[:, 3:515],
                            P["bb2c"][:, dt_:dt_ + 1]),
                        deps=[toks[j], cpb2])
                for j in (0, 1):
                    toks[j] = S.op("vector",
                        lambda v, dt_=2 * dp + j, xcx=xce[j]: v.tensor_copy(
                            xcx[:, 0:3], ctx[:, 3 * dt_:3 * (dt_ + 1)]),
                        deps=[toks[j]])
                for kk in range(4):
                    for j in (0, 1):
                        dt_ = 2 * dp + j
                        if kk == 0:
                            toks[j] = S.op("vector",
                                lambda v, dt_=dt_, xcx=xce[j], ctm=cte[j]:
                                v.tensor_scalar_mul(
                                    ctm, xcx[:, 0:512],
                                    convw[:, 4 * dt_:4 * dt_ + 1]),
                                deps=[toks[j]])
                        else:
                            toks[j] = S.op("vector",
                                lambda v, dt_=dt_, kk=kk, xcx=xce[j],
                                ctm=cte[j]:
                                v.scalar_tensor_tensor(
                                    ctm, xcx[:, kk:512 + kk],
                                    convw[:, 4 * dt_ + kk:4 * dt_ + kk + 1],
                                    ctm, ALU.mult, ALU.add),
                                deps=[toks[j]])
                for j in (0, 1):
                    dt_ = 2 * dp + j
                    cx_t = S.op("vector",
                        lambda v, dt_=dt_, xcx=xce[j]: v.tensor_copy(
                            ctx[:, 3 * dt_:3 * (dt_ + 1)], xcx[:, 512:515]),
                        deps=[toks[j]])
                    sl = S.op("scalar",
                        lambda s, dt_=dt_, ctm=cte[j]: s.activation(
                            out=xs_bf[:, 512 * dt_:512 * (dt_ + 1)],
                            in_=ctm, func=ACT.Silu,
                            bias=P["w_convB_c"][:, dt_:dt_ + 1],
                            scale=1.0),
                        deps=[toks[j], t_cvb] + tok_mamba_out)
                    xs_toks.append(sl)
                    prev_ch = [cx_t, sl]
            xd_mm = []
            for dt_ in range(16):
                xd_mm.append(S.op("tensor", lambda t, dt_=dt_: t.matmul(
                    self.PSB[1][:, :], xpj[:, 128 * dt_:128 * (dt_ + 1)],
                    xs_bf[:, 512 * dt_:512 * (dt_ + 1)],
                    start=(dt_ == 0), stop=(dt_ == 15)),
                    deps=xs_toks + xpj_toks + self.psb_free.get(1, [])))
            exd1 = S.op("scalar", lambda s: s.activation(
                out=dtin_bf[:, :], in_=self.PSB[1][0:64, :], func=ACT.Copy,
                bias=0.0, scale=1.0), deps=xd_mm)
            exd2 = S.op("scalar", lambda s: s.activation(
                out=Bbf[:, :], in_=self.PSB[1][64:80, :], func=ACT.Copy,
                bias=0.0, scale=1.0), deps=xd_mm)
            exd3 = S.op("scalar", lambda s: s.activation(
                out=Cbf[:, :], in_=self.PSB[1][96:112, :], func=ACT.Copy,
                bias=0.0, scale=1.0), deps=xd_mm)
            self.psb_free[1] = [exd1, exd2, exd3]
            wlz = [S.dma("sync", wA[:, 2048 * k:2048 * (k + 1)],
                         P["h_winp"].ap()[128 * k:128 * (k + 1), 2048:4096],
                         deps=xs_toks + xd_mm) for k in range(8)]
            for dt_ in range(16):
                dmm = S.op("tensor", lambda t, dt_=dt_: t.matmul(
                    self.PSB[2][:, :], dtpj[:, 128 * dt_:128 * (dt_ + 1)],
                    dtin_bf[:, :], start=True, stop=True),
                    deps=[exd1, dtpj_tok] + self.psb_free.get(2, []))
                d0 = S.op("vector", lambda v, dt_=dt_: v.tensor_scalar_add(
                    ztb, self.PSB[2][:, :], P["w_dtb_c"][:, dt_:dt_ + 1]),
                    deps=[dmm, t_dtb])
                self.psb_free[2] = [d0]
                d1 = S.op("scalar", lambda s: s.activation(
                    out=gb, in_=ztb, func=ACT.Abs, bias=0.0, scale=1.0),
                    deps=[d0])
                d2 = S.op("scalar", lambda s: s.activation(
                    out=gb, in_=gb, func=ACT.Exp, bias=0.0, scale=-1.0),
                    deps=[d1])
                d3 = S.op("scalar", lambda s: s.activation(
                    out=gb, in_=gb, func=ACT.Ln, bias=1.0, scale=1.0),
                    deps=[d2])
                d4 = S.op("vector", lambda v: v.tensor_scalar_max(
                    dtb_, ztb, 0.0), deps=[d1])
                dact = S.op("vector", lambda v: v.tensor_add(
                    dtb_, dtb_, gb), deps=[d3, d4])
                dtx = S.op("vector", lambda v, dt_=dt_: v.tensor_tensor(
                    dtxb, dtb_, xs_bf[:, 512 * dt_:512 * (dt_ + 1)],
                    ALU.mult), deps=[dact] + xs_toks)
                ytoks = []
                pptoks = []
                BMB = [3, 1]
                CMB = [4, 7]
                hbx = [hb, xc[:, 0:512]]
                for sp in range(8):
                    das = {}
                    vvs = {}
                    scs = {}
                    for pb in (0, 1):
                        s_ = 2 * sp + pb
                        das[pb] = S.op("scalar",
                            lambda s, s_=s_, dt_=dt_, dAx=dAb[pb]:
                            s.activation(
                                out=dAx, in_=dtb_, func=ACT.Exp,
                                scale=negA[:, 16 * dt_ + s_:
                                           16 * dt_ + s_ + 1],
                                bias=0.0),
                            deps=[dact] + na_toks + pptoks[-2:])
                    for pb in (0, 1):
                        s_ = 2 * sp + pb
                        bb_ = BMB[pb]
                        bm = S.op("tensor", lambda t, s_=s_, bb_=bb_:
                            t.matmul(
                                self.PSB[bb_][:, :],
                                eoh[:, 128 * s_:128 * (s_ + 1)],
                                Bbf[:, :], start=True, stop=True),
                            deps=[exd2, eoh_tok] + self.psb_free.get(bb_, []))
                        vvs[pb] = S.op("vector",
                            lambda v, vx=vb[pb], bb_=bb_: v.tensor_tensor(
                                vx, dtxb, self.PSB[bb_][:, :], ALU.mult),
                            deps=[dtx, bm])
                        self.psb_free[bb_] = [vvs[pb]]
                    for pb in (0, 1):
                        s_ = 2 * sp + pb
                        init = 0.0 if ch == 0 else carry[:, 16 * dt_ + s_:
                                                         16 * dt_ + s_ + 1]
                        scs[pb] = S.op("vector",
                            lambda v, dAx=dAb[pb], vx=vb[pb], hx=hbx[pb],
                            init=init:
                            v.tensor_tensor_scan(hx, dAx, vx, init,
                                                 ALU.mult, ALU.add),
                            deps=[das[pb], vvs[pb]])
                    for pb in (0, 1):
                        s_ = 2 * sp + pb
                        S.op("vector", lambda v, dt_=dt_, s_=s_, hx=hbx[pb]:
                             v.tensor_copy(
                                 carry[:, 16 * dt_ + s_:16 * dt_ + s_ + 1],
                                 hx[:, 511:512]), deps=[scs[pb]])
                    for pb in (0, 1):
                        s_ = 2 * sp + pb
                        cb_ = CMB[pb]
                        cm = S.op("tensor", lambda t, s_=s_, cb_=cb_:
                            t.matmul(
                                self.PSB[cb_][:, :],
                                eoh[:, 128 * s_:128 * (s_ + 1)],
                                Cbf[:, :], start=True, stop=True),
                            deps=[exd3, eoh_tok] + self.psb_free.get(cb_, []))
                        pp = S.op("vector", lambda v, s_=s_, cb_=cb_,
                                  hx=hbx[pb]: v.tensor_tensor(
                                pint[:, 512 * (s_ % 4):512 * (s_ % 4 + 1)],
                                hx, self.PSB[cb_][:, :], ALU.mult),
                            deps=[scs[pb], cm])
                        self.psb_free[cb_] = [pp]
                        pptoks.append(pp)
                    if sp % 2 == 1:
                        grp = sp // 2
                        p4s = pptoks[-4:]
                        if grp == 0:
                            rd = S.op("vector", lambda v: v.tensor_add(
                                yb, pint[:, 0:512], pint[:, 512:1024]),
                                deps=p4s)
                        else:
                            rd = S.op("vector", lambda v: v.tensor_add(
                                yb, yb, pint[:, 0:512]), deps=p4s)
                            rd = S.op("vector", lambda v: v.tensor_add(
                                yb, yb, pint[:, 512:1024]), deps=[rd])
                        rd = S.op("vector", lambda v: v.tensor_add(
                            yb, yb, pint[:, 1024:1536]), deps=[rd])
                        rd = S.op("vector", lambda v: v.tensor_add(
                            yb, yb, pint[:, 1536:2048]), deps=[rd])
                        ytoks.append(rd)
                yd = S.op("vector", lambda v, dt_=dt_: v.scalar_tensor_tensor(
                    yb, xs_bf[:, 512 * dt_:512 * (dt_ + 1)],
                    P["w_Dp_c"][:, dt_:dt_ + 1], yb, ALU.mult, ALU.add),
                    deps=ytoks + [t_dp])
                zmm = []
                for k in range(8):
                    zmm.append(S.op("tensor", lambda t, dt_=dt_, k=k, co=co:
                        t.matmul(
                            self.PSB[5][:, :],
                            wA[:, 2048 * k + 128 * dt_:2048 * k + 128 * (dt_ + 1)],
                            qx_bf[:, T * k + co:T * k + co + TC],
                            start=(k == 0), stop=(k == 7)),
                        deps=wlz + self.psb_free.get(5, [])))
                z1 = S.op("vector", lambda v, co=co: v.tensor_tensor(
                    ztb, self.PSB[5][:, :], r2_bc[:, co:co + TC], ALU.mult),
                    deps=zmm)
                self.psb_free[5] = [z1]
                z2 = S.op("scalar", lambda s, dt_=dt_: s.activation(
                    out=gb, in_=ztb, func=ACT.Silu,
                    bias=P["bb2c"][:, 16 + dt_:16 + dt_ + 1], scale=1.0),
                    deps=[z1, cpb2])
                yg = S.op("vector", lambda v, dt_=dt_: v.tensor_tensor(
                    yg_bf[:, 8192 + 512 * dt_:8192 + 512 * (dt_ + 1)],
                    yb, gb, ALU.mult), deps=[yd, z2])
                tok_mamba_out.append(yg)
            wlo = [S.dma("sync", wB[:, 1024 * k:1024 * (k + 1)],
                         P["h_woutbf"].ap()[128 * k:128 * (k + 1), :],
                         deps=wobf + tok_mamba_out[-1:]) for k in range(16)]
            op_toks = []
            for dc in range(8):
                mms = []
                for k in range(16):
                    mms.append(S.op("tensor", lambda t, dc=dc, k=k: t.matmul(
                        self.PSB[6][:, :],
                        wB[:, 1024 * k + 128 * dc:1024 * k + 128 * (dc + 1)],
                        yg_bf[:, 8192 + 512 * k:8192 + 512 * (k + 1)],
                        start=(k == 0), stop=(k == 15)),
                        deps=wlo + tok_mamba_out[-16:] +
                             self.psb_free.get(6, [])))
                tl = S.dma("sync", A[:, 0:512],
                           P["x_fm"].ap()[128 * dc:128 * (dc + 1), co:co + TC],
                           deps=op_toks[-1:] + mms[:1])
                tlq = S.dma("sync", A[:, 512:1024],
                            P["h_qxf"].ap()[128 * dc:128 * (dc + 1), co:co + TC],
                            deps=op_toks[-1:] + mms[:1] + self.tok_qx)
                ad = S.op("vector", lambda v: v.tensor_add(
                    A[:, 0:512], A[:, 0:512], A[:, 512:1024]),
                    deps=[tl, tlq])
                ad2 = S.op("vector", lambda v: v.tensor_add(
                    A[:, 0:512], A[:, 0:512], self.PSB[6][:, :]),
                    deps=[ad] + mms)
                self.psb_free[6] = [ad2]
                sp = S.dma("sync",
                           P["h_xacc"].ap()[128 * dc:128 * (dc + 1), co:co + TC],
                           A[:, 0:512], deps=[ad2])
                op_toks.append(sp)
            prev_ch = op_toks + prev_ch
            self.tok_mamba_spill = op_toks

        self.dbg_out("xs", xs_bf[:, 0:8192], xs_toks)
        self.dbg_out("dtc", dtb_, tok_mamba_out)
        self.dbg_out("ygate", yg_bf[:, 8192:16384], tok_mamba_out)
        self.dbg_dram("xacc", P["h_xacc"], [D, T], self.tok_mamba_spill)
        self.dbg_dram("qxf", P["h_qxf"], [D, T],
                      self.tok_mamba_spill + self.tok_qx)


    def p5_ffn(self):
        S, P = self.S, self.P
        A = self.scr
        Abf = self.scrbf
        xq = self.abufC
        psbf = self.PSA.bitcast(BF16)

        def h_sl(c):
            if c < 8:
                return self.abufA[:, 2048 * c:2048 * (c + 1)]
            if c < 16:
                return self.wbufA[:, 2048 * (c - 8):2048 * (c - 7)]
            return self.wbufB[:, 2048 * (c - 16):2048 * (c - 15)]

        last_m = self.tok_mamba_spill
        xq_toks = [S.dma("sync", xq[:, T * k:T * (k + 1)],
                         P["h_xq"].ap()[128 * k:128 * (k + 1), :],
                         deps=last_m + self.tok_xq_spill) for k in range(8)]
        clipd = self.sbuf("clipd_sb", [1, 1], F32)
        tcd = S.dma("sync", clipd[:], P["w_clipd"].ap())
        qd = S.op("scalar", lambda s: s.activation(
            out=P["clipd_sb"][:], in_=P["clipd_sb"][:], func=ACT.Sigmoid,
            bias=0.0, scale=1.0), deps=[tcd])

        wnat = A[:, 0:2816]
        wqbf = [Abf[:, 5632:7680], Abf[:, 7680:9728]]
        wql = [Abf[:, 9728:10752], Abf[:, 10752:11776]]
        sg_ = A[:, 5888:6400]
        up_ = A[:, 6400:6912]
        rcol = self.sbuf("rcol5", [128, 1], F32)
        comb = self.sbuf("comb5", [128, 2], F32)
        hmax = self.sbuf("hmax", [128, 1], F32)
        hmaxt = self.sbuf("hmaxt", [128, 1], F32)
        hm0 = S.op("vector", lambda v: v.memset(P["hmax"][:], 0.0))
        h_toks = []
        prev_w = list(xq_toks)
        prev_h = []
        for c in range(FFC):
            ev_toks = []
            for wi, wdram in [(0, "w_wg"), (1, "w_wu")]:
                tl = S.dma("sync", wnat[:, 0:1024],
                           P[wdram].ap()[128 * c:128 * (c + 1), :],
                           deps=prev_w)
                r0 = S.op("vector", lambda v: v.tensor_reduce(
                    P["rcol5"][:], wnat[:, 0:1024], AXL.X, ALU.max,
                    apply_absolute_value=True), deps=[tl])
                c0 = S.op("vector", lambda v, wi=wi: v.tensor_scalar_mul(
                    P["comb5"][:, wi:wi + 1], P["rcol5"][:], 1.0 / 127.0),
                    deps=[r0])
                c1 = S.op("vector", lambda v, wi=wi: v.tensor_tensor(
                    P["comb5"][:, wi:wi + 1], P["comb5"][:, wi:wi + 1],
                    P["s1c"][:, 0:1], ALU.mult), deps=[c0, self.tok_s1c])
                rr = S.op("vector", lambda v: v.reciprocal(
                    P["rcol5"][:], P["rcol5"][:]), deps=[c0])
                rr2 = S.op("vector", lambda v: v.tensor_scalar_mul(
                    P["rcol5"][:], P["rcol5"][:], 127.0), deps=[rr])
                a1 = S.op("vector", lambda v: v.tensor_scalar(
                    wnat[:, 0:1024], wnat[:, 0:1024], P["rcol5"][:, 0:1],
                    RMAGIC, ALU.mult, ALU.add), deps=[rr2])
                a2 = S.op("vector", lambda v: v.tensor_scalar(
                    wnat[:, 0:1024], wnat[:, 0:1024], -RMAGIC, 127.0,
                    ALU.add, ALU.min), deps=[a1])
                a3 = S.op("vector", lambda v, wi=wi: v.tensor_scalar(
                    wqbf[wi][:, 0:1024], wnat[:, 0:1024],
                    -128.0, None, ALU.max), deps=[a2])
                prev_w = [a3]
                tt = []
                for hc in range(8):
                    tt.append(S.op("tensor", lambda t, hc=hc, wi=wi: t.transpose(
                        psbf[:, 7168 + 128 * hc:7168 + 128 * (hc + 1)],
                        wqbf[wi][:, 128 * hc:128 * (hc + 1)],
                        P["id_bf"][:]),
                        deps=[a3] + self.tok_const + self.psb_free.get(7, [])))
                ev = S.op("scalar", lambda s, wi=wi: s.activation(
                    out=wql[wi][:, 0:1024], in_=psbf[:, 7168:8192],
                    func=ACT.Copy, bias=0.0, scale=1.0), deps=tt)
                self.psb_free[7] = [ev]
                ev_toks.append((ev, c1))
            for tc in range(4):
                mg = []
                mu = []
                for hc in range(8):
                    mg.append(S.op("tensor", lambda t, tc=tc, hc=hc: t.matmul(
                        self.PSB[0][:, :], wql[0][:, 128 * hc:128 * (hc + 1)],
                        xq[:, T * hc + 512 * tc:T * hc + 512 * (tc + 1)],
                        start=(hc == 0), stop=(hc == 7)),
                        deps=[ev_toks[0][0]] + xq_toks +
                             self.psb_free.get(0, [])))
                    mu.append(S.op("tensor", lambda t, tc=tc, hc=hc: t.matmul(
                        self.PSB[1][:, :], wql[1][:, 128 * hc:128 * (hc + 1)],
                        xq[:, T * hc + 512 * tc:T * hc + 512 * (tc + 1)],
                        start=(hc == 0), stop=(hc == 7)),
                        deps=[ev_toks[1][0]] + xq_toks +
                             self.psb_free.get(1, [])))
                e1 = S.op("scalar", lambda s: s.activation(
                    out=sg_, in_=self.PSB[0][:, :], func=ACT.Silu,
                    scale=P["comb5"][:, 0:1], bias=0.0),
                    deps=mg + [ev_toks[0][1]] + prev_h)
                self.psb_free[0] = [e1]
                e2 = S.op("vector", lambda v: v.tensor_scalar_mul(
                    up_, self.PSB[1][:, :], P["comb5"][:, 1:2]),
                    deps=mu + [ev_toks[1][1]] + prev_h)
                self.psb_free[1] = [e2]
                h1 = S.op("vector", lambda v, c=c, tc=tc: v.tensor_tensor(
                    h_sl(c)[:, 512 * tc:512 * (tc + 1)], sg_, up_, ALU.mult),
                    deps=[e1, e2])
                r1_ = S.op("vector", lambda v, c=c, tc=tc: v.tensor_reduce(
                    P["hmaxt"][:], h_sl(c)[:, 512 * tc:512 * (tc + 1)],
                    AXL.X, ALU.max, apply_absolute_value=True), deps=[h1])
                r2_ = S.op("vector", lambda v: v.tensor_max(
                    P["hmax"][:], P["hmax"][:], P["hmaxt"][:]),
                    deps=[r1_, hm0])
                h_toks.append(h1)
                prev_h = [h1, r2_]
            prev_w = prev_w + prev_h

        tmm = S.op("tensor", lambda t: t.matmul(
            self.PSB[2][0:1, 0:128], P["hmax"][:], P["id_f"][:],
            start=True, stop=True),
            deps=prev_h + self.tok_const + self.psb_free.get(2, []))
        cpm = S.op("scalar", lambda s: s.activation(
            out=P["mxrow"][:], in_=self.PSB[2][0:1, 0:128], func=ACT.Copy,
            bias=0.0, scale=1.0), deps=[tmm])
        self.psb_free[2] = [cpm]
        pr = S.op("vector", lambda v: v.tensor_reduce(
            P["mxall"][:], P["mxrow"][:], AXL.X, ALU.max), deps=[cpm])
        z1 = S.op("vector", lambda v: v.memset(P["ccst"][:], 0.0),
                  deps=[pr] + self.tok_cc)
        z2 = S.op("vector", lambda v: v.tensor_copy(
            P["ccst"][0:1, 0:1], P["mxall"][:]), deps=[z1])
        cct, ccs2 = self.collective_max(1, [z2], "s2")
        s2s = self.sbuf("s2s", [1, 2], F32)
        q2 = S.op("vector", lambda v, ccs2=ccs2: v.tensor_tensor(
            P["s2s"][0:1, 0:1], P["clipd_sb"][:], ccs2[0:1, 0:1], ALU.mult),
            deps=[qd, cct])
        q3 = S.op("vector", lambda v: v.tensor_scalar_mul(
            P["s2s"][0:1, 0:1], P["s2s"][0:1, 0:1], 1.0 / 127.0), deps=[q2])
        q4 = S.op("vector", lambda v: v.reciprocal(
            P["s2s"][0:1, 1:2], P["s2s"][0:1, 0:1]), deps=[q3])
        s2c = self.sbuf("s2c", [128, 2], F32)
        mm1 = S.op("tensor", lambda t: t.matmul(
            self.PSB[3][:, 0:2], P["ones1_f"][:], P["s2s"][:],
            start=True, stop=True),
            deps=[q4] + self.tok_const + self.psb_free.get(3, []))
        cp1 = S.op("scalar", lambda s: s.activation(
            out=P["s2c"][:], in_=self.PSB[3][:, 0:2], func=ACT.Copy,
            bias=0.0, scale=1.0), deps=[mm1])
        self.psb_free[3] = [cp1]

        hq_toks = []
        prevq = [cp1] + h_toks
        for c in range(FFC):
            for tc in range(4):
                a1 = S.op("vector", lambda v, c=c, tc=tc: v.tensor_scalar(
                    sg_, h_sl(c)[:, 512 * tc:512 * (tc + 1)],
                    P["s2c"][:, 1:2], RMAGIC, ALU.mult, ALU.add), deps=prevq)
                a2 = S.op("vector", lambda v: v.tensor_scalar(
                    sg_, sg_, -RMAGIC, 127.0, ALU.add, ALU.min), deps=[a1])
                a3 = S.op("vector", lambda v, c=c, tc=tc: v.tensor_scalar(
                    h_sl(c)[:, 512 * tc:512 * (tc + 1)], sg_,
                    -128.0, None, ALU.max), deps=[a2])
                hq_toks.append(a3)
                prevq = [a3]

        wdl = Abf[:, 9728:15360]
        xacc5 = A[:, 7808:8320]
        combd = self.sbuf("combd", [128, 1], F32)
        self.tok_ffn_acc = []
        prev_d = list(hq_toks)
        for dc in range(8):
            tl = S.dma("sync", wnat, P["w_wd"].ap()[128 * dc:128 * (dc + 1), :],
                       deps=prev_d)
            r0 = S.op("vector", lambda v: v.tensor_reduce(
                P["rcol5"][:], wnat, AXL.X, ALU.max,
                apply_absolute_value=True), deps=[tl])
            c0 = S.op("vector", lambda v: v.tensor_scalar_mul(
                P["combd"][:], P["rcol5"][:], 1.0 / 127.0), deps=[r0])
            c1 = S.op("vector", lambda v: v.tensor_tensor(
                P["combd"][:], P["combd"][:], P["s2c"][:, 0:1], ALU.mult),
                deps=[c0, cp1])
            rr = S.op("vector", lambda v: v.reciprocal(
                P["rcol5"][:], P["rcol5"][:]), deps=[c0])
            rr2 = S.op("vector", lambda v: v.tensor_scalar_mul(
                P["rcol5"][:], P["rcol5"][:], 127.0), deps=[rr])
            a1 = S.op("vector", lambda v: v.tensor_scalar(
                wnat, wnat, P["rcol5"][:, 0:1], RMAGIC, ALU.mult, ALU.add),
                deps=[rr2])
            a2 = S.op("vector", lambda v: v.tensor_scalar(
                wnat, wnat, -RMAGIC, 127.0, ALU.add, ALU.min), deps=[a1])
            a3 = S.op("vector", lambda v: v.tensor_scalar(
                Abf[:, 5632:8448], wnat, -128.0, None, ALU.max), deps=[a2])
            ev_all = []
            for b2 in range(3):
                tt = []
                for ic in range(8):
                    gidx = 8 * b2 + ic
                    if gidx >= FFC:
                        break
                    tt.append(S.op("tensor", lambda t, gidx=gidx: t.transpose(
                        psbf[:, 7168 + 128 * (gidx % 8):
                             7168 + 128 * (gidx % 8 + 1)],
                        Abf[:, 5632 + 128 * gidx:5632 + 128 * (gidx + 1)],
                        P["id_bf"][:]),
                        deps=[a3] + self.tok_const + self.psb_free.get(7, [])))
                n = len(tt)
                ev = S.op("scalar", lambda s, b2=b2, n=n: s.activation(
                    out=wdl[:, 1024 * b2:1024 * b2 + 128 * n],
                    in_=psbf[:, 7168:7168 + 128 * n],
                    func=ACT.Copy, bias=0.0, scale=1.0), deps=tt)
                self.psb_free[7] = [ev]
                ev_all.append(ev)
            for tc in range(4):
                mms = []
                for ic in range(FFC):
                    mms.append(S.op("tensor", lambda t, tc=tc, ic=ic: t.matmul(
                        self.PSB[4][:, :],
                        wdl[:, 1024 * (ic // 8) + 128 * (ic % 8):
                            1024 * (ic // 8) + 128 * (ic % 8 + 1)],
                        h_sl(ic)[:, 512 * tc:512 * (tc + 1)],
                        start=(ic == 0), stop=(ic == FFC - 1)),
                        deps=ev_all + hq_toks + self.psb_free.get(4, [])))
                tl2 = S.dma("sync", xacc5,
                            P["h_xacc"].ap()[128 * dc:128 * (dc + 1),
                                             512 * tc:512 * (tc + 1)],
                            deps=self.tok_mamba_spill + self.tok_ffn_acc[-1:])
                ad = S.op("vector", lambda v: v.scalar_tensor_tensor(
                    xacc5, self.PSB[4][:, :], P["combd"][:, 0:1], xacc5,
                    ALU.mult, ALU.add), deps=mms + [tl2, c1])
                self.psb_free[4] = [ad]
                sp = S.dma("sync",
                           P["h_xacc"].ap()[128 * dc:128 * (dc + 1),
                                            512 * tc:512 * (tc + 1)],
                           xacc5, deps=[ad])
                self.tok_ffn_acc.append(sp)
            prev_d = [a3] + self.tok_ffn_acc[-1:]

        self.dbg_out("hq", h_sl(0), hq_toks)
        self.dbg_out("s2", ccs2[:], [cct])
        self.dbg_dram("xacc5", P["h_xacc"], [D, T], self.tok_ffn_acc)

    def p6_attn(self):
        S, P = self.S, self.P
        A = self.scr
        Abf = self.scrbf
        xn1 = self.abufA
        Qb = self.abufC
        Kb = self.wbufA
        Va = self.wbufB
        slopes = alibi_slopes(NH)

        xa = A[:, 0:2048]
        xsq = A[:, 2048:4096]
        prev_mm = list(self.tok_ffn_acc)
        for k in range(8):
            tl = S.dma("sync", xa, P["h_xacc"].ap()[128 * k:128 * (k + 1), :],
                       deps=prev_mm + self.tok_ffn_acc)
            tsq = S.op("scalar", lambda s: s.activation(
                out=xsq, in_=xa, func=ACT.Square, bias=0.0, scale=1.0),
                deps=[tl])
            prev_mm = []
            for tc in range(4):
                m1 = S.op("tensor", lambda t, tc=tc, k=k: t.matmul(
                    self.PSB[tc][0:1, :], P["onescol_f"][:],
                    xa[:, 512 * tc:512 * (tc + 1)],
                    start=(k == 0), stop=(k == 7)),
                    deps=[tl, tsq] + (self.psb_free.get(tc, []) if k == 0 else []))
                m2 = S.op("tensor", lambda t, tc=tc, k=k: t.matmul(
                    self.PSB[4 + tc][0:1, :], P["onescol_f"][:],
                    xsq[:, 512 * tc:512 * (tc + 1)],
                    start=(k == 0), stop=(k == 7)),
                    deps=[tsq] + (self.psb_free.get(4 + tc, []) if k == 0 else []))
                prev_mm += [m1, m2]
        mrow = A[0:1, 0:2048]
        vrow = A[0:1, 2048:4096]
        ev = []
        for tc in range(4):
            ev.append(S.op("scalar", lambda s, tc=tc: s.activation(
                out=mrow[0:1, 512 * tc:512 * (tc + 1)],
                in_=self.PSB[tc][0:1, :], func=ACT.Copy, bias=0.0,
                scale=1.0 / D), deps=prev_mm))
            ev.append(S.op("scalar", lambda s, tc=tc: s.activation(
                out=vrow[0:1, 512 * tc:512 * (tc + 1)],
                in_=self.PSB[4 + tc][0:1, :], func=ACT.Copy, bias=0.0,
                scale=1.0 / D), deps=prev_mm))
        for i in range(8):
            self.psb_free[i] = list(ev)
        rt1 = A[0:1, 4096:6144]
        rt2 = A[0:1, 6144:8192]
        o3 = S.op("vector", lambda v: v.tensor_tensor(
            rt1, mrow, mrow, ALU.mult), deps=ev)
        o4 = S.op("vector", lambda v: v.tensor_sub(vrow, vrow, rt1), deps=[o3])
        a_ = S.op("scalar", lambda s: s.activation(
            out=rt1, in_=vrow, func=ACT.Sqrt, bias=P["eps1"][:, 0:1],
            scale=1.0), deps=[o4] + self.tok_const)
        y0 = S.op("vector", lambda v: v.reciprocal(rt2, rt1), deps=[a_])
        b1 = S.op("vector", lambda v: v.tensor_scalar_add(rt1, vrow, EPS),
                  deps=[y0])
        b2 = S.op("vector", lambda v: v.tensor_tensor(rt1, rt1, rt2, ALU.mult),
                  deps=[b1])
        b3 = S.op("vector", lambda v: v.tensor_tensor(rt1, rt1, rt2, ALU.mult),
                  deps=[b2])
        b4 = S.op("vector", lambda v: v.tensor_scalar(
            rt1, rt1, -0.5, 1.5, ALU.mult, ALU.add), deps=[b3])
        rdone = S.op("vector", lambda v: v.tensor_tensor(
            vrow, rt1, rt2, ALU.mult), deps=[b4])
        rrow = vrow
        m_bc = A[:, 0:2048]
        r_bc = A[:, 2048:4096]
        bc_last = [rdone]
        for row, dstb in [(mrow, m_bc), (rrow, r_bc)]:
            for tc in range(4):
                mm = self.bcast_row_psum(
                    self.PSB[0][:, :], row[0:1, 512 * tc:512 * (tc + 1)],
                    deps=bc_last + self.tok_const + self.psb_free.get(0, []))
                cp = S.op("scalar", lambda s, dstb=dstb, tc=tc: s.activation(
                    out=dstb[:, 512 * tc:512 * (tc + 1)], in_=self.PSB[0][:],
                    func=ACT.Copy, bias=0.0, scale=1.0), deps=[mm])
                self.psb_free[0] = [cp]
                bc_last = [cp]
        xnf = A[:, 4096:6144]
        xn1_toks = []
        prev = list(bc_last)
        for k in range(8):
            tl = S.dma("sync", xa, P["h_xacc"].ap()[128 * k:128 * (k + 1), :],
                       deps=prev)
            s1 = S.op("vector", lambda v: v.scalar_tensor_tensor(
                xnf, m_bc, -1.0, xa, ALU.mult, ALU.add), deps=[tl])
            s2 = S.op("vector", lambda v, k=k: v.tensor_tensor(
                xn1[:, T * k:T * (k + 1)], xnf, r_bc, ALU.mult), deps=[s1])
            prev = [s2]
            xn1_toks.append(s2)
        self.tok_xn1 = xn1_toks

        wstage = Abf[:, 0:8192]

        def cast_w(dram, deps, dst=None):
            dst = wstage if dst is None else dst
            toks = []
            for k in range(8):
                tl = S.dma("sync", A[:, 6144:7168],
                           P[dram].ap()[128 * k:128 * (k + 1), :], deps=deps +
                           toks[-1:])
                toks.append(S.op("vector", lambda v, k=k, dst=dst: v.tensor_copy(
                    dst[:, 1024 * k:1024 * (k + 1)], A[:, 6144:7168]),
                    deps=[tl]))
            return toks

        qk_toks = []
        for dst, wnm in [(Qb, "w_WqT"), (Kb, "w_WkT")]:
            ct = cast_w(wnm, xn1_toks + qk_toks)
            for oc in range(8):
                for tc in range(4):
                    mms = []
                    for k in range(8):
                        mms.append(S.op("tensor",
                            lambda t, oc=oc, tc=tc, k=k: t.matmul(
                                self.PSB[oc % 2][:, :],
                                wstage[:, 1024 * k + 128 * oc:
                                       1024 * k + 128 * (oc + 1)],
                                xn1[:, T * k + 512 * tc:T * k + 512 * (tc + 1)],
                                start=(k == 0), stop=(k == 7)),
                            deps=ct + xn1_toks + self.psb_free.get(oc % 2, [])))
                    e1 = S.op("scalar", lambda s, dst=dst, oc=oc, tc=tc:
                              s.activation(
                                  out=dst[:, 2048 * oc + 512 * tc:
                                          2048 * oc + 512 * (tc + 1)],
                                  in_=self.PSB[oc % 2][:, :], func=ACT.Copy,
                                  bias=0.0, scale=1.0), deps=mms)
                    self.psb_free[oc % 2] = [e1]
                    qk_toks.append(e1)
        ct = cast_w("w_WvT", qk_toks)
        vset = S.op("vector", lambda v: v.memset(
            Va.rearrange("p (tk h o) -> p tk h o", h=NH, o=65)[:, :, :, 64:65],
            1.0), deps=qk_toks)
        v_toks = [vset]
        for tk in range(16):
            for ch in range(2):
                mms = []
                for k in range(8):
                    mms.append(S.op("tensor", lambda t, tk=tk, ch=ch, k=k:
                        t.matmul(
                            self.PSB[2 + ch][:, :],
                            xn1[:, T * k + 128 * tk:T * k + 128 * (tk + 1)],
                            wstage[:, 1024 * k + 512 * ch:
                                   1024 * k + 512 * (ch + 1)],
                            start=(k == 0), stop=(k == 7)),
                        deps=ct + xn1_toks + self.psb_free.get(2 + ch, [])))
                e1 = S.op("scalar", lambda s, tk=tk, ch=ch: s.activation(
                    out=Va.rearrange("p (tk h o) -> p tk h o", h=NH, o=65)
                        [:, tk, 8 * ch:8 * (ch + 1), 0:64],
                    in_=self.PSB[2 + ch][:, :], func=ACT.Copy,
                    bias=0.0, scale=1.0), deps=mms + [vset])
                self.psb_free[2 + ch] = [e1]
                v_toks.append(e1)

        kbase = self.sbuf("kbase", [128, 16], F32)
        kb0 = S.op("gpsimd", lambda g: g.iota(
            P["kbase"][:], pattern=[[128, 16]], base=0, channel_multiplier=1,
            allow_small_or_imprecise_dtypes=True), deps=v_toks)
        masks = Abf[:, 16384:18432]
        maskp = self.sbuf("maskp64", [128, 64], BF16)
        mk = []
        m0 = S.op("vector", lambda v: v.memset(A[:, 9216:9728], 1.0),
                  deps=v_toks + self.tok_mamba_spill)
        for j in range(4):
            mk.append(S.op("gpsimd", lambda g, j=j: g.affine_select(
                masks[:, 512 * j:512 * (j + 1)], A[:, 9216:9728],
                pattern=[[1, 512]], compare_op=ALU.is_ge, fill=0.0,
                base=-128 * j, channel_multiplier=-1), deps=[m0]))
        mk.append(S.op("gpsimd", lambda g: g.affine_select(
            maskp[:], A[:, 9216:9280], pattern=[[1, 64]],
            compare_op=ALU.is_ge, fill=0.0, base=64, channel_multiplier=-1),
            deps=[m0]))
        bias_t = self.sbuf("bias_t", [128, 16], F32)
        ptile = [Abf[:, 18432:19456], Abf[:, 19456:20480]]
        rcrow = self.PSB[7][0:1, 0:512]
        attn_un = Abf[:, 0:16384]

        at_toks = []
        pt_tok = {0: [], 1: []}
        SMB = [4, 3]
        AVB = [5, 2]
        self.sbuf("bias_t2", [128, 16], F32)
        btl = [P["bias_t"], P["bias_t2"]]
        bt_war = {0: [], 1: []}
        qbi = 0
        for h in range(NH):
            qB = 64 if h < 2 else (128 if h < 5 else 512)
            nqb = T // qB
            kslice = Kb[64 * (h % 2):64 * (h % 2) + 64, :]
            qslice = Qb[64 * (h % 2):64 * (h % 2) + 64, :]
            for qb in range(nqb):
                qbi += 1
                avb = AVB[qbi % 2]
                btx = btl[qbi % 2]
                qs = qB * qb
                qref = float(qs + qB - 1)
                kb_lo = max(0, (qs - 384) // 128) if qB < 512 else 0
                kb_hi = (qs + qB - 1) // 128
                bt = S.op("vector", lambda v, h=h, qref=qref, btx=btx:
                    v.tensor_scalar(
                        btx[:], P["kbase"][:], -qref,
                        float(slopes[h]), ALU.add, ALU.mult),
                    deps=[kb0] + bt_war[qbi % 2])
                bt_war[qbi % 2] = []
                o_mms = []
                for kb in range(kb_lo, kb_hi + 1):
                    smb = SMB[kb % 2]
                    smm = S.op("tensor", lambda t, h=h, qs=qs, qB=qB, kb=kb,
                               kslice=kslice, qslice=qslice, smb=smb:
                        t.matmul(
                            self.PSB[smb][:, 0:qB],
                            kslice[:, 2048 * (h // 2) + 128 * kb:
                                   2048 * (h // 2) + 128 * (kb + 1)],
                            qslice[:, 2048 * (h // 2) + qs:
                                   2048 * (h // 2) + qs + qB],
                            start=True, stop=True),
                        deps=qk_toks + self.psb_free.get(smb, []))
                    pb = kb % 2
                    ex = S.op("scalar", lambda s, kb=kb, pb=pb, qB=qB,
                              smb=smb, btx=btx:
                              s.activation(
                                  out=ptile[pb][:, 0:qB],
                                  in_=self.PSB[smb][:, 0:qB],
                                  func=ACT.Exp, scale=0.125,
                                  bias=btx[:, kb:kb + 1]),
                              deps=[smm, bt] + pt_tok[pb])
                    self.psb_free[smb] = [ex]
                    bt_war[qbi % 2].append(ex)
                    ptok = [ex]
                    if kb == kb_hi and qB < 512:
                        off = qs - 128 * kb
                        msl = (maskp[:, 0:qB] if off == 64 else
                               masks[:, 0:qB])
                        ptok = [S.op("vector", lambda v, pb=pb, qB=qB, msl=msl:
                            v.tensor_tensor(
                                ptile[pb][:, 0:qB], ptile[pb][:, 0:qB],
                                msl, ALU.mult), deps=[ex] + mk)]
                    elif qB == 512 and kb >= 4 * qb:
                        j = kb - 4 * qb
                        ptok = [S.op("vector", lambda v, j=j, pb=pb:
                            v.tensor_tensor(
                                ptile[pb][:, 0:512], ptile[pb][:, 0:512],
                                masks[:, 512 * j:512 * (j + 1)], ALU.mult),
                            deps=[ex] + mk)]
                    omm = S.op("tensor", lambda t, h=h, kb=kb, pb=pb, qB=qB,
                               kb_lo=kb_lo, kb_hi=kb_hi, avb=avb: t.matmul(
                        self.PSB[avb][0:65, 0:qB],
                        Va[:, 1040 * kb + 65 * h:1040 * kb + 65 * (h + 1)],
                        ptile[pb][:, 0:qB],
                        start=(kb == kb_lo), stop=(kb == kb_hi)),
                        deps=ptok + v_toks + (self.psb_free.get(avb, [])
                                              if kb == kb_lo else []))
                    pt_tok[pb] = [omm]
                    o_mms.append(omm)
                l1 = S.op("scalar", lambda s, qB=qB, avb=avb: s.activation(
                    out=rcrow[0:1, 0:qB], in_=self.PSB[avb][64:65, 0:qB],
                    func=ACT.Ln, bias=0.0, scale=1.0),
                    deps=o_mms + self.psb_free.get(7, []))
                self.psb_free[7] = [l1]
                l2 = S.op("scalar", lambda s, qB=qB: s.activation(
                    out=self.xnsl.bitcast(F32)[0:1, 3072:3584][0:1, 0:qB], in_=rcrow[0:1, 0:qB],
                    func=ACT.Exp, bias=0.0, scale=-1.0), deps=[l1])
                bmm = S.op("tensor", lambda t, qB=qB: t.matmul(
                    self.PSB[6][0:64, 0:qB], P["ones1_f"][0:1, 0:64],
                    self.xnsl.bitcast(F32)[0:1, 3072:3584][0:1, 0:qB],
                    start=True, stop=True),
                    deps=[l2] + self.tok_const + self.psb_free.get(6, []))
                asl = attn_un[64 * (h % 2):64 * (h % 2) + 64,
                              2048 * (h // 2) + qs:2048 * (h // 2) + qs + qB]
                ev1 = S.op("scalar", lambda s, asl=asl, qB=qB, avb=avb:
                    s.activation(
                        out=asl, in_=self.PSB[avb][0:64, 0:qB], func=ACT.Copy,
                        bias=0.0, scale=1.0), deps=o_mms + [bmm])
                nm = S.op("vector", lambda v, asl=asl, qB=qB: v.tensor_tensor(
                    asl, asl, self.PSB[6][0:64, 0:qB], ALU.mult), deps=[ev1])
                self.psb_free[avb] = [ev1]
                self.psb_free[6] = [nm]
                at_toks.append(nm)

        wpst = Qb[:, 0:8192]
        wp = []
        for k in range(8):
            tl = S.dma("sync", A[:, 8192:9216],
                       P["w_WpT"].ap()[128 * k:128 * (k + 1), :],
                       deps=at_toks + wp[-1:])
            wp.append(S.op("vector", lambda v, k=k: v.tensor_copy(
                wpst[:, 1024 * k:1024 * (k + 1)], A[:, 8192:9216]),
                deps=[tl]))
        pr_toks = []
        for dc in range(8):
            for tc in range(4):
                mms = []
                for k in range(8):
                    mms.append(S.op("tensor", lambda t, dc=dc, tc=tc, k=k:
                        t.matmul(
                            self.PSB[0][:, :],
                            wpst[:, 1024 * k + 128 * dc:
                                 1024 * k + 128 * (dc + 1)],
                            attn_un[:, 2048 * k + 512 * tc:
                                    2048 * k + 512 * (tc + 1)],
                            start=(k == 0), stop=(k == 7)),
                        deps=wp + at_toks + self.psb_free.get(0, [])))
                tl = S.dma("sync", A[:, 8704:9216],
                           P["h_xacc"].ap()[128 * dc:128 * (dc + 1),
                                            512 * tc:512 * (tc + 1)],
                           deps=pr_toks[-1:] + self.tok_ffn_acc)
                ad = S.op("vector", lambda v: v.tensor_add(
                    A[:, 8704:9216], A[:, 8704:9216], self.PSB[0][:, :]),
                    deps=[tl] + mms)
                self.psb_free[0] = [ad]
                sp = S.dma("sync",
                           P["h_xacc"].ap()[128 * dc:128 * (dc + 1),
                                            512 * tc:512 * (tc + 1)],
                           A[:, 8704:9216], deps=[ad])
                pr_toks.append(sp)
        self.tok_attn_acc = pr_toks
        self.dbg_out("attnu", self.scrbf[:, 0:2048], at_toks)
        self.dbg_dram("xacc6", P["h_xacc"], [D, T], pr_toks)


    def p7_moe(self):
        S, P = self.S, self.P
        A = self.scr
        Abf = self.scrbf
        xn1 = self.abufA
        hpr = self.abufC
        wst = [Abf[:, 0:8192], Abf[:, 8192:16384]]
        hraw = Abf[:, 16384:17408]
        t_b1, _ = self.load_cols("w_moeb1", 32)
        wc_toks = []
        prev_c = list(self.tok_attn_acc)
        for e in range(NE):
            for k in range(8):
                tl = S.dma("sync", A[:, 0:512],
                           P["w_moe1"].ap()[e, 128 * k:128 * (k + 1), :],
                           deps=prev_c)
                cv = S.op("vector", lambda v: v.tensor_copy(
                    Abf[:, 1024:1536], A[:, 0:512]), deps=[tl])
                sp = S.dma("sync", P["h_moe1"].ap()[e, 128 * k:128 * (k + 1), :],
                           Abf[:, 1024:1536], deps=[cv])
                wc_toks.append(sp)
                prev_c = [sp, cv]
            for ic in range(4):
                tl = S.dma("sync", A[:, 0:1024],
                           P["w_moe2"].ap()[e, 128 * ic:128 * (ic + 1), :],
                           deps=prev_c)
                cv = S.op("vector", lambda v: v.tensor_copy(
                    Abf[:, 2048:3072], A[:, 0:1024]), deps=[tl])
                sp = S.dma("sync", P["h_moe2"].ap()[e, 128 * ic:128 * (ic + 1), :],
                           Abf[:, 2048:3072], deps=[cv])
                wc_toks.append(sp)
                prev_c = [sp, cv]
        gw_toks = []
        for k in range(8):
            tl = S.dma("sync", A[:, 8448:8456],
                       P["w_gw"].ap()[128 * k:128 * (k + 1), :],
                       deps=self.tok_attn_acc + gw_toks[-1:])
            gw_toks.append(S.op("vector", lambda v, k=k: v.tensor_copy(
                self.xnsl[:, 4096:4160][:, 8 * k:8 * (k + 1)], A[:, 8448:8456]), deps=[tl]))

        oh8 = self.xnsl[0:8, 2048:3072]
        z0 = S.op("vector", lambda v: v.memset(A[0:8, 0:1024], 1.0),
                  deps=self.tok_attn_acc)
        zoh = S.op("gpsimd", lambda g: g.affine_select(
            oh8[:], A[0:8, 0:1024], pattern=[[1, 8], [0, 128]],
            compare_op=ALU.is_equal, fill=0.0, base=0, channel_multiplier=-1),
            deps=[z0])

        wr8 = self.xnsl[0:8, 0:2048]
        lg = A[:, 8456:8464]
        pb_ = A[:, 8464:8472]
        col1 = A[:, 8472:8473]
        col2 = A[:, 8473:8474]
        wr_toks = []
        for tk in range(16):
            mms = []
            for k in range(8):
                mms.append(S.op("tensor", lambda t, tk=tk, k=k: t.matmul(
                    self.PSB[0][:, 0:8],
                    xn1[:, T * k + 128 * tk:T * k + 128 * (tk + 1)],
                    self.xnsl[:, 4096:4160][:, 8 * k:8 * (k + 1)],
                    start=(k == 0), stop=(k == 7)),
                    deps=gw_toks + self.tok_xn1 + self.psb_free.get(0, [])))
            g0 = S.op("vector", lambda v: v.tensor_reduce(
                col1, self.PSB[0][:, 0:8], AXL.X, ALU.max), deps=mms)
            g1 = S.op("vector", lambda v: v.tensor_scalar_mul(
                col2, col1, -1.0), deps=[g0])
            g2 = S.op("scalar", lambda s: s.activation(
                out=lg, in_=self.PSB[0][:, 0:8], func=ACT.Exp,
                bias=col2[:, 0:1], scale=1.0), deps=[g1] + mms)
            self.psb_free[0] = [g2]
            g3 = S.op("vector", lambda v: v.tensor_reduce(
                col1, lg, AXL.X, ALU.add), deps=[g2])
            g4 = S.op("vector", lambda v: v.reciprocal(col2, col1), deps=[g3])
            g5 = S.op("vector", lambda v: v.tensor_scalar(
                pb_, lg, col2[:, 0:1], None, ALU.mult), deps=[g4])
            g6 = S.op("vector", lambda v: v.tensor_reduce(
                col1, pb_, AXL.X, ALU.max), deps=[g5])
            g7 = S.op("vector", lambda v: v.tensor_scalar(
                lg, pb_, col1[:, 0:1], None, ALU.is_ge), deps=[g6])
            g8 = S.op("vector", lambda v: v.scalar_tensor_tensor(
                lg, lg, -2.0, pb_, ALU.mult, ALU.add), deps=[g7])
            g9 = S.op("vector", lambda v: v.tensor_reduce(
                col2, lg, AXL.X, ALU.max), deps=[g8])
            ga = S.op("vector", lambda v: v.tensor_scalar(
                lg, pb_, col2[:, 0:1], None, ALU.is_ge), deps=[g9])
            gb = S.op("vector", lambda v: v.tensor_add(
                col1, col1, col2), deps=[ga])
            gc = S.op("vector", lambda v: v.tensor_scalar(
                lg, lg, col1[:, 0:1], None, ALU.mult), deps=[gb])
            gd = S.op("vector", lambda v: v.tensor_copy(
                self.xnsl[:, 4160:4168][:], lg), deps=[gc])
            tmm = S.op("tensor", lambda t: t.transpose(
                self.PSA.bitcast(BF16)[0:8, 2048:2176], self.xnsl[:, 4160:4168][:],
                P["id_bf"][:]), deps=[gd] + self.tok_const +
                self.psb_free.get(1, []))
            ev = S.op("scalar", lambda s, tk=tk: s.activation(
                out=wr8[:, 128 * tk:128 * (tk + 1)],
                in_=self.PSA.bitcast(BF16)[0:8, 2048:2176],
                func=ACT.Copy, bias=0.0, scale=1.0), deps=[tmm])
            self.psb_free[1] = [ev]
            wr_toks.append(ev)

        b2t = self.xnsl[0:8, 3072:4096]
        tb2a = S.dma("sync", A[0:8, 7168:8192], P["w_moeb2"].ap(),
                     deps=self.tok_attn_acc)
        tok_b2t = S.op("vector", lambda v: v.tensor_copy(
            b2t[:], A[0:8, 7168:8192]), deps=[tb2a])
        out_toks = []
        for tr in range(4):
            to = 512 * tr
            h_toks = []
            prev_init = list(wr_toks) + wc_toks[-1:] + out_toks[-1:]
            prevbuf = {0: list(prev_init), 1: list(prev_init)}
            for e in range(NE):
                st = wst[e % 2]
                wl = []
                for k in range(8):
                    wl.append(S.dma("sync", st[:, 512 * k:512 * (k + 1)],
                                    P["h_moe1"].ap()[e, 128 * k:128 * (k + 1), :],
                                    deps=prevbuf[e % 2]))
                emms = []
                for ic in range(4):
                    mms = []
                    for k in range(8):
                        mms.append(S.op("tensor",
                            lambda t, ic=ic, k=k, st=st, to=to: t.matmul(
                                self.PSB[2][:, :],
                                st[:, 512 * k + 128 * ic:512 * k + 128 * (ic + 1)],
                                xn1[:, T * k + to:T * k + to + 512],
                                start=(k == 0), stop=(k == 7)),
                            deps=wl + self.tok_xn1 + self.psb_free.get(2, [])))
                    ge = S.op("scalar", lambda s, e=e, ic=ic: s.activation(
                        out=hraw[:, 0:512], in_=self.PSB[2][:, :],
                        func=ACT.Gelu, bias=P["w_moeb1_c"][:, 4 * e + ic:
                                              4 * e + ic + 1], scale=1.0),
                        deps=mms + [t_b1])
                    self.psb_free[2] = [ge]
                    bmm = S.op("tensor", lambda t, e=e, tr=tr: t.matmul(
                        self.PSB[3][:, :], oh8[:, 128 * e:128 * (e + 1)],
                        wr8[:, 512 * tr:512 * (tr + 1)],
                        start=True, stop=True),
                        deps=[zoh] + wr_toks + self.psb_free.get(3, []))
                    hp = S.op("vector", lambda v, e=e, ic=ic: v.tensor_tensor(
                        hpr[:, 2048 * e + 512 * ic:2048 * e + 512 * (ic + 1)],
                        hraw[:, 0:512], self.PSB[3][:, :], ALU.mult),
                        deps=[ge, bmm])
                    self.psb_free[3] = [hp]
                    h_toks.append(hp)
                    emms += mms
                prevbuf[e % 2] = emms
            mm_all = []
            prevbuf2 = {0: list(h_toks), 1: list(h_toks)}
            for e in range(NE):
                st = wst[e % 2]
                wl = [S.dma("sync", st[:, 1024 * ic:1024 * (ic + 1)],
                            P["h_moe2"].ap()[e, 128 * ic:128 * (ic + 1), :],
                            deps=prevbuf2[e % 2]) for ic in range(4)]
                emms = []
                for dc in range(8):
                    for ic in range(4):
                        emms.append(S.op("tensor",
                            lambda t, e=e, dc=dc, ic=ic, st=st: t.matmul(
                                self.PSB[dc][:, :],
                                st[:, 1024 * ic + 128 * dc:
                                   1024 * ic + 128 * (dc + 1)],
                                hpr[:, 2048 * e + 512 * ic:
                                    2048 * e + 512 * (ic + 1)],
                                start=(e == 0 and ic == 0), stop=False),
                            deps=wl + h_toks +
                                 (self.psb_free.get(dc, []) if e == 0 and
                                  ic == 0 else [])))
                mm_all += emms
                prevbuf2[e % 2] = emms
            for dc in range(8):
                fmm = S.op("tensor", lambda t, dc=dc, tr=tr: t.matmul(
                    self.PSB[dc][:, :], b2t[:, 128 * dc:128 * (dc + 1)],
                    wr8[:, 512 * tr:512 * (tr + 1)],
                    start=False, stop=True),
                    deps=mm_all + [tok_b2t] + wr_toks)
                tl = S.dma("sync", A[:, 8704:9216],
                           P["h_xacc"].ap()[128 * dc:128 * (dc + 1),
                                            to:to + 512],
                           deps=self.tok_attn_acc + out_toks[-1:])
                ad = S.op("vector", lambda v, dc=dc: v.tensor_add(
                    A[:, 8704:9216], A[:, 8704:9216], self.PSB[dc][:, :]),
                    deps=[tl, fmm])
                self.psb_free[dc] = [ad]
                sp = S.dma("sync",
                           P["out_fm"].ap()[128 * dc:128 * (dc + 1),
                                            to:to + 512],
                           A[:, 8704:9216], deps=[ad])
                out_toks.append(sp)
        self.final_deps += out_toks


_BUILD_CACHE = {}


def _get_nc():
    if "nc" not in _BUILD_CACHE:
        _BUILD_CACHE["nc"] = Builder().build()
    return _BUILD_CACHE["nc"]


def kernel(**inputs):
    w, xs = prep_inputs(inputs)
    nc = _get_nc()
    in_maps = [dict(w, x_fm=xs[c]) for c in range(8)]
    res = run_bass_kernel_spmd(nc, in_maps, core_ids=list(range(8)))
    out = np.stack([r["out_fm"].T for r in res.results], axis=0)
    return np.ascontiguousarray(out.astype(np.float32))
